# version 16
# baseline (speedup 1.0000x reference)
import numpy as np
import concourse.bass as bass
import concourse.mybir as mybir
from concourse.bass_utils import run_bass_kernel_spmd

F32 = mybir.dt.float32
BF16 = mybir.dt.bfloat16
AF = mybir.ActivationFunctionType
ALU = mybir.AluOpType
AX = mybir.AxisListType

D = 1024
KC = 8
NCORE = 8
TOK = 2048
NSEQ = 4
POS = 512


class Prog:

    def __init__(self):
        self.nc = bass.Bass("TRN2", target_bir_lowering=False)
        self._ctxs = []
        self.sems = {}
        self.cnt = {}

    def sem(self, name):
        if name not in self.sems:
            self.sems[name] = self._enter(self.nc.semaphore(name))
            self.cnt[name] = 0
        return self.sems[name]

    def _enter(self, cm):
        v = cm.__enter__()
        self._ctxs.append(cm)
        return v

    def _uname(self, name):
        if not hasattr(self, "_names"):
            self._names = {}
        n = self._names.get(name, 0)
        self._names[name] = n + 1
        return name if n == 0 else f"{name}__{n}"

    def sb(self, name, shape, dt):
        return self._enter(self.nc.sbuf_tensor(self._uname(name), shape, dt))

    def psum(self, name, shape, dt=F32):
        return self._enter(self.nc.psum_tensor(self._uname(name), shape, dt))

    def inc(self, instr, name, n=None):
        if n is None:
            n = 16 if name.startswith("d") else 1
        instr.then_inc(self.sem(name), n)
        self.cnt[name] += n
        return self.cnt[name]

    def wait(self, eng, name, val):
        if val > 0:
            eng.wait_ge(self.sem(name), val)

    def dma(self, eng, name, out, in_):
        c = self.inc(eng.dma_start(out, in_), name)
        eng.wait_ge(self.sem(name), c)
        return c

    def dma_t(self, eng, name, out, in_):
        c = self.inc(eng.dma_start_transpose(out, in_), name)
        eng.wait_ge(self.sem(name), c)
        return c

    def dma_nb(self, eng, name, out, in_):
        return self.inc(eng.dma_start(out, in_), name)

    def dma_tnb(self, eng, name, out, in_):
        return self.inc(eng.dma_start_transpose(out, in_), name)

    def dbar(self, eng, name):
        eng.wait_ge(self.sem(name), self.cnt[name])
        return self.cnt[name]

    def block(self):
        for s in ("dS", "dG", "dA", "cT", "cV", "cS", "cG", "cC"):
            self.sem(s)
        return self._enter(self.nc.Block())

    def scope(self):
        P = self

        class _Scope:
            def __enter__(s):
                s.mark = len(P._ctxs)
                return s

            def __exit__(s, *a):
                for cm in reversed(P._ctxs[s.mark:]):
                    cm.__exit__(None, None, None)
                del P._ctxs[s.mark:]

        return _Scope()

    def barrier(self):
        nc = self.nc
        engs = [nc.tensor, nc.vector, nc.scalar, nc.gpsimd, nc.sync]
        snap = [(n, self.cnt[n]) for n in ("cT", "cV", "cS", "cG", "cC", "dS", "dG")]
        for e in engs:
            for n, v in snap:
                if v > 0:
                    e.wait_ge(self.sem(n), v)

    def close(self):
        for cm in reversed(self._ctxs):
            cm.__exit__(None, None, None)
        self._ctxs = []


def ln_aux(P, ntok=TOK, psums=None):
    if psums is None:
        psums = {
            "pS": P.psum("ln_pS", [1, 512], F32),
            "pQ": P.psum("ln_pQ", [1, 512], F32),
            "pA": P.psum("ln_pA", [128, 512], F32),
            "pB": P.psum("ln_pB", [128, 512], F32),
        }
    return {
        **psums,
        "ones_bf": P.sb("ln_ones_bf", [128, 1], BF16),
        "ones_f": P.sb("ln_ones_f", [1, 128], F32),
        "xf32_t": P.sb("ln_xf32_t", [128, KC * 512], F32),
        "xbf": P.sb("ln_xbf", [128, 512], BF16),
        "sq": P.sb("ln_sq", [128, 512], BF16),
        "arow": P.sb("ln_arow", [1, 512], F32),
        "brow": P.sb("ln_brow", [1, 512], F32),
        "mrow": P.sb("ln_mrow", [1, 512], F32),
        "init": False,
    }


def emit_ln_init(P, aux):
    nc = P.nc
    P.inc(nc.vector.memset(aux["ones_bf"][:], 1.0), "cV")
    P.inc(nc.vector.memset(aux["ones_f"][:], 1.0), "cV")
    aux["init"] = True


def emit_layernorm(P, xT_dram, xhat_sb, aux, ntok=TOK, eps=1e-5, keep_x=None):
    nc = P.nc
    assert aux["init"]
    ones_bf, ones_f = aux["ones_bf"], aux["ones_f"]
    xt, xbf, sq = aux["xf32_t"], aux["xbf"], aux["sq"]
    pS, pQ, pA, pB = aux["pS"], aux["pQ"], aux["pA"], aux["pB"]
    arow, brow, mrow = aux["arow"], aux["brow"], aux["mrow"]
    vrow = brow

    for tt in range(ntok // 512):
        prev_apply = P.cnt["cV"]
        for kc in range(KC):
            P.wait(nc.sync, "cV", prev_apply)
            P.dma(nc.sync, "dS", xt[:, kc * 512:(kc + 1) * 512],
                xT_dram[kc * 128:(kc + 1) * 128, tt * 512:(tt + 1) * 512])
        ld = P.cnt["dS"]
        prev_rows = aux.get("rows_read", 0)
        for kc in range(KC):
            P.wait(nc.vector, "dS", ld)
            P.wait(nc.vector, "cT", aux.get("stats_mm", 0))
            c1 = P.inc(nc.vector.tensor_copy(xbf[:], xt[:, kc * 512:(kc + 1) * 512]), "cV")
            P.wait(nc.scalar, "dS", ld)
            P.wait(nc.scalar, "cT", aux.get("stats_mm", 0))
            c2 = P.inc(nc.scalar.activation(sq[:], xt[:, kc * 512:(kc + 1) * 512], AF.Square), "cS")
            P.wait(nc.tensor, "cV", c1)
            P.wait(nc.tensor, "cS", c2)
            P.wait(nc.tensor, "cV", prev_rows)
            nc.tensor.matmul(pS[:, :], ones_bf[:, :], xbf[:, :],
                             start=(kc == 0), stop=(kc == KC - 1))
            aux["stats_mm"] = P.inc(nc.tensor.matmul(pQ[:, :], ones_bf[:, :], sq[:, :],
                                                     start=(kc == 0), stop=(kc == KC - 1)), "cT")
        P.wait(nc.vector, "cT", aux["stats_mm"])
        P.inc(nc.vector.tensor_scalar(mrow[:], pS[:, :], 1.0 / D, None, ALU.mult), "cV")
        P.inc(nc.vector.tensor_scalar(vrow[:], pQ[:, :], 1.0 / D, None, ALU.mult), "cV")
        aux["rows_read"] = P.cnt["cV"]
        P.wait(nc.vector, "cV", P.cnt["cV"])
        P.inc(nc.vector.tensor_tensor(arow[:], mrow[:], mrow[:], ALU.mult), "cV")
        P.wait(nc.vector, "cV", P.cnt["cV"])
        P.inc(nc.vector.tensor_tensor(vrow[:], arow[:], vrow[:], ALU.subtract), "cV")
        P.wait(nc.scalar, "cV", P.cnt["cV"])
        P.inc(nc.scalar.activation(vrow[:], vrow[:], AF.Ln, bias=np.float32(eps), scale=np.float32(-1.0)), "cS")
        P.wait(nc.scalar, "cS", P.cnt["cS"])
        sva = P.inc(nc.scalar.activation(arow[:], vrow[:], AF.Exp, scale=np.float32(-0.5)), "cS")
        P.wait(nc.vector, "cS", sva)
        P.inc(nc.vector.scalar_tensor_tensor(brow[:], mrow[:], -1.0, arow[:],
                                             ALU.mult, ALU.mult), "cV")
        rows_done = P.cnt["cV"]
        P.wait(nc.tensor, "cV", rows_done)
        nc.tensor.matmul(pA[:, :], ones_f[:, :], arow[:, :], start=True, stop=True)
        bcast_mm = P.inc(nc.tensor.matmul(pB[:, :], ones_f[:, :], brow[:, :],
                                          start=True, stop=True), "cT")
        P.wait(nc.vector, "cT", bcast_mm)
        for kc in range(KC):
            xsl = xt[:, kc * 512:(kc + 1) * 512]
            osl = xhat_sb[:, kc * ntok + tt * 512: kc * ntok + (tt + 1) * 512]
            P.wait(nc.vector, "cV", P.cnt["cV"])
            P.inc(nc.vector.tensor_tensor(xbf[:], xsl, pA[:, :], ALU.mult), "cV")
            P.wait(nc.vector, "cV", P.cnt["cV"])
            P.inc(nc.vector.tensor_tensor(osl, xbf[:], pB[:, :], ALU.add), "cV")
            if keep_x is not None:
                P.inc(nc.vector.tensor_copy(
                    keep_x[:, kc * ntok + tt * 512: kc * ntok + (tt + 1) * 512], xsl), "cV")
    return P.cnt["cV"]



def build_l0ffn(P):
    nc = P.nc
    xT = nc.dram_tensor("xT", [D, TOK], F32, kind="ExternalInput").ap()
    wg = nc.dram_tensor("wg", [D, 3 * D], F32, kind="ExternalInput").ap()
    ug = nc.dram_tensor("ug", [D, 3 * D], F32, kind="ExternalInput").ap()
    wo = nc.dram_tensor("wo", [D, D], F32, kind="ExternalInput").ap()
    uo = nc.dram_tensor("uo", [D, D], F32, kind="ExternalInput").ap()
    bg = nc.dram_tensor("bg", [3 * D], F32, kind="ExternalInput").ap()
    bo = nc.dram_tensor("bo", [D], F32, kind="ExternalInput").ap()
    inc_ = nc.dram_tensor("inc_", [D], F32, kind="ExternalInput").ap()
    yT = nc.dram_tensor("yT", [D, TOK], F32, kind="ExternalOutput").ap()

    wg_sb = P.sb("wg_sb", [128, KC * 3 * D], BF16)
    ug_sb = P.sb("ug_sb", [128, KC * 3 * D], BF16)
    big = P.sb("big", [128, 16384], BF16)
    gf = big[:, 0:KC * POS]
    gi = big[:, KC * POS:2 * KC * POS]
    gg = big[:, 2 * KC * POS:3 * KC * POS]
    xhat = P.sb("xhat", [128, KC * TOK], BF16)
    c_sb = P.sb("c_sb", [128, KC * POS], BF16)
    tmp1 = P.sb("tmp1", [128, POS], BF16)
    tmp2 = P.sb("tmp2", [128, POS], BF16)
    hout = P.sb("hout", [128, 512], F32)
    res_t = P.sb("res_t", [128, 512], F32)
    yout = P.sb("yout", [128, 512], F32)
    bgs = P.sb("bgs", [128, 24], F32)
    bos = P.sb("bos", [128, KC], F32)
    incs = P.sb("incs", [128, KC], F32)
    aux = ln_aux(P)
    pg = [P.psum(f"pg{i}", [128, 512], F32) for i in range(4)]

    P.block()
    emit_ln_init(P, aux)
    for kc in range(KC):
        P.dma(nc.gpsimd, "dG", wg_sb[:, kc * 3 * D:(kc + 1) * 3 * D],
                                  wg[kc * 128:(kc + 1) * 128, :])
        P.dma(nc.gpsimd, "dG", ug_sb[:, kc * 3 * D:(kc + 1) * 3 * D],
                                  ug[kc * 128:(kc + 1) * 128, :])
    for j in range(24):
        P.dma(nc.sync, "dS", bgs[:, j:j + 1], bg[j * 128:(j + 1) * 128].unsqueeze(-1))
    for j in range(KC):
        P.dma(nc.sync, "dS", bos[:, j:j + 1], bo[j * 128:(j + 1) * 128].unsqueeze(-1))
        P.dma(nc.sync, "dS", incs[:, j:j + 1], inc_[j * 128:(j + 1) * 128].unsqueeze(-1))
    w_done = P.cnt["dG"]
    const_done = P.cnt["dS"]
    nc.sync.wait_ge(P.sem("dS"), const_done)

    emit_layernorm(P, xT, xhat, aux)

    P.wait(nc.vector, "dS", const_done)
    for kc in range(KC):
        P.inc(nc.vector.tensor_copy(
            c_sb[:, kc * POS:(kc + 1) * POS],
            incs[:, kc:kc + 1].broadcast_to([128, POS])), "cV")
    c_ready = P.cnt["cV"]

    P.wait(nc.tensor, "dG", w_done)
    gate_view = {0: gf, 1: gi, 2: gg}
    act_fn = {0: AF.Sigmoid, 1: AF.Sigmoid, 2: AF.Tanh}
    pclean = [0, 0, 0, 0]
    for t in range(4):
        P.wait(nc.tensor, "cV", c_ready)
        for gate in range(3):
            for m in range(KC):
                bi = (gate * KC + m) % 4
                bank = pg[bi]
                P.wait(nc.tensor, "cS", pclean[bi])
                for kc in range(KC):
                    nc.tensor.matmul(
                        bank[:, :],
                        wg_sb[:, kc * 3 * D + gate * D + m * 128: kc * 3 * D + gate * D + (m + 1) * 128],
                        xhat[:, kc * TOK + t * POS: kc * TOK + (t + 1) * POS],
                        start=(kc == 0), stop=False)
                for kc in range(KC):
                    last = P.inc(nc.tensor.matmul(
                        bank[:, :],
                        ug_sb[:, kc * 3 * D + gate * D + m * 128: kc * 3 * D + gate * D + (m + 1) * 128],
                        c_sb[:, kc * POS:(kc + 1) * POS],
                        start=False, stop=(kc == KC - 1)), "cT")
                P.wait(nc.scalar, "cT", last)
                pclean[bi] = P.inc(nc.scalar.activation(
                    gate_view[gate][:, m * POS:(m + 1) * POS], bank[:, :],
                    act_fn[gate], bias=bgs[:, gate * KC + m: gate * KC + m + 1]), "cS")
        gates_done = P.cnt["cS"]
        all_mm = P.cnt["cT"]
        P.wait(nc.vector, "cS", gates_done)
        P.wait(nc.vector, "cT", all_mm)
        for m in range(KC):
            sl = slice(m * POS, (m + 1) * POS)
            P.wait(nc.vector, "cV", P.cnt["cV"])
            P.inc(nc.vector.tensor_tensor(tmp1[:], gi[:, sl], gg[:, sl], ALU.mult), "cV")
            P.wait(nc.vector, "cV", P.cnt["cV"])
            P.inc(nc.vector.tensor_tensor(tmp2[:], gf[:, sl], c_sb[:, sl], ALU.mult), "cV")
            P.wait(nc.vector, "cV", P.cnt["cV"])
            P.inc(nc.vector.tensor_tensor(c_sb[:, sl], tmp1[:], tmp2[:], ALU.add), "cV")
        c_ready = P.cnt["cV"]

    P.wait(nc.gpsimd, "cV", c_ready)
    wo_sb = big[:, 0:KC * D]
    uo_sb = big[:, KC * D:2 * KC * D]
    for kc in range(KC):
        P.dma(nc.gpsimd, "dG", wo_sb[:, kc * D:(kc + 1) * D],
                                  wo[kc * 128:(kc + 1) * 128, :])
        P.dma(nc.gpsimd, "dG", uo_sb[:, kc * D:(kc + 1) * D],
                                  uo[kc * 128:(kc + 1) * 128, :])
    wo_done = P.cnt["dG"]

    P.wait(nc.tensor, "dG", wo_done)
    res_ld = P.cnt["dS"]
    for t in range(4):
        for m in range(KC):
            bi = (t * KC + m) % 4
            bank = pg[bi]
            P.wait(nc.sync, "cV", P.cnt["cV"])
            P.wait(nc.sync, "dS", res_ld)
            res_ld = P.dma(nc.sync, "dS", res_t[:], xT[m * 128:(m + 1) * 128, t * POS:(t + 1) * POS])
            P.wait(nc.tensor, "cS", pclean[bi])
            for kc in range(KC):
                nc.tensor.matmul(
                    bank[:, :],
                    wo_sb[:, kc * D + m * 128: kc * D + (m + 1) * 128],
                    xhat[:, kc * TOK + t * POS: kc * TOK + (t + 1) * POS],
                    start=(kc == 0), stop=False)
            for kc in range(KC):
                last = P.inc(nc.tensor.matmul(
                    bank[:, :],
                    uo_sb[:, kc * D + m * 128: kc * D + (m + 1) * 128],
                    c_sb[:, kc * POS:(kc + 1) * POS],
                    start=False, stop=(kc == KC - 1)), "cT")
            P.wait(nc.scalar, "cT", last)
            P.wait(nc.scalar, "cV", P.cnt["cV"])
            pclean[bi] = P.inc(nc.scalar.activation(
                hout[:], bank[:, :], AF.Tanh, bias=bos[:, m:m + 1]), "cS")
            P.wait(nc.vector, "cS", pclean[bi])
            P.wait(nc.vector, "dS", res_ld)
            cv = P.inc(nc.vector.scalar_tensor_tensor(
                yout[:], hout[:], 0.5, res_t[:], ALU.mult, ALU.add), "cV")
            P.wait(nc.sync, "cV", cv)
            P.dma(nc.sync, "dS", yT[m * 128:(m + 1) * 128, t * POS:(t + 1) * POS],
                                    yout[:])
    nc.sync.wait_ge(P.sem("dS"), P.cnt["dS"])
    return P



def shard_xT(x):
    out = []
    for i in range(NCORE):
        sl = x[:, i * POS:(i + 1) * POS, :]
        out.append(np.ascontiguousarray(sl.transpose(2, 0, 1).reshape(D, TOK)))
    return out


def unshard_xT(shards):
    x = np.empty((NSEQ, NCORE * POS, D), dtype=np.float32)
    for i, sh in enumerate(shards):
        x[:, i * POS:(i + 1) * POS, :] = sh.reshape(D, NSEQ, POS).transpose(1, 2, 0)
    return x


def run_l0ffn(xT_shards, W):
    P = Prog()
    build_l0ffn(P)
    P.close()
    wg = np.ascontiguousarray(np.concatenate([W["wf"].T, W["wi"].T, W["wc"].T], axis=1))
    ug = np.ascontiguousarray(np.concatenate([W["uf"].T, W["ui"].T, W["uc"].T], axis=1))
    wo = np.ascontiguousarray(W["wo"].T)
    uo = np.ascontiguousarray(W["uo"].T)
    bgc = np.ascontiguousarray(np.concatenate([W["bf"], W["bi"], W["bc"]]))
    in_maps = [{"xT": s, "wg": wg, "ug": ug, "wo": wo, "uo": uo,
                "bg": bgc, "bo": np.ascontiguousarray(W["bo"]),
                "inc_": np.ascontiguousarray(W["in_c"])} for s in xT_shards]
    res = run_bass_kernel_spmd(P.nc, in_maps, core_ids=list(range(NCORE)))
    return [r["yT"] for r in res.results]


def kernel(**inputs):
    x = np.asarray(inputs["L0_attn_x"], dtype=np.float32)
    W0 = {k: np.asarray(inputs[f"L0_ffn_{k}"], dtype=np.float32)
          for k in ["wf", "uf", "bf", "wi", "ui", "bi", "wc", "uc", "bc", "wo", "uo", "bo", "in_c"]}
    W1a = np.asarray(inputs["L0_attn_W1"], dtype=np.float32)
    b1a = np.asarray(inputs["L0_attn_b1"], dtype=np.float32)
    W2a = np.asarray(inputs["L0_attn_W2"], dtype=np.float32)
    b2a = np.asarray(inputs["L0_attn_b2"], dtype=np.float32)
    W1f = {k: np.asarray(inputs[f"L1_ffn_{k}"], dtype=np.float32)
           for k in ["w_gate", "w_up", "w_down"]}
    W1t = {k: np.asarray(inputs[f"L1_attn_{k}"], dtype=np.float32)
           for k in ["wq", "wk", "wv", "wo", "q_norm_w", "k_norm_w"]}

    shards = shard_xT(x)
    shards = run_l0ffn(shards, W0)
    shards = run_l0attn_all(shards, W1a, b1a, W2a, b2a)
    shards = run_l0ffn(shards, W0)
    qw = run_l1prep(W1f)
    shards = run_l1ffn(shards, qw)
    x_mid = unshard_xT(shards)
    shards = run_l1attn(x_mid, W1t)
    shards = run_l1ffn(shards, qw)
    return unshard_xT(shards)



def emit_quant(P, x_ap, out_ap, scr, np_, nb, after=None):
    nc = P.nc
    a, t, u = scr["a"], scr["t"], scr["u"]
    bm, scale, rsc = scr["bm"], scr["scale"], scr["rsc"]
    F = nb * 16
    xv = x_ap.rearrange("p (b k) -> p b k", k=16)
    if after:
        for s, v in after:
            P.wait(nc.vector, s, v)
    P.wait(nc.vector, "cV", scr.get("busy", 0))
    P.wait(nc.vector, "cS", scr.get("busy_s", 0))
    P.inc(nc.vector.tensor_reduce(bm[:np_, :nb], xv, AX.X, ALU.max,
                                  apply_absolute_value=True), "cV")
    P.wait(nc.vector, "cV", P.cnt["cV"])
    P.inc(nc.vector.tensor_scalar(scale[:np_, :nb], bm[:np_, :nb],
                                  1.0 / 6.0, 1e-12, ALU.mult, ALU.max), "cV")
    P.wait(nc.vector, "cV", P.cnt["cV"])
    P.inc(nc.vector.reciprocal(rsc[:np_, :nb], scale[:np_, :nb]), "cV")
    P.wait(nc.vector, "cV", P.cnt["cV"])
    rscb = rsc[:np_, :nb].unsqueeze(-1).broadcast_to([np_, nb, 16])
    scaleb = scale[:np_, :nb].unsqueeze(-1).broadcast_to([np_, nb, 16])
    tv16 = t[:np_, :F].rearrange("p (b k) -> p b k", k=16)
    P.inc(nc.vector.tensor_tensor(tv16, xv, rscb, ALU.mult), "cV")
    P.wait(nc.scalar, "cV", P.cnt["cV"])
    ab = P.inc(nc.scalar.activation(a[:np_, :F], t[:np_, :F], AF.Abs), "cS")
    P.wait(nc.vector, "cS", ab)
    P.inc(nc.vector.tensor_scalar(t[:np_, :F], a[:np_, :F], 0.25, None, ALU.is_gt), "cV")
    for th in (0.75, 1.25, 1.75):
        P.wait(nc.vector, "cV", P.cnt["cV"])
        P.inc(nc.vector.scalar_tensor_tensor(t[:np_, :F], a[:np_, :F], th,
                                             t[:np_, :F], ALU.is_gt, ALU.add), "cV")
    P.wait(nc.vector, "cV", P.cnt["cV"])
    P.inc(nc.vector.tensor_scalar(u[:np_, :F], a[:np_, :F], 2.5, None, ALU.is_gt), "cV")
    P.wait(nc.vector, "cV", P.cnt["cV"])
    P.inc(nc.vector.scalar_tensor_tensor(u[:np_, :F], a[:np_, :F], 3.5,
                                         u[:np_, :F], ALU.is_gt, ALU.add), "cV")
    P.wait(nc.vector, "cV", P.cnt["cV"])
    P.inc(nc.vector.scalar_tensor_tensor(t[:np_, :F], t[:np_, :F], 0.5,
                                         u[:np_, :F], ALU.mult, ALU.add), "cV")
    P.wait(nc.vector, "cV", P.cnt["cV"])
    P.inc(nc.vector.tensor_scalar(a[:np_, :F], a[:np_, :F], 5.0, 2.0,
                                  ALU.is_gt, ALU.mult), "cV")
    P.wait(nc.gpsimd, "cV", P.cnt["cV"])
    P.wait(nc.gpsimd, "cG", scr.get("busy_g", 0))
    P.inc(nc.gpsimd.tensor_tensor(t[:np_, :F], t[:np_, :F], a[:np_, :F], ALU.add), "cG")
    if after:
        for s, v in after:
            P.wait(nc.scalar, s, v)
    P.wait(nc.scalar, "cV", P.cnt["cV"])
    sg = P.inc(nc.scalar.activation(u[:np_, :F], x_ap, AF.Sign), "cS")
    P.wait(nc.gpsimd, "cS", sg)
    P.wait(nc.gpsimd, "cG", P.cnt["cG"])
    gg_ = P.inc(nc.gpsimd.tensor_tensor(t[:np_, :F], t[:np_, :F], u[:np_, :F], ALU.mult), "cG")
    scr["busy_g"] = gg_
    P.wait(nc.vector, "cG", gg_)
    tv = t[:np_, :F].rearrange("p (b k) -> p b k", k=16)
    ov = out_ap.rearrange("p (b k) -> p b k", k=16)
    done = P.inc(nc.vector.tensor_tensor(ov, tv, scaleb, ALU.mult), "cV")
    scr["busy"] = done
    scr["busy_s"] = sg
    return done



def build_l1prep(P):
    nc = P.nc
    NW = 2816 // NCORE
    ND = D // NCORE
    wgs = nc.dram_tensor("wgs", [NW, D], F32, kind="ExternalInput").ap()
    wus = nc.dram_tensor("wus", [NW, D], F32, kind="ExternalInput").ap()
    wds = nc.dram_tensor("wds", [ND, 2816], F32, kind="ExternalInput").ap()
    qgs = nc.dram_tensor("qgs", [NW, D], BF16, kind="ExternalOutput").ap()
    qus = nc.dram_tensor("qus", [NW, D], BF16, kind="ExternalOutput").ap()
    qds = nc.dram_tensor("qds", [ND, 2816], BF16, kind="ExternalOutput").ap()
    xin = P.sb("xin", [128, 2816], F32)
    qout = P.sb("qout", [128, 2816], BF16)
    scr = {
        "a": P.sb("qa", [128, 2816], F32),
        "t": P.sb("qt", [128, 2816], F32),
        "u": P.sb("qu", [128, 2816], F32),
        "bm": P.sb("qbm", [128, 176], F32),
        "scale": P.sb("qscale", [128, 176], F32),
        "rsc": P.sb("qrsc", [128, 176], F32),
    }
    P.block()
    jobs = []
    for r0 in range(0, NW, 128):
        npart = min(128, NW - r0)
        jobs.append((wgs, qgs, r0, npart, D))
        jobs.append((wus, qus, r0, npart, D))
    for r0 in range(0, ND, 128):
        npart = min(128, ND - r0)
        jobs.append((wds, qds, r0, npart, 2816))
    st_done = 0
    for (src, dst, r0, npart, F) in jobs:
        P.wait(nc.sync, "cV", scr.get("busy", 0))
        P.wait(nc.sync, "cS", scr.get("busy_s", 0))
        P.wait(nc.sync, "dS", st_done)
        ld = P.dma(nc.sync, "dS", xin[:npart, :F], src[r0:r0 + npart, :])
        emit_quant(P, xin[:npart, :F], qout[:npart, :F], scr, npart, F // 16,
                   after=[("dS", ld)])
        P.wait(nc.sync, "cV", P.cnt["cV"])
        st_done = P.dma(nc.sync, "dS", dst[r0:r0 + npart, :], qout[:npart, :F])
    nc.sync.wait_ge(P.sem("dS"), P.cnt["dS"])
    return P


def run_l1prep(W1f):
    P = Prog()
    build_l1prep(P)
    P.close()
    NW = 2816 // NCORE
    ND = D // NCORE
    in_maps = []
    for c in range(NCORE):
        in_maps.append({
            "wgs": np.ascontiguousarray(W1f["w_gate"][c * NW:(c + 1) * NW]),
            "wus": np.ascontiguousarray(W1f["w_up"][c * NW:(c + 1) * NW]),
            "wds": np.ascontiguousarray(W1f["w_down"][c * ND:(c + 1) * ND]),
        })
    res = run_bass_kernel_spmd(P.nc, in_maps, core_ids=list(range(NCORE)))
    qg = np.concatenate([r["qgs"] for r in res.results], axis=0)
    qu = np.concatenate([r["qus"] for r in res.results], axis=0)
    qd = np.concatenate([r["qds"] for r in res.results], axis=0)
    return (np.ascontiguousarray(qg.T), np.ascontiguousarray(qu.T),
            np.ascontiguousarray(qd.T))



NSL = [(0, 512), (512, 512), (1024, 512), (1536, 512), (2048, 512), (2560, 256)]
HTT = 256


def build_l1ffn(P):
    nc = P.nc
    x_tok = nc.dram_tensor("x_tok", [TOK, D], F32, kind="ExternalInput").ap()
    xT = nc.dram_tensor("xT", [D, TOK], F32, kind="ExternalInput").ap()
    wgq = nc.dram_tensor("wgq", [D, 2816], BF16, kind="ExternalInput").ap()
    wuq = nc.dram_tensor("wuq", [D, 2816], BF16, kind="ExternalInput").ap()
    wdq = nc.dram_tensor("wdq", [2816, D], BF16, kind="ExternalInput").ap()
    yT = nc.dram_tensor("yT", [D, TOK], F32, kind="ExternalOutput").ap()
    xq_d = nc.dram_tensor("xq_d", [TOK, D], BF16)
    hq_d = nc.dram_tensor("hq_d", [TOK, 2816], BF16)

    wbig = P.sb("wbig", [128, 45056], BF16)
    wgq_sb = wbig[:, 0:KC * 2816]
    wuq_sb = wbig[:, KC * 2816:2 * KC * 2816]
    wdq_sb = wbig[:, 0:22 * D]
    xqT = P.sb("xqT", [128, KC * TOK], BF16)
    hqT = P.sb("hqT", [128, 22 * HTT], BF16)
    xtc = P.sb("xtc", [128, D], F32)
    sqc = P.sb("sqc", [128, D], BF16)
    xqc = P.sb("xqc", [128, D], BF16)
    h_sb = P.sb("h_sb", [128, 2816], F32)
    hq_sb = P.sb("hq_sb", [128, 2816], BF16)
    sil = P.sb("sil", [128, 512], F32)
    res_t = P.sb("res_t", [128, 512], F32)
    yout = P.sb("yout", [128, 512], F32)
    scr = {
        "a": P.sb("qa", [128, 2816], F32), "t": P.sb("qt", [128, 2816], F32),
        "u": P.sb("qu", [128, 2816], F32), "bm": P.sb("qbm", [128, 176], F32),
        "scale": P.sb("qscale", [128, 176], F32), "rsc": P.sb("qrsc", [128, 176], F32),
    }
    cs1 = P.sb("cs1", [128, 1], F32)
    cs2 = P.sb("cs2", [128, 1], F32)
    mean = P.sb("mean", [128, 1], F32)
    rstd = P.sb("rstd", [128, 1], F32)
    bcol = P.sb("bcol", [128, 1], F32)
    pb = [P.psum(f"pb{i}", [128, 512], F32) for i in range(6)]

    P.block()
    for kc in range(KC):
        P.dma(nc.gpsimd, "dG", wgq_sb[:, kc * 2816:(kc + 1) * 2816],
                                  wgq[kc * 128:(kc + 1) * 128, :])
        P.dma(nc.gpsimd, "dG", wuq_sb[:, kc * 2816:(kc + 1) * 2816],
                                  wuq[kc * 128:(kc + 1) * 128, :])
    wA_done = P.cnt["dG"]

    xq_st = 0
    for i in range(TOK // 128):
        P.wait(nc.sync, "cV", scr.get("busy", 0))
        P.wait(nc.sync, "cS", scr.get("busy_s", 0))
        ld = P.dma(nc.sync, "dS", xtc[:], x_tok[i * 128:(i + 1) * 128, :])
        P.wait(nc.vector, "dS", ld)
        P.inc(nc.vector.tensor_reduce(cs1[:], xtc[:], AX.X, ALU.add), "cV")
        P.wait(nc.scalar, "dS", ld)
        sq_done = P.inc(nc.scalar.activation(sqc[:], xtc[:], AF.Square), "cS")
        P.wait(nc.vector, "cS", sq_done)
        P.inc(nc.vector.tensor_reduce(cs2[:], sqc[:], AX.X, ALU.add), "cV")
        P.wait(nc.vector, "cV", P.cnt["cV"])
        P.inc(nc.vector.tensor_scalar(mean[:], cs1[:], 1.0 / D, None, ALU.mult), "cV")
        P.wait(nc.vector, "cV", P.cnt["cV"])
        P.inc(nc.vector.tensor_scalar(cs2[:], cs2[:], 1.0 / D, None, ALU.mult), "cV")
        P.wait(nc.vector, "cV", P.cnt["cV"])
        P.inc(nc.vector.tensor_tensor(cs1[:], mean[:], mean[:], ALU.mult), "cV")
        P.wait(nc.vector, "cV", P.cnt["cV"])
        P.inc(nc.vector.tensor_tensor(cs2[:], cs1[:], cs2[:], ALU.subtract), "cV")
        P.wait(nc.scalar, "cV", P.cnt["cV"])
        P.inc(nc.scalar.activation(cs2[:], cs2[:], AF.Ln,
                                   bias=np.float32(1e-5), scale=np.float32(-1.0)), "cS")
        P.wait(nc.scalar, "cS", P.cnt["cS"])
        sd = P.inc(nc.scalar.activation(rstd[:], cs2[:], AF.Exp, scale=np.float32(-0.5)), "cS")
        P.wait(nc.vector, "cS", sd)
        P.inc(nc.vector.scalar_tensor_tensor(bcol[:], mean[:], -1.0, rstd[:],
                                             ALU.mult, ALU.mult), "cV")
        P.wait(nc.vector, "cV", P.cnt["cV"])
        P.inc(nc.vector.tensor_scalar(xtc[:], xtc[:], rstd[:, 0:1], bcol[:, 0:1],
                                      ALU.mult, ALU.add), "cV")
        P.wait(nc.vector, "dG", xq_st)
        emit_quant(P, xtc[:, :], xqc[:, :], scr, 128, D // 16,
                   after=[("cV", P.cnt["cV"])])
        qv = P.cnt["cV"]
        P.wait(nc.gpsimd, "cV", qv)
        xq_st = P.dma(nc.gpsimd, "dG", xq_d[i * 128:(i + 1) * 128, :], xqc[:])
    P.wait(nc.sync, "dG", P.cnt["dG"])
    for kc in range(KC):
        P.dma_t(nc.sync, "dS", xqT[:, kc * TOK:(kc + 1) * TOK],
                                          xq_d[:, kc * 128:(kc + 1) * 128])
    xqT_done = P.cnt["dS"]

    P.wait(nc.tensor, "dS", xqT_done)
    P.wait(nc.tensor, "dG", wA_done)
    pclean_v = [0] * 6
    pclean_s = [0] * 6
    sil_busy = 0
    for i in range(TOK // 128):
        for si, (n0, ns) in enumerate(NSL):
            bg_, bu_ = pb[2 * (si % 3)], pb[2 * (si % 3) + 1]
            P.wait(nc.tensor, "cV", pclean_v[2 * (si % 3)])
            P.wait(nc.tensor, "cS", pclean_s[2 * (si % 3)])
            P.wait(nc.tensor, "cV", pclean_v[2 * (si % 3) + 1])
            P.wait(nc.tensor, "cS", pclean_s[2 * (si % 3) + 1])
            for kc in range(KC):
                nc.tensor.matmul(bg_[:, 0:ns],
                                 xqT[:, kc * TOK + i * 128: kc * TOK + (i + 1) * 128],
                                 wgq_sb[:, kc * 2816 + n0: kc * 2816 + n0 + ns],
                                 start=(kc == 0), stop=(kc == KC - 1))
            for kc in range(KC):
                mmu = P.inc(nc.tensor.matmul(bu_[:, 0:ns],
                                 xqT[:, kc * TOK + i * 128: kc * TOK + (i + 1) * 128],
                                 wuq_sb[:, kc * 2816 + n0: kc * 2816 + n0 + ns],
                                 start=(kc == 0), stop=(kc == KC - 1)), "cT")
            P.wait(nc.scalar, "cT", mmu)
            P.wait(nc.scalar, "cV", sil_busy)
            ssil = P.inc(nc.scalar.activation(sil[:, 0:ns], bg_[:, 0:ns], AF.Silu), "cS")
            pclean_s[2 * (si % 3)] = ssil
            pclean_s[2 * (si % 3) + 1] = ssil
            P.wait(nc.vector, "cS", ssil)
            sil_busy = P.inc(nc.vector.tensor_tensor(
                h_sb[:, n0:n0 + ns], sil[:, 0:ns], bu_[:, 0:ns], ALU.mult), "cV")
            pclean_v[2 * (si % 3)] = sil_busy
            pclean_v[2 * (si % 3) + 1] = sil_busy
        P.wait(nc.vector, "dG", P.cnt["dG"])
        emit_quant(P, h_sb[:, :], hq_sb[:, :], scr, 128, 176,
                   after=[("cV", P.cnt["cV"])])
        qv = P.cnt["cV"]
        P.wait(nc.gpsimd, "cV", qv)
        P.dma(nc.gpsimd, "dG", hq_d[i * 128:(i + 1) * 128, :], hq_sb[:])
    hq_all = P.cnt["dG"]

    P.wait(nc.gpsimd, "cT", P.cnt["cT"])
    for kk in range(22):
        P.dma(nc.gpsimd, "dG", wdq_sb[:, kk * D:(kk + 1) * D],
                                  wdq[kk * 128:(kk + 1) * 128, :])
    wB_done = P.cnt["dG"]
    P.wait(nc.sync, "dG", hq_all)
    res_busy = 0
    yst = 0
    for tt in range(TOK // HTT):
        P.wait(nc.sync, "cT", P.cnt["cT"])
        for kk in range(22):
            P.dma_t(nc.sync, "dS", hqT[:, kk * HTT:(kk + 1) * HTT],
                hq_d[tt * HTT:(tt + 1) * HTT, kk * 128:(kk + 1) * 128])
        hqT_ld = P.cnt["dS"]
        P.wait(nc.tensor, "dS", hqT_ld)
        P.wait(nc.tensor, "dG", wB_done)
        for m in range(KC):
            bi = m % 4
            bank = pb[bi]
            P.wait(nc.tensor, "cV", pclean_v[bi])
            P.wait(nc.tensor, "cS", pclean_s[bi])
            for kk in range(22):
                mm = P.inc(nc.tensor.matmul(bank[:, 0:HTT],
                                wdq_sb[:, kk * D + m * 128: kk * D + (m + 1) * 128],
                                hqT[:, kk * HTT:(kk + 1) * HTT],
                                start=(kk == 0), stop=(kk == 21)), "cT")
            P.wait(nc.sync, "cV", res_busy)
            rld = P.dma(nc.sync, "dS", res_t[:, 0:HTT], xT[m * 128:(m + 1) * 128, tt * HTT:(tt + 1) * HTT])
            P.wait(nc.vector, "cT", mm)
            P.wait(nc.vector, "dS", rld)
            P.wait(nc.vector, "dS", yst)
            cv = P.inc(nc.vector.scalar_tensor_tensor(
                yout[:, 0:HTT], bank[:, 0:HTT], 0.5, res_t[:, 0:HTT],
                ALU.mult, ALU.add), "cV")
            pclean_v[bi] = cv
            res_busy = cv
            P.wait(nc.sync, "cV", cv)
            yst = P.dma(nc.sync, "dS", yT[m * 128:(m + 1) * 128, tt * HTT:(tt + 1) * HTT], yout[:, 0:HTT])
    nc.sync.wait_ge(P.sem("dS"), P.cnt["dS"])
    return P


def run_l1ffn(xT_shards, qw):
    wgq_T, wuq_T, wdq_T = qw
    P = Prog()
    build_l1ffn(P)
    P.close()
    in_maps = []
    for s in xT_shards:
        x_tok = np.ascontiguousarray(s.T)
        in_maps.append({"x_tok": x_tok, "xT": s, "wgq": wgq_T, "wuq": wuq_T,
                        "wdq": wdq_T})
    res = run_bass_kernel_spmd(P.nc, in_maps, core_ids=list(range(NCORE)))
    return [r["yT"] for r in res.results]



SEQT = 640
NTK = NSEQ * SEQT
WIN = 256


def build_l1attn(P, dbg=False):
    nc = P.nc
    xkv = nc.dram_tensor("xkv", [D, NTK], F32, kind="ExternalInput").ap()
    wq = nc.dram_tensor("wq", [D, D], F32, kind="ExternalInput").ap()
    wk = nc.dram_tensor("wk", [D, D], F32, kind="ExternalInput").ap()
    wv = nc.dram_tensor("wv", [D, D], F32, kind="ExternalInput").ap()
    wo = nc.dram_tensor("wo", [D, D], F32, kind="ExternalInput").ap()
    qnw = nc.dram_tensor("qnw", [D], F32, kind="ExternalInput").ap()
    knw = nc.dram_tensor("knw", [D], F32, kind="ExternalInput").ap()
    bias0 = nc.dram_tensor("bias0", [128, 8 * WIN], BF16, kind="ExternalInput").ap()
    biasn = nc.dram_tensor("biasn", [128, 8 * WIN], BF16, kind="ExternalInput").ap()
    yT = nc.dram_tensor("yT", [D, TOK], F32, kind="ExternalOutput").ap()

    xhat = P.sb("xhat", [128, KC * NTK], BF16)
    attnT = xhat[:, 0:KC * TOK]
    pr_bf = xhat[:, KC * TOK:KC * TOK + 2048]
    prT = xhat[:, KC * TOK + 2048:KC * TOK + 4096]
    qnT = P.sb("qnT", [128, KC * TOK], BF16)
    knT = P.sb("knT", [128, KC * NTK], BF16)
    v_sb = P.sb("v_sb", [128, (NTK // 128) * D], BF16)
    w_sb = P.sb("w_sb", [128, KC * D], BF16)
    bias_sb = P.sb("bias_sb", [128, 2 * 8 * WIN], BF16)
    t1 = P.sb("t1", [128, 512], F32)
    rowr = P.sb("rowr", [128, 1], F32)
    rows_sb = P.sb("rows_sb", [128, 1], F32)
    ident = P.sb("ident", [128, 128], BF16)
    nw_sb = P.sb("nw_sb", [128, 2 * KC], F32)
    qsc = P.sb("qsc", [128, 512], BF16)

    ps = [P.psum(f"ps{i}", [128, 512], F32) for i in range(4)]
    pav = [P.psum(f"pav{i}", [128, 512], F32) for i in range(2)]
    tps = [P.psum("tps0", [128, 128], BF16), P.psum("tps1", [128, 128], BF16)]
    aux = ln_aux(P, psums={"pA": pav[0], "pB": pav[1],
                           "pS": ps[0][0:1, :], "pQ": ps[1][0:1, :]})
    rr_row = aux["mrow"]

    P.block()
    emit_ln_init(P, aux)
    idf = t1[:, 0:128]
    P.inc(nc.gpsimd.iota(idf, [[-1, 128]], base=0, channel_multiplier=1,
                         allow_small_or_imprecise_dtypes=True), "cG")
    P.wait(nc.vector, "cG", P.cnt["cG"])
    P.inc(nc.vector.tensor_scalar(ident[:], idf, 0.0, None, ALU.is_equal), "cV")
    id_done = P.cnt["cV"]
    P.dma(nc.sync, "dS", bias_sb[:, 0:8 * WIN], bias0)
    P.dma(nc.sync, "dS", bias_sb[:, 8 * WIN:16 * WIN], biasn)
    for j in range(KC):
        P.dma(nc.sync, "dS", nw_sb[:, j:j + 1], qnw[j * 128:(j + 1) * 128].unsqueeze(-1))
        P.dma(nc.sync, "dS", nw_sb[:, KC + j:KC + j + 1], knw[j * 128:(j + 1) * 128].unsqueeze(-1))
    const_ld = P.cnt["dS"]
    nc.sync.wait_ge(P.sem("dS"), const_ld)

    emit_layernorm(P, xkv, xhat, aux, ntok=NTK)
    if dbg:
        d_xh = nc.dram_tensor("d_xh", [128, KC * NTK], BF16, kind="ExternalOutput").ap()
        P.wait(nc.gpsimd, "cV", P.cnt["cV"])
        xh_dump = P.dma(nc.gpsimd, "dG", d_xh, xhat[:, :])
    else:
        xh_dump = 0
    P.wait(nc.tensor, "dG", xh_dump)

    own_cols = [(s * SEQT + 16, 512) for s in range(NSEQ)]

    def proj(dst, w_dram, cols, rms_w_col, ncol_dst):
        P.wait(nc.gpsimd, "cT", P.cnt["cT"])
        for kc_ in range(KC):
            P.dma(nc.gpsimd, "dG", w_sb[:, kc_ * D:(kc_ + 1) * D],
                                      w_dram[kc_ * 128:(kc_ + 1) * 128, :])
        wd = P.cnt["dG"]
        P.wait(nc.tensor, "dG", wd)
        for (c0, n, d0) in cols:
            for off in range(0, n, 512):
                ns = min(512, n - off)
                for m in range(KC):
                    bi = 2 + (m % 2)
                    bank = ps[bi]
                    P.wait(nc.tensor, "cV", P.cnt["cV"])
                    P.wait(nc.tensor, "cS", P.cnt["cS"])
                    for kc_ in range(KC):
                        mm = P.inc(nc.tensor.matmul(
                            bank[:, 0:ns],
                            w_sb[:, kc_ * D + m * 128:(kc_ * D + (m + 1) * 128)],
                            xhat[:, kc_ * NTK + c0 + off: kc_ * NTK + c0 + off + ns],
                            start=(kc_ == 0), stop=(kc_ == KC - 1)), "cT")
                    P.wait(nc.vector, "cT", mm)
                    P.inc(nc.vector.tensor_copy(
                        dst[:, m * ncol_dst + d0 + off: m * ncol_dst + d0 + off + ns],
                        bank[:, 0:ns]), "cV")
                if rms_w_col is not None:
                    evac = P.cnt["cV"]
                    for m in range(KC):
                        P.wait(nc.scalar, "cV", evac)
                        P.wait(nc.scalar, "cT", P.cnt["cT"])
                        sqd = P.inc(nc.scalar.activation(
                            qsc[:, 0:ns],
                            dst[:, m * ncol_dst + d0 + off: m * ncol_dst + d0 + off + ns],
                            AF.Square), "cS")
                        P.wait(nc.tensor, "cS", sqd)
                        P.wait(nc.tensor, "cV", P.cnt["cV"])
                        smm = P.inc(nc.tensor.matmul(ps[0][0:1, 0:ns], aux["ones_bf"][:, :],
                                                     qsc[:, 0:ns],
                                                     start=(m == 0), stop=(m == KC - 1)), "cT")
                    P.wait(nc.vector, "cT", smm)
                    P.inc(nc.vector.tensor_scalar(rr_row[:, 0:ns], ps[0][0:1, 0:ns],
                                                  1.0 / D, None, ALU.mult), "cV")
                    P.wait(nc.scalar, "cV", P.cnt["cV"])
                    P.inc(nc.scalar.activation(rr_row[:, 0:ns], rr_row[:, 0:ns],
                                               AF.Ln, bias=np.float32(1e-6)), "cS")
                    P.wait(nc.scalar, "cS", P.cnt["cS"])
                    sdd = P.inc(nc.scalar.activation(rr_row[:, 0:ns], rr_row[:, 0:ns],
                                                     AF.Exp, scale=np.float32(-0.5)), "cS")
                    rrd = P.cnt["cV"]
                    P.wait(nc.tensor, "cV", rrd)
                    P.wait(nc.tensor, "cS", sdd)
                    bc = P.inc(nc.tensor.matmul(pav[0][:, 0:ns], aux["ones_f"][:, :],
                                                rr_row[:, 0:ns], start=True, stop=True), "cT")
                    P.wait(nc.vector, "cT", bc)
                    for m in range(KC):
                        sl = dst[:, m * ncol_dst + d0 + off: m * ncol_dst + d0 + off + ns]
                        P.wait(nc.vector, "cV", P.cnt["cV"])
                        P.inc(nc.vector.tensor_tensor(t1[:, 0:ns], sl, pav[0][:, 0:ns],
                                                      ALU.mult), "cV")
                        P.wait(nc.vector, "cV", P.cnt["cV"])
                        P.inc(nc.vector.tensor_scalar(sl, t1[:, 0:ns],
                                                      nw_sb[:, rms_w_col + m:rms_w_col + m + 1],
                                                      None, ALU.mult), "cV")

    P.wait(nc.vector, "dS", const_ld)
    proj(knT, wk, [(0, NTK, 0)], KC, NTK)
    proj(qnT, wq, [(c, n, s * 512) for s, (c, n) in enumerate(own_cols)], 0, TOK)
    P.wait(nc.gpsimd, "cT", P.cnt["cT"])
    for kc_ in range(KC):
        P.dma(nc.gpsimd, "dG", w_sb[:, kc_ * D:(kc_ + 1) * D],
                                  wv[kc_ * 128:(kc_ + 1) * 128, :])
    wd = P.cnt["dG"]
    P.wait(nc.tensor, "dG", wd)
    for ch in range(NTK // 128):
        for half in range(2):
            bank = ps[2 + half]
            P.wait(nc.tensor, "cV", P.cnt["cV"])
            for kc_ in range(KC):
                mm = P.inc(nc.tensor.matmul(
                    bank[:, 0:512],
                    xhat[:, kc_ * NTK + ch * 128: kc_ * NTK + (ch + 1) * 128],
                    w_sb[:, kc_ * D + half * 512: kc_ * D + half * 512 + 512],
                    start=(kc_ == 0), stop=(kc_ == KC - 1)), "cT")
            P.wait(nc.vector, "cT", mm)
            P.inc(nc.vector.tensor_copy(
                v_sb[:, ch * D + half * 512: ch * D + half * 512 + 512],
                bank[:, 0:512]), "cV")
    v_done = P.cnt["cV"]

    P.wait(nc.tensor, "cV", max(v_done, id_done))
    INV = float(1.0 / np.sqrt(128.0))
    attn_done_v = 0
    for s in range(NSEQ):
        for qt in range(4):
            qcol = s * 512 + qt * 128
            kvcol = s * SEQT + qt * 128
            bvar = 0 if qt == 0 else 8 * WIN
            sc_mm = [0] * 4
            for h in range(8):
                bank = ps[h // 2]
                bo_ = (h % 2) * WIN
                if h % 2 == 0:
                    P.wait(nc.tensor, "cV", P.cnt["cV"])
                sc_mm[h // 2] = P.inc(nc.tensor.matmul(
                    bank[:, bo_:bo_ + WIN],
                    qnT[:, h * TOK + qcol: h * TOK + qcol + 128],
                    knT[:, h * NTK + kvcol: h * NTK + kvcol + WIN],
                    start=True, stop=True), "cT")
            for h in range(8):
                bank = ps[h // 2]
                bo_ = (h % 2) * WIN
                P.wait(nc.vector, "cT", sc_mm[h // 2])
                P.wait(nc.vector, "cS", P.cnt["cS"])
                P.inc(nc.vector.scalar_tensor_tensor(
                    t1[:, 0:WIN], bank[:, bo_:bo_ + WIN], INV,
                    bias_sb[:, bvar + h * WIN: bvar + (h + 1) * WIN],
                    ALU.mult, ALU.add), "cV")
                P.wait(nc.scalar, "cV", P.cnt["cV"])
                ex = P.inc(nc.scalar.activation(pr_bf[:, h * WIN:(h + 1) * WIN],
                                                t1[:, 0:WIN], AF.Exp), "cS")
                P.wait(nc.vector, "cS", ex)
                P.inc(nc.vector.tensor_reduce(rows_sb[:], pr_bf[:, h * WIN:(h + 1) * WIN],
                                              AX.X, ALU.add), "cV")
                P.wait(nc.vector, "cV", P.cnt["cV"])
                P.inc(nc.vector.reciprocal(rowr[:], rows_sb[:]), "cV")
                P.wait(nc.vector, "cV", P.cnt["cV"])
                P.inc(nc.vector.tensor_scalar(pr_bf[:, h * WIN:(h + 1) * WIN],
                                              pr_bf[:, h * WIN:(h + 1) * WIN],
                                              rowr[:, 0:1], None, ALU.mult), "cV")
            norm_done = P.cnt["cV"]
            P.wait(nc.tensor, "cV", norm_done)
            tp_copy = [0, 0]
            for b in range(16):
                tb = tps[b % 2]
                P.wait(nc.tensor, "cV", tp_copy[b % 2])
                tp = P.inc(nc.tensor.transpose(tb[:, :], pr_bf[:, b * 128:(b + 1) * 128],
                                               ident[:, :]), "cT")
                P.wait(nc.vector, "cT", tp)
                tp_copy[b % 2] = P.inc(nc.vector.tensor_copy(
                    prT[:, b * 128:(b + 1) * 128], tb[:, :]), "cV")
            prT_done = P.cnt["cV"]
            P.wait(nc.tensor, "cV", prT_done)
            av_copy = [0, 0]
            for h in range(8):
                bank = pav[h % 2]
                c0 = 5 * s + qt
                P.wait(nc.tensor, "cV", av_copy[h % 2])
                av1 = nc.tensor.matmul(
                    bank[:, 0:128],
                    v_sb[:, c0 * D + h * 128:(c0 * D + (h + 1) * 128)],
                    prT[:, h * 256:h * 256 + 128],
                    start=True, stop=False)
                av = P.inc(nc.tensor.matmul(
                    bank[:, 0:128],
                    v_sb[0:16, (c0 + 1) * D + h * 128:((c0 + 1) * D + (h + 1) * 128)],
                    prT[0:16, h * 256 + 128:h * 256 + 256],
                    start=False, stop=True), "cT")
                P.wait(nc.vector, "cT", av)
                attn_done_v = P.inc(nc.vector.tensor_copy(
                    attnT[:, h * TOK + qcol: h * TOK + qcol + 128], bank[:, 0:128]), "cV")
                av_copy[h % 2] = attn_done_v

    P.wait(nc.gpsimd, "cT", P.cnt["cT"])
    for kc_ in range(KC):
        P.dma(nc.gpsimd, "dG", w_sb[:, kc_ * D:(kc_ + 1) * D],
                                  wo[kc_ * 128:(kc_ + 1) * 128, :])
    wd = P.cnt["dG"]
    P.wait(nc.tensor, "dG", wd)
    res_busy = P.cnt["cV"]
    yst = 0
    for s in range(NSEQ):
        for m in range(KC):
            bank = ps[m % 4]
            P.wait(nc.tensor, "cV", P.cnt["cV"])
            for kc_ in range(KC):
                mm = P.inc(nc.tensor.matmul(
                    bank[:, 0:512],
                    w_sb[:, kc_ * D + m * 128: kc_ * D + (m + 1) * 128],
                    attnT[:, kc_ * TOK + s * 512:(kc_ * TOK + (s + 1) * 512)],
                    start=(kc_ == 0), stop=(kc_ == KC - 1)), "cT")
            P.wait(nc.sync, "cV", res_busy)
            rld = P.dma(nc.sync, "dS", t1[:, 0:512],
                xkv[m * 128:(m + 1) * 128, s * SEQT + 16: s * SEQT + 528])
            P.wait(nc.vector, "cT", mm)
            P.wait(nc.vector, "dS", rld)
            P.wait(nc.vector, "dS", yst)
            cv = P.inc(nc.vector.tensor_tensor(t1[:, 0:512],
                                               bank[:, 0:512], t1[:, 0:512], ALU.add), "cV")
            res_busy = cv
            P.wait(nc.sync, "cV", cv)
            yst = P.dma(nc.sync, "dS", yT[m * 128:(m + 1) * 128, s * 512:(s + 1) * 512], t1[:, 0:512])
    if dbg:
        d_qnT = nc.dram_tensor("d_qnT", [128, KC * TOK], BF16, kind="ExternalOutput").ap()
        d_knT = nc.dram_tensor("d_knT", [128, KC * NTK], BF16, kind="ExternalOutput").ap()
        d_v = nc.dram_tensor("d_v", [128, (NTK // 128) * D], BF16, kind="ExternalOutput").ap()
        d_at = nc.dram_tensor("d_at", [128, KC * TOK], BF16, kind="ExternalOutput").ap()
        P.wait(nc.gpsimd, "cV", P.cnt["cV"])
        P.wait(nc.gpsimd, "cT", P.cnt["cT"])
        P.dma(nc.gpsimd, "dG", d_qnT, qnT[:, :])
        P.dma(nc.gpsimd, "dG", d_knT, knT[:, :])
        P.dma(nc.gpsimd, "dG", d_v, v_sb[:, :])
        P.dma(nc.gpsimd, "dG", d_at, attnT[:, :])
        nc.gpsimd.wait_ge(P.sem("dG"), P.cnt["dG"])
    nc.sync.wait_ge(P.sem("dS"), P.cnt["dS"])
    return P


def make_l1attn_bias(core):
    import ml_dtypes
    slopes = (2.0 ** (-np.arange(1, 9))).astype(np.float32)
    qrow = np.arange(128)[:, None]
    kidx = np.arange(WIN)[None, :]
    rel = (kidx - 16) - qrow
    inwin = (rel <= 0) & (rel >= -16)
    b0 = np.where(inwin[None], slopes[:, None, None] * rel[None], -1e9).astype(np.float32)
    bn = b0.copy()
    if core == 0:
        b0 = np.where((kidx < 16)[None], np.float32(-1e9), b0)
    def pack(b):
        return np.ascontiguousarray(
            b.transpose(1, 0, 2).reshape(128, 8 * WIN)).astype(ml_dtypes.bfloat16)
    return pack(b0), pack(bn)


def make_xkv(x, core):
    out = np.zeros((D, NTK), dtype=np.float32)
    for s in range(NSEQ):
        p0 = core * POS
        own = x[s, p0:p0 + POS, :].T
        out[:, s * SEQT + 16: s * SEQT + 528] = own
        if core > 0:
            out[:, s * SEQT: s * SEQT + 16] = x[s, p0 - 16:p0, :].T
    return out


def run_l1attn(x_full, W, dbg=False):
    P = Prog()
    build_l1attn(P, dbg=dbg)
    P.close()
    in_maps = []
    for c in range(NCORE):
        b0, bn = make_l1attn_bias(c)
        in_maps.append({
            "xkv": make_xkv(x_full, c),
            "wq": np.ascontiguousarray(W["wq"].T), "wk": np.ascontiguousarray(W["wk"].T),
            "wv": np.ascontiguousarray(W["wv"].T), "wo": np.ascontiguousarray(W["wo"].T),
            "qnw": np.ascontiguousarray(W["q_norm_w"]), "knw": np.ascontiguousarray(W["k_norm_w"]),
            "bias0": b0, "biasn": bn,
        })
    res = run_bass_kernel_spmd(P.nc, in_maps, core_ids=list(range(NCORE)))
    if dbg:
        return res.results
    return [r["yT"] for r in res.results]



def build_l0qkv(P):
    nc = P.nc
    xT = nc.dram_tensor("xT", [D, TOK], F32, kind="ExternalInput").ap()
    W1 = nc.dram_tensor("W1", [D, 3 * D], F32, kind="ExternalInput").ap()
    b1 = nc.dram_tensor("b1", [3 * D], F32, kind="ExternalInput").ap()
    qkvT = nc.dram_tensor("qkvT", [3 * D, TOK], BF16, kind="ExternalOutput").ap()

    w_sb = P.sb("w_sb", [128, KC * 3 * D], BF16)
    xhat = P.sb("xhat", [128, KC * TOK], BF16)
    b1_sb = P.sb("b1_sb", [1, 3 * D], BF16)
    ones_row = P.sb("ones_row", [1, 512], BF16)
    ev = P.sb("ev", [128, 512], BF16)
    psA = P.psum("psA", [1, 512], F32)
    psQ = P.psum("psQ", [1, 512], F32)
    pA = P.psum("pA", [128, 512], F32)
    pB = P.psum("pB", [128, 512], F32)
    pg = [P.psum(f"pg{i}", [128, 512], F32) for i in range(4)]
    aux = ln_aux(P, psums={"pS": psA, "pQ": psQ, "pA": pA, "pB": pB})

    P.block()
    emit_ln_init(P, aux)
    P.inc(nc.vector.memset(ones_row[:], 1.0), "cV")
    for kc in range(KC):
        P.dma(nc.gpsimd, "dG", w_sb[:, kc * 3 * D:(kc + 1) * 3 * D],
                                  W1[kc * 128:(kc + 1) * 128, :])
    P.dma(nc.gpsimd, "dG", b1_sb[:, :], b1.unsqueeze(0))
    w_done = P.cnt["dG"]
    emit_layernorm(P, xT, xhat, aux)
    ln_done = P.cnt["cV"]
    P.wait(nc.tensor, "dG", w_done)
    P.wait(nc.tensor, "cV", ln_done)
    pclean = [0] * 4
    st = 0
    for m in range(3 * KC):
        for tt in range(TOK // 512):
            bi = (m * 4 + tt) % 4
            bank = pg[bi]
            P.wait(nc.tensor, "cV", pclean[bi])
            nc.tensor.matmul(bank[:, :], b1_sb[:, m * 128:(m + 1) * 128],
                             ones_row[:, :], start=True, stop=False)
            for kc in range(KC):
                mm = P.inc(nc.tensor.matmul(
                    bank[:, :],
                    w_sb[:, kc * 3 * D + m * 128: kc * 3 * D + (m + 1) * 128],
                    xhat[:, kc * TOK + tt * 512: kc * TOK + (tt + 1) * 512],
                    start=False, stop=(kc == KC - 1)), "cT")
            P.wait(nc.vector, "cT", mm)
            P.wait(nc.vector, "dS", st)
            cv = P.inc(nc.vector.tensor_copy(ev[:], bank[:, :]), "cV")
            pclean[bi] = cv
            P.wait(nc.sync, "cV", cv)
            st = P.dma(nc.sync, "dS", qkvT[m * 128:(m + 1) * 128, tt * 512:(tt + 1) * 512], ev[:])
    nc.sync.wait_ge(P.sem("dS"), P.cnt["dS"])
    return P


def build_l0attn(P):
    nc = P.nc
    S = 4096
    NCH = S // 128
    qT = nc.dram_tensor("qT", [128, NSEQ * S], BF16, kind="ExternalInput").ap()
    kT = nc.dram_tensor("kT", [128, NSEQ * S], BF16, kind="ExternalInput").ap()
    v_in = nc.dram_tensor("v_in", [NSEQ * S, 132], BF16, kind="ExternalInput").ap()
    attn_o = nc.dram_tensor("attn_o", [NSEQ * S, 128], BF16, kind="ExternalOutput").ap()

    q_sb = P.sb("q_sb", [128, S], BF16)
    k_sb = P.sb("k_sb", [128, S], BF16)
    v_sb = P.sb("v_sb", [128, NCH * 132], BF16)
    prT = [P.sb(f"prT{i}", [128, 512], BF16) for i in range(2)]
    maskT = P.sb("maskT", [128, 128], BF16)
    idf = P.sb("idf", [128, 128], F32)
    out_sb = P.sb("out_sb", [128, 8 * 64], BF16)
    rcol = P.sb("rcol", [128, 1], F32)
    pS_ = [P.psum(f"pS{i}", [128, 512], F32) for i in range(2)]
    pO = [P.psum(f"pO{i}", [128, 264], F32) for i in range(2)]

    P.block()
    P.inc(nc.gpsimd.iota(idf[:], [[-1, 128]], base=0, channel_multiplier=1,
                         allow_small_or_imprecise_dtypes=True), "cG")
    P.wait(nc.vector, "cG", P.cnt["cG"])
    P.inc(nc.vector.tensor_scalar(maskT[:], idf[:], 0.0, None, ALU.is_le), "cV")
    INV8 = float(1.0 / 8.0)
    ld = 0
    st = 0
    for s in range(NSEQ):
        for h in range(2):
            hb = h * 64
            if h == 0:
                P.wait(nc.sync, "cT", P.cnt["cT"])
                P.dma(nc.sync, "dS", q_sb[:, :], qT[:, s * S:(s + 1) * S])
                P.dma(nc.sync, "dS", k_sb[:, :], kT[:, s * S:(s + 1) * S])
                vv = v_in[s * S:(s + 1) * S, :].rearrange("(c p) w -> p c w", p=128)
                ld = P.dma(nc.sync, "dS", v_sb[:].rearrange("p (c w) -> p c w", w=132), vv)
            P.wait(nc.tensor, "dS", ld)
            for qs in range(8):
                po = pO[qs % 2]
                po_clean = P.cnt["cV"]
                first_kc = [True] * 4
                for kc in range(4 * qs + 4):
                    q0 = max(qs * 512, kc * 128)
                    w = qs * 512 + 512 - q0
                    bank = pS_[kc % 2]
                    prb = prT[kc % 2]
                    P.wait(nc.tensor, "cS", P.cnt["cS"])
                    smm = P.inc(nc.tensor.matmul(
                        bank[:, 0:w],
                        k_sb[hb:hb + 64, kc * 128:(kc + 1) * 128],
                        q_sb[hb:hb + 64, q0:q0 + w],
                        start=True, stop=True), "cT")
                    P.wait(nc.scalar, "cT", smm)
                    P.wait(nc.scalar, "cV", P.cnt["cV"])
                    ex = P.inc(nc.scalar.activation(prb[:, 0:w], bank[:, 0:w],
                                                    AF.Exp, scale=np.float32(INV8)), "cS")
                    mdone = ex
                    if q0 == kc * 128:
                        P.wait(nc.vector, "cS", ex)
                        mdone = P.inc(nc.vector.tensor_tensor(
                            prb[:, 0:128], prb[:, 0:128], maskT[:], ALU.mult), "cV")
                        P.wait(nc.tensor, "cV", mdone)
                    P.wait(nc.tensor, "cS", ex)
                    for j in range(4):
                        jq = qs * 512 + j * 128
                        if jq < q0:
                            continue
                        last = (kc == 4 * qs + 3)
                        av = P.inc(nc.tensor.matmul(
                            po[:, j * 66: j * 66 + 65],
                            prb[:, jq - q0: jq - q0 + 128],
                            v_sb[:, kc * 132 + h * 66: kc * 132 + h * 66 + 65],
                            start=(kc == 0 and j == 0), stop=last), "cT")
                        first_kc[j] = False
                P.wait(nc.vector, "cT", P.cnt["cT"])
                for j in range(4):
                    P.wait(nc.vector, "cV", P.cnt["cV"])
                    P.inc(nc.vector.reciprocal(rcol[:], po[:, j * 66 + 64: j * 66 + 65]), "cV")
                    P.wait(nc.vector, "cV", P.cnt["cV"])
                    P.wait(nc.vector, "dS", st)
                    P.inc(nc.vector.tensor_scalar(
                        out_sb[:, (qs % 2) * 256 + j * 64: (qs % 2) * 256 + (j + 1) * 64],
                        po[:, j * 66: j * 66 + 64], rcol[:, 0:1], None, ALU.mult), "cV")
                ev = P.cnt["cV"]
                if qs % 2 == 1:
                    P.wait(nc.sync, "cV", ev)
                    dst = attn_o[s * S + (qs - 1) * 512: s * S + (qs + 1) * 512,
                                 h * 64:(h + 1) * 64]
                    dd = dst.rearrange("(j p) c -> p j c", p=128)
                    st = P.dma(nc.sync, "dS", dd, out_sb[:].rearrange("p (j c) -> p j c", c=64))
                    nc.sync.wait_ge(P.sem("dS"), st)
    nc.sync.wait_ge(P.sem("dS"), P.cnt["dS"])
    return P


def build_l0proj(P):
    nc = P.nc
    xT = nc.dram_tensor("xT", [D, TOK], F32, kind="ExternalInput").ap()
    attnT = nc.dram_tensor("attnT", [D, TOK], BF16, kind="ExternalInput").ap()
    W2 = nc.dram_tensor("W2", [D, D], F32, kind="ExternalInput").ap()
    b2 = nc.dram_tensor("b2", [D], F32, kind="ExternalInput").ap()
    yT = nc.dram_tensor("yT", [D, TOK], F32, kind="ExternalOutput").ap()

    w_sb = P.sb("w_sb", [128, KC * D], BF16)
    a_sb = P.sb("a_sb", [128, KC * TOK], BF16)
    b2_sb = P.sb("b2_sb", [1, D], BF16)
    ones_row = P.sb("ones_row", [1, 512], BF16)
    res_t = P.sb("res_t", [128, 512], F32)
    yout = P.sb("yout", [128, 512], F32)
    pg = [P.psum(f"pg{i}", [128, 512], F32) for i in range(4)]

    P.block()
    P.inc(nc.vector.memset(ones_row[:], 1.0), "cV")
    for kc in range(KC):
        P.dma(nc.gpsimd, "dG", w_sb[:, kc * D:(kc + 1) * D],
                                  W2[kc * 128:(kc + 1) * 128, :])
        P.dma(nc.gpsimd, "dG", a_sb[:, kc * TOK:(kc + 1) * TOK],
                                  attnT[kc * 128:(kc + 1) * 128, :])
    P.dma(nc.gpsimd, "dG", b2_sb[:, :], b2.unsqueeze(0))
    w_done = P.cnt["dG"]
    P.wait(nc.tensor, "dG", w_done)
    P.wait(nc.tensor, "cV", 1)
    pclean = [0] * 4
    res_busy = 0
    yst = 0
    for m in range(KC):
        for tt in range(TOK // 512):
            bi = (m * 4 + tt) % 4
            bank = pg[bi]
            P.wait(nc.tensor, "cV", pclean[bi])
            nc.tensor.matmul(bank[:, :], b2_sb[:, m * 128:(m + 1) * 128],
                             ones_row[:, :], start=True, stop=False)
            for kc in range(KC):
                mm = P.inc(nc.tensor.matmul(
                    bank[:, :],
                    w_sb[:, kc * D + m * 128: kc * D + (m + 1) * 128],
                    a_sb[:, kc * TOK + tt * 512: kc * TOK + (tt + 1) * 512],
                    start=False, stop=(kc == KC - 1)), "cT")
            P.wait(nc.sync, "cV", res_busy)
            rld = P.dma(nc.sync, "dS", res_t[:], xT[m * 128:(m + 1) * 128, tt * 512:(tt + 1) * 512])
            P.wait(nc.vector, "cT", mm)
            P.wait(nc.vector, "dS", rld)
            P.wait(nc.vector, "dS", yst)
            cv = P.inc(nc.vector.tensor_tensor(yout[:], bank[:, :], res_t[:], ALU.add), "cV")
            pclean[bi] = cv
            res_busy = cv
            P.wait(nc.sync, "cV", cv)
            yst = P.dma(nc.sync, "dS", yT[m * 128:(m + 1) * 128, tt * 512:(tt + 1) * 512], yout[:])
    nc.sync.wait_ge(P.sem("dS"), P.cnt["dS"])
    return P


def run_l0attn_all(xT_shards, W1, b1, W2, b2):
    import ml_dtypes
    P = Prog(); build_l0qkv(P); P.close()
    in_maps = [{"xT": s, "W1": W1, "b1": b1} for s in xT_shards]
    res = run_bass_kernel_spmd(P.nc, in_maps, core_ids=list(range(NCORE)))
    qkv = [r["qkvT"] for r in res.results]
    S = 4096
    qkv_full = np.concatenate(qkv, axis=1)
    cols = np.empty((NCORE * TOK,), np.int64)
    idx = np.arange(NCORE * TOK)
    core, rem = idx // TOK, idx % TOK
    s_, p_ = rem // POS, rem % POS
    gcol = s_ * S + core * POS + p_
    qkv_g = np.empty_like(qkv_full)
    qkv_g[:, gcol] = qkv_full[:, idx]
    in_maps = []
    for c in range(NCORE):
        h0 = 2 * c
        qTh = np.ascontiguousarray(np.concatenate(
            [qkv_g[h * 192: h * 192 + 64, :] for h in (h0, h0 + 1)], axis=0))
        kTh = np.ascontiguousarray(np.concatenate(
            [qkv_g[h * 192 + 64: h * 192 + 128, :] for h in (h0, h0 + 1)], axis=0))
        v_rows = np.concatenate(
            [qkv_g[h * 192 + 128: h * 192 + 192, :] for h in (h0, h0 + 1)], axis=0).T
        v_in = np.zeros((NSEQ * S, 132), dtype=ml_dtypes.bfloat16)
        v_in[:, 0:64] = v_rows[:, 0:64]
        v_in[:, 64] = 1.0
        v_in[:, 66:130] = v_rows[:, 64:128]
        v_in[:, 130] = 1.0
        in_maps.append({"qT": qTh, "kT": kTh, "v_in": v_in})
    P = Prog(); build_l0attn(P); P.close()
    res = run_bass_kernel_spmd(P.nc, in_maps, core_ids=list(range(NCORE)))
    attn = [r["attn_o"] for r in res.results]
    attn_g = np.concatenate(attn, axis=1)
    attn_T = np.ascontiguousarray(attn_g.T)
    in_maps = []
    for c in range(NCORE):
        cols_c = (np.arange(NSEQ)[:, None] * S + c * POS + np.arange(POS)[None, :]).reshape(-1)
        aT = np.ascontiguousarray(attn_T[:, cols_c])
        in_maps.append({"xT": xT_shards[c], "attnT": aT, "W2": W2, "b2": b2})
    P = Prog(); build_l0proj(P); P.close()
    res = run_bass_kernel_spmd(P.nc, in_maps, core_ids=list(range(NCORE)))
    return [r["yT"] for r in res.results]



NWQ = 2816 // NCORE
NDQ = D // NCORE


def fused_aux(P):
    aux = {
        "ones_bf": P.sb("g_ones_bf", [128, 1], BF16),
        "ones_f": P.sb("g_ones_f", [1, 128], F32),
        "xbf": P.sb("g_ln_xbf", [128, 512], BF16),
        "sq": P.sb("g_ln_sq", [128, 512], BF16),
        "arow": P.sb("g_ln_arow", [1, 512], F32),
        "brow": P.sb("g_ln_brow", [1, 512], F32),
        "mrow": P.sb("g_ln_mrow", [1, 512], F32),
        "init": False,
    }
    aux["ones_row"] = P.sb("g_ones_row", [1, 512], BF16)
    aux["ident"] = P.sb("g_ident", [128, 128], BF16)
    return aux


def emit_fused_init(P, aux):
    nc = P.nc
    P.inc(nc.vector.memset(aux["ones_bf"][:], 1.0), "cV")
    P.inc(nc.vector.memset(aux["ones_f"][:], 1.0), "cV")
    P.inc(nc.vector.memset(aux["ones_row"][:], 1.0), "cV")
    idf = P.sb("g_idf", [128, 128], F32)
    P.inc(nc.gpsimd.iota(idf[:], [[-1, 128]], base=0, channel_multiplier=1,
                         allow_small_or_imprecise_dtypes=True), "cG")
    P.wait(nc.vector, "cG", P.cnt["cG"])
    P.inc(nc.vector.tensor_scalar(aux["ident"][:], idf[:], 0.0, None, ALU.is_equal), "cV")
    aux["init"] = True
    aux["init_cv"] = P.cnt["cV"]


def emit_prep_f(P, aux, wgs, wus, wds, agw):
    nc = P.nc
    with P.scope():
        xin = P.sb("p_xin", [128, 2816], F32)
        qout = P.sb("p_qout", [128, 2816], BF16)
        scr = {
            "a": P.sb("p_qa", [128, 2816], F32), "t": P.sb("p_qt", [128, 2816], F32),
            "u": P.sb("p_qu", [128, 2816], F32), "bm": P.sb("p_qbm", [128, 176], F32),
            "scale": P.sb("p_qsc", [128, 176], F32), "rsc": P.sb("p_qrs", [128, 176], F32),
        }
        jobs = []
        for r0 in range(0, NWQ, 128):
            npart = min(128, NWQ - r0)
            jobs.append((wgs, agw["g_in"], r0, npart, D))
            jobs.append((wus, agw["u_in"], r0, npart, D))
        jobs.append((wds, agw["d_in"], 0, NDQ, 2816))
        st_done = 0
        for (src_, dst, r0, npart, F) in jobs:
            P.wait(nc.sync, "cV", scr.get("busy", 0))
            P.wait(nc.sync, "cS", scr.get("busy_s", 0))
            ld = P.dma(nc.sync, "dS", xin[:npart, :F], src_[r0:r0 + npart, :])
            emit_quant(P, xin[:npart, :F], qout[:npart, :F], scr, npart, F // 16,
                       after=[("dS", ld)])
            P.wait(nc.sync, "cV", P.cnt["cV"])
            st_done = P.dma(nc.sync, "dS", dst[r0:r0 + npart, :], qout[:npart, :F])
        P.wait(nc.gpsimd, "dS", st_done)
        for nm in ("g", "u", "d"):
            P.inc(nc.gpsimd.collective_compute(
                "AllGather", ALU.bypass, replica_groups=[list(range(NCORE))],
                ins=[agw[nm + "_in"].ap().opt()], outs=[agw[nm + "_out"].ap().opt()]), "cC")
    P.barrier()


def emit_l0ffn_f(P, aux, xin_d, yout_d, Wd, rbf_d=None):
    nc = P.nc
    with P.scope():
        wg_sb = P.sb("f_wg", [128, KC * 3 * D], BF16)
        ug_sb = P.sb("f_ug", [128, KC * 3 * D], BF16)
        big = P.sb("f_big", [128, 16384], BF16)
        gf = big[:, 0:KC * POS]; gi = big[:, KC * POS:2 * KC * POS]; gg = big[:, 2 * KC * POS:3 * KC * POS]
        xhat = P.sb("f_xhat", [128, KC * TOK], BF16)
        c_sb = P.sb("f_c", [128, KC * POS], BF16)
        tmp1 = P.sb("f_tmp1", [128, POS], BF16)
        tmp2 = P.sb("f_tmp2", [128, POS], BF16)
        hout = P.sb("f_hout", [128, 512], F32)
        res_t = P.sb("f_res", [128, 512], F32)
        yout = P.sb("f_yout", [128, 512], F32)
        ybf = P.sb("f_ybf", [128, 512], BF16)
        bgs = P.sb("f_bgs", [128, 24], F32)
        bos = P.sb("f_bos", [128, KC], F32)
        incs = P.sb("f_incs", [128, KC], F32)
        pg = [P.psum(f"f_pg{i}", [128, 512], F32) for i in range(4)]
        aux.pop("stats_mm", None); aux.pop("rows_read", None)

        for kc in range(KC):
            P.dma_nb(nc.gpsimd, "dG", wg_sb[:, kc * 3 * D:(kc + 1) * 3 * D],
                     Wd["wg"][kc * 128:(kc + 1) * 128, :])
            P.dma_nb(nc.gpsimd, "dG", ug_sb[:, kc * 3 * D:(kc + 1) * 3 * D],
                     Wd["ug"][kc * 128:(kc + 1) * 128, :])
        w_done = P.dbar(nc.gpsimd, "dG")
        for j in range(24):
            P.dma_nb(nc.sync, "dS", bgs[:, j:j + 1], Wd["bg"][j * 128:(j + 1) * 128].unsqueeze(-1))
        for j in range(KC):
            P.dma_nb(nc.sync, "dS", bos[:, j:j + 1], Wd["bo"][j * 128:(j + 1) * 128].unsqueeze(-1))
            P.dma_nb(nc.sync, "dS", incs[:, j:j + 1], Wd["inc_"][j * 128:(j + 1) * 128].unsqueeze(-1))
        const_done = P.dbar(nc.sync, "dS")

        with P.scope():
            aux["xf32_t"] = P.sb("f_lnx", [128, KC * 512], F32)
            aux["pS"] = P.psum("f_pS", [1, 512], F32)
            aux["pQ"] = P.psum("f_pQ", [1, 512], F32)
            aux["pA"] = P.psum("f_pA", [128, 512], F32)
            aux["pB"] = P.psum("f_pB", [128, 512], F32)
            emit_layernorm(P, xin_d, xhat, aux)

        P.wait(nc.vector, "dS", const_done)
        for kc in range(KC):
            P.wait(nc.vector, "cV", P.cnt["cV"])
            P.inc(nc.vector.tensor_copy(
                c_sb[:, kc * POS:(kc + 1) * POS],
                incs[:, kc:kc + 1].broadcast_to([128, POS])), "cV")
        c_ready = P.cnt["cV"]

        P.wait(nc.tensor, "dG", w_done)
        gate_view = {0: gf, 1: gi, 2: gg}
        act_fn = {0: AF.Sigmoid, 1: AF.Sigmoid, 2: AF.Tanh}
        pclean = [0, 0, 0, 0]
        for t in range(4):
            for gate in range(3):
                for m in range(KC):
                    bi = (gate * KC + m) % 4
                    bank = pg[bi]
                    P.wait(nc.tensor, "cS", pclean[bi])
                    for kc in range(KC):
                        nc.tensor.matmul(
                            bank[:, :],
                            wg_sb[:, kc * 3 * D + gate * D + m * 128: kc * 3 * D + gate * D + (m + 1) * 128],
                            xhat[:, kc * TOK + t * POS: kc * TOK + (t + 1) * POS],
                            start=(kc == 0), stop=False)
                    P.wait(nc.tensor, "cV", c_ready)
                    for kc in range(KC):
                        last = P.inc(nc.tensor.matmul(
                            bank[:, :],
                            ug_sb[:, kc * 3 * D + gate * D + m * 128: kc * 3 * D + gate * D + (m + 1) * 128],
                            c_sb[:, kc * POS:(kc + 1) * POS],
                            start=False, stop=(kc == KC - 1)), "cT")
                    P.wait(nc.scalar, "cT", last)
                    pclean[bi] = P.inc(nc.scalar.activation(
                        gate_view[gate][:, m * POS:(m + 1) * POS], bank[:, :],
                        act_fn[gate], bias=bgs[:, gate * KC + m: gate * KC + m + 1]), "cS")
            gates_done = P.cnt["cS"]
            all_mm = P.cnt["cT"]
            P.wait(nc.vector, "cS", gates_done)
            P.wait(nc.vector, "cT", all_mm)
            for m in range(KC):
                sl = slice(m * POS, (m + 1) * POS)
                P.wait(nc.vector, "cV", P.cnt["cV"])
                P.inc(nc.vector.tensor_tensor(tmp1[:], gi[:, sl], gg[:, sl], ALU.mult), "cV")
                P.wait(nc.vector, "cV", P.cnt["cV"])
                P.inc(nc.vector.tensor_tensor(tmp2[:], gf[:, sl], c_sb[:, sl], ALU.mult), "cV")
                P.wait(nc.vector, "cV", P.cnt["cV"])
                P.inc(nc.vector.tensor_tensor(c_sb[:, sl], tmp1[:], tmp2[:], ALU.add), "cV")
            c_ready = P.cnt["cV"]

        P.wait(nc.gpsimd, "cV", c_ready)
        wo_sb = big[:, 0:KC * D]
        uo_sb = big[:, KC * D:2 * KC * D]
        for kc in range(KC):
            P.dma_nb(nc.gpsimd, "dG", wo_sb[:, kc * D:(kc + 1) * D],
                     Wd["wo"][kc * 128:(kc + 1) * 128, :])
            P.dma_nb(nc.gpsimd, "dG", uo_sb[:, kc * D:(kc + 1) * D],
                     Wd["uo"][kc * 128:(kc + 1) * 128, :])
        wo_done = P.dbar(nc.gpsimd, "dG")

        P.wait(nc.tensor, "dG", wo_done)
        P.wait(nc.gpsimd, "cV", c_ready)
        res2 = P.sb("f_res2", [128, 512], F32)
        yout2 = P.sb("f_yout2", [128, 512], F32)
        ybf2 = P.sb("f_ybf2", [128, 512], BF16)
        hout2 = P.sb("f_hout2", [128, 512], F32)
        res_ld = [0, 0]
        y_free = [0, 0]
        y_st = [0, 0]
        h_free = [0, 0]
        for t in range(4):
            for m in range(KC):
                it = t * KC + m
                par = it % 2
                bi = it % 4
                bank = pg[bi]
                rt = res_t if par == 0 else res2
                yo = yout if par == 0 else yout2
                yb = ybf if par == 0 else ybf2
                ho = hout if par == 0 else hout2
                P.wait(nc.gpsimd, "cV", y_free[par])
                res_ld[par] = P.dma(nc.gpsimd, "dG",
                                    rt[:], xin_d[m * 128:(m + 1) * 128, t * POS:(t + 1) * POS])
                P.wait(nc.tensor, "cS", pclean[bi])
                for kc in range(KC):
                    nc.tensor.matmul(
                        bank[:, :],
                        wo_sb[:, kc * D + m * 128: kc * D + (m + 1) * 128],
                        xhat[:, kc * TOK + t * POS: kc * TOK + (t + 1) * POS],
                        start=(kc == 0), stop=False)
                for kc in range(KC):
                    last = P.inc(nc.tensor.matmul(
                        bank[:, :],
                        uo_sb[:, kc * D + m * 128: kc * D + (m + 1) * 128],
                        c_sb[:, kc * POS:(kc + 1) * POS],
                        start=False, stop=(kc == KC - 1)), "cT")
                P.wait(nc.scalar, "cT", last)
                P.wait(nc.scalar, "cV", h_free[par])
                pclean[bi] = P.inc(nc.scalar.activation(
                    ho[:], bank[:, :], AF.Tanh, bias=bos[:, m:m + 1]), "cS")
                P.wait(nc.vector, "cS", pclean[bi])
                P.wait(nc.vector, "dG", res_ld[par])
                P.wait(nc.vector, "dS", y_st[par])
                cv = P.inc(nc.vector.scalar_tensor_tensor(
                    yo[:], ho[:], 0.5, rt[:], ALU.mult, ALU.add), "cV")
                h_free[par] = cv
                y_free[par] = cv
                P.wait(nc.sync, "cV", cv)
                y_st[par] = P.dma(nc.sync, "dS",
                                  yout_d[m * 128:(m + 1) * 128, t * POS:(t + 1) * POS], yo[:])
                if rbf_d is not None:
                    P.wait(nc.vector, "cV", cv)
                    cb = P.inc(nc.vector.tensor_copy(yb[:], yo[:]), "cV")
                    y_free[par] = cb
                    P.wait(nc.sync, "cV", cb)
                    y_st[par] = P.dma(nc.sync, "dS",
                                      rbf_d[m * 128:(m + 1) * 128, t * POS:(t + 1) * POS], yb[:])
    P.barrier()


def emit_l0qkv_f(P, aux, xin_d, W1qk_d, b1qk_d, W1v_d, b1v_d, a2aqk, a2av):
    nc = P.nc
    with P.scope():
        wqk_sb = P.sb("qk_wqk", [128, KC * 2 * D], BF16)
        wv_sb = P.sb("qk_wv", [128, KC * D], BF16)
        xhat = P.sb("qk_xhat", [128, KC * TOK], BF16)
        bqk_sb = P.sb("qk_bqk", [1, 2 * D], BF16)
        bv_sb = P.sb("qk_bv", [1, D], BF16)
        ev = P.sb("qk_ev", [128, 512], BF16)
        aux["xf32_t"] = P.sb("qk_lnx", [128, KC * 512], F32)
        aux["pS"] = P.psum("qk_pS", [1, 512], F32)
        aux["pQ"] = P.psum("qk_pQ", [1, 512], F32)
        aux["pA"] = P.psum("qk_pA", [128, 512], F32)
        aux["pB"] = P.psum("qk_pB", [128, 512], F32)
        aux.pop("stats_mm", None); aux.pop("rows_read", None)
        pg = [P.psum(f"qk_pg{i}", [128, 512], F32) for i in range(4)]

        for kc in range(KC):
            P.dma_nb(nc.gpsimd, "dG", wqk_sb[:, kc * 2 * D:(kc + 1) * 2 * D],
                     W1qk_d[kc * 128:(kc + 1) * 128, :])
            P.dma_nb(nc.gpsimd, "dG", wv_sb[:, kc * D:(kc + 1) * D],
                     W1v_d[kc * 128:(kc + 1) * 128, :])
        P.dma_nb(nc.gpsimd, "dG", bqk_sb[:, :], b1qk_d.unsqueeze(0))
        P.dma_nb(nc.gpsimd, "dG", bv_sb[:, :], b1v_d.unsqueeze(0))
        w_done = P.dbar(nc.gpsimd, "dG")
        emit_layernorm(P, xin_d, xhat, aux)
        ln_done = P.cnt["cV"]
        P.wait(nc.tensor, "dG", w_done)
        P.wait(nc.tensor, "cV", ln_done)
        P.wait(nc.tensor, "cV", aux["init_cv"])
        pclean = [0] * 4
        st = 0
        for part in range(2):
            for j in range(8):
                for tt in range(TOK // 512):
                    bi = ((part * 8 + j) * 4 + tt) % 4
                    bank = pg[bi]
                    P.wait(nc.tensor, "cV", pclean[bi])
                    nc.tensor.matmul(
                        bank[:, :],
                        bqk_sb[:, part * D + j * 128: part * D + (j + 1) * 128],
                        aux["ones_row"][:, :], start=True, stop=False)
                    for kc in range(KC):
                        mm = P.inc(nc.tensor.matmul(
                            bank[:, :],
                            wqk_sb[:, kc * 2 * D + part * D + j * 128:
                                   kc * 2 * D + part * D + (j + 1) * 128],
                            xhat[:, kc * TOK + tt * 512: kc * TOK + (tt + 1) * 512],
                            start=False, stop=(kc == KC - 1)), "cT")
                    P.wait(nc.vector, "cT", mm)
                    P.wait(nc.vector, "dS", st)
                    cv = P.inc(nc.vector.tensor_copy(ev[:], bank[:, :]), "cV")
                    pclean[bi] = cv
                    P.wait(nc.sync, "cV", cv)
                    st = P.dma(nc.sync, "dS",
                               a2aqk["in"][256 * j + 128 * part: 256 * j + 128 * part + 128,
                                           tt * 512:(tt + 1) * 512],
                               ev[:])
        for ch in range(TOK // 128):
            for js in range(2):
                bi = (ch * 2 + js) % 4
                bank = pg[bi]
                P.wait(nc.tensor, "cV", pclean[bi])
                nc.tensor.matmul(bank[:, :], aux["ones_row"][:, 0:128],
                                 bv_sb[:, js * 512:(js + 1) * 512], start=True, stop=False)
                for kc in range(KC):
                    mm = P.inc(nc.tensor.matmul(
                        bank[:, :],
                        xhat[:, kc * TOK + ch * 128: kc * TOK + (ch + 1) * 128],
                        wv_sb[:, kc * D + js * 512: kc * D + (js + 1) * 512],
                        start=False, stop=(kc == KC - 1)), "cT")
                P.wait(nc.vector, "cT", mm)
                P.wait(nc.vector, "dS", st)
                cv = P.inc(nc.vector.tensor_copy(ev[:], bank[:, :]), "cV")
                pclean[bi] = cv
                P.wait(nc.sync, "cV", cv)
                dst = bass.AP(a2av["in"], (2048 * 4 * js + ch * 128) * 128,
                              [[128, 128], [2048 * 128, 4], [1, 128]])
                st = P.dma(nc.sync, "dS", dst,
                           ev[:].rearrange("p (pr w) -> p pr w", w=128))
        P.wait(nc.gpsimd, "dS", st)
        P.wait(nc.gpsimd, "cV", P.cnt["cV"])
        from concourse.aot_env import allow_ici_all_to_all
        with allow_ici_all_to_all():
            P.inc(nc.gpsimd.collective_compute(
                "AllToAll", ALU.bypass, replica_groups=[list(range(NCORE))],
                ins=[a2aqk["in"].ap().opt()], outs=[a2aqk["out"].ap().opt()]), "cC")
            P.inc(nc.gpsimd.collective_compute(
                "AllToAll", ALU.bypass, replica_groups=[list(range(NCORE))],
                ins=[a2av["in"].ap().opt()], outs=[a2av["out"].ap().opt()]), "cC")
        P.barrier()


def emit_l0attn_f(P, aux, a2aqk_out, a2av_out, a2aat_in, a2aat_out):
    nc = P.nc
    S = 4096
    with P.scope():
        q_sb = P.sb("at_q", [128, S], BF16)
        k_sb = P.sb("at_k", [128, S], BF16)
        v_sb = P.sb("at_v", [128, 32 * 132], BF16)
        prb = [[P.sb(f"at_pr{h}{i}", [128, 512], BF16) for i in range(2)] for h in range(2)]
        maskT = P.sb("at_maskT", [128, 128], BF16)
        out_sb = [P.sb(f"at_out{h}", [128, 8 * 64], BF16) for h in range(2)]
        rcol = P.sb("at_rcol", [128, 1], F32)
        pS_ = [[P.psum(f"at_pS{h}{i}", [128, 512], F32) for i in range(2)] for h in range(2)]
        pO = [[P.psum(f"at_pO{h}{i}", [128, 264], F32) for i in range(2)] for h in range(2)]

        idf2 = P.sb("at_idf", [128, 128], F32)
        P.inc(nc.gpsimd.iota(idf2[:], [[-1, 128]], base=0, channel_multiplier=1,
                             allow_small_or_imprecise_dtypes=True), "cG")
        P.wait(nc.vector, "cG", P.cnt["cG"])
        P.inc(nc.vector.tensor_scalar(maskT[:], idf2[:], 0.0, None, ALU.is_le), "cV")
        vv = v_sb[:].rearrange("p (c w) -> p c w", w=132)
        P.inc(nc.vector.memset(vv[:, :, 64:65], 1.0), "cV")
        P.inc(nc.vector.memset(vv[:, :, 130:131], 1.0), "cV")
        mask_done = P.cnt["cV"]
        P.wait(nc.sync, "cV", mask_done)

        INV8 = float(1.0 / 8.0)
        ld = 0
        st = [0, 0]
        for s in range(NSEQ):
            P.wait(nc.sync, "cT", P.cnt["cT"])
            for cp in range(NCORE):
                P.dma_nb(nc.sync, "dS", q_sb[:, cp * 512:(cp + 1) * 512],
                         a2aqk_out[256 * cp: 256 * cp + 128, s * 512:(s + 1) * 512])
                P.dma_nb(nc.sync, "dS", k_sb[:, cp * 512:(cp + 1) * 512],
                         a2aqk_out[256 * cp + 128: 256 * cp + 256, s * 512:(s + 1) * 512])
            for ch in range(32):
                cp, rr = ch // 4, ch % 4
                dst = bass.AP(v_sb, ch * 132, [[32 * 132, 128], [66, 2], [1, 64]])
                P.dma_nb(nc.sync, "dS", dst,
                         a2av_out[2048 * cp + s * 512 + rr * 128:
                                  2048 * cp + s * 512 + (rr + 1) * 128, :])
            ld = P.dbar(nc.sync, "dS")
            P.wait(nc.tensor, "dS", ld)
            s_free = [[0, 0], [0, 0]]
            p_free = [[0, 0], [0, 0]]
            pm_extra = [[0, 0], [0, 0]]
            ex_cnt = [[0, 0], [0, 0]]
            po_free = [[0, 0], [0, 0]]
            for qs in range(8):
                nkc = 4 * qs + 4
                first_av = [True, True]

                def emit_S(kc):
                    q0 = max(qs * 512, kc * 128)
                    w = qs * 512 + 512 - q0
                    for h in range(2):
                        hb = h * 64
                        bank = pS_[h][kc % 2]
                        P.wait(nc.tensor, "cS", s_free[h][kc % 2])
                        smm = P.inc(nc.tensor.matmul(
                            bank[0:128, 0:w] if False else bank[:, 0:w],
                            k_sb[hb:hb + 64, kc * 128:(kc + 1) * 128],
                            q_sb[hb:hb + 64, q0:q0 + w],
                            start=True, stop=True), "cT")
                        P.wait(nc.scalar, "cT", max(smm, p_free[h][kc % 2]))
                        ex = P.inc(nc.scalar.activation(
                            prb[h][kc % 2][:, 0:w], bank[:, 0:w],
                            AF.Exp, scale=np.float32(INV8)), "cS")
                        s_free[h][kc % 2] = ex
                        ex_cnt[h][kc % 2] = ex
                        if q0 == kc * 128:
                            P.wait(nc.vector, "cS", ex)
                            md = P.inc(nc.vector.tensor_tensor(
                                prb[h][kc % 2][:, 0:128], prb[h][kc % 2][:, 0:128],
                                maskT[:], ALU.mult), "cV")
                            pm_extra[h][kc % 2] = md

                def emit_AV(kc):
                    q0 = max(qs * 512, kc * 128)
                    w = qs * 512 + 512 - q0
                    for h in range(2):
                        po = pO[h][qs % 2]
                        P.wait(nc.tensor, "cS", ex_cnt[h][kc % 2])
                        P.wait(nc.tensor, "cV", pm_extra[h][kc % 2])
                        if first_av[h]:
                            P.wait(nc.tensor, "cV", po_free[h][qs % 2])
                        for j in range(4):
                            jq = qs * 512 + j * 128
                            if jq < q0:
                                continue
                            P.inc(nc.tensor.matmul(
                                po[:, j * 66: j * 66 + 65],
                                prb[h][kc % 2][:, jq - q0: jq - q0 + 128],
                                v_sb[:, kc * 132 + h * 66: kc * 132 + h * 66 + 65],
                                start=(first_av[h] and j == 0), stop=(kc == nkc - 1)), "cT")
                        first_av[h] = False
                        p_free[h][kc % 2] = P.cnt["cT"]

                for kc in range(nkc):
                    emit_S(kc)
                    if kc > 0:
                        emit_AV(kc - 1)
                emit_AV(nkc - 1)
                av_done = P.cnt["cT"]
                for h in range(2):
                    po = pO[h][qs % 2]
                    P.wait(nc.vector, "cT", av_done)
                    for j in range(4):
                        P.wait(nc.vector, "cV", P.cnt["cV"])
                        P.inc(nc.vector.reciprocal(rcol[:], po[:, j * 66 + 64: j * 66 + 65]), "cV")
                        P.wait(nc.vector, "cV", P.cnt["cV"])
                        P.wait(nc.vector, "dS", st[h])
                        P.inc(nc.vector.tensor_scalar(
                            out_sb[h][:, (qs % 2) * 256 + j * 64: (qs % 2) * 256 + (j + 1) * 64],
                            po[:, j * 66: j * 66 + 64], rcol[:, 0:1], None, ALU.mult), "cV")
                    po_free[h][qs % 2] = P.cnt["cV"]
                if qs % 2 == 1:
                    for h in range(2):
                        hb = h * 64
                        ev2 = po_free[h][1]
                        P.wait(nc.sync, "cV", ev2)
                        for jj in range(2):
                            base = (2048 * (qs - 1 + jj) + s * 512) * 128 + hb
                            dstp = bass.AP(a2aat_in, base,
                                           [[128, 128], [128 * 128, 4], [1, 64]])
                            st[h] = P.inc(nc.sync.dma_start(
                                dstp, out_sb[h][:, jj * 256:(jj + 1) * 256]
                                .rearrange("p (jr c) -> p jr c", c=64)), "dS")
                        nc.sync.wait_ge(P.sem("dS"), st[h])
        from concourse.aot_env import allow_ici_all_to_all
        P.wait(nc.gpsimd, "dS", P.cnt["dS"])
        with allow_ici_all_to_all():
            P.inc(nc.gpsimd.collective_compute(
                "AllToAll", ALU.bypass, replica_groups=[list(range(NCORE))],
                ins=[a2aat_in.ap().opt()], outs=[a2aat_out.ap().opt()]), "cC")
        P.barrier()


def emit_l0proj_f(P, aux, xin_d, a2aat_out, W2_d, b2_d, yout_d):
    nc = P.nc
    with P.scope():
        w_sb = P.sb("pj_w", [128, KC * D], BF16)
        a_sb = P.sb("pj_a", [128, KC * TOK], BF16)
        b2_sb = P.sb("pj_b2", [1, D], BF16)
        res_t = P.sb("pj_res", [128, 512], F32)
        yout = P.sb("pj_yout", [128, 512], F32)
        pg = [P.psum(f"pj_pg{i}", [128, 512], F32) for i in range(4)]

        for kc in range(KC):
            P.dma_nb(nc.gpsimd, "dG", w_sb[:, kc * D:(kc + 1) * D],
                     W2_d[kc * 128:(kc + 1) * 128, :])
        P.dma_nb(nc.gpsimd, "dG", b2_sb[:, :], b2_d.unsqueeze(0))
        w_done = P.dbar(nc.gpsimd, "dG")
        for kc in range(KC):
            for tt in range(TOK // 512):
                P.dma_tnb(nc.sync, "dS", a_sb[:, kc * TOK + tt * 512: kc * TOK + (tt + 1) * 512],
                          a2aat_out[kc * 2048 + tt * 512: kc * 2048 + (tt + 1) * 512, :])
        a_done = P.dbar(nc.sync, "dS")
        P.wait(nc.tensor, "dG", w_done)
        P.wait(nc.tensor, "dS", a_done)
        P.wait(nc.tensor, "cV", aux["init_cv"])
        pclean = [0] * 4
        res_busy = 0
        yst = 0
        for m in range(KC):
            for tt in range(TOK // 512):
                bi = (m * 4 + tt) % 4
                bank = pg[bi]
                P.wait(nc.tensor, "cV", pclean[bi])
                nc.tensor.matmul(bank[:, :], b2_sb[:, m * 128:(m + 1) * 128],
                                 aux["ones_row"][:, :], start=True, stop=False)
                for kc in range(KC):
                    mm = P.inc(nc.tensor.matmul(
                        bank[:, :],
                        w_sb[:, kc * D + m * 128: kc * D + (m + 1) * 128],
                        a_sb[:, kc * TOK + tt * 512: kc * TOK + (tt + 1) * 512],
                        start=False, stop=(kc == KC - 1)), "cT")
                P.wait(nc.sync, "cV", res_busy)
                rld = P.dma(nc.sync, "dS",
                            res_t[:], xin_d[m * 128:(m + 1) * 128, tt * 512:(tt + 1) * 512])
                P.wait(nc.vector, "cT", mm)
                P.wait(nc.vector, "dS", rld)
                P.wait(nc.vector, "dS", yst)
                cv = P.inc(nc.vector.tensor_tensor(yout[:], bank[:, :], res_t[:], ALU.add), "cV")
                pclean[bi] = cv
                res_busy = cv
                P.wait(nc.sync, "cV", cv)
                yst = P.dma(nc.sync, "dS",
                            yout_d[m * 128:(m + 1) * 128, tt * 512:(tt + 1) * 512], yout[:])
    P.barrier()


def emit_l1ffn_f(P, aux, xin_d, rbf_d, agw, yout_d, xq_d, hq_d, rbf_out=None):
    nc = P.nc
    HT = 256
    with P.scope():
        wbig = P.sb("l1_wbig", [128, 45056], BF16)
        wgq_sb = wbig[:, 0:KC * 2816]
        wuq_sb = wbig[:, KC * 2816:2 * KC * 2816]
        wdq_sb = wbig[:, 0:22 * D]
        xqT = P.sb("l1_xqT", [128, KC * TOK], BF16)
        pb = [P.psum(f"l1_pb{i}", [128, 512], F32) for i in range(6)]
        tps = [P.psum(f"l1_tp{i}", [128, 128], BF16) for i in range(2)]
        scopeA = P.scope()
        scopeA.__enter__()
        xtb = P.sb("l1_xtb", [128, D], BF16)
        xtc = P.sb("l1_xtc", [128, D], F32)
        sqc = P.sb("l1_sqc", [128, D], BF16)
        xqc = P.sb("l1_xqc", [128, D], BF16)
        h_sb = P.sb("l1_h", [128, 2816], F32)
        hq_sb = P.sb("l1_hq", [128, 2816], BF16)
        sil = P.sb("l1_sil", [128, 512], F32)
        upe = P.sb("l1_upe", [128, 512], F32)
        scr = {
            "a": P.sb("l1_qa", [128, 2816], F32), "t": P.sb("l1_qt", [128, 2816], F32),
            "u": P.sb("l1_qu", [128, 2816], F32), "bm": P.sb("l1_qbm", [128, 176], F32),
            "scale": P.sb("l1_qsc", [128, 176], F32), "rsc": P.sb("l1_qrs", [128, 176], F32),
        }
        cs1 = P.sb("l1_cs1", [128, 1], F32)
        cs2 = P.sb("l1_cs2", [128, 1], F32)
        mean = P.sb("l1_mean", [128, 1], F32)
        rstd = P.sb("l1_rstd", [128, 1], F32)
        bcol = P.sb("l1_bcol", [128, 1], F32)

        P.wait(nc.sync, "cC", P.cnt["cC"])
        for kc in range(KC):
            P.dma_tnb(nc.sync, "dS", wgq_sb[:, kc * 2816:(kc + 1) * 2816],
                      agw["g_out"].ap()[:, kc * 128:(kc + 1) * 128])
            P.dma_tnb(nc.sync, "dS", wuq_sb[:, kc * 2816:(kc + 1) * 2816],
                      agw["u_out"].ap()[:, kc * 128:(kc + 1) * 128])
        wA_done = P.dbar(nc.sync, "dS")

        pclean_v = [0] * 6
        pclean_s = [0] * 6
        sil_busy = 0
        tp_free = [0, 0]
        xqT_ready = 0
        hq_st = 0
        P.wait(nc.tensor, "dS", wA_done)
        for i in range(TOK // 128):
            P.wait(nc.sync, "cG", scr.get("xtb_free", 0))
            ld = P.dma_t(nc.sync, "dS", xtb[:], rbf_d[:, i * 128:(i + 1) * 128])
            P.wait(nc.gpsimd, "dS", ld)
            P.wait(nc.gpsimd, "cV", scr.get("busy", 0))
            P.wait(nc.gpsimd, "cS", scr.get("busy_s", 0))
            xc = P.inc(nc.gpsimd.tensor_copy(xtc[:], xtb[:]), "cG")
            scr["xtb_free"] = xc
            P.wait(nc.vector, "cG", xc)
            P.inc(nc.vector.tensor_reduce(cs1[:], xtc[:], AX.X, ALU.add), "cV")
            P.wait(nc.scalar, "cG", xc)
            sq_done = P.inc(nc.scalar.activation(sqc[:], xtc[:], AF.Square), "cS")
            P.wait(nc.vector, "cS", sq_done)
            P.inc(nc.vector.tensor_reduce(cs2[:], sqc[:], AX.X, ALU.add), "cV")
            P.wait(nc.vector, "cV", P.cnt["cV"])
            P.inc(nc.vector.tensor_scalar(mean[:], cs1[:], 1.0 / D, None, ALU.mult), "cV")
            P.wait(nc.vector, "cV", P.cnt["cV"])
            P.inc(nc.vector.tensor_scalar(cs2[:], cs2[:], 1.0 / D, None, ALU.mult), "cV")
            P.wait(nc.vector, "cV", P.cnt["cV"])
            P.inc(nc.vector.tensor_tensor(cs1[:], mean[:], mean[:], ALU.mult), "cV")
            P.wait(nc.vector, "cV", P.cnt["cV"])
            P.inc(nc.vector.tensor_tensor(cs2[:], cs1[:], cs2[:], ALU.subtract), "cV")
            P.wait(nc.scalar, "cV", P.cnt["cV"])
            P.inc(nc.scalar.activation(cs2[:], cs2[:], AF.Ln,
                                       bias=np.float32(1e-5), scale=np.float32(-1.0)), "cS")
            P.wait(nc.scalar, "cS", P.cnt["cS"])
            sd = P.inc(nc.scalar.activation(rstd[:], cs2[:], AF.Exp, scale=np.float32(-0.5)), "cS")
            P.wait(nc.vector, "cS", sd)
            P.inc(nc.vector.scalar_tensor_tensor(bcol[:], mean[:], -1.0, rstd[:],
                                                 ALU.mult, ALU.mult), "cV")
            P.wait(nc.vector, "cV", P.cnt["cV"])
            P.inc(nc.vector.tensor_scalar(xtc[:], xtc[:], rstd[:, 0:1], bcol[:, 0:1],
                                          ALU.mult, ALU.add), "cV")
            emit_quant(P, xtc[:, :], xqc[:, :], scr, 128, D // 16,
                       after=[("cV", P.cnt["cV"])])
            qv = P.cnt["cV"]
            P.wait(nc.tensor, "cV", qv)
            for kc in range(KC):
                tb = tps[kc % 2]
                P.wait(nc.tensor, "cS", tp_free[kc % 2])
                tpc = P.inc(nc.tensor.transpose(tb[:, :], xqc[:, kc * 128:(kc + 1) * 128],
                                                aux["ident"][:, :]), "cT")
                P.wait(nc.scalar, "cT", tpc)
                tp_free[kc % 2] = P.inc(nc.scalar.activation(
                    xqT[:, kc * TOK + i * 128: kc * TOK + (i + 1) * 128],
                    tb[:, :], AF.Copy), "cS")
            xqT_ready = P.cnt["cS"]
            for si, (n0, ns) in enumerate(NSL):
                bg_, bu_ = pb[2 * (si % 3)], pb[2 * (si % 3) + 1]
                P.wait(nc.tensor, "cS", xqT_ready)
                P.wait(nc.tensor, "cV", pclean_v[2 * (si % 3)])
                P.wait(nc.tensor, "cS", pclean_s[2 * (si % 3)])
                P.wait(nc.tensor, "cV", pclean_v[2 * (si % 3) + 1])
                P.wait(nc.tensor, "cS", pclean_s[2 * (si % 3) + 1])
                for kc in range(KC):
                    nc.tensor.matmul(bg_[:, 0:ns],
                                     xqT[:, kc * TOK + i * 128: kc * TOK + (i + 1) * 128],
                                     wgq_sb[:, kc * 2816 + n0: kc * 2816 + n0 + ns],
                                     start=(kc == 0), stop=(kc == KC - 1))
                for kc in range(KC):
                    mmu = P.inc(nc.tensor.matmul(bu_[:, 0:ns],
                                     xqT[:, kc * TOK + i * 128: kc * TOK + (i + 1) * 128],
                                     wuq_sb[:, kc * 2816 + n0: kc * 2816 + n0 + ns],
                                     start=(kc == 0), stop=(kc == KC - 1)), "cT")
                P.wait(nc.scalar, "cT", mmu)
                P.wait(nc.scalar, "cG", sil_busy)
                ssil = P.inc(nc.scalar.activation(sil[:, 0:ns], bg_[:, 0:ns], AF.Silu), "cS")
                P.wait(nc.scalar, "cS", ssil)
                sup = P.inc(nc.scalar.activation(upe[:, 0:ns], bu_[:, 0:ns], AF.Copy), "cS")
                pclean_s[2 * (si % 3)] = sup
                pclean_s[2 * (si % 3) + 1] = sup
                P.wait(nc.gpsimd, "cS", sup)
                sil_busy = P.inc(nc.gpsimd.tensor_tensor(
                    h_sb[:, n0:n0 + ns], sil[:, 0:ns], upe[:, 0:ns], ALU.mult), "cG")
                pclean_v[2 * (si % 3)] = 0
                pclean_v[2 * (si % 3) + 1] = 0
            P.wait(nc.vector, "dG", hq_st)
            P.wait(nc.vector, "cG", sil_busy)
            P.wait(nc.scalar, "cG", sil_busy)
            emit_quant(P, h_sb[:, :], hq_sb[:, :], scr, 128, 176,
                       after=[("cV", P.cnt["cV"]), ("cG", sil_busy)])
            qv = P.cnt["cV"]
            P.wait(nc.gpsimd, "cV", qv)
            hq_st = P.dma(nc.gpsimd, "dG", hq_d.ap()[i * 128:(i + 1) * 128, :], hq_sb[:])
        hq_all = P.cnt["dG"]
        scopeA.__exit__(None, None, None)
        P.barrier()
        hql = P.sb("l1_hql", [128, 2 * 2816], BF16)
        hqT = P.sb("l1_hqT", [128, 22 * HT], BF16)
        res_t = P.sb("l1_res", [128, HT], F32)
        res2 = P.sb("l1_res2", [128, HT], F32)
        yout = P.sb("l1_yout", [128, HT], F32)
        yout2 = P.sb("l1_yout2", [128, HT], F32)
        ybf = P.sb("l1_ybf", [128, HT], BF16)
        ybf2 = P.sb("l1_ybf2", [128, HT], BF16)

        P.wait(nc.sync, "cT", P.cnt["cT"])
        for kk in range(22):
            P.dma_tnb(nc.sync, "dS", wdq_sb[:, kk * D:(kk + 1) * D],
                      agw["d_out"].ap()[:, kk * 128:(kk + 1) * 128])
        wB_done = P.dbar(nc.sync, "dS")
        P.wait(nc.sync, "dG", hq_all)
        res_busy = [0, 0]
        yst = [0, 0]
        hql_free = 0
        for tt in range(TOK // HT):
            P.wait(nc.sync, "cT", hql_free)
            for half in range(2):
                P.dma_nb(nc.sync, "dS", hql[:, half * 2816:(half + 1) * 2816],
                         hq_d.ap()[tt * HT + half * 128: tt * HT + (half + 1) * 128, :])
            hl = P.dbar(nc.sync, "dS")
            P.wait(nc.tensor, "dS", hl)
            P.wait(nc.tensor, "cT", P.cnt["cT"])
            for kk in range(22):
                for half in range(2):
                    tb = tps[(kk * 2 + half) % 2]
                    P.wait(nc.tensor, "cS", tp_free[(kk * 2 + half) % 2])
                    tpc = P.inc(nc.tensor.transpose(
                        tb[:, :], hql[:, half * 2816 + kk * 128: half * 2816 + (kk + 1) * 128],
                        aux["ident"][:, :]), "cT")
                    P.wait(nc.scalar, "cT", tpc)
                    tp_free[(kk * 2 + half) % 2] = P.inc(nc.scalar.activation(
                        hqT[:, kk * HT + half * 128: kk * HT + (half + 1) * 128],
                        tb[:, :], AF.Copy), "cS")
            hqT_ready = P.cnt["cS"]
            hql_free = P.cnt["cT"]
            P.wait(nc.tensor, "cS", hqT_ready)
            P.wait(nc.tensor, "dS", wB_done)
            for m in range(KC):
                it = tt * KC + m
                par = it % 2
                bi = 4 + (it % 2)
                bank = pb[bi]
                rt = res_t if par == 0 else res2
                yo = yout if par == 0 else yout2
                yb = ybf if par == 0 else ybf2
                P.wait(nc.gpsimd, "cV", res_busy[par])
                rld = P.dma(nc.gpsimd, "dG", rt[:, 0:HT],
                            xin_d[m * 128:(m + 1) * 128, tt * HT:(tt + 1) * HT])
                P.wait(nc.tensor, "cV", pclean_v[bi])
                for kk in range(22):
                    mm = P.inc(nc.tensor.matmul(bank[:, 0:HT],
                                    wdq_sb[:, kk * D + m * 128: kk * D + (m + 1) * 128],
                                    hqT[:, kk * HT:(kk + 1) * HT],
                                    start=(kk == 0), stop=(kk == 21)), "cT")
                P.wait(nc.vector, "cT", mm)
                P.wait(nc.vector, "dG", rld)
                P.wait(nc.vector, "dS", yst[par])
                cv = P.inc(nc.vector.scalar_tensor_tensor(
                    yo[:, 0:HT], bank[:, 0:HT], 0.5, rt[:, 0:HT],
                    ALU.mult, ALU.add), "cV")
                pclean_v[bi] = cv
                res_busy[par] = cv
                P.wait(nc.sync, "cV", cv)
                yst[par] = P.dma(nc.sync, "dS",
                                 yout_d[m * 128:(m + 1) * 128, tt * HT:(tt + 1) * HT],
                                 yo[:, 0:HT])
                if rbf_out is not None:
                    P.wait(nc.vector, "cV", cv)
                    cb = P.inc(nc.vector.tensor_copy(yb[:, 0:HT], yo[:, 0:HT]), "cV")
                    res_busy[par] = cb
                    P.wait(nc.sync, "cV", cb)
                    yst[par] = P.dma(nc.sync, "dS",
                                     rbf_out[m * 128:(m + 1) * 128, tt * HT:(tt + 1) * HT],
                                     yb[:, 0:HT])
    P.barrier()


def emit_l1attn_f(P, aux, xin_d, xkv_d, agh, halo_idx_d, hmask_d, Wd, bias0_d,
                  biasn_d, yout_d, rbf_out):
    nc = P.nc
    from concourse.bass import IndirectOffsetOnAxis
    with P.scope():
        xhat = P.sb("w_xhat", [128, KC * NTK], BF16)
        attnT = xhat[:, 0:KC * TOK]
        pr_bf = xhat[:, KC * TOK:KC * TOK + 2048]
        prT = xhat[:, KC * TOK + 2048:KC * TOK + 4096]
        qnT = P.sb("w_qnT", [128, KC * TOK], BF16)
        knT = P.sb("w_knT", [128, KC * NTK], BF16)
        v_sb = P.sb("w_v", [128, (NTK // 128) * D], BF16)
        w_sb = P.sb("w_w", [128, KC * D], BF16)
        bias_sb = P.sb("w_bias", [128, 2 * 8 * WIN], BF16)
        t1 = P.sb("w_t1", [128, 512], F32)
        rowr = P.sb("w_rowr", [128, 1], F32)
        rows_sb = P.sb("w_rows", [128, 1], F32)
        nw_sb = P.sb("w_nw", [128, 2 * KC], F32)
        qsc = P.sb("w_qsc", [128, 512], BF16)
        halo_sb = P.sb("w_halo", [128, 64], F32)
        halo2_sb = P.sb("w_halo2", [128, 64], F32)
        hzero_sb = P.sb("w_hzero", [128, 64], F32)
        hmask_sb = P.sb("w_hmask", [128, 64], mybir.dt.int32)
        hidx_sb = P.sb("w_hidx", [128, KC], mybir.dt.int32)
        ybf = P.sb("w_ybf", [128, 512], BF16)
        ps = [P.psum(f"w_ps{i}", [128, 512], F32) for i in range(4)]
        pav = [P.psum(f"w_pav{i}", [128, 512], F32) for i in range(2)]
        tps = [P.psum("w_tps0", [128, 128], BF16), P.psum("w_tps1", [128, 128], BF16)]
        aux["pA"] = pav[0]; aux["pB"] = pav[1]
        aux["pS"] = ps[0][0:1, :]; aux["pQ"] = ps[1][0:1, :]
        aux.pop("stats_mm", None); aux.pop("rows_read", None)
        rr_row = aux["mrow"]

        hsrc = bass.AP(xin_d.tensor if hasattr(xin_d, "tensor") else xin_d, 496,
                       [[TOK, D], [512, 4], [1, 16]])
        P.dma(nc.sync, "dS", agh["in"].ap(), hsrc)
        P.wait(nc.gpsimd, "dS", P.cnt["dS"])
        P.inc(nc.gpsimd.collective_compute(
            "AllGather", ALU.bypass, replica_groups=[list(range(NCORE))],
            ins=[agh["in"].ap().opt()], outs=[agh["out"].ap().opt()]), "cC")
        for s in range(NSEQ):
            P.dma_nb(nc.sync, "dS",
                     xkv_d.ap()[:, s * SEQT + 16: s * SEQT + 528],
                     xin_d[:, s * 512:(s + 1) * 512])
        P.wait(nc.vector, "cV", P.cnt["cV"])
        zz = P.inc(nc.vector.memset(t1[:, 0:448], 0.0), "cV")
        P.wait(nc.sync, "cV", zz)
        for kc in range(KC):
            dstp = bass.AP(xkv_d, kc * 128 * NTK + 528, [[NTK, 128], [SEQT, 4], [1, 112]])
            P.dma_nb(nc.sync, "dS", dstp, t1[:, 0:448].rearrange("p (s w) -> p s w", w=112))
        pads_done = P.dbar(nc.sync, "dS")
        hld = P.dma(nc.sync, "dS", hidx_sb[:, :], halo_idx_d)
        hld = P.dma(nc.sync, "dS", hmask_sb[:, :], hmask_d)
        P.wait(nc.vector, "dS", hld)
        P.wait(nc.vector, "cV", P.cnt["cV"])
        P.inc(nc.vector.memset(halo_sb[:], 0.0), "cV")
        P.wait(nc.vector, "cV", P.cnt["cV"])
        hz = P.inc(nc.vector.memset(hzero_sb[:], 0.0), "cV")
        P.wait(nc.gpsimd, "cV", hz)
        P.wait(nc.gpsimd, "cC", P.cnt["cC"])
        for kc in range(KC):
            g = P.inc(nc.gpsimd.indirect_dma_start(
                halo_sb[:, :], None, agh["out"].ap(),
                IndirectOffsetOnAxis(hidx_sb[:, kc:kc + 1], 0)), "dG")
            nc.gpsimd.wait_ge(P.sem("dG"), g)
            P.wait(nc.vector, "dG", g)
            P.wait(nc.vector, "cV", P.cnt["cV"])
            sel = P.inc(nc.vector.select(halo2_sb[:], hmask_sb[:], halo_sb[:],
                                         hzero_sb[:]), "cV")
            P.wait(nc.gpsimd, "cV", sel)
            dsth = bass.AP(xkv_d, kc * 128 * NTK, [[NTK, 128], [SEQT, 4], [1, 16]])
            P.dma(nc.gpsimd, "dG", dsth, halo2_sb[:].rearrange("p (s w) -> p s w", w=16))
        halo_done = P.cnt["dG"]
        P.wait(nc.sync, "dG", halo_done)

        P.dma_nb(nc.sync, "dS", bias_sb[:, 0:8 * WIN], bias0_d)
        P.dma_nb(nc.sync, "dS", bias_sb[:, 8 * WIN:16 * WIN], biasn_d)
        for j in range(KC):
            P.dma_nb(nc.sync, "dS", nw_sb[:, j:j + 1], Wd["qnw"][j * 128:(j + 1) * 128].unsqueeze(-1))
            P.dma_nb(nc.sync, "dS", nw_sb[:, KC + j:KC + j + 1], Wd["knw"][j * 128:(j + 1) * 128].unsqueeze(-1))
        const_ld = P.dbar(nc.sync, "dS")

        ln_scope = P.scope()
        ln_scope.__enter__()
        aux["xf32_t"] = P.sb("w_lnx", [128, KC * 512], F32)
        emit_layernorm(P, xkv_d.ap(), xhat, aux, ntok=NTK)
        ln_scope.__exit__(None, None, None)

        own_cols = [(s * SEQT + 16, 512) for s in range(NSEQ)]

        def proj(dst, w_dram, cols, rms_w_col, ncol_dst):
            P.wait(nc.gpsimd, "cT", P.cnt["cT"])
            for kc_ in range(KC):
                P.dma_nb(nc.gpsimd, "dG", w_sb[:, kc_ * D:(kc_ + 1) * D],
                         w_dram[kc_ * 128:(kc_ + 1) * 128, :])
            wd = P.dbar(nc.gpsimd, "dG")
            P.wait(nc.tensor, "dG", wd)
            pj_free = getattr(P, "_pj_free", [0, 0])
            for (c0, n, d0) in cols:
                for off in range(0, n, 512):
                    ns = min(512, n - off)
                    for m in range(KC):
                        bank = ps[2 + (m % 2)]
                        P.wait(nc.tensor, "cS", pj_free[m % 2])
                        for kc_ in range(KC):
                            mm = P.inc(nc.tensor.matmul(
                                bank[:, 0:ns],
                                w_sb[:, kc_ * D + m * 128:(kc_ * D + (m + 1) * 128)],
                                xhat[:, kc_ * NTK + c0 + off: kc_ * NTK + c0 + off + ns],
                                start=(kc_ == 0), stop=(kc_ == KC - 1)), "cT")
                        P.wait(nc.scalar, "cT", mm)
                        pj_free[m % 2] = P.inc(nc.scalar.activation(
                            dst[:, m * ncol_dst + d0 + off: m * ncol_dst + d0 + off + ns],
                            bank[:, 0:ns], AF.Copy), "cS")
                    P._pj_free = pj_free
                    if rms_w_col is not None:
                        evac = P.cnt["cS"]
                        for m in range(KC):
                            P.wait(nc.scalar, "cS", evac)
                            P.wait(nc.scalar, "cT", P.cnt["cT"])
                            sqd = P.inc(nc.scalar.activation(
                                qsc[:, 0:ns],
                                dst[:, m * ncol_dst + d0 + off: m * ncol_dst + d0 + off + ns],
                                AF.Square), "cS")
                            P.wait(nc.tensor, "cS", sqd)
                            P.wait(nc.tensor, "cV", P.cnt["cV"])
                            smm = P.inc(nc.tensor.matmul(ps[0][0:1, 0:ns], aux["ones_bf"][:, :],
                                                         qsc[:, 0:ns],
                                                         start=(m == 0), stop=(m == KC - 1)), "cT")
                        P.wait(nc.vector, "cT", smm)
                        P.inc(nc.vector.tensor_scalar(rr_row[:, 0:ns], ps[0][0:1, 0:ns],
                                                      1.0 / D, None, ALU.mult), "cV")
                        P.wait(nc.scalar, "cV", P.cnt["cV"])
                        P.inc(nc.scalar.activation(rr_row[:, 0:ns], rr_row[:, 0:ns],
                                                   AF.Ln, bias=np.float32(1e-6)), "cS")
                        P.wait(nc.scalar, "cS", P.cnt["cS"])
                        sdd = P.inc(nc.scalar.activation(rr_row[:, 0:ns], rr_row[:, 0:ns],
                                                         AF.Exp, scale=np.float32(-0.5)), "cS")
                        rrd = P.cnt["cV"]
                        P.wait(nc.tensor, "cV", rrd)
                        P.wait(nc.tensor, "cS", sdd)
                        bc = P.inc(nc.tensor.matmul(pav[0][:, 0:ns], aux["ones_f"][:, :],
                                                    rr_row[:, 0:ns], start=True, stop=True), "cT")
                        P.wait(nc.vector, "cT", bc)
                        for m in range(KC):
                            sl = dst[:, m * ncol_dst + d0 + off: m * ncol_dst + d0 + off + ns]
                            P.wait(nc.vector, "cV", P.cnt["cV"])
                            P.inc(nc.vector.tensor_tensor(t1[:, 0:ns], sl, pav[0][:, 0:ns],
                                                          ALU.mult), "cV")
                            P.wait(nc.vector, "cV", P.cnt["cV"])
                            P.inc(nc.vector.tensor_scalar(sl, t1[:, 0:ns],
                                                          nw_sb[:, rms_w_col + m:rms_w_col + m + 1],
                                                          None, ALU.mult), "cV")

        P.wait(nc.vector, "dS", const_ld)
        proj(knT, Wd["wk"], [(0, NTK, 0)], KC, NTK)
        proj(qnT, Wd["wq"], [(c, n, s * 512) for s, (c, n) in enumerate(own_cols)], 0, TOK)
        P.wait(nc.gpsimd, "cT", P.cnt["cT"])
        for kc_ in range(KC):
            P.dma_nb(nc.gpsimd, "dG", w_sb[:, kc_ * D:(kc_ + 1) * D],
                     Wd["wv"][kc_ * 128:(kc_ + 1) * 128, :])
        wd = P.dbar(nc.gpsimd, "dG")
        P.wait(nc.tensor, "dG", wd)
        vp_free = [0, 0]
        for ch in range(NTK // 128):
            for half in range(2):
                bank = ps[2 + half]
                P.wait(nc.tensor, "cS", vp_free[half])
                for kc_ in range(KC):
                    mm = P.inc(nc.tensor.matmul(
                        bank[:, 0:512],
                        xhat[:, kc_ * NTK + ch * 128: kc_ * NTK + (ch + 1) * 128],
                        w_sb[:, kc_ * D + half * 512: kc_ * D + half * 512 + 512],
                        start=(kc_ == 0), stop=(kc_ == KC - 1)), "cT")
                P.wait(nc.scalar, "cT", mm)
                vp_free[half] = P.inc(nc.scalar.activation(
                    v_sb[:, ch * D + half * 512: ch * D + half * 512 + 512],
                    bank[:, 0:512], AF.Copy), "cS")
        v_done = P.cnt["cS"]

        P.wait(nc.tensor, "cS", v_done)
        P.wait(nc.tensor, "cV", aux["init_cv"])
        INV = float(1.0 / np.sqrt(128.0))
        t1b = P.sb("w_t1b", [128, 512], F32)
        rsum8 = P.sb("w_rsum8", [128, 8], F32)
        rr8 = P.sb("w_rr8", [128, 8], F32)
        s_clean = [0] * 4
        t1_free = [0, 0]
        pr_free = 0
        tp_copy = [0, 0]
        prT_free = 0
        av_copy = [0, 0]
        for s in range(NSEQ):
            for qt in range(4):
                qcol = s * 512 + qt * 128
                kvcol = s * SEQT + qt * 128
                bvar = 0 if qt == 0 else 8 * WIN
                sc_mm = [0] * 4
                for h in range(8):
                    bank = ps[h // 2]
                    bo_ = (h % 2) * WIN
                    if h % 2 == 0:
                        P.wait(nc.tensor, "cV", s_clean[h // 2])
                    sc_mm[h // 2] = P.inc(nc.tensor.matmul(
                        bank[:, bo_:bo_ + WIN],
                        qnT[:, h * TOK + qcol: h * TOK + qcol + 128],
                        knT[:, h * NTK + kvcol: h * NTK + kvcol + WIN],
                        start=True, stop=True), "cT")
                for b in range(4):
                    tb1 = t1 if b % 2 == 0 else t1b
                    P.wait(nc.vector, "cT", sc_mm[b])
                    P.wait(nc.vector, "cS", t1_free[b % 2])
                    sc = P.inc(nc.vector.scalar_tensor_tensor(
                        tb1[:, 0:512], ps[b][:, 0:512], INV,
                        bias_sb[:, bvar + 2 * b * WIN: bvar + (2 * b + 2) * WIN],
                        ALU.mult, ALU.add), "cV")
                    s_clean[b] = sc
                    P.wait(nc.scalar, "cV", sc)
                    if b == 0:
                        P.wait(nc.scalar, "cT", pr_free)
                    ex = P.inc(nc.scalar.activation(pr_bf[:, 2 * b * WIN:(2 * b + 2) * WIN],
                                                    tb1[:, 0:512], AF.Exp), "cS")
                    t1_free[b % 2] = ex
                    P.wait(nc.vector, "cS", ex)
                    P.inc(nc.vector.tensor_reduce(
                        rsum8[:, 2 * b:2 * b + 2],
                        pr_bf[:, 2 * b * WIN:(2 * b + 2) * WIN].rearrange(
                            "p (h w) -> p h w", w=WIN),
                        AX.X, ALU.add), "cV")
                P.wait(nc.vector, "cV", P.cnt["cV"])
                P.inc(nc.vector.reciprocal(rr8[:, :], rsum8[:, :]), "cV")
                P.wait(nc.vector, "cV", P.cnt["cV"])
                for b in range(4):
                    P.inc(nc.vector.tensor_tensor(
                        pr_bf[:, 2 * b * WIN:(2 * b + 2) * WIN].rearrange(
                            "p (h w) -> p h w", w=WIN),
                        pr_bf[:, 2 * b * WIN:(2 * b + 2) * WIN].rearrange(
                            "p (h w) -> p h w", w=WIN),
                        rr8[:, 2 * b:2 * b + 2].unsqueeze(-1).broadcast_to([128, 2, WIN]),
                        ALU.mult), "cV")
                norm_done = P.cnt["cV"]
                P.wait(nc.tensor, "cV", norm_done)
                for b in range(16):
                    tb = tps[b % 2]
                    P.wait(nc.tensor, "cS", tp_copy[b % 2])
                    if b == 0:
                        P.wait(nc.tensor, "cT", prT_free)
                    tp = P.inc(nc.tensor.transpose(tb[:, :], pr_bf[:, b * 128:(b + 1) * 128],
                                                   aux["ident"][:, :]), "cT")
                    P.wait(nc.scalar, "cT", tp)
                    tp_copy[b % 2] = P.inc(nc.scalar.activation(
                        prT[:, b * 128:(b + 1) * 128], tb[:, :], AF.Copy), "cS")
                pr_free = P.cnt["cT"]
                prT_done = P.cnt["cS"]
                P.wait(nc.tensor, "cS", prT_done)
                for h in range(8):
                    bank = pav[h % 2]
                    c0 = 5 * s + qt
                    P.wait(nc.tensor, "cS", av_copy[h % 2])
                    nc.tensor.matmul(
                        bank[:, 0:128],
                        v_sb[:, c0 * D + h * 128:(c0 * D + (h + 1) * 128)],
                        prT[:, h * 256:h * 256 + 128],
                        start=True, stop=False)
                    av = P.inc(nc.tensor.matmul(
                        bank[:, 0:128],
                        v_sb[0:16, (c0 + 1) * D + h * 128:((c0 + 1) * D + (h + 1) * 128)],
                        prT[0:16, h * 256 + 128:h * 256 + 256],
                        start=False, stop=True), "cT")
                    P.wait(nc.scalar, "cT", av)
                    av_copy[h % 2] = P.inc(nc.scalar.activation(
                        attnT[:, h * TOK + qcol: h * TOK + qcol + 128],
                        bank[:, 0:128], AF.Copy), "cS")
                prT_free = P.cnt["cT"]
        attn_done_v = P.cnt["cV"]
        attn_done_s = P.cnt["cS"]

        P.wait(nc.gpsimd, "cT", P.cnt["cT"])
        for kc_ in range(KC):
            P.dma_nb(nc.gpsimd, "dG", w_sb[:, kc_ * D:(kc_ + 1) * D],
                     Wd["wo"][kc_ * 128:(kc_ + 1) * 128, :])
        wd = P.dbar(nc.gpsimd, "dG")
        P.wait(nc.tensor, "dG", wd)
        P.wait(nc.tensor, "cS", attn_done_s)
        res_busy = P.cnt["cV"]
        yst = 0
        for s in range(NSEQ):
            for m in range(KC):
                bank = ps[m % 4]
                P.wait(nc.tensor, "cV", P.cnt["cV"])
                for kc_ in range(KC):
                    mm = P.inc(nc.tensor.matmul(
                        bank[:, 0:512],
                        w_sb[:, kc_ * D + m * 128: kc_ * D + (m + 1) * 128],
                        attnT[:, kc_ * TOK + s * 512:(kc_ * TOK + (s + 1) * 512)],
                        start=(kc_ == 0), stop=(kc_ == KC - 1)), "cT")
                P.wait(nc.sync, "cV", res_busy)
                rld = P.dma(nc.sync, "dS", t1[:, 0:512],
                            xin_d[m * 128:(m + 1) * 128, s * 512:(s + 1) * 512])
                P.wait(nc.vector, "cT", mm)
                P.wait(nc.vector, "dS", rld)
                P.wait(nc.vector, "dS", yst)
                cv = P.inc(nc.vector.tensor_tensor(t1[:, 0:512],
                                                   bank[:, 0:512], t1[:, 0:512], ALU.add), "cV")
                res_busy = cv
                P.wait(nc.sync, "cV", cv)
                yst = P.dma(nc.sync, "dS",
                            yout_d[m * 128:(m + 1) * 128, s * 512:(s + 1) * 512], t1[:, 0:512])
                P.wait(nc.vector, "cV", cv)
                P.wait(nc.vector, "dS", P.cnt["dS"])
                cb = P.inc(nc.vector.tensor_copy(ybf[:], t1[:, 0:512]), "cV")
                res_busy = cb
                P.wait(nc.sync, "cV", cb)
                yst = P.dma(nc.sync, "dS",
                            rbf_out[m * 128:(m + 1) * 128, s * 512:(s + 1) * 512], ybf[:])
    P.barrier()


def build_fused(P, dbg=False):
    nc = P.nc
    xT = nc.dram_tensor("xT", [D, TOK], F32, kind="ExternalInput")
    l0 = {
        "wg": nc.dram_tensor("l0_wg", [D, 3 * D], F32, kind="ExternalInput").ap(),
        "ug": nc.dram_tensor("l0_ug", [D, 3 * D], F32, kind="ExternalInput").ap(),
        "wo": nc.dram_tensor("l0_wo", [D, D], F32, kind="ExternalInput").ap(),
        "uo": nc.dram_tensor("l0_uo", [D, D], F32, kind="ExternalInput").ap(),
        "bg": nc.dram_tensor("l0_bg", [3 * D], F32, kind="ExternalInput").ap(),
        "bo": nc.dram_tensor("l0_bo", [D], F32, kind="ExternalInput").ap(),
        "inc_": nc.dram_tensor("l0_inc", [D], F32, kind="ExternalInput").ap(),
    }
    W1qk_d = nc.dram_tensor("W1qk", [D, 2 * D], F32, kind="ExternalInput").ap()
    b1qk_d = nc.dram_tensor("b1qk", [2 * D], F32, kind="ExternalInput").ap()
    W1v_d = nc.dram_tensor("W1v", [D, D], F32, kind="ExternalInput").ap()
    b1v_d = nc.dram_tensor("b1v", [D], F32, kind="ExternalInput").ap()
    W2_d = nc.dram_tensor("W2", [D, D], F32, kind="ExternalInput").ap()
    b2_d = nc.dram_tensor("b2", [D], F32, kind="ExternalInput").ap()
    wgs = nc.dram_tensor("wgs", [NWQ, D], F32, kind="ExternalInput").ap()
    wus = nc.dram_tensor("wus", [NWQ, D], F32, kind="ExternalInput").ap()
    wds = nc.dram_tensor("wds", [NDQ, 2816], F32, kind="ExternalInput").ap()
    wat = {
        "wq": nc.dram_tensor("wq", [D, D], F32, kind="ExternalInput").ap(),
        "wk": nc.dram_tensor("wk", [D, D], F32, kind="ExternalInput").ap(),
        "wv": nc.dram_tensor("wv", [D, D], F32, kind="ExternalInput").ap(),
        "wo": nc.dram_tensor("wo1", [D, D], F32, kind="ExternalInput").ap(),
        "qnw": nc.dram_tensor("qnw", [D], F32, kind="ExternalInput").ap(),
        "knw": nc.dram_tensor("knw", [D], F32, kind="ExternalInput").ap(),
    }
    bias0_d = nc.dram_tensor("bias0", [128, 8 * WIN], BF16, kind="ExternalInput").ap()
    biasn_d = nc.dram_tensor("biasn", [128, 8 * WIN], BF16, kind="ExternalInput").ap()
    halo_idx_d = nc.dram_tensor("halo_idx", [128, KC], mybir.dt.int32, kind="ExternalInput").ap()
    hmask_d = nc.dram_tensor("hmask", [128, 64], mybir.dt.int32, kind="ExternalInput").ap()
    yT = nc.dram_tensor("yT", [D, TOK], F32, kind="ExternalOutput")

    rd = [nc.dram_tensor(f"rd{i}", [D, TOK], F32) for i in range(2)]
    rbf = nc.dram_tensor("rbf", [D, TOK], BF16)
    xq_d = nc.dram_tensor("xq_d", [TOK, D], BF16)
    hq_d = nc.dram_tensor("hq_d", [TOK, 2816], BF16)
    agw = {
        "g_in": nc.dram_tensor("agw_g_in", [NWQ, D], BF16),
        "g_out": nc.dram_tensor("agw_g_out", [2816, D], BF16, addr_space="Shared"),
        "u_in": nc.dram_tensor("agw_u_in", [NWQ, D], BF16),
        "u_out": nc.dram_tensor("agw_u_out", [2816, D], BF16, addr_space="Shared"),
        "d_in": nc.dram_tensor("agw_d_in", [NDQ, 2816], BF16),
        "d_out": nc.dram_tensor("agw_d_out", [D, 2816], BF16, addr_space="Shared"),
    }
    a2aqk = {"in": nc.dram_tensor("a2aqk_in", [TOK, TOK], BF16),
             "out": nc.dram_tensor("a2aqk_out", [TOK, TOK], BF16)}
    a2av = {"in": nc.dram_tensor("a2av_in", [NCORE * TOK, 128], BF16),
            "out": nc.dram_tensor("a2av_out", [NCORE * TOK, 128], BF16)}
    a2aat_in = nc.dram_tensor("a2aat_in", [NCORE * TOK, 128], BF16)
    a2aat_out = nc.dram_tensor("a2aat_out", [NCORE * TOK, 128], BF16)
    agh = {"in": nc.dram_tensor("agh_in", [D, 64], F32),
           "out": nc.dram_tensor("agh_out", [NCORE * D, 64], F32, addr_space="Shared")}
    xkv_d = nc.dram_tensor("xkv_d", [D, NTK], F32)

    P.block()
    aux = fused_aux(P)

    def snap(name, src_t):
        if not dbg:
            return
        dt_ = nc.dram_tensor(name, list(src_t.shape), src_t.dtype, kind="ExternalOutput")
        P.barrier()
        P.dma(nc.sync, "dS", dt_.ap(), src_t.ap())
        P.barrier()

    emit_fused_init(P, aux)
    emit_prep_f(P, aux, wgs, wus, wds, agw)
    emit_l0ffn_f(P, aux, xT, rd[0], l0)
    snap("s_ffn0", rd[0])
    emit_l0qkv_f(P, aux, rd[0], W1qk_d, b1qk_d, W1v_d, b1v_d, a2aqk, a2av)
    snap("s_qk", a2aqk["out"])
    snap("s_v", a2av["out"])
    emit_l0attn_f(P, aux, a2aqk["out"], a2av["out"], a2aat_in, a2aat_out)
    snap("s_at", a2aat_out)
    emit_l0proj_f(P, aux, rd[0], a2aat_out, W2_d, b2_d, rd[1])
    snap("s_proj", rd[1])
    emit_l0ffn_f(P, aux, rd[1], rd[0], l0, rbf_d=rbf)
    snap("s_ffn1", rd[0])
    emit_l1ffn_f(P, aux, rd[0], rbf, agw, rd[1], xq_d, hq_d)
    snap("s_l1f0", rd[1])
    snap("s_agwg", agw["g_out"])
    emit_l1attn_f(P, aux, rd[1], xkv_d, agh, halo_idx_d, hmask_d, wat, bias0_d,
                  biasn_d, rd[0], rbf)
    snap("s_l1at", rd[0])
    snap("s_xkv", xkv_d)
    emit_l1ffn_f(P, aux, rd[0], rbf, agw, yT, xq_d, hq_d)
    return P


def kernel(**inputs, ):
    return _kernel_impl(False, inputs)


def _kernel_impl(dbg, inputs):
    from concourse.aot_env import allow_ici_all_to_all
    x = np.asarray(inputs["L0_attn_x"], dtype=np.float32)
    W0 = {k: np.asarray(inputs[f"L0_ffn_{k}"], dtype=np.float32)
          for k in ["wf", "uf", "bf", "wi", "ui", "bi", "wc", "uc", "bc", "wo", "uo", "bo", "in_c"]}
    W1a = np.asarray(inputs["L0_attn_W1"], dtype=np.float32)
    b1a = np.asarray(inputs["L0_attn_b1"], dtype=np.float32)
    W2a = np.asarray(inputs["L0_attn_W2"], dtype=np.float32)
    b2a = np.asarray(inputs["L0_attn_b2"], dtype=np.float32)
    W1f = {k: np.asarray(inputs[f"L1_ffn_{k}"], dtype=np.float32)
           for k in ["w_gate", "w_up", "w_down"]}
    W1t = {k: np.asarray(inputs[f"L1_attn_{k}"], dtype=np.float32)
           for k in ["wq", "wk", "wv", "wo", "q_norm_w", "k_norm_w"]}

    with allow_ici_all_to_all():
        P = Prog()
        build_fused(P, dbg=dbg)
        P.close()

    w3 = W1a.reshape(1024, 16, 3, 64)
    W1qk = np.ascontiguousarray(
        np.concatenate([w3[:, :, 0, :].reshape(1024, 1024),
                        w3[:, :, 1, :].reshape(1024, 1024)], axis=1))
    W1v = np.ascontiguousarray(w3[:, :, 2, :].reshape(1024, 1024))
    b3 = b1a.reshape(16, 3, 64)
    b1qk = np.ascontiguousarray(
        np.concatenate([b3[:, 0, :].reshape(-1), b3[:, 1, :].reshape(-1)]))
    b1v = np.ascontiguousarray(b3[:, 2, :].reshape(-1))

    shards = shard_xT(x)
    wg = np.ascontiguousarray(np.concatenate([W0["wf"].T, W0["wi"].T, W0["wc"].T], axis=1))
    ug = np.ascontiguousarray(np.concatenate([W0["uf"].T, W0["ui"].T, W0["uc"].T], axis=1))
    common = {
        "l0_wg": wg, "l0_ug": ug,
        "l0_wo": np.ascontiguousarray(W0["wo"].T), "l0_uo": np.ascontiguousarray(W0["uo"].T),
        "l0_bg": np.ascontiguousarray(np.concatenate([W0["bf"], W0["bi"], W0["bc"]])),
        "l0_bo": np.ascontiguousarray(W0["bo"]), "l0_inc": np.ascontiguousarray(W0["in_c"]),
        "W1qk": W1qk, "b1qk": b1qk, "W1v": W1v, "b1v": b1v, "W2": W2a, "b2": b2a,
        "wq": np.ascontiguousarray(W1t["wq"].T), "wk": np.ascontiguousarray(W1t["wk"].T),
        "wv": np.ascontiguousarray(W1t["wv"].T), "wo1": np.ascontiguousarray(W1t["wo"].T),
        "qnw": np.ascontiguousarray(W1t["q_norm_w"]), "knw": np.ascontiguousarray(W1t["k_norm_w"]),
    }
    in_maps = []
    for c in range(NCORE):
        b0, bn = make_l1attn_bias(c)
        if c > 0:
            hidx = ((c - 1) * D + np.arange(KC)[None, :] * 128
                    + np.arange(128)[:, None]).astype(np.int32)
        else:
            hidx = np.zeros((128, KC), dtype=np.int32)
        in_maps.append({
            **common,
            "xT": shards[c],
            "wgs": np.ascontiguousarray(W1f["w_gate"][c * NWQ:(c + 1) * NWQ]),
            "wus": np.ascontiguousarray(W1f["w_up"][c * NWQ:(c + 1) * NWQ]),
            "wds": np.ascontiguousarray(W1f["w_down"][c * NDQ:(c + 1) * NDQ]),
            "bias0": b0, "biasn": bn,
            "halo_idx": np.ascontiguousarray(hidx),
            "hmask": np.full((128, 64), 0 if c == 0 else 1, dtype=np.int32),
        })
    with allow_ici_all_to_all():
        res = run_bass_kernel_spmd(P.nc, in_maps, core_ids=list(range(NCORE)))
    if dbg:
        return res.results
    return unshard_xT([r["yT"] for r in res.results])


# version 17
# speedup vs baseline: 1.0147x; 1.0147x over previous
import numpy as np
import concourse.bass as bass
import concourse.mybir as mybir
from concourse.bass_utils import run_bass_kernel_spmd

F32 = mybir.dt.float32
BF16 = mybir.dt.bfloat16
AF = mybir.ActivationFunctionType
ALU = mybir.AluOpType
AX = mybir.AxisListType

D = 1024
KC = 8
NCORE = 8
TOK = 2048
NSEQ = 4
POS = 512


class Prog:

    def __init__(self):
        self.nc = bass.Bass("TRN2", target_bir_lowering=False)
        self._ctxs = []
        self.sems = {}
        self.cnt = {}

    def sem(self, name):
        if name not in self.sems:
            self.sems[name] = self._enter(self.nc.semaphore(name))
            self.cnt[name] = 0
        return self.sems[name]

    def _enter(self, cm):
        v = cm.__enter__()
        self._ctxs.append(cm)
        return v

    def _uname(self, name):
        if not hasattr(self, "_names"):
            self._names = {}
        n = self._names.get(name, 0)
        self._names[name] = n + 1
        return name if n == 0 else f"{name}__{n}"

    def sb(self, name, shape, dt):
        return self._enter(self.nc.sbuf_tensor(self._uname(name), shape, dt))

    def psum(self, name, shape, dt=F32):
        return self._enter(self.nc.psum_tensor(self._uname(name), shape, dt))

    def inc(self, instr, name, n=None):
        if n is None:
            n = 16 if name.startswith("d") else 1
        instr.then_inc(self.sem(name), n)
        self.cnt[name] += n
        return self.cnt[name]

    def wait(self, eng, name, val):
        if val > 0:
            eng.wait_ge(self.sem(name), val)

    def dma(self, eng, name, out, in_):
        c = self.inc(eng.dma_start(out, in_), name)
        eng.wait_ge(self.sem(name), c)
        return c

    def dma_t(self, eng, name, out, in_):
        c = self.inc(eng.dma_start_transpose(out, in_), name)
        eng.wait_ge(self.sem(name), c)
        return c

    def dma_nb(self, eng, name, out, in_):
        return self.inc(eng.dma_start(out, in_), name)

    def dma_tnb(self, eng, name, out, in_):
        return self.inc(eng.dma_start_transpose(out, in_), name)

    def dbar(self, eng, name):
        eng.wait_ge(self.sem(name), self.cnt[name])
        return self.cnt[name]

    def block(self):
        for s in ("dS", "dG", "dA", "cT", "cV", "cS", "cG", "cC"):
            self.sem(s)
        return self._enter(self.nc.Block())

    def scope(self):
        P = self

        class _Scope:
            def __enter__(s):
                s.mark = len(P._ctxs)
                return s

            def __exit__(s, *a):
                for cm in reversed(P._ctxs[s.mark:]):
                    cm.__exit__(None, None, None)
                del P._ctxs[s.mark:]

        return _Scope()

    def barrier(self):
        nc = self.nc
        engs = [nc.tensor, nc.vector, nc.scalar, nc.gpsimd, nc.sync]
        snap = [(n, self.cnt[n]) for n in ("cT", "cV", "cS", "cG", "cC", "dS", "dG")]
        for e in engs:
            for n, v in snap:
                if v > 0:
                    e.wait_ge(self.sem(n), v)

    def close(self):
        for cm in reversed(self._ctxs):
            cm.__exit__(None, None, None)
        self._ctxs = []


def ln_aux(P, ntok=TOK, psums=None):
    if psums is None:
        psums = {
            "pS": P.psum("ln_pS", [1, 512], F32),
            "pQ": P.psum("ln_pQ", [1, 512], F32),
            "pA": P.psum("ln_pA", [128, 512], F32),
            "pB": P.psum("ln_pB", [128, 512], F32),
        }
    return {
        **psums,
        "ones_bf": P.sb("ln_ones_bf", [128, 1], BF16),
        "ones_f": P.sb("ln_ones_f", [1, 128], F32),
        "xf32_t": P.sb("ln_xf32_t", [128, KC * 512], F32),
        "xbf": P.sb("ln_xbf", [128, 512], BF16),
        "sq": P.sb("ln_sq", [128, 512], BF16),
        "arow": P.sb("ln_arow", [1, 512], F32),
        "brow": P.sb("ln_brow", [1, 512], F32),
        "mrow": P.sb("ln_mrow", [1, 512], F32),
        "init": False,
    }


def emit_ln_init(P, aux):
    nc = P.nc
    P.inc(nc.vector.memset(aux["ones_bf"][:], 1.0), "cV")
    P.inc(nc.vector.memset(aux["ones_f"][:], 1.0), "cV")
    aux["init"] = True


def emit_layernorm(P, xT_dram, xhat_sb, aux, ntok=TOK, eps=1e-5, keep_x=None):
    nc = P.nc
    assert aux["init"]
    ones_bf, ones_f = aux["ones_bf"], aux["ones_f"]
    xt, xbf, sq = aux["xf32_t"], aux["xbf"], aux["sq"]
    pS, pQ, pA, pB = aux["pS"], aux["pQ"], aux["pA"], aux["pB"]
    arow, brow, mrow = aux["arow"], aux["brow"], aux["mrow"]
    vrow = brow

    for tt in range(ntok // 512):
        prev_apply = P.cnt["cV"]
        for kc in range(KC):
            P.wait(nc.sync, "cV", prev_apply)
            P.dma(nc.sync, "dS", xt[:, kc * 512:(kc + 1) * 512],
                xT_dram[kc * 128:(kc + 1) * 128, tt * 512:(tt + 1) * 512])
        ld = P.cnt["dS"]
        prev_rows = aux.get("rows_read", 0)
        for kc in range(KC):
            P.wait(nc.vector, "dS", ld)
            P.wait(nc.vector, "cT", aux.get("stats_mm", 0))
            c1 = P.inc(nc.vector.tensor_copy(xbf[:], xt[:, kc * 512:(kc + 1) * 512]), "cV")
            P.wait(nc.scalar, "dS", ld)
            P.wait(nc.scalar, "cT", aux.get("stats_mm", 0))
            c2 = P.inc(nc.scalar.activation(sq[:], xt[:, kc * 512:(kc + 1) * 512], AF.Square), "cS")
            P.wait(nc.tensor, "cV", c1)
            P.wait(nc.tensor, "cS", c2)
            P.wait(nc.tensor, "cV", prev_rows)
            nc.tensor.matmul(pS[:, :], ones_bf[:, :], xbf[:, :],
                             start=(kc == 0), stop=(kc == KC - 1))
            aux["stats_mm"] = P.inc(nc.tensor.matmul(pQ[:, :], ones_bf[:, :], sq[:, :],
                                                     start=(kc == 0), stop=(kc == KC - 1)), "cT")
        P.wait(nc.vector, "cT", aux["stats_mm"])
        P.inc(nc.vector.tensor_scalar(mrow[:], pS[:, :], 1.0 / D, None, ALU.mult), "cV")
        P.inc(nc.vector.tensor_scalar(vrow[:], pQ[:, :], 1.0 / D, None, ALU.mult), "cV")
        aux["rows_read"] = P.cnt["cV"]
        P.wait(nc.vector, "cV", P.cnt["cV"])
        P.inc(nc.vector.tensor_tensor(arow[:], mrow[:], mrow[:], ALU.mult), "cV")
        P.wait(nc.vector, "cV", P.cnt["cV"])
        P.inc(nc.vector.tensor_tensor(vrow[:], arow[:], vrow[:], ALU.subtract), "cV")
        P.wait(nc.scalar, "cV", P.cnt["cV"])
        P.inc(nc.scalar.activation(vrow[:], vrow[:], AF.Ln, bias=np.float32(eps), scale=np.float32(-1.0)), "cS")
        P.wait(nc.scalar, "cS", P.cnt["cS"])
        sva = P.inc(nc.scalar.activation(arow[:], vrow[:], AF.Exp, scale=np.float32(-0.5)), "cS")
        P.wait(nc.vector, "cS", sva)
        P.inc(nc.vector.scalar_tensor_tensor(brow[:], mrow[:], -1.0, arow[:],
                                             ALU.mult, ALU.mult), "cV")
        rows_done = P.cnt["cV"]
        P.wait(nc.tensor, "cV", rows_done)
        nc.tensor.matmul(pA[:, :], ones_f[:, :], arow[:, :], start=True, stop=True)
        bcast_mm = P.inc(nc.tensor.matmul(pB[:, :], ones_f[:, :], brow[:, :],
                                          start=True, stop=True), "cT")
        P.wait(nc.vector, "cT", bcast_mm)
        for kc in range(KC):
            xsl = xt[:, kc * 512:(kc + 1) * 512]
            osl = xhat_sb[:, kc * ntok + tt * 512: kc * ntok + (tt + 1) * 512]
            P.wait(nc.vector, "cV", P.cnt["cV"])
            P.inc(nc.vector.tensor_tensor(xbf[:], xsl, pA[:, :], ALU.mult), "cV")
            P.wait(nc.vector, "cV", P.cnt["cV"])
            P.inc(nc.vector.tensor_tensor(osl, xbf[:], pB[:, :], ALU.add), "cV")
            if keep_x is not None:
                P.inc(nc.vector.tensor_copy(
                    keep_x[:, kc * ntok + tt * 512: kc * ntok + (tt + 1) * 512], xsl), "cV")
    return P.cnt["cV"]



def build_l0ffn(P):
    nc = P.nc
    xT = nc.dram_tensor("xT", [D, TOK], F32, kind="ExternalInput").ap()
    wg = nc.dram_tensor("wg", [D, 3 * D], F32, kind="ExternalInput").ap()
    ug = nc.dram_tensor("ug", [D, 3 * D], F32, kind="ExternalInput").ap()
    wo = nc.dram_tensor("wo", [D, D], F32, kind="ExternalInput").ap()
    uo = nc.dram_tensor("uo", [D, D], F32, kind="ExternalInput").ap()
    bg = nc.dram_tensor("bg", [3 * D], F32, kind="ExternalInput").ap()
    bo = nc.dram_tensor("bo", [D], F32, kind="ExternalInput").ap()
    inc_ = nc.dram_tensor("inc_", [D], F32, kind="ExternalInput").ap()
    yT = nc.dram_tensor("yT", [D, TOK], F32, kind="ExternalOutput").ap()

    wg_sb = P.sb("wg_sb", [128, KC * 3 * D], BF16)
    ug_sb = P.sb("ug_sb", [128, KC * 3 * D], BF16)
    big = P.sb("big", [128, 16384], BF16)
    gf = big[:, 0:KC * POS]
    gi = big[:, KC * POS:2 * KC * POS]
    gg = big[:, 2 * KC * POS:3 * KC * POS]
    xhat = P.sb("xhat", [128, KC * TOK], BF16)
    c_sb = P.sb("c_sb", [128, KC * POS], BF16)
    tmp1 = P.sb("tmp1", [128, POS], BF16)
    tmp2 = P.sb("tmp2", [128, POS], BF16)
    hout = P.sb("hout", [128, 512], F32)
    res_t = P.sb("res_t", [128, 512], F32)
    yout = P.sb("yout", [128, 512], F32)
    bgs = P.sb("bgs", [128, 24], F32)
    bos = P.sb("bos", [128, KC], F32)
    incs = P.sb("incs", [128, KC], F32)
    aux = ln_aux(P)
    pg = [P.psum(f"pg{i}", [128, 512], F32) for i in range(4)]

    P.block()
    emit_ln_init(P, aux)
    for kc in range(KC):
        P.dma(nc.gpsimd, "dG", wg_sb[:, kc * 3 * D:(kc + 1) * 3 * D],
                                  wg[kc * 128:(kc + 1) * 128, :])
        P.dma(nc.gpsimd, "dG", ug_sb[:, kc * 3 * D:(kc + 1) * 3 * D],
                                  ug[kc * 128:(kc + 1) * 128, :])
    for j in range(24):
        P.dma(nc.sync, "dS", bgs[:, j:j + 1], bg[j * 128:(j + 1) * 128].unsqueeze(-1))
    for j in range(KC):
        P.dma(nc.sync, "dS", bos[:, j:j + 1], bo[j * 128:(j + 1) * 128].unsqueeze(-1))
        P.dma(nc.sync, "dS", incs[:, j:j + 1], inc_[j * 128:(j + 1) * 128].unsqueeze(-1))
    w_done = P.cnt["dG"]
    const_done = P.cnt["dS"]
    nc.sync.wait_ge(P.sem("dS"), const_done)

    emit_layernorm(P, xT, xhat, aux)

    P.wait(nc.vector, "dS", const_done)
    for kc in range(KC):
        P.inc(nc.vector.tensor_copy(
            c_sb[:, kc * POS:(kc + 1) * POS],
            incs[:, kc:kc + 1].broadcast_to([128, POS])), "cV")
    c_ready = P.cnt["cV"]

    P.wait(nc.tensor, "dG", w_done)
    gate_view = {0: gf, 1: gi, 2: gg}
    act_fn = {0: AF.Sigmoid, 1: AF.Sigmoid, 2: AF.Tanh}
    pclean = [0, 0, 0, 0]
    for t in range(4):
        P.wait(nc.tensor, "cV", c_ready)
        for gate in range(3):
            for m in range(KC):
                bi = (gate * KC + m) % 4
                bank = pg[bi]
                P.wait(nc.tensor, "cS", pclean[bi])
                for kc in range(KC):
                    nc.tensor.matmul(
                        bank[:, :],
                        wg_sb[:, kc * 3 * D + gate * D + m * 128: kc * 3 * D + gate * D + (m + 1) * 128],
                        xhat[:, kc * TOK + t * POS: kc * TOK + (t + 1) * POS],
                        start=(kc == 0), stop=False)
                for kc in range(KC):
                    last = P.inc(nc.tensor.matmul(
                        bank[:, :],
                        ug_sb[:, kc * 3 * D + gate * D + m * 128: kc * 3 * D + gate * D + (m + 1) * 128],
                        c_sb[:, kc * POS:(kc + 1) * POS],
                        start=False, stop=(kc == KC - 1)), "cT")
                P.wait(nc.scalar, "cT", last)
                pclean[bi] = P.inc(nc.scalar.activation(
                    gate_view[gate][:, m * POS:(m + 1) * POS], bank[:, :],
                    act_fn[gate], bias=bgs[:, gate * KC + m: gate * KC + m + 1]), "cS")
        gates_done = P.cnt["cS"]
        all_mm = P.cnt["cT"]
        P.wait(nc.vector, "cS", gates_done)
        P.wait(nc.vector, "cT", all_mm)
        for m in range(KC):
            sl = slice(m * POS, (m + 1) * POS)
            P.wait(nc.vector, "cV", P.cnt["cV"])
            P.inc(nc.vector.tensor_tensor(tmp1[:], gi[:, sl], gg[:, sl], ALU.mult), "cV")
            P.wait(nc.vector, "cV", P.cnt["cV"])
            P.inc(nc.vector.tensor_tensor(tmp2[:], gf[:, sl], c_sb[:, sl], ALU.mult), "cV")
            P.wait(nc.vector, "cV", P.cnt["cV"])
            P.inc(nc.vector.tensor_tensor(c_sb[:, sl], tmp1[:], tmp2[:], ALU.add), "cV")
        c_ready = P.cnt["cV"]

    P.wait(nc.gpsimd, "cV", c_ready)
    wo_sb = big[:, 0:KC * D]
    uo_sb = big[:, KC * D:2 * KC * D]
    for kc in range(KC):
        P.dma(nc.gpsimd, "dG", wo_sb[:, kc * D:(kc + 1) * D],
                                  wo[kc * 128:(kc + 1) * 128, :])
        P.dma(nc.gpsimd, "dG", uo_sb[:, kc * D:(kc + 1) * D],
                                  uo[kc * 128:(kc + 1) * 128, :])
    wo_done = P.cnt["dG"]

    P.wait(nc.tensor, "dG", wo_done)
    res_ld = P.cnt["dS"]
    for t in range(4):
        for m in range(KC):
            bi = (t * KC + m) % 4
            bank = pg[bi]
            P.wait(nc.sync, "cV", P.cnt["cV"])
            P.wait(nc.sync, "dS", res_ld)
            res_ld = P.dma(nc.sync, "dS", res_t[:], xT[m * 128:(m + 1) * 128, t * POS:(t + 1) * POS])
            P.wait(nc.tensor, "cS", pclean[bi])
            for kc in range(KC):
                nc.tensor.matmul(
                    bank[:, :],
                    wo_sb[:, kc * D + m * 128: kc * D + (m + 1) * 128],
                    xhat[:, kc * TOK + t * POS: kc * TOK + (t + 1) * POS],
                    start=(kc == 0), stop=False)
            for kc in range(KC):
                last = P.inc(nc.tensor.matmul(
                    bank[:, :],
                    uo_sb[:, kc * D + m * 128: kc * D + (m + 1) * 128],
                    c_sb[:, kc * POS:(kc + 1) * POS],
                    start=False, stop=(kc == KC - 1)), "cT")
            P.wait(nc.scalar, "cT", last)
            P.wait(nc.scalar, "cV", P.cnt["cV"])
            pclean[bi] = P.inc(nc.scalar.activation(
                hout[:], bank[:, :], AF.Tanh, bias=bos[:, m:m + 1]), "cS")
            P.wait(nc.vector, "cS", pclean[bi])
            P.wait(nc.vector, "dS", res_ld)
            cv = P.inc(nc.vector.scalar_tensor_tensor(
                yout[:], hout[:], 0.5, res_t[:], ALU.mult, ALU.add), "cV")
            P.wait(nc.sync, "cV", cv)
            P.dma(nc.sync, "dS", yT[m * 128:(m + 1) * 128, t * POS:(t + 1) * POS],
                                    yout[:])
    nc.sync.wait_ge(P.sem("dS"), P.cnt["dS"])
    return P



def shard_xT(x):
    out = []
    for i in range(NCORE):
        sl = x[:, i * POS:(i + 1) * POS, :]
        out.append(np.ascontiguousarray(sl.transpose(2, 0, 1).reshape(D, TOK)))
    return out


def unshard_xT(shards):
    x = np.empty((NSEQ, NCORE * POS, D), dtype=np.float32)
    for i, sh in enumerate(shards):
        x[:, i * POS:(i + 1) * POS, :] = sh.reshape(D, NSEQ, POS).transpose(1, 2, 0)
    return x


def run_l0ffn(xT_shards, W):
    P = Prog()
    build_l0ffn(P)
    P.close()
    wg = np.ascontiguousarray(np.concatenate([W["wf"].T, W["wi"].T, W["wc"].T], axis=1))
    ug = np.ascontiguousarray(np.concatenate([W["uf"].T, W["ui"].T, W["uc"].T], axis=1))
    wo = np.ascontiguousarray(W["wo"].T)
    uo = np.ascontiguousarray(W["uo"].T)
    bgc = np.ascontiguousarray(np.concatenate([W["bf"], W["bi"], W["bc"]]))
    in_maps = [{"xT": s, "wg": wg, "ug": ug, "wo": wo, "uo": uo,
                "bg": bgc, "bo": np.ascontiguousarray(W["bo"]),
                "inc_": np.ascontiguousarray(W["in_c"])} for s in xT_shards]
    res = run_bass_kernel_spmd(P.nc, in_maps, core_ids=list(range(NCORE)))
    return [r["yT"] for r in res.results]


def kernel(**inputs):
    x = np.asarray(inputs["L0_attn_x"], dtype=np.float32)
    W0 = {k: np.asarray(inputs[f"L0_ffn_{k}"], dtype=np.float32)
          for k in ["wf", "uf", "bf", "wi", "ui", "bi", "wc", "uc", "bc", "wo", "uo", "bo", "in_c"]}
    W1a = np.asarray(inputs["L0_attn_W1"], dtype=np.float32)
    b1a = np.asarray(inputs["L0_attn_b1"], dtype=np.float32)
    W2a = np.asarray(inputs["L0_attn_W2"], dtype=np.float32)
    b2a = np.asarray(inputs["L0_attn_b2"], dtype=np.float32)
    W1f = {k: np.asarray(inputs[f"L1_ffn_{k}"], dtype=np.float32)
           for k in ["w_gate", "w_up", "w_down"]}
    W1t = {k: np.asarray(inputs[f"L1_attn_{k}"], dtype=np.float32)
           for k in ["wq", "wk", "wv", "wo", "q_norm_w", "k_norm_w"]}

    shards = shard_xT(x)
    shards = run_l0ffn(shards, W0)
    shards = run_l0attn_all(shards, W1a, b1a, W2a, b2a)
    shards = run_l0ffn(shards, W0)
    qw = run_l1prep(W1f)
    shards = run_l1ffn(shards, qw)
    x_mid = unshard_xT(shards)
    shards = run_l1attn(x_mid, W1t)
    shards = run_l1ffn(shards, qw)
    return unshard_xT(shards)



def emit_quant(P, x_ap, out_ap, scr, np_, nb, after=None):
    nc = P.nc
    a, t, u = scr["a"], scr["t"], scr["u"]
    bm, scale, rsc = scr["bm"], scr["scale"], scr["rsc"]
    F = nb * 16
    xv = x_ap.rearrange("p (b k) -> p b k", k=16)
    if after:
        for s, v in after:
            P.wait(nc.vector, s, v)
    P.wait(nc.vector, "cV", scr.get("busy", 0))
    P.wait(nc.vector, "cS", scr.get("busy_s", 0))
    P.inc(nc.vector.tensor_reduce(bm[:np_, :nb], xv, AX.X, ALU.max,
                                  apply_absolute_value=True), "cV")
    P.wait(nc.vector, "cV", P.cnt["cV"])
    P.inc(nc.vector.tensor_scalar(scale[:np_, :nb], bm[:np_, :nb],
                                  1.0 / 6.0, 1e-12, ALU.mult, ALU.max), "cV")
    P.wait(nc.vector, "cV", P.cnt["cV"])
    P.inc(nc.vector.reciprocal(rsc[:np_, :nb], scale[:np_, :nb]), "cV")
    P.wait(nc.vector, "cV", P.cnt["cV"])
    rscb = rsc[:np_, :nb].unsqueeze(-1).broadcast_to([np_, nb, 16])
    scaleb = scale[:np_, :nb].unsqueeze(-1).broadcast_to([np_, nb, 16])
    tv16 = t[:np_, :F].rearrange("p (b k) -> p b k", k=16)
    P.inc(nc.vector.tensor_tensor(tv16, xv, rscb, ALU.mult), "cV")
    P.wait(nc.scalar, "cV", P.cnt["cV"])
    ab = P.inc(nc.scalar.activation(a[:np_, :F], t[:np_, :F], AF.Abs), "cS")
    P.wait(nc.vector, "cS", ab)
    P.inc(nc.vector.tensor_scalar(t[:np_, :F], a[:np_, :F], 0.25, None, ALU.is_gt), "cV")
    for th in (0.75, 1.25, 1.75):
        P.wait(nc.vector, "cV", P.cnt["cV"])
        P.inc(nc.vector.scalar_tensor_tensor(t[:np_, :F], a[:np_, :F], th,
                                             t[:np_, :F], ALU.is_gt, ALU.add), "cV")
    P.wait(nc.vector, "cV", P.cnt["cV"])
    P.inc(nc.vector.tensor_scalar(u[:np_, :F], a[:np_, :F], 2.5, None, ALU.is_gt), "cV")
    P.wait(nc.vector, "cV", P.cnt["cV"])
    P.inc(nc.vector.scalar_tensor_tensor(u[:np_, :F], a[:np_, :F], 3.5,
                                         u[:np_, :F], ALU.is_gt, ALU.add), "cV")
    P.wait(nc.vector, "cV", P.cnt["cV"])
    P.inc(nc.vector.scalar_tensor_tensor(t[:np_, :F], t[:np_, :F], 0.5,
                                         u[:np_, :F], ALU.mult, ALU.add), "cV")
    P.wait(nc.vector, "cV", P.cnt["cV"])
    P.inc(nc.vector.tensor_scalar(a[:np_, :F], a[:np_, :F], 5.0, 2.0,
                                  ALU.is_gt, ALU.mult), "cV")
    P.wait(nc.gpsimd, "cV", P.cnt["cV"])
    P.wait(nc.gpsimd, "cG", scr.get("busy_g", 0))
    P.inc(nc.gpsimd.tensor_tensor(t[:np_, :F], t[:np_, :F], a[:np_, :F], ALU.add), "cG")
    if after:
        for s, v in after:
            P.wait(nc.scalar, s, v)
    P.wait(nc.scalar, "cV", P.cnt["cV"])
    sg = P.inc(nc.scalar.activation(u[:np_, :F], x_ap, AF.Sign), "cS")
    P.wait(nc.gpsimd, "cS", sg)
    P.wait(nc.gpsimd, "cG", P.cnt["cG"])
    gg_ = P.inc(nc.gpsimd.tensor_tensor(t[:np_, :F], t[:np_, :F], u[:np_, :F], ALU.mult), "cG")
    scr["busy_g"] = gg_
    P.wait(nc.vector, "cG", gg_)
    tv = t[:np_, :F].rearrange("p (b k) -> p b k", k=16)
    ov = out_ap.rearrange("p (b k) -> p b k", k=16)
    done = P.inc(nc.vector.tensor_tensor(ov, tv, scaleb, ALU.mult), "cV")
    scr["busy"] = done
    scr["busy_s"] = sg
    return done



def build_l1prep(P):
    nc = P.nc
    NW = 2816 // NCORE
    ND = D // NCORE
    wgs = nc.dram_tensor("wgs", [NW, D], F32, kind="ExternalInput").ap()
    wus = nc.dram_tensor("wus", [NW, D], F32, kind="ExternalInput").ap()
    wds = nc.dram_tensor("wds", [ND, 2816], F32, kind="ExternalInput").ap()
    qgs = nc.dram_tensor("qgs", [NW, D], BF16, kind="ExternalOutput").ap()
    qus = nc.dram_tensor("qus", [NW, D], BF16, kind="ExternalOutput").ap()
    qds = nc.dram_tensor("qds", [ND, 2816], BF16, kind="ExternalOutput").ap()
    xin = P.sb("xin", [128, 2816], F32)
    qout = P.sb("qout", [128, 2816], BF16)
    scr = {
        "a": P.sb("qa", [128, 2816], F32),
        "t": P.sb("qt", [128, 2816], F32),
        "u": P.sb("qu", [128, 2816], F32),
        "bm": P.sb("qbm", [128, 176], F32),
        "scale": P.sb("qscale", [128, 176], F32),
        "rsc": P.sb("qrsc", [128, 176], F32),
    }
    P.block()
    jobs = []
    for r0 in range(0, NW, 128):
        npart = min(128, NW - r0)
        jobs.append((wgs, qgs, r0, npart, D))
        jobs.append((wus, qus, r0, npart, D))
    for r0 in range(0, ND, 128):
        npart = min(128, ND - r0)
        jobs.append((wds, qds, r0, npart, 2816))
    st_done = 0
    for (src, dst, r0, npart, F) in jobs:
        P.wait(nc.sync, "cV", scr.get("busy", 0))
        P.wait(nc.sync, "cS", scr.get("busy_s", 0))
        P.wait(nc.sync, "dS", st_done)
        ld = P.dma(nc.sync, "dS", xin[:npart, :F], src[r0:r0 + npart, :])
        emit_quant(P, xin[:npart, :F], qout[:npart, :F], scr, npart, F // 16,
                   after=[("dS", ld)])
        P.wait(nc.sync, "cV", P.cnt["cV"])
        st_done = P.dma(nc.sync, "dS", dst[r0:r0 + npart, :], qout[:npart, :F])
    nc.sync.wait_ge(P.sem("dS"), P.cnt["dS"])
    return P


def run_l1prep(W1f):
    P = Prog()
    build_l1prep(P)
    P.close()
    NW = 2816 // NCORE
    ND = D // NCORE
    in_maps = []
    for c in range(NCORE):
        in_maps.append({
            "wgs": np.ascontiguousarray(W1f["w_gate"][c * NW:(c + 1) * NW]),
            "wus": np.ascontiguousarray(W1f["w_up"][c * NW:(c + 1) * NW]),
            "wds": np.ascontiguousarray(W1f["w_down"][c * ND:(c + 1) * ND]),
        })
    res = run_bass_kernel_spmd(P.nc, in_maps, core_ids=list(range(NCORE)))
    qg = np.concatenate([r["qgs"] for r in res.results], axis=0)
    qu = np.concatenate([r["qus"] for r in res.results], axis=0)
    qd = np.concatenate([r["qds"] for r in res.results], axis=0)
    return (np.ascontiguousarray(qg.T), np.ascontiguousarray(qu.T),
            np.ascontiguousarray(qd.T))



NSL = [(0, 512), (512, 512), (1024, 512), (1536, 512), (2048, 512), (2560, 256)]
HTT = 256


def build_l1ffn(P):
    nc = P.nc
    x_tok = nc.dram_tensor("x_tok", [TOK, D], F32, kind="ExternalInput").ap()
    xT = nc.dram_tensor("xT", [D, TOK], F32, kind="ExternalInput").ap()
    wgq = nc.dram_tensor("wgq", [D, 2816], BF16, kind="ExternalInput").ap()
    wuq = nc.dram_tensor("wuq", [D, 2816], BF16, kind="ExternalInput").ap()
    wdq = nc.dram_tensor("wdq", [2816, D], BF16, kind="ExternalInput").ap()
    yT = nc.dram_tensor("yT", [D, TOK], F32, kind="ExternalOutput").ap()
    xq_d = nc.dram_tensor("xq_d", [TOK, D], BF16)
    hq_d = nc.dram_tensor("hq_d", [TOK, 2816], BF16)

    wbig = P.sb("wbig", [128, 45056], BF16)
    wgq_sb = wbig[:, 0:KC * 2816]
    wuq_sb = wbig[:, KC * 2816:2 * KC * 2816]
    wdq_sb = wbig[:, 0:22 * D]
    xqT = P.sb("xqT", [128, KC * TOK], BF16)
    hqT = P.sb("hqT", [128, 22 * HTT], BF16)
    xtc = P.sb("xtc", [128, D], F32)
    sqc = P.sb("sqc", [128, D], BF16)
    xqc = P.sb("xqc", [128, D], BF16)
    h_sb = P.sb("h_sb", [128, 2816], F32)
    hq_sb = P.sb("hq_sb", [128, 2816], BF16)
    sil = P.sb("sil", [128, 512], F32)
    res_t = P.sb("res_t", [128, 512], F32)
    yout = P.sb("yout", [128, 512], F32)
    scr = {
        "a": P.sb("qa", [128, 2816], F32), "t": P.sb("qt", [128, 2816], F32),
        "u": P.sb("qu", [128, 2816], F32), "bm": P.sb("qbm", [128, 176], F32),
        "scale": P.sb("qscale", [128, 176], F32), "rsc": P.sb("qrsc", [128, 176], F32),
    }
    cs1 = P.sb("cs1", [128, 1], F32)
    cs2 = P.sb("cs2", [128, 1], F32)
    mean = P.sb("mean", [128, 1], F32)
    rstd = P.sb("rstd", [128, 1], F32)
    bcol = P.sb("bcol", [128, 1], F32)
    pb = [P.psum(f"pb{i}", [128, 512], F32) for i in range(6)]

    P.block()
    for kc in range(KC):
        P.dma(nc.gpsimd, "dG", wgq_sb[:, kc * 2816:(kc + 1) * 2816],
                                  wgq[kc * 128:(kc + 1) * 128, :])
        P.dma(nc.gpsimd, "dG", wuq_sb[:, kc * 2816:(kc + 1) * 2816],
                                  wuq[kc * 128:(kc + 1) * 128, :])
    wA_done = P.cnt["dG"]

    xq_st = 0
    for i in range(TOK // 128):
        P.wait(nc.sync, "cV", scr.get("busy", 0))
        P.wait(nc.sync, "cS", scr.get("busy_s", 0))
        ld = P.dma(nc.sync, "dS", xtc[:], x_tok[i * 128:(i + 1) * 128, :])
        P.wait(nc.vector, "dS", ld)
        P.inc(nc.vector.tensor_reduce(cs1[:], xtc[:], AX.X, ALU.add), "cV")
        P.wait(nc.scalar, "dS", ld)
        sq_done = P.inc(nc.scalar.activation(sqc[:], xtc[:], AF.Square), "cS")
        P.wait(nc.vector, "cS", sq_done)
        P.inc(nc.vector.tensor_reduce(cs2[:], sqc[:], AX.X, ALU.add), "cV")
        P.wait(nc.vector, "cV", P.cnt["cV"])
        P.inc(nc.vector.tensor_scalar(mean[:], cs1[:], 1.0 / D, None, ALU.mult), "cV")
        P.wait(nc.vector, "cV", P.cnt["cV"])
        P.inc(nc.vector.tensor_scalar(cs2[:], cs2[:], 1.0 / D, None, ALU.mult), "cV")
        P.wait(nc.vector, "cV", P.cnt["cV"])
        P.inc(nc.vector.tensor_tensor(cs1[:], mean[:], mean[:], ALU.mult), "cV")
        P.wait(nc.vector, "cV", P.cnt["cV"])
        P.inc(nc.vector.tensor_tensor(cs2[:], cs1[:], cs2[:], ALU.subtract), "cV")
        P.wait(nc.scalar, "cV", P.cnt["cV"])
        P.inc(nc.scalar.activation(cs2[:], cs2[:], AF.Ln,
                                   bias=np.float32(1e-5), scale=np.float32(-1.0)), "cS")
        P.wait(nc.scalar, "cS", P.cnt["cS"])
        sd = P.inc(nc.scalar.activation(rstd[:], cs2[:], AF.Exp, scale=np.float32(-0.5)), "cS")
        P.wait(nc.vector, "cS", sd)
        P.inc(nc.vector.scalar_tensor_tensor(bcol[:], mean[:], -1.0, rstd[:],
                                             ALU.mult, ALU.mult), "cV")
        P.wait(nc.vector, "cV", P.cnt["cV"])
        P.inc(nc.vector.tensor_scalar(xtc[:], xtc[:], rstd[:, 0:1], bcol[:, 0:1],
                                      ALU.mult, ALU.add), "cV")
        P.wait(nc.vector, "dG", xq_st)
        emit_quant(P, xtc[:, :], xqc[:, :], scr, 128, D // 16,
                   after=[("cV", P.cnt["cV"])])
        qv = P.cnt["cV"]
        P.wait(nc.gpsimd, "cV", qv)
        xq_st = P.dma(nc.gpsimd, "dG", xq_d[i * 128:(i + 1) * 128, :], xqc[:])
    P.wait(nc.sync, "dG", P.cnt["dG"])
    for kc in range(KC):
        P.dma_t(nc.sync, "dS", xqT[:, kc * TOK:(kc + 1) * TOK],
                                          xq_d[:, kc * 128:(kc + 1) * 128])
    xqT_done = P.cnt["dS"]

    P.wait(nc.tensor, "dS", xqT_done)
    P.wait(nc.tensor, "dG", wA_done)
    pclean_v = [0] * 6
    pclean_s = [0] * 6
    sil_busy = 0
    for i in range(TOK // 128):
        for si, (n0, ns) in enumerate(NSL):
            bg_, bu_ = pb[2 * (si % 3)], pb[2 * (si % 3) + 1]
            P.wait(nc.tensor, "cV", pclean_v[2 * (si % 3)])
            P.wait(nc.tensor, "cS", pclean_s[2 * (si % 3)])
            P.wait(nc.tensor, "cV", pclean_v[2 * (si % 3) + 1])
            P.wait(nc.tensor, "cS", pclean_s[2 * (si % 3) + 1])
            for kc in range(KC):
                nc.tensor.matmul(bg_[:, 0:ns],
                                 xqT[:, kc * TOK + i * 128: kc * TOK + (i + 1) * 128],
                                 wgq_sb[:, kc * 2816 + n0: kc * 2816 + n0 + ns],
                                 start=(kc == 0), stop=(kc == KC - 1))
            for kc in range(KC):
                mmu = P.inc(nc.tensor.matmul(bu_[:, 0:ns],
                                 xqT[:, kc * TOK + i * 128: kc * TOK + (i + 1) * 128],
                                 wuq_sb[:, kc * 2816 + n0: kc * 2816 + n0 + ns],
                                 start=(kc == 0), stop=(kc == KC - 1)), "cT")
            P.wait(nc.scalar, "cT", mmu)
            P.wait(nc.scalar, "cV", sil_busy)
            ssil = P.inc(nc.scalar.activation(sil[:, 0:ns], bg_[:, 0:ns], AF.Silu), "cS")
            pclean_s[2 * (si % 3)] = ssil
            pclean_s[2 * (si % 3) + 1] = ssil
            P.wait(nc.vector, "cS", ssil)
            sil_busy = P.inc(nc.vector.tensor_tensor(
                h_sb[:, n0:n0 + ns], sil[:, 0:ns], bu_[:, 0:ns], ALU.mult), "cV")
            pclean_v[2 * (si % 3)] = sil_busy
            pclean_v[2 * (si % 3) + 1] = sil_busy
        P.wait(nc.vector, "dG", P.cnt["dG"])
        emit_quant(P, h_sb[:, :], hq_sb[:, :], scr, 128, 176,
                   after=[("cV", P.cnt["cV"])])
        qv = P.cnt["cV"]
        P.wait(nc.gpsimd, "cV", qv)
        P.dma(nc.gpsimd, "dG", hq_d[i * 128:(i + 1) * 128, :], hq_sb[:])
    hq_all = P.cnt["dG"]

    P.wait(nc.gpsimd, "cT", P.cnt["cT"])
    for kk in range(22):
        P.dma(nc.gpsimd, "dG", wdq_sb[:, kk * D:(kk + 1) * D],
                                  wdq[kk * 128:(kk + 1) * 128, :])
    wB_done = P.cnt["dG"]
    P.wait(nc.sync, "dG", hq_all)
    res_busy = 0
    yst = 0
    for tt in range(TOK // HTT):
        P.wait(nc.sync, "cT", P.cnt["cT"])
        for kk in range(22):
            P.dma_t(nc.sync, "dS", hqT[:, kk * HTT:(kk + 1) * HTT],
                hq_d[tt * HTT:(tt + 1) * HTT, kk * 128:(kk + 1) * 128])
        hqT_ld = P.cnt["dS"]
        P.wait(nc.tensor, "dS", hqT_ld)
        P.wait(nc.tensor, "dG", wB_done)
        for m in range(KC):
            bi = m % 4
            bank = pb[bi]
            P.wait(nc.tensor, "cV", pclean_v[bi])
            P.wait(nc.tensor, "cS", pclean_s[bi])
            for kk in range(22):
                mm = P.inc(nc.tensor.matmul(bank[:, 0:HTT],
                                wdq_sb[:, kk * D + m * 128: kk * D + (m + 1) * 128],
                                hqT[:, kk * HTT:(kk + 1) * HTT],
                                start=(kk == 0), stop=(kk == 21)), "cT")
            P.wait(nc.sync, "cV", res_busy)
            rld = P.dma(nc.sync, "dS", res_t[:, 0:HTT], xT[m * 128:(m + 1) * 128, tt * HTT:(tt + 1) * HTT])
            P.wait(nc.vector, "cT", mm)
            P.wait(nc.vector, "dS", rld)
            P.wait(nc.vector, "dS", yst)
            cv = P.inc(nc.vector.scalar_tensor_tensor(
                yout[:, 0:HTT], bank[:, 0:HTT], 0.5, res_t[:, 0:HTT],
                ALU.mult, ALU.add), "cV")
            pclean_v[bi] = cv
            res_busy = cv
            P.wait(nc.sync, "cV", cv)
            yst = P.dma(nc.sync, "dS", yT[m * 128:(m + 1) * 128, tt * HTT:(tt + 1) * HTT], yout[:, 0:HTT])
    nc.sync.wait_ge(P.sem("dS"), P.cnt["dS"])
    return P


def run_l1ffn(xT_shards, qw):
    wgq_T, wuq_T, wdq_T = qw
    P = Prog()
    build_l1ffn(P)
    P.close()
    in_maps = []
    for s in xT_shards:
        x_tok = np.ascontiguousarray(s.T)
        in_maps.append({"x_tok": x_tok, "xT": s, "wgq": wgq_T, "wuq": wuq_T,
                        "wdq": wdq_T})
    res = run_bass_kernel_spmd(P.nc, in_maps, core_ids=list(range(NCORE)))
    return [r["yT"] for r in res.results]



SEQT = 640
NTK = NSEQ * SEQT
WIN = 256


def build_l1attn(P, dbg=False):
    nc = P.nc
    xkv = nc.dram_tensor("xkv", [D, NTK], F32, kind="ExternalInput").ap()
    wq = nc.dram_tensor("wq", [D, D], F32, kind="ExternalInput").ap()
    wk = nc.dram_tensor("wk", [D, D], F32, kind="ExternalInput").ap()
    wv = nc.dram_tensor("wv", [D, D], F32, kind="ExternalInput").ap()
    wo = nc.dram_tensor("wo", [D, D], F32, kind="ExternalInput").ap()
    qnw = nc.dram_tensor("qnw", [D], F32, kind="ExternalInput").ap()
    knw = nc.dram_tensor("knw", [D], F32, kind="ExternalInput").ap()
    bias0 = nc.dram_tensor("bias0", [128, 8 * WIN], BF16, kind="ExternalInput").ap()
    biasn = nc.dram_tensor("biasn", [128, 8 * WIN], BF16, kind="ExternalInput").ap()
    yT = nc.dram_tensor("yT", [D, TOK], F32, kind="ExternalOutput").ap()

    xhat = P.sb("xhat", [128, KC * NTK], BF16)
    attnT = xhat[:, 0:KC * TOK]
    pr_bf = xhat[:, KC * TOK:KC * TOK + 2048]
    prT = xhat[:, KC * TOK + 2048:KC * TOK + 4096]
    qnT = P.sb("qnT", [128, KC * TOK], BF16)
    knT = P.sb("knT", [128, KC * NTK], BF16)
    v_sb = P.sb("v_sb", [128, (NTK // 128) * D], BF16)
    w_sb = P.sb("w_sb", [128, KC * D], BF16)
    bias_sb = P.sb("bias_sb", [128, 2 * 8 * WIN], BF16)
    t1 = P.sb("t1", [128, 512], F32)
    rowr = P.sb("rowr", [128, 1], F32)
    rows_sb = P.sb("rows_sb", [128, 1], F32)
    ident = P.sb("ident", [128, 128], BF16)
    nw_sb = P.sb("nw_sb", [128, 2 * KC], F32)
    qsc = P.sb("qsc", [128, 512], BF16)

    ps = [P.psum(f"ps{i}", [128, 512], F32) for i in range(4)]
    pav = [P.psum(f"pav{i}", [128, 512], F32) for i in range(2)]
    tps = [P.psum("tps0", [128, 128], BF16), P.psum("tps1", [128, 128], BF16)]
    aux = ln_aux(P, psums={"pA": pav[0], "pB": pav[1],
                           "pS": ps[0][0:1, :], "pQ": ps[1][0:1, :]})
    rr_row = aux["mrow"]

    P.block()
    emit_ln_init(P, aux)
    idf = t1[:, 0:128]
    P.inc(nc.gpsimd.iota(idf, [[-1, 128]], base=0, channel_multiplier=1,
                         allow_small_or_imprecise_dtypes=True), "cG")
    P.wait(nc.vector, "cG", P.cnt["cG"])
    P.inc(nc.vector.tensor_scalar(ident[:], idf, 0.0, None, ALU.is_equal), "cV")
    id_done = P.cnt["cV"]
    P.dma(nc.sync, "dS", bias_sb[:, 0:8 * WIN], bias0)
    P.dma(nc.sync, "dS", bias_sb[:, 8 * WIN:16 * WIN], biasn)
    for j in range(KC):
        P.dma(nc.sync, "dS", nw_sb[:, j:j + 1], qnw[j * 128:(j + 1) * 128].unsqueeze(-1))
        P.dma(nc.sync, "dS", nw_sb[:, KC + j:KC + j + 1], knw[j * 128:(j + 1) * 128].unsqueeze(-1))
    const_ld = P.cnt["dS"]
    nc.sync.wait_ge(P.sem("dS"), const_ld)

    emit_layernorm(P, xkv, xhat, aux, ntok=NTK)
    if dbg:
        d_xh = nc.dram_tensor("d_xh", [128, KC * NTK], BF16, kind="ExternalOutput").ap()
        P.wait(nc.gpsimd, "cV", P.cnt["cV"])
        xh_dump = P.dma(nc.gpsimd, "dG", d_xh, xhat[:, :])
    else:
        xh_dump = 0
    P.wait(nc.tensor, "dG", xh_dump)

    own_cols = [(s * SEQT + 16, 512) for s in range(NSEQ)]

    def proj(dst, w_dram, cols, rms_w_col, ncol_dst):
        P.wait(nc.gpsimd, "cT", P.cnt["cT"])
        for kc_ in range(KC):
            P.dma(nc.gpsimd, "dG", w_sb[:, kc_ * D:(kc_ + 1) * D],
                                      w_dram[kc_ * 128:(kc_ + 1) * 128, :])
        wd = P.cnt["dG"]
        P.wait(nc.tensor, "dG", wd)
        for (c0, n, d0) in cols:
            for off in range(0, n, 512):
                ns = min(512, n - off)
                for m in range(KC):
                    bi = 2 + (m % 2)
                    bank = ps[bi]
                    P.wait(nc.tensor, "cV", P.cnt["cV"])
                    P.wait(nc.tensor, "cS", P.cnt["cS"])
                    for kc_ in range(KC):
                        mm = P.inc(nc.tensor.matmul(
                            bank[:, 0:ns],
                            w_sb[:, kc_ * D + m * 128:(kc_ * D + (m + 1) * 128)],
                            xhat[:, kc_ * NTK + c0 + off: kc_ * NTK + c0 + off + ns],
                            start=(kc_ == 0), stop=(kc_ == KC - 1)), "cT")
                    P.wait(nc.vector, "cT", mm)
                    P.inc(nc.vector.tensor_copy(
                        dst[:, m * ncol_dst + d0 + off: m * ncol_dst + d0 + off + ns],
                        bank[:, 0:ns]), "cV")
                if rms_w_col is not None:
                    evac = P.cnt["cV"]
                    for m in range(KC):
                        P.wait(nc.scalar, "cV", evac)
                        P.wait(nc.scalar, "cT", P.cnt["cT"])
                        sqd = P.inc(nc.scalar.activation(
                            qsc[:, 0:ns],
                            dst[:, m * ncol_dst + d0 + off: m * ncol_dst + d0 + off + ns],
                            AF.Square), "cS")
                        P.wait(nc.tensor, "cS", sqd)
                        P.wait(nc.tensor, "cV", P.cnt["cV"])
                        smm = P.inc(nc.tensor.matmul(ps[0][0:1, 0:ns], aux["ones_bf"][:, :],
                                                     qsc[:, 0:ns],
                                                     start=(m == 0), stop=(m == KC - 1)), "cT")
                    P.wait(nc.vector, "cT", smm)
                    P.inc(nc.vector.tensor_scalar(rr_row[:, 0:ns], ps[0][0:1, 0:ns],
                                                  1.0 / D, None, ALU.mult), "cV")
                    P.wait(nc.scalar, "cV", P.cnt["cV"])
                    P.inc(nc.scalar.activation(rr_row[:, 0:ns], rr_row[:, 0:ns],
                                               AF.Ln, bias=np.float32(1e-6)), "cS")
                    P.wait(nc.scalar, "cS", P.cnt["cS"])
                    sdd = P.inc(nc.scalar.activation(rr_row[:, 0:ns], rr_row[:, 0:ns],
                                                     AF.Exp, scale=np.float32(-0.5)), "cS")
                    rrd = P.cnt["cV"]
                    P.wait(nc.tensor, "cV", rrd)
                    P.wait(nc.tensor, "cS", sdd)
                    bc = P.inc(nc.tensor.matmul(pav[0][:, 0:ns], aux["ones_f"][:, :],
                                                rr_row[:, 0:ns], start=True, stop=True), "cT")
                    P.wait(nc.vector, "cT", bc)
                    for m in range(KC):
                        sl = dst[:, m * ncol_dst + d0 + off: m * ncol_dst + d0 + off + ns]
                        P.wait(nc.vector, "cV", P.cnt["cV"])
                        P.inc(nc.vector.tensor_tensor(t1[:, 0:ns], sl, pav[0][:, 0:ns],
                                                      ALU.mult), "cV")
                        P.wait(nc.vector, "cV", P.cnt["cV"])
                        P.inc(nc.vector.tensor_scalar(sl, t1[:, 0:ns],
                                                      nw_sb[:, rms_w_col + m:rms_w_col + m + 1],
                                                      None, ALU.mult), "cV")

    P.wait(nc.vector, "dS", const_ld)
    proj(knT, wk, [(0, NTK, 0)], KC, NTK)
    proj(qnT, wq, [(c, n, s * 512) for s, (c, n) in enumerate(own_cols)], 0, TOK)
    P.wait(nc.gpsimd, "cT", P.cnt["cT"])
    for kc_ in range(KC):
        P.dma(nc.gpsimd, "dG", w_sb[:, kc_ * D:(kc_ + 1) * D],
                                  wv[kc_ * 128:(kc_ + 1) * 128, :])
    wd = P.cnt["dG"]
    P.wait(nc.tensor, "dG", wd)
    for ch in range(NTK // 128):
        for half in range(2):
            bank = ps[2 + half]
            P.wait(nc.tensor, "cV", P.cnt["cV"])
            for kc_ in range(KC):
                mm = P.inc(nc.tensor.matmul(
                    bank[:, 0:512],
                    xhat[:, kc_ * NTK + ch * 128: kc_ * NTK + (ch + 1) * 128],
                    w_sb[:, kc_ * D + half * 512: kc_ * D + half * 512 + 512],
                    start=(kc_ == 0), stop=(kc_ == KC - 1)), "cT")
            P.wait(nc.vector, "cT", mm)
            P.inc(nc.vector.tensor_copy(
                v_sb[:, ch * D + half * 512: ch * D + half * 512 + 512],
                bank[:, 0:512]), "cV")
    v_done = P.cnt["cV"]

    P.wait(nc.tensor, "cV", max(v_done, id_done))
    INV = float(1.0 / np.sqrt(128.0))
    attn_done_v = 0
    for s in range(NSEQ):
        for qt in range(4):
            qcol = s * 512 + qt * 128
            kvcol = s * SEQT + qt * 128
            bvar = 0 if qt == 0 else 8 * WIN
            sc_mm = [0] * 4
            for h in range(8):
                bank = ps[h // 2]
                bo_ = (h % 2) * WIN
                if h % 2 == 0:
                    P.wait(nc.tensor, "cV", P.cnt["cV"])
                sc_mm[h // 2] = P.inc(nc.tensor.matmul(
                    bank[:, bo_:bo_ + WIN],
                    qnT[:, h * TOK + qcol: h * TOK + qcol + 128],
                    knT[:, h * NTK + kvcol: h * NTK + kvcol + WIN],
                    start=True, stop=True), "cT")
            for h in range(8):
                bank = ps[h // 2]
                bo_ = (h % 2) * WIN
                P.wait(nc.vector, "cT", sc_mm[h // 2])
                P.wait(nc.vector, "cS", P.cnt["cS"])
                P.inc(nc.vector.scalar_tensor_tensor(
                    t1[:, 0:WIN], bank[:, bo_:bo_ + WIN], INV,
                    bias_sb[:, bvar + h * WIN: bvar + (h + 1) * WIN],
                    ALU.mult, ALU.add), "cV")
                P.wait(nc.scalar, "cV", P.cnt["cV"])
                ex = P.inc(nc.scalar.activation(pr_bf[:, h * WIN:(h + 1) * WIN],
                                                t1[:, 0:WIN], AF.Exp), "cS")
                P.wait(nc.vector, "cS", ex)
                P.inc(nc.vector.tensor_reduce(rows_sb[:], pr_bf[:, h * WIN:(h + 1) * WIN],
                                              AX.X, ALU.add), "cV")
                P.wait(nc.vector, "cV", P.cnt["cV"])
                P.inc(nc.vector.reciprocal(rowr[:], rows_sb[:]), "cV")
                P.wait(nc.vector, "cV", P.cnt["cV"])
                P.inc(nc.vector.tensor_scalar(pr_bf[:, h * WIN:(h + 1) * WIN],
                                              pr_bf[:, h * WIN:(h + 1) * WIN],
                                              rowr[:, 0:1], None, ALU.mult), "cV")
            norm_done = P.cnt["cV"]
            P.wait(nc.tensor, "cV", norm_done)
            tp_copy = [0, 0]
            for b in range(16):
                tb = tps[b % 2]
                P.wait(nc.tensor, "cV", tp_copy[b % 2])
                tp = P.inc(nc.tensor.transpose(tb[:, :], pr_bf[:, b * 128:(b + 1) * 128],
                                               ident[:, :]), "cT")
                P.wait(nc.vector, "cT", tp)
                tp_copy[b % 2] = P.inc(nc.vector.tensor_copy(
                    prT[:, b * 128:(b + 1) * 128], tb[:, :]), "cV")
            prT_done = P.cnt["cV"]
            P.wait(nc.tensor, "cV", prT_done)
            av_copy = [0, 0]
            for h in range(8):
                bank = pav[h % 2]
                c0 = 5 * s + qt
                P.wait(nc.tensor, "cV", av_copy[h % 2])
                av1 = nc.tensor.matmul(
                    bank[:, 0:128],
                    v_sb[:, c0 * D + h * 128:(c0 * D + (h + 1) * 128)],
                    prT[:, h * 256:h * 256 + 128],
                    start=True, stop=False)
                av = P.inc(nc.tensor.matmul(
                    bank[:, 0:128],
                    v_sb[0:16, (c0 + 1) * D + h * 128:((c0 + 1) * D + (h + 1) * 128)],
                    prT[0:16, h * 256 + 128:h * 256 + 256],
                    start=False, stop=True), "cT")
                P.wait(nc.vector, "cT", av)
                attn_done_v = P.inc(nc.vector.tensor_copy(
                    attnT[:, h * TOK + qcol: h * TOK + qcol + 128], bank[:, 0:128]), "cV")
                av_copy[h % 2] = attn_done_v

    P.wait(nc.gpsimd, "cT", P.cnt["cT"])
    for kc_ in range(KC):
        P.dma(nc.gpsimd, "dG", w_sb[:, kc_ * D:(kc_ + 1) * D],
                                  wo[kc_ * 128:(kc_ + 1) * 128, :])
    wd = P.cnt["dG"]
    P.wait(nc.tensor, "dG", wd)
    res_busy = P.cnt["cV"]
    yst = 0
    for s in range(NSEQ):
        for m in range(KC):
            bank = ps[m % 4]
            P.wait(nc.tensor, "cV", P.cnt["cV"])
            for kc_ in range(KC):
                mm = P.inc(nc.tensor.matmul(
                    bank[:, 0:512],
                    w_sb[:, kc_ * D + m * 128: kc_ * D + (m + 1) * 128],
                    attnT[:, kc_ * TOK + s * 512:(kc_ * TOK + (s + 1) * 512)],
                    start=(kc_ == 0), stop=(kc_ == KC - 1)), "cT")
            P.wait(nc.sync, "cV", res_busy)
            rld = P.dma(nc.sync, "dS", t1[:, 0:512],
                xkv[m * 128:(m + 1) * 128, s * SEQT + 16: s * SEQT + 528])
            P.wait(nc.vector, "cT", mm)
            P.wait(nc.vector, "dS", rld)
            P.wait(nc.vector, "dS", yst)
            cv = P.inc(nc.vector.tensor_tensor(t1[:, 0:512],
                                               bank[:, 0:512], t1[:, 0:512], ALU.add), "cV")
            res_busy = cv
            P.wait(nc.sync, "cV", cv)
            yst = P.dma(nc.sync, "dS", yT[m * 128:(m + 1) * 128, s * 512:(s + 1) * 512], t1[:, 0:512])
    if dbg:
        d_qnT = nc.dram_tensor("d_qnT", [128, KC * TOK], BF16, kind="ExternalOutput").ap()
        d_knT = nc.dram_tensor("d_knT", [128, KC * NTK], BF16, kind="ExternalOutput").ap()
        d_v = nc.dram_tensor("d_v", [128, (NTK // 128) * D], BF16, kind="ExternalOutput").ap()
        d_at = nc.dram_tensor("d_at", [128, KC * TOK], BF16, kind="ExternalOutput").ap()
        P.wait(nc.gpsimd, "cV", P.cnt["cV"])
        P.wait(nc.gpsimd, "cT", P.cnt["cT"])
        P.dma(nc.gpsimd, "dG", d_qnT, qnT[:, :])
        P.dma(nc.gpsimd, "dG", d_knT, knT[:, :])
        P.dma(nc.gpsimd, "dG", d_v, v_sb[:, :])
        P.dma(nc.gpsimd, "dG", d_at, attnT[:, :])
        nc.gpsimd.wait_ge(P.sem("dG"), P.cnt["dG"])
    nc.sync.wait_ge(P.sem("dS"), P.cnt["dS"])
    return P


def make_l1attn_bias(core):
    import ml_dtypes
    slopes = (2.0 ** (-np.arange(1, 9))).astype(np.float32)
    qrow = np.arange(128)[:, None]
    kidx = np.arange(WIN)[None, :]
    rel = (kidx - 16) - qrow
    inwin = (rel <= 0) & (rel >= -16)
    b0 = np.where(inwin[None], slopes[:, None, None] * rel[None], -1e9).astype(np.float32)
    bn = b0.copy()
    if core == 0:
        b0 = np.where((kidx < 16)[None], np.float32(-1e9), b0)
    def pack(b):
        return np.ascontiguousarray(
            b.transpose(1, 0, 2).reshape(128, 8 * WIN)).astype(ml_dtypes.bfloat16)
    return pack(b0), pack(bn)


def make_xkv(x, core):
    out = np.zeros((D, NTK), dtype=np.float32)
    for s in range(NSEQ):
        p0 = core * POS
        own = x[s, p0:p0 + POS, :].T
        out[:, s * SEQT + 16: s * SEQT + 528] = own
        if core > 0:
            out[:, s * SEQT: s * SEQT + 16] = x[s, p0 - 16:p0, :].T
    return out


def run_l1attn(x_full, W, dbg=False):
    P = Prog()
    build_l1attn(P, dbg=dbg)
    P.close()
    in_maps = []
    for c in range(NCORE):
        b0, bn = make_l1attn_bias(c)
        in_maps.append({
            "xkv": make_xkv(x_full, c),
            "wq": np.ascontiguousarray(W["wq"].T), "wk": np.ascontiguousarray(W["wk"].T),
            "wv": np.ascontiguousarray(W["wv"].T), "wo": np.ascontiguousarray(W["wo"].T),
            "qnw": np.ascontiguousarray(W["q_norm_w"]), "knw": np.ascontiguousarray(W["k_norm_w"]),
            "bias0": b0, "biasn": bn,
        })
    res = run_bass_kernel_spmd(P.nc, in_maps, core_ids=list(range(NCORE)))
    if dbg:
        return res.results
    return [r["yT"] for r in res.results]



def build_l0qkv(P):
    nc = P.nc
    xT = nc.dram_tensor("xT", [D, TOK], F32, kind="ExternalInput").ap()
    W1 = nc.dram_tensor("W1", [D, 3 * D], F32, kind="ExternalInput").ap()
    b1 = nc.dram_tensor("b1", [3 * D], F32, kind="ExternalInput").ap()
    qkvT = nc.dram_tensor("qkvT", [3 * D, TOK], BF16, kind="ExternalOutput").ap()

    w_sb = P.sb("w_sb", [128, KC * 3 * D], BF16)
    xhat = P.sb("xhat", [128, KC * TOK], BF16)
    b1_sb = P.sb("b1_sb", [1, 3 * D], BF16)
    ones_row = P.sb("ones_row", [1, 512], BF16)
    ev = P.sb("ev", [128, 512], BF16)
    psA = P.psum("psA", [1, 512], F32)
    psQ = P.psum("psQ", [1, 512], F32)
    pA = P.psum("pA", [128, 512], F32)
    pB = P.psum("pB", [128, 512], F32)
    pg = [P.psum(f"pg{i}", [128, 512], F32) for i in range(4)]
    aux = ln_aux(P, psums={"pS": psA, "pQ": psQ, "pA": pA, "pB": pB})

    P.block()
    emit_ln_init(P, aux)
    P.inc(nc.vector.memset(ones_row[:], 1.0), "cV")
    for kc in range(KC):
        P.dma(nc.gpsimd, "dG", w_sb[:, kc * 3 * D:(kc + 1) * 3 * D],
                                  W1[kc * 128:(kc + 1) * 128, :])
    P.dma(nc.gpsimd, "dG", b1_sb[:, :], b1.unsqueeze(0))
    w_done = P.cnt["dG"]
    emit_layernorm(P, xT, xhat, aux)
    ln_done = P.cnt["cV"]
    P.wait(nc.tensor, "dG", w_done)
    P.wait(nc.tensor, "cV", ln_done)
    pclean = [0] * 4
    st = 0
    for m in range(3 * KC):
        for tt in range(TOK // 512):
            bi = (m * 4 + tt) % 4
            bank = pg[bi]
            P.wait(nc.tensor, "cV", pclean[bi])
            nc.tensor.matmul(bank[:, :], b1_sb[:, m * 128:(m + 1) * 128],
                             ones_row[:, :], start=True, stop=False)
            for kc in range(KC):
                mm = P.inc(nc.tensor.matmul(
                    bank[:, :],
                    w_sb[:, kc * 3 * D + m * 128: kc * 3 * D + (m + 1) * 128],
                    xhat[:, kc * TOK + tt * 512: kc * TOK + (tt + 1) * 512],
                    start=False, stop=(kc == KC - 1)), "cT")
            P.wait(nc.vector, "cT", mm)
            P.wait(nc.vector, "dS", st)
            cv = P.inc(nc.vector.tensor_copy(ev[:], bank[:, :]), "cV")
            pclean[bi] = cv
            P.wait(nc.sync, "cV", cv)
            st = P.dma(nc.sync, "dS", qkvT[m * 128:(m + 1) * 128, tt * 512:(tt + 1) * 512], ev[:])
    nc.sync.wait_ge(P.sem("dS"), P.cnt["dS"])
    return P


def build_l0attn(P):
    nc = P.nc
    S = 4096
    NCH = S // 128
    qT = nc.dram_tensor("qT", [128, NSEQ * S], BF16, kind="ExternalInput").ap()
    kT = nc.dram_tensor("kT", [128, NSEQ * S], BF16, kind="ExternalInput").ap()
    v_in = nc.dram_tensor("v_in", [NSEQ * S, 132], BF16, kind="ExternalInput").ap()
    attn_o = nc.dram_tensor("attn_o", [NSEQ * S, 128], BF16, kind="ExternalOutput").ap()

    q_sb = P.sb("q_sb", [128, S], BF16)
    k_sb = P.sb("k_sb", [128, S], BF16)
    v_sb = P.sb("v_sb", [128, NCH * 132], BF16)
    prT = [P.sb(f"prT{i}", [128, 512], BF16) for i in range(2)]
    maskT = P.sb("maskT", [128, 128], BF16)
    idf = P.sb("idf", [128, 128], F32)
    out_sb = P.sb("out_sb", [128, 8 * 64], BF16)
    rcol = P.sb("rcol", [128, 1], F32)
    pS_ = [P.psum(f"pS{i}", [128, 512], F32) for i in range(2)]
    pO = [P.psum(f"pO{i}", [128, 264], F32) for i in range(2)]

    P.block()
    P.inc(nc.gpsimd.iota(idf[:], [[-1, 128]], base=0, channel_multiplier=1,
                         allow_small_or_imprecise_dtypes=True), "cG")
    P.wait(nc.vector, "cG", P.cnt["cG"])
    P.inc(nc.vector.tensor_scalar(maskT[:], idf[:], 0.0, None, ALU.is_le), "cV")
    INV8 = float(1.0 / 8.0)
    ld = 0
    st = 0
    for s in range(NSEQ):
        for h in range(2):
            hb = h * 64
            if h == 0:
                P.wait(nc.sync, "cT", P.cnt["cT"])
                P.dma(nc.sync, "dS", q_sb[:, :], qT[:, s * S:(s + 1) * S])
                P.dma(nc.sync, "dS", k_sb[:, :], kT[:, s * S:(s + 1) * S])
                vv = v_in[s * S:(s + 1) * S, :].rearrange("(c p) w -> p c w", p=128)
                ld = P.dma(nc.sync, "dS", v_sb[:].rearrange("p (c w) -> p c w", w=132), vv)
            P.wait(nc.tensor, "dS", ld)
            for qs in range(8):
                po = pO[qs % 2]
                po_clean = P.cnt["cV"]
                first_kc = [True] * 4
                for kc in range(4 * qs + 4):
                    q0 = max(qs * 512, kc * 128)
                    w = qs * 512 + 512 - q0
                    bank = pS_[kc % 2]
                    prb = prT[kc % 2]
                    P.wait(nc.tensor, "cS", P.cnt["cS"])
                    smm = P.inc(nc.tensor.matmul(
                        bank[:, 0:w],
                        k_sb[hb:hb + 64, kc * 128:(kc + 1) * 128],
                        q_sb[hb:hb + 64, q0:q0 + w],
                        start=True, stop=True), "cT")
                    P.wait(nc.scalar, "cT", smm)
                    P.wait(nc.scalar, "cV", P.cnt["cV"])
                    ex = P.inc(nc.scalar.activation(prb[:, 0:w], bank[:, 0:w],
                                                    AF.Exp, scale=np.float32(INV8)), "cS")
                    mdone = ex
                    if q0 == kc * 128:
                        P.wait(nc.vector, "cS", ex)
                        mdone = P.inc(nc.vector.tensor_tensor(
                            prb[:, 0:128], prb[:, 0:128], maskT[:], ALU.mult), "cV")
                        P.wait(nc.tensor, "cV", mdone)
                    P.wait(nc.tensor, "cS", ex)
                    for j in range(4):
                        jq = qs * 512 + j * 128
                        if jq < q0:
                            continue
                        last = (kc == 4 * qs + 3)
                        av = P.inc(nc.tensor.matmul(
                            po[:, j * 66: j * 66 + 65],
                            prb[:, jq - q0: jq - q0 + 128],
                            v_sb[:, kc * 132 + h * 66: kc * 132 + h * 66 + 65],
                            start=(kc == 0 and j == 0), stop=last), "cT")
                        first_kc[j] = False
                P.wait(nc.vector, "cT", P.cnt["cT"])
                for j in range(4):
                    P.wait(nc.vector, "cV", P.cnt["cV"])
                    P.inc(nc.vector.reciprocal(rcol[:], po[:, j * 66 + 64: j * 66 + 65]), "cV")
                    P.wait(nc.vector, "cV", P.cnt["cV"])
                    P.wait(nc.vector, "dS", st)
                    P.inc(nc.vector.tensor_scalar(
                        out_sb[:, (qs % 2) * 256 + j * 64: (qs % 2) * 256 + (j + 1) * 64],
                        po[:, j * 66: j * 66 + 64], rcol[:, 0:1], None, ALU.mult), "cV")
                ev = P.cnt["cV"]
                if qs % 2 == 1:
                    P.wait(nc.sync, "cV", ev)
                    dst = attn_o[s * S + (qs - 1) * 512: s * S + (qs + 1) * 512,
                                 h * 64:(h + 1) * 64]
                    dd = dst.rearrange("(j p) c -> p j c", p=128)
                    st = P.dma(nc.sync, "dS", dd, out_sb[:].rearrange("p (j c) -> p j c", c=64))
                    nc.sync.wait_ge(P.sem("dS"), st)
    nc.sync.wait_ge(P.sem("dS"), P.cnt["dS"])
    return P


def build_l0proj(P):
    nc = P.nc
    xT = nc.dram_tensor("xT", [D, TOK], F32, kind="ExternalInput").ap()
    attnT = nc.dram_tensor("attnT", [D, TOK], BF16, kind="ExternalInput").ap()
    W2 = nc.dram_tensor("W2", [D, D], F32, kind="ExternalInput").ap()
    b2 = nc.dram_tensor("b2", [D], F32, kind="ExternalInput").ap()
    yT = nc.dram_tensor("yT", [D, TOK], F32, kind="ExternalOutput").ap()

    w_sb = P.sb("w_sb", [128, KC * D], BF16)
    a_sb = P.sb("a_sb", [128, KC * TOK], BF16)
    b2_sb = P.sb("b2_sb", [1, D], BF16)
    ones_row = P.sb("ones_row", [1, 512], BF16)
    res_t = P.sb("res_t", [128, 512], F32)
    yout = P.sb("yout", [128, 512], F32)
    pg = [P.psum(f"pg{i}", [128, 512], F32) for i in range(4)]

    P.block()
    P.inc(nc.vector.memset(ones_row[:], 1.0), "cV")
    for kc in range(KC):
        P.dma(nc.gpsimd, "dG", w_sb[:, kc * D:(kc + 1) * D],
                                  W2[kc * 128:(kc + 1) * 128, :])
        P.dma(nc.gpsimd, "dG", a_sb[:, kc * TOK:(kc + 1) * TOK],
                                  attnT[kc * 128:(kc + 1) * 128, :])
    P.dma(nc.gpsimd, "dG", b2_sb[:, :], b2.unsqueeze(0))
    w_done = P.cnt["dG"]
    P.wait(nc.tensor, "dG", w_done)
    P.wait(nc.tensor, "cV", 1)
    pclean = [0] * 4
    res_busy = 0
    yst = 0
    for m in range(KC):
        for tt in range(TOK // 512):
            bi = (m * 4 + tt) % 4
            bank = pg[bi]
            P.wait(nc.tensor, "cV", pclean[bi])
            nc.tensor.matmul(bank[:, :], b2_sb[:, m * 128:(m + 1) * 128],
                             ones_row[:, :], start=True, stop=False)
            for kc in range(KC):
                mm = P.inc(nc.tensor.matmul(
                    bank[:, :],
                    w_sb[:, kc * D + m * 128: kc * D + (m + 1) * 128],
                    a_sb[:, kc * TOK + tt * 512: kc * TOK + (tt + 1) * 512],
                    start=False, stop=(kc == KC - 1)), "cT")
            P.wait(nc.sync, "cV", res_busy)
            rld = P.dma(nc.sync, "dS", res_t[:], xT[m * 128:(m + 1) * 128, tt * 512:(tt + 1) * 512])
            P.wait(nc.vector, "cT", mm)
            P.wait(nc.vector, "dS", rld)
            P.wait(nc.vector, "dS", yst)
            cv = P.inc(nc.vector.tensor_tensor(yout[:], bank[:, :], res_t[:], ALU.add), "cV")
            pclean[bi] = cv
            res_busy = cv
            P.wait(nc.sync, "cV", cv)
            yst = P.dma(nc.sync, "dS", yT[m * 128:(m + 1) * 128, tt * 512:(tt + 1) * 512], yout[:])
    nc.sync.wait_ge(P.sem("dS"), P.cnt["dS"])
    return P


def run_l0attn_all(xT_shards, W1, b1, W2, b2):
    import ml_dtypes
    P = Prog(); build_l0qkv(P); P.close()
    in_maps = [{"xT": s, "W1": W1, "b1": b1} for s in xT_shards]
    res = run_bass_kernel_spmd(P.nc, in_maps, core_ids=list(range(NCORE)))
    qkv = [r["qkvT"] for r in res.results]
    S = 4096
    qkv_full = np.concatenate(qkv, axis=1)
    cols = np.empty((NCORE * TOK,), np.int64)
    idx = np.arange(NCORE * TOK)
    core, rem = idx // TOK, idx % TOK
    s_, p_ = rem // POS, rem % POS
    gcol = s_ * S + core * POS + p_
    qkv_g = np.empty_like(qkv_full)
    qkv_g[:, gcol] = qkv_full[:, idx]
    in_maps = []
    for c in range(NCORE):
        h0 = 2 * c
        qTh = np.ascontiguousarray(np.concatenate(
            [qkv_g[h * 192: h * 192 + 64, :] for h in (h0, h0 + 1)], axis=0))
        kTh = np.ascontiguousarray(np.concatenate(
            [qkv_g[h * 192 + 64: h * 192 + 128, :] for h in (h0, h0 + 1)], axis=0))
        v_rows = np.concatenate(
            [qkv_g[h * 192 + 128: h * 192 + 192, :] for h in (h0, h0 + 1)], axis=0).T
        v_in = np.zeros((NSEQ * S, 132), dtype=ml_dtypes.bfloat16)
        v_in[:, 0:64] = v_rows[:, 0:64]
        v_in[:, 64] = 1.0
        v_in[:, 66:130] = v_rows[:, 64:128]
        v_in[:, 130] = 1.0
        in_maps.append({"qT": qTh, "kT": kTh, "v_in": v_in})
    P = Prog(); build_l0attn(P); P.close()
    res = run_bass_kernel_spmd(P.nc, in_maps, core_ids=list(range(NCORE)))
    attn = [r["attn_o"] for r in res.results]
    attn_g = np.concatenate(attn, axis=1)
    attn_T = np.ascontiguousarray(attn_g.T)
    in_maps = []
    for c in range(NCORE):
        cols_c = (np.arange(NSEQ)[:, None] * S + c * POS + np.arange(POS)[None, :]).reshape(-1)
        aT = np.ascontiguousarray(attn_T[:, cols_c])
        in_maps.append({"xT": xT_shards[c], "attnT": aT, "W2": W2, "b2": b2})
    P = Prog(); build_l0proj(P); P.close()
    res = run_bass_kernel_spmd(P.nc, in_maps, core_ids=list(range(NCORE)))
    return [r["yT"] for r in res.results]



NWQ = 2816 // NCORE
NDQ = D // NCORE


def fused_aux(P):
    aux = {
        "ones_bf": P.sb("g_ones_bf", [128, 1], BF16),
        "ones_f": P.sb("g_ones_f", [1, 128], F32),
        "xbf": P.sb("g_ln_xbf", [128, 512], BF16),
        "sq": P.sb("g_ln_sq", [128, 512], BF16),
        "arow": P.sb("g_ln_arow", [1, 512], F32),
        "brow": P.sb("g_ln_brow", [1, 512], F32),
        "mrow": P.sb("g_ln_mrow", [1, 512], F32),
        "init": False,
    }
    aux["ones_row"] = P.sb("g_ones_row", [1, 512], BF16)
    aux["ident"] = P.sb("g_ident", [128, 128], BF16)
    return aux


def emit_fused_init(P, aux):
    nc = P.nc
    P.inc(nc.vector.memset(aux["ones_bf"][:], 1.0), "cV")
    P.inc(nc.vector.memset(aux["ones_f"][:], 1.0), "cV")
    P.inc(nc.vector.memset(aux["ones_row"][:], 1.0), "cV")
    idf = P.sb("g_idf", [128, 128], F32)
    P.inc(nc.gpsimd.iota(idf[:], [[-1, 128]], base=0, channel_multiplier=1,
                         allow_small_or_imprecise_dtypes=True), "cG")
    P.wait(nc.vector, "cG", P.cnt["cG"])
    P.inc(nc.vector.tensor_scalar(aux["ident"][:], idf[:], 0.0, None, ALU.is_equal), "cV")
    aux["init"] = True
    aux["init_cv"] = P.cnt["cV"]


def emit_prep_f(P, aux, wgs, wus, wds, agw):
    nc = P.nc
    with P.scope():
        xin = P.sb("p_xin", [128, 2816], F32)
        qout = P.sb("p_qout", [128, 2816], BF16)
        scr = {
            "a": P.sb("p_qa", [128, 2816], F32), "t": P.sb("p_qt", [128, 2816], F32),
            "u": P.sb("p_qu", [128, 2816], F32), "bm": P.sb("p_qbm", [128, 176], F32),
            "scale": P.sb("p_qsc", [128, 176], F32), "rsc": P.sb("p_qrs", [128, 176], F32),
        }
        jobs = []
        for r0 in range(0, NWQ, 128):
            npart = min(128, NWQ - r0)
            jobs.append((wgs, agw["g_in"], r0, npart, D))
            jobs.append((wus, agw["u_in"], r0, npart, D))
        jobs.append((wds, agw["d_in"], 0, NDQ, 2816))
        st_done = 0
        for (src_, dst, r0, npart, F) in jobs:
            P.wait(nc.sync, "cV", scr.get("busy", 0))
            P.wait(nc.sync, "cS", scr.get("busy_s", 0))
            ld = P.dma(nc.sync, "dS", xin[:npart, :F], src_[r0:r0 + npart, :])
            emit_quant(P, xin[:npart, :F], qout[:npart, :F], scr, npart, F // 16,
                       after=[("dS", ld)])
            P.wait(nc.sync, "cV", P.cnt["cV"])
            st_done = P.dma(nc.sync, "dS", dst[r0:r0 + npart, :], qout[:npart, :F])
        P.wait(nc.gpsimd, "dS", st_done)
        for nm in ("g", "u", "d"):
            P.inc(nc.gpsimd.collective_compute(
                "AllGather", ALU.bypass, replica_groups=[list(range(NCORE))],
                ins=[agw[nm + "_in"].ap().opt()], outs=[agw[nm + "_out"].ap().opt()]), "cC")
    P.barrier()


def emit_l0ffn_f(P, aux, xin_d, yout_d, Wd, rbf_d=None):
    nc = P.nc
    with P.scope():
        wg_sb = P.sb("f_wg", [128, KC * 3 * D], BF16)
        ug_sb = P.sb("f_ug", [128, KC * 3 * D], BF16)
        big = P.sb("f_big", [128, 16384], BF16)
        gf = big[:, 0:KC * POS]; gi = big[:, KC * POS:2 * KC * POS]; gg = big[:, 2 * KC * POS:3 * KC * POS]
        xhat = P.sb("f_xhat", [128, KC * TOK], BF16)
        c_sb = P.sb("f_c", [128, KC * POS], BF16)
        tmp1 = P.sb("f_tmp1", [128, POS], BF16)
        tmp2 = P.sb("f_tmp2", [128, POS], BF16)
        hout = P.sb("f_hout", [128, 512], F32)
        res_t = P.sb("f_res", [128, 512], F32)
        yout = P.sb("f_yout", [128, 512], F32)
        ybf = P.sb("f_ybf", [128, 512], BF16)
        bgs = P.sb("f_bgs", [128, 24], F32)
        bos = P.sb("f_bos", [128, KC], F32)
        incs = P.sb("f_incs", [128, KC], F32)
        pg = [P.psum(f"f_pg{i}", [128, 512], F32) for i in range(4)]
        aux.pop("stats_mm", None); aux.pop("rows_read", None)

        for kc in range(KC):
            P.dma_nb(nc.gpsimd, "dG", wg_sb[:, kc * 3 * D:(kc + 1) * 3 * D],
                     Wd["wg"][kc * 128:(kc + 1) * 128, :])
            P.dma_nb(nc.gpsimd, "dG", ug_sb[:, kc * 3 * D:(kc + 1) * 3 * D],
                     Wd["ug"][kc * 128:(kc + 1) * 128, :])
        w_done = P.dbar(nc.gpsimd, "dG")
        for j in range(24):
            P.dma_nb(nc.sync, "dS", bgs[:, j:j + 1], Wd["bg"][j * 128:(j + 1) * 128].unsqueeze(-1))
        for j in range(KC):
            P.dma_nb(nc.sync, "dS", bos[:, j:j + 1], Wd["bo"][j * 128:(j + 1) * 128].unsqueeze(-1))
            P.dma_nb(nc.sync, "dS", incs[:, j:j + 1], Wd["inc_"][j * 128:(j + 1) * 128].unsqueeze(-1))
        const_done = P.dbar(nc.sync, "dS")

        with P.scope():
            aux["xf32_t"] = P.sb("f_lnx", [128, KC * 512], F32)
            aux["pS"] = P.psum("f_pS", [1, 512], F32)
            aux["pQ"] = P.psum("f_pQ", [1, 512], F32)
            aux["pA"] = P.psum("f_pA", [128, 512], F32)
            aux["pB"] = P.psum("f_pB", [128, 512], F32)
            emit_layernorm(P, xin_d, xhat, aux)

        P.wait(nc.vector, "dS", const_done)
        for kc in range(KC):
            P.wait(nc.vector, "cV", P.cnt["cV"])
            P.inc(nc.vector.tensor_copy(
                c_sb[:, kc * POS:(kc + 1) * POS],
                incs[:, kc:kc + 1].broadcast_to([128, POS])), "cV")
        c_ready = P.cnt["cV"]

        P.wait(nc.tensor, "dG", w_done)
        gate_view = {0: gf, 1: gi, 2: gg}
        act_fn = {0: AF.Sigmoid, 1: AF.Sigmoid, 2: AF.Tanh}
        pclean = [0, 0, 0, 0]
        for t in range(4):
            P.wait(nc.tensor, "cV", c_ready)
            for gate in range(3):
                for m in range(KC):
                    bi = (gate * KC + m) % 4
                    bank = pg[bi]
                    P.wait(nc.tensor, "cS", pclean[bi])
                    for kc in range(KC):
                        nc.tensor.matmul(
                            bank[:, :],
                            wg_sb[:, kc * 3 * D + gate * D + m * 128: kc * 3 * D + gate * D + (m + 1) * 128],
                            xhat[:, kc * TOK + t * POS: kc * TOK + (t + 1) * POS],
                            start=(kc == 0), stop=False)
                    for kc in range(KC):
                        last = P.inc(nc.tensor.matmul(
                            bank[:, :],
                            ug_sb[:, kc * 3 * D + gate * D + m * 128: kc * 3 * D + gate * D + (m + 1) * 128],
                            c_sb[:, kc * POS:(kc + 1) * POS],
                            start=False, stop=(kc == KC - 1)), "cT")
                    P.wait(nc.scalar, "cT", last)
                    pclean[bi] = P.inc(nc.scalar.activation(
                        gate_view[gate][:, m * POS:(m + 1) * POS], bank[:, :],
                        act_fn[gate], bias=bgs[:, gate * KC + m: gate * KC + m + 1]), "cS")
            gates_done = P.cnt["cS"]
            all_mm = P.cnt["cT"]
            P.wait(nc.vector, "cS", gates_done)
            P.wait(nc.vector, "cT", all_mm)
            for m in range(KC):
                sl = slice(m * POS, (m + 1) * POS)
                P.wait(nc.vector, "cV", P.cnt["cV"])
                P.inc(nc.vector.tensor_tensor(tmp1[:], gi[:, sl], gg[:, sl], ALU.mult), "cV")
                P.wait(nc.vector, "cV", P.cnt["cV"])
                P.inc(nc.vector.tensor_tensor(tmp2[:], gf[:, sl], c_sb[:, sl], ALU.mult), "cV")
                P.wait(nc.vector, "cV", P.cnt["cV"])
                P.inc(nc.vector.tensor_tensor(c_sb[:, sl], tmp1[:], tmp2[:], ALU.add), "cV")
            c_ready = P.cnt["cV"]

        P.wait(nc.gpsimd, "cV", c_ready)
        wo_sb = big[:, 0:KC * D]
        uo_sb = big[:, KC * D:2 * KC * D]
        for kc in range(KC):
            P.dma_nb(nc.gpsimd, "dG", wo_sb[:, kc * D:(kc + 1) * D],
                     Wd["wo"][kc * 128:(kc + 1) * 128, :])
            P.dma_nb(nc.gpsimd, "dG", uo_sb[:, kc * D:(kc + 1) * D],
                     Wd["uo"][kc * 128:(kc + 1) * 128, :])
        wo_done = P.dbar(nc.gpsimd, "dG")

        P.wait(nc.tensor, "dG", wo_done)
        P.wait(nc.gpsimd, "cV", c_ready)
        res2 = P.sb("f_res2", [128, 512], F32)
        yout2 = P.sb("f_yout2", [128, 512], F32)
        ybf2 = P.sb("f_ybf2", [128, 512], BF16)
        hout2 = P.sb("f_hout2", [128, 512], F32)
        res_ld = [0, 0]
        y_free = [0, 0]
        y_st = [0, 0]
        h_free = [0, 0]
        for t in range(4):
            for m in range(KC):
                it = t * KC + m
                par = it % 2
                bi = it % 4
                bank = pg[bi]
                rt = res_t if par == 0 else res2
                yo = yout if par == 0 else yout2
                yb = ybf if par == 0 else ybf2
                ho = hout if par == 0 else hout2
                P.wait(nc.gpsimd, "cV", y_free[par])
                res_ld[par] = P.dma(nc.gpsimd, "dG",
                                    rt[:], xin_d[m * 128:(m + 1) * 128, t * POS:(t + 1) * POS])
                P.wait(nc.tensor, "cS", pclean[bi])
                for kc in range(KC):
                    nc.tensor.matmul(
                        bank[:, :],
                        wo_sb[:, kc * D + m * 128: kc * D + (m + 1) * 128],
                        xhat[:, kc * TOK + t * POS: kc * TOK + (t + 1) * POS],
                        start=(kc == 0), stop=False)
                for kc in range(KC):
                    last = P.inc(nc.tensor.matmul(
                        bank[:, :],
                        uo_sb[:, kc * D + m * 128: kc * D + (m + 1) * 128],
                        c_sb[:, kc * POS:(kc + 1) * POS],
                        start=False, stop=(kc == KC - 1)), "cT")
                P.wait(nc.scalar, "cT", last)
                P.wait(nc.scalar, "cV", h_free[par])
                pclean[bi] = P.inc(nc.scalar.activation(
                    ho[:], bank[:, :], AF.Tanh, bias=bos[:, m:m + 1]), "cS")
                P.wait(nc.vector, "cS", pclean[bi])
                P.wait(nc.vector, "dG", res_ld[par])
                P.wait(nc.vector, "dS", y_st[par])
                cv = P.inc(nc.vector.scalar_tensor_tensor(
                    yo[:], ho[:], 0.5, rt[:], ALU.mult, ALU.add), "cV")
                h_free[par] = cv
                y_free[par] = cv
                P.wait(nc.sync, "cV", cv)
                y_st[par] = P.dma(nc.sync, "dS",
                                  yout_d[m * 128:(m + 1) * 128, t * POS:(t + 1) * POS], yo[:])
                if rbf_d is not None:
                    P.wait(nc.vector, "cV", cv)
                    cb = P.inc(nc.vector.tensor_copy(yb[:], yo[:]), "cV")
                    y_free[par] = cb
                    P.wait(nc.sync, "cV", cb)
                    y_st[par] = P.dma(nc.sync, "dS",
                                      rbf_d[m * 128:(m + 1) * 128, t * POS:(t + 1) * POS], yb[:])
    P.barrier()


def emit_l0qkv_f(P, aux, xin_d, W1qk_d, b1qk_d, W1v_d, b1v_d, a2aqk, a2av):
    nc = P.nc
    with P.scope():
        wqk_sb = P.sb("qk_wqk", [128, KC * 2 * D], BF16)
        wv_sb = P.sb("qk_wv", [128, KC * D], BF16)
        xhat = P.sb("qk_xhat", [128, KC * TOK], BF16)
        bqk_sb = P.sb("qk_bqk", [1, 2 * D], BF16)
        bv_sb = P.sb("qk_bv", [1, D], BF16)
        ev = P.sb("qk_ev", [128, 512], BF16)
        aux["xf32_t"] = P.sb("qk_lnx", [128, KC * 512], F32)
        aux["pS"] = P.psum("qk_pS", [1, 512], F32)
        aux["pQ"] = P.psum("qk_pQ", [1, 512], F32)
        aux["pA"] = P.psum("qk_pA", [128, 512], F32)
        aux["pB"] = P.psum("qk_pB", [128, 512], F32)
        aux.pop("stats_mm", None); aux.pop("rows_read", None)
        pg = [P.psum(f"qk_pg{i}", [128, 512], F32) for i in range(4)]

        for kc in range(KC):
            P.dma_nb(nc.gpsimd, "dG", wqk_sb[:, kc * 2 * D:(kc + 1) * 2 * D],
                     W1qk_d[kc * 128:(kc + 1) * 128, :])
            P.dma_nb(nc.gpsimd, "dG", wv_sb[:, kc * D:(kc + 1) * D],
                     W1v_d[kc * 128:(kc + 1) * 128, :])
        P.dma_nb(nc.gpsimd, "dG", bqk_sb[:, :], b1qk_d.unsqueeze(0))
        P.dma_nb(nc.gpsimd, "dG", bv_sb[:, :], b1v_d.unsqueeze(0))
        w_done = P.dbar(nc.gpsimd, "dG")
        emit_layernorm(P, xin_d, xhat, aux)
        ln_done = P.cnt["cV"]
        P.wait(nc.tensor, "dG", w_done)
        P.wait(nc.tensor, "cV", ln_done)
        P.wait(nc.tensor, "cV", aux["init_cv"])
        pclean = [0] * 4
        st = 0
        for part in range(2):
            for j in range(8):
                for tt in range(TOK // 512):
                    bi = ((part * 8 + j) * 4 + tt) % 4
                    bank = pg[bi]
                    P.wait(nc.tensor, "cV", pclean[bi])
                    nc.tensor.matmul(
                        bank[:, :],
                        bqk_sb[:, part * D + j * 128: part * D + (j + 1) * 128],
                        aux["ones_row"][:, :], start=True, stop=False)
                    for kc in range(KC):
                        mm = P.inc(nc.tensor.matmul(
                            bank[:, :],
                            wqk_sb[:, kc * 2 * D + part * D + j * 128:
                                   kc * 2 * D + part * D + (j + 1) * 128],
                            xhat[:, kc * TOK + tt * 512: kc * TOK + (tt + 1) * 512],
                            start=False, stop=(kc == KC - 1)), "cT")
                    P.wait(nc.vector, "cT", mm)
                    P.wait(nc.vector, "dS", st)
                    cv = P.inc(nc.vector.tensor_copy(ev[:], bank[:, :]), "cV")
                    pclean[bi] = cv
                    P.wait(nc.sync, "cV", cv)
                    st = P.dma(nc.sync, "dS",
                               a2aqk["in"][256 * j + 128 * part: 256 * j + 128 * part + 128,
                                           tt * 512:(tt + 1) * 512],
                               ev[:])
        for ch in range(TOK // 128):
            for js in range(2):
                bi = (ch * 2 + js) % 4
                bank = pg[bi]
                P.wait(nc.tensor, "cV", pclean[bi])
                nc.tensor.matmul(bank[:, :], aux["ones_row"][:, 0:128],
                                 bv_sb[:, js * 512:(js + 1) * 512], start=True, stop=False)
                for kc in range(KC):
                    mm = P.inc(nc.tensor.matmul(
                        bank[:, :],
                        xhat[:, kc * TOK + ch * 128: kc * TOK + (ch + 1) * 128],
                        wv_sb[:, kc * D + js * 512: kc * D + (js + 1) * 512],
                        start=False, stop=(kc == KC - 1)), "cT")
                P.wait(nc.vector, "cT", mm)
                P.wait(nc.vector, "dS", st)
                cv = P.inc(nc.vector.tensor_copy(ev[:], bank[:, :]), "cV")
                pclean[bi] = cv
                P.wait(nc.sync, "cV", cv)
                dst = bass.AP(a2av["in"], (2048 * 4 * js + ch * 128) * 128,
                              [[128, 128], [2048 * 128, 4], [1, 128]])
                st = P.dma(nc.sync, "dS", dst,
                           ev[:].rearrange("p (pr w) -> p pr w", w=128))
        P.wait(nc.gpsimd, "dS", st)
        P.wait(nc.gpsimd, "cV", P.cnt["cV"])
        from concourse.aot_env import allow_ici_all_to_all
        with allow_ici_all_to_all():
            P.inc(nc.gpsimd.collective_compute(
                "AllToAll", ALU.bypass, replica_groups=[list(range(NCORE))],
                ins=[a2aqk["in"].ap().opt()], outs=[a2aqk["out"].ap().opt()]), "cC")
            P.inc(nc.gpsimd.collective_compute(
                "AllToAll", ALU.bypass, replica_groups=[list(range(NCORE))],
                ins=[a2av["in"].ap().opt()], outs=[a2av["out"].ap().opt()]), "cC")
        P.barrier()


def emit_l0attn_f(P, aux, a2aqk_out, a2av_out, a2aat_in, a2aat_out):
    nc = P.nc
    S = 4096
    with P.scope():
        q_b = [P.sb(f"at_q{i}", [128, S], BF16) for i in range(2)]
        k_b = [P.sb(f"at_k{i}", [128, S], BF16) for i in range(2)]
        v_b = [P.sb(f"at_v{i}", [128, 32 * 132], BF16) for i in range(2)]
        prb = [[P.sb(f"at_pr{h}{i}", [128, 512], BF16) for i in range(2)] for h in range(2)]
        maskT = P.sb("at_maskT", [128, 128], BF16)
        out_sb = [P.sb(f"at_out{h}", [128, 8 * 64], BF16) for h in range(2)]
        rcol = P.sb("at_rcol", [128, 1], F32)
        pS_ = [[P.psum(f"at_pS{h}{i}", [128, 512], F32) for i in range(2)] for h in range(2)]
        pO = [[P.psum(f"at_pO{h}{i}", [128, 264], F32) for i in range(2)] for h in range(2)]

        idf2 = P.sb("at_idf", [128, 128], F32)
        P.inc(nc.gpsimd.iota(idf2[:], [[-1, 128]], base=0, channel_multiplier=1,
                             allow_small_or_imprecise_dtypes=True), "cG")
        P.wait(nc.vector, "cG", P.cnt["cG"])
        P.inc(nc.vector.tensor_scalar(maskT[:], idf2[:], 0.0, None, ALU.is_le), "cV")
        for vb_ in v_b:
            vv = vb_[:].rearrange("p (c w) -> p c w", w=132)
            P.inc(nc.vector.memset(vv[:, :, 64:65], 1.0), "cV")
            P.inc(nc.vector.memset(vv[:, :, 130:131], 1.0), "cV")
        mask_done = P.cnt["cV"]
        P.wait(nc.sync, "cV", mask_done)

        INV8 = float(1.0 / 8.0)
        st = [0, 0]
        kv_free = [0, 0]
        ld_done = [0, 0]

        def emit_loads(s):
            par = s % 2
            q_sb, k_sb, v_sb = q_b[par], k_b[par], v_b[par]
            P.wait(nc.sync, "cT", kv_free[par])
            for cp in range(NCORE):
                P.dma_nb(nc.sync, "dS", q_sb[:, cp * 512:(cp + 1) * 512],
                         a2aqk_out[256 * cp: 256 * cp + 128, s * 512:(s + 1) * 512])
                P.dma_nb(nc.sync, "dS", k_sb[:, cp * 512:(cp + 1) * 512],
                         a2aqk_out[256 * cp + 128: 256 * cp + 256, s * 512:(s + 1) * 512])
            for ch in range(32):
                cp, rr = ch // 4, ch % 4
                dst = bass.AP(v_sb, ch * 132, [[32 * 132, 128], [66, 2], [1, 64]])
                P.dma_nb(nc.sync, "dS", dst,
                         a2av_out[2048 * cp + s * 512 + rr * 128:
                                  2048 * cp + s * 512 + (rr + 1) * 128, :])
            ld_done[par] = P.dbar(nc.sync, "dS")

        emit_loads(0)
        for s in range(NSEQ):
            par = s % 2
            q_sb, k_sb, v_sb = q_b[par], k_b[par], v_b[par]
            if s + 1 < NSEQ:
                emit_loads(s + 1)
            P.wait(nc.tensor, "dS", ld_done[par])
            s_free = [[0, 0], [0, 0]]
            p_free = [[0, 0], [0, 0]]
            pm_extra = [[0, 0], [0, 0]]
            ex_cnt = [[0, 0], [0, 0]]
            po_free = [[0, 0], [0, 0]]
            for qs in range(8):
                nkc = 4 * qs + 4
                first_av = [True, True]

                def emit_S(kc):
                    q0 = max(qs * 512, kc * 128)
                    w = qs * 512 + 512 - q0
                    for h in range(2):
                        hb = h * 64
                        bank = pS_[h][kc % 2]
                        P.wait(nc.tensor, "cS", s_free[h][kc % 2])
                        smm = P.inc(nc.tensor.matmul(
                            bank[0:128, 0:w] if False else bank[:, 0:w],
                            k_sb[hb:hb + 64, kc * 128:(kc + 1) * 128],
                            q_sb[hb:hb + 64, q0:q0 + w],
                            start=True, stop=True), "cT")
                        P.wait(nc.scalar, "cT", max(smm, p_free[h][kc % 2]))
                        ex = P.inc(nc.scalar.activation(
                            prb[h][kc % 2][:, 0:w], bank[:, 0:w],
                            AF.Exp, scale=np.float32(INV8)), "cS")
                        s_free[h][kc % 2] = ex
                        ex_cnt[h][kc % 2] = ex
                        if q0 == kc * 128:
                            P.wait(nc.vector, "cS", ex)
                            md = P.inc(nc.vector.tensor_tensor(
                                prb[h][kc % 2][:, 0:128], prb[h][kc % 2][:, 0:128],
                                maskT[:], ALU.mult), "cV")
                            pm_extra[h][kc % 2] = md

                def emit_AV(kc):
                    q0 = max(qs * 512, kc * 128)
                    w = qs * 512 + 512 - q0
                    for h in range(2):
                        po = pO[h][qs % 2]
                        P.wait(nc.tensor, "cS", ex_cnt[h][kc % 2])
                        P.wait(nc.tensor, "cV", pm_extra[h][kc % 2])
                        if first_av[h]:
                            P.wait(nc.tensor, "cV", po_free[h][qs % 2])
                        for j in range(4):
                            jq = qs * 512 + j * 128
                            if jq < q0:
                                continue
                            P.inc(nc.tensor.matmul(
                                po[:, j * 66: j * 66 + 65],
                                prb[h][kc % 2][:, jq - q0: jq - q0 + 128],
                                v_sb[:, kc * 132 + h * 66: kc * 132 + h * 66 + 65],
                                start=(first_av[h] and j == 0), stop=(kc == nkc - 1)), "cT")
                        first_av[h] = False
                        p_free[h][kc % 2] = P.cnt["cT"]

                for kc in range(nkc):
                    emit_S(kc)
                    if kc > 0:
                        emit_AV(kc - 1)
                emit_AV(nkc - 1)
                av_done = P.cnt["cT"]
                for h in range(2):
                    po = pO[h][qs % 2]
                    P.wait(nc.vector, "cT", av_done)
                    for j in range(4):
                        P.wait(nc.vector, "cV", P.cnt["cV"])
                        P.inc(nc.vector.reciprocal(rcol[:], po[:, j * 66 + 64: j * 66 + 65]), "cV")
                        P.wait(nc.vector, "cV", P.cnt["cV"])
                        P.wait(nc.vector, "dS", st[h])
                        P.inc(nc.vector.tensor_scalar(
                            out_sb[h][:, (qs % 2) * 256 + j * 64: (qs % 2) * 256 + (j + 1) * 64],
                            po[:, j * 66: j * 66 + 64], rcol[:, 0:1], None, ALU.mult), "cV")
                    po_free[h][qs % 2] = P.cnt["cV"]
                if qs % 2 == 1:
                    for h in range(2):
                        hb = h * 64
                        ev2 = po_free[h][1]
                        P.wait(nc.sync, "cV", ev2)
                        for jj in range(2):
                            base = (2048 * (qs - 1 + jj) + s * 512) * 128 + hb
                            dstp = bass.AP(a2aat_in, base,
                                           [[128, 128], [128 * 128, 4], [1, 64]])
                            st[h] = P.inc(nc.sync.dma_start(
                                dstp, out_sb[h][:, jj * 256:(jj + 1) * 256]
                                .rearrange("p (jr c) -> p jr c", c=64)), "dS")
                        nc.sync.wait_ge(P.sem("dS"), st[h])
            kv_free[par] = P.cnt["cT"]
        from concourse.aot_env import allow_ici_all_to_all
        P.wait(nc.gpsimd, "dS", P.cnt["dS"])
        with allow_ici_all_to_all():
            P.inc(nc.gpsimd.collective_compute(
                "AllToAll", ALU.bypass, replica_groups=[list(range(NCORE))],
                ins=[a2aat_in.ap().opt()], outs=[a2aat_out.ap().opt()]), "cC")
        P.barrier()


def emit_l0proj_f(P, aux, xin_d, a2aat_out, W2_d, b2_d, yout_d):
    nc = P.nc
    with P.scope():
        w_sb = P.sb("pj_w", [128, KC * D], BF16)
        a_sb = P.sb("pj_a", [128, KC * TOK], BF16)
        b2_sb = P.sb("pj_b2", [1, D], BF16)
        res_t = P.sb("pj_res", [128, 512], F32)
        yout = P.sb("pj_yout", [128, 512], F32)
        pg = [P.psum(f"pj_pg{i}", [128, 512], F32) for i in range(4)]

        for kc in range(KC):
            P.dma_nb(nc.gpsimd, "dG", w_sb[:, kc * D:(kc + 1) * D],
                     W2_d[kc * 128:(kc + 1) * 128, :])
        P.dma_nb(nc.gpsimd, "dG", b2_sb[:, :], b2_d.unsqueeze(0))
        w_done = P.dbar(nc.gpsimd, "dG")
        for kc in range(KC):
            for tt in range(TOK // 512):
                P.dma_tnb(nc.sync, "dS", a_sb[:, kc * TOK + tt * 512: kc * TOK + (tt + 1) * 512],
                          a2aat_out[kc * 2048 + tt * 512: kc * 2048 + (tt + 1) * 512, :])
        a_done = P.dbar(nc.sync, "dS")
        P.wait(nc.tensor, "dG", w_done)
        P.wait(nc.tensor, "dS", a_done)
        P.wait(nc.tensor, "cV", aux["init_cv"])
        pclean = [0] * 4
        res_busy = 0
        yst = 0
        for m in range(KC):
            for tt in range(TOK // 512):
                bi = (m * 4 + tt) % 4
                bank = pg[bi]
                P.wait(nc.tensor, "cV", pclean[bi])
                nc.tensor.matmul(bank[:, :], b2_sb[:, m * 128:(m + 1) * 128],
                                 aux["ones_row"][:, :], start=True, stop=False)
                for kc in range(KC):
                    mm = P.inc(nc.tensor.matmul(
                        bank[:, :],
                        w_sb[:, kc * D + m * 128: kc * D + (m + 1) * 128],
                        a_sb[:, kc * TOK + tt * 512: kc * TOK + (tt + 1) * 512],
                        start=False, stop=(kc == KC - 1)), "cT")
                P.wait(nc.sync, "cV", res_busy)
                rld = P.dma(nc.sync, "dS",
                            res_t[:], xin_d[m * 128:(m + 1) * 128, tt * 512:(tt + 1) * 512])
                P.wait(nc.vector, "cT", mm)
                P.wait(nc.vector, "dS", rld)
                P.wait(nc.vector, "dS", yst)
                cv = P.inc(nc.vector.tensor_tensor(yout[:], bank[:, :], res_t[:], ALU.add), "cV")
                pclean[bi] = cv
                res_busy = cv
                P.wait(nc.sync, "cV", cv)
                yst = P.dma(nc.sync, "dS",
                            yout_d[m * 128:(m + 1) * 128, tt * 512:(tt + 1) * 512], yout[:])
    P.barrier()


def emit_l1ffn_f(P, aux, xin_d, rbf_d, agw, yout_d, xq_d, hq_d, rbf_out=None):
    nc = P.nc
    HT = 256
    with P.scope():
        wbig = P.sb("l1_wbig", [128, 45056], BF16)
        wgq_sb = wbig[:, 0:KC * 2816]
        wuq_sb = wbig[:, KC * 2816:2 * KC * 2816]
        wdq_sb = wbig[:, 0:22 * D]
        xqT = P.sb("l1_xqT", [128, KC * TOK], BF16)
        pb = [P.psum(f"l1_pb{i}", [128, 512], F32) for i in range(6)]
        tps = [P.psum(f"l1_tp{i}", [128, 128], BF16) for i in range(2)]
        scopeA = P.scope()
        scopeA.__enter__()
        xtb = P.sb("l1_xtb", [128, D], BF16)
        xtc = P.sb("l1_xtc", [128, D], F32)
        sqc = P.sb("l1_sqc", [128, D], BF16)
        xqc = P.sb("l1_xqc", [128, D], BF16)
        h_sb = P.sb("l1_h", [128, 2816], F32)
        hq_sb = P.sb("l1_hq", [128, 2816], BF16)
        sil = P.sb("l1_sil", [128, 512], F32)
        scr = {
            "a": P.sb("l1_qa", [128, 2816], F32), "t": P.sb("l1_qt", [128, 2816], F32),
            "u": P.sb("l1_qu", [128, 2816], F32), "bm": P.sb("l1_qbm", [128, 176], F32),
            "scale": P.sb("l1_qsc", [128, 176], F32), "rsc": P.sb("l1_qrs", [128, 176], F32),
        }
        cs1 = P.sb("l1_cs1", [128, 1], F32)
        cs2 = P.sb("l1_cs2", [128, 1], F32)
        mean = P.sb("l1_mean", [128, 1], F32)
        rstd = P.sb("l1_rstd", [128, 1], F32)
        bcol = P.sb("l1_bcol", [128, 1], F32)

        P.wait(nc.sync, "cC", P.cnt["cC"])
        for kc in range(KC):
            P.dma_tnb(nc.sync, "dS", wgq_sb[:, kc * 2816:(kc + 1) * 2816],
                      agw["g_out"].ap()[:, kc * 128:(kc + 1) * 128])
            P.dma_tnb(nc.sync, "dS", wuq_sb[:, kc * 2816:(kc + 1) * 2816],
                      agw["u_out"].ap()[:, kc * 128:(kc + 1) * 128])
        wA_done = P.dbar(nc.sync, "dS")

        pclean_v = [0] * 6
        pclean_s = [0] * 6
        sil_busy = 0
        tp_free = [0, 0]
        xqT_ready = 0
        hq_st = 0
        P.wait(nc.tensor, "dS", wA_done)
        for i in range(TOK // 128):
            P.wait(nc.sync, "cG", scr.get("xtb_free", 0))
            ld = P.dma_t(nc.sync, "dS", xtb[:], rbf_d[:, i * 128:(i + 1) * 128])
            P.wait(nc.gpsimd, "dS", ld)
            P.wait(nc.gpsimd, "cV", scr.get("busy", 0))
            P.wait(nc.gpsimd, "cS", scr.get("busy_s", 0))
            xc = P.inc(nc.gpsimd.tensor_copy(xtc[:], xtb[:]), "cG")
            scr["xtb_free"] = xc
            P.wait(nc.vector, "cG", xc)
            P.inc(nc.vector.tensor_reduce(cs1[:], xtc[:], AX.X, ALU.add), "cV")
            P.wait(nc.scalar, "cG", xc)
            sq_done = P.inc(nc.scalar.activation(sqc[:], xtc[:], AF.Square), "cS")
            P.wait(nc.vector, "cS", sq_done)
            P.inc(nc.vector.tensor_reduce(cs2[:], sqc[:], AX.X, ALU.add), "cV")
            P.wait(nc.vector, "cV", P.cnt["cV"])
            P.inc(nc.vector.tensor_scalar(mean[:], cs1[:], 1.0 / D, None, ALU.mult), "cV")
            P.wait(nc.vector, "cV", P.cnt["cV"])
            P.inc(nc.vector.tensor_scalar(cs2[:], cs2[:], 1.0 / D, None, ALU.mult), "cV")
            P.wait(nc.vector, "cV", P.cnt["cV"])
            P.inc(nc.vector.tensor_tensor(cs1[:], mean[:], mean[:], ALU.mult), "cV")
            P.wait(nc.vector, "cV", P.cnt["cV"])
            P.inc(nc.vector.tensor_tensor(cs2[:], cs1[:], cs2[:], ALU.subtract), "cV")
            P.wait(nc.scalar, "cV", P.cnt["cV"])
            P.inc(nc.scalar.activation(cs2[:], cs2[:], AF.Ln,
                                       bias=np.float32(1e-5), scale=np.float32(-1.0)), "cS")
            P.wait(nc.scalar, "cS", P.cnt["cS"])
            sd = P.inc(nc.scalar.activation(rstd[:], cs2[:], AF.Exp, scale=np.float32(-0.5)), "cS")
            P.wait(nc.vector, "cS", sd)
            P.inc(nc.vector.scalar_tensor_tensor(bcol[:], mean[:], -1.0, rstd[:],
                                                 ALU.mult, ALU.mult), "cV")
            P.wait(nc.vector, "cV", P.cnt["cV"])
            P.inc(nc.vector.tensor_scalar(xtc[:], xtc[:], rstd[:, 0:1], bcol[:, 0:1],
                                          ALU.mult, ALU.add), "cV")
            emit_quant(P, xtc[:, :], xqc[:, :], scr, 128, D // 16,
                       after=[("cV", P.cnt["cV"])])
            qv = P.cnt["cV"]
            P.wait(nc.tensor, "cV", qv)
            for kc in range(KC):
                tb = tps[kc % 2]
                P.wait(nc.tensor, "cS", tp_free[kc % 2])
                tpc = P.inc(nc.tensor.transpose(tb[:, :], xqc[:, kc * 128:(kc + 1) * 128],
                                                aux["ident"][:, :]), "cT")
                P.wait(nc.scalar, "cT", tpc)
                tp_free[kc % 2] = P.inc(nc.scalar.activation(
                    xqT[:, kc * TOK + i * 128: kc * TOK + (i + 1) * 128],
                    tb[:, :], AF.Copy), "cS")
            xqT_ready = P.cnt["cS"]
            for si, (n0, ns) in enumerate(NSL):
                bg_, bu_ = pb[2 * (si % 3)], pb[2 * (si % 3) + 1]
                P.wait(nc.tensor, "cS", xqT_ready)
                P.wait(nc.tensor, "cV", pclean_v[2 * (si % 3)])
                P.wait(nc.tensor, "cS", pclean_s[2 * (si % 3)])
                P.wait(nc.tensor, "cV", pclean_v[2 * (si % 3) + 1])
                P.wait(nc.tensor, "cS", pclean_s[2 * (si % 3) + 1])
                for kc in range(KC):
                    nc.tensor.matmul(bg_[:, 0:ns],
                                     xqT[:, kc * TOK + i * 128: kc * TOK + (i + 1) * 128],
                                     wgq_sb[:, kc * 2816 + n0: kc * 2816 + n0 + ns],
                                     start=(kc == 0), stop=(kc == KC - 1))
                for kc in range(KC):
                    mmu = P.inc(nc.tensor.matmul(bu_[:, 0:ns],
                                     xqT[:, kc * TOK + i * 128: kc * TOK + (i + 1) * 128],
                                     wuq_sb[:, kc * 2816 + n0: kc * 2816 + n0 + ns],
                                     start=(kc == 0), stop=(kc == KC - 1)), "cT")
                P.wait(nc.scalar, "cT", mmu)
                P.wait(nc.scalar, "cV", sil_busy)
                ssil = P.inc(nc.scalar.activation(sil[:, 0:ns], bg_[:, 0:ns], AF.Silu), "cS")
                pclean_s[2 * (si % 3)] = ssil
                pclean_s[2 * (si % 3) + 1] = ssil
                P.wait(nc.vector, "cS", ssil)
                sil_busy = P.inc(nc.vector.tensor_tensor(
                    h_sb[:, n0:n0 + ns], sil[:, 0:ns], bu_[:, 0:ns], ALU.mult), "cV")
                pclean_v[2 * (si % 3)] = sil_busy
                pclean_v[2 * (si % 3) + 1] = sil_busy
            P.wait(nc.vector, "dG", hq_st)
            emit_quant(P, h_sb[:, :], hq_sb[:, :], scr, 128, 176,
                       after=[("cV", P.cnt["cV"])])
            qv = P.cnt["cV"]
            P.wait(nc.gpsimd, "cV", qv)
            hq_st = P.dma(nc.gpsimd, "dG", hq_d.ap()[i * 128:(i + 1) * 128, :], hq_sb[:])
        hq_all = P.cnt["dG"]
        scopeA.__exit__(None, None, None)
        P.barrier()
        hql = P.sb("l1_hql", [128, 2 * 2816], BF16)
        hqT = P.sb("l1_hqT", [128, 22 * HT], BF16)
        res_t = P.sb("l1_res", [128, HT], F32)
        res2 = P.sb("l1_res2", [128, HT], F32)
        yout = P.sb("l1_yout", [128, HT], F32)
        yout2 = P.sb("l1_yout2", [128, HT], F32)
        ybf = P.sb("l1_ybf", [128, HT], BF16)
        ybf2 = P.sb("l1_ybf2", [128, HT], BF16)

        P.wait(nc.sync, "cT", P.cnt["cT"])
        for kk in range(22):
            P.dma_tnb(nc.sync, "dS", wdq_sb[:, kk * D:(kk + 1) * D],
                      agw["d_out"].ap()[:, kk * 128:(kk + 1) * 128])
        wB_done = P.dbar(nc.sync, "dS")
        P.wait(nc.sync, "dG", hq_all)
        res_busy = [0, 0]
        yst = [0, 0]
        hql_free = 0
        for tt in range(TOK // HT):
            P.wait(nc.sync, "cT", hql_free)
            for half in range(2):
                P.dma_nb(nc.sync, "dS", hql[:, half * 2816:(half + 1) * 2816],
                         hq_d.ap()[tt * HT + half * 128: tt * HT + (half + 1) * 128, :])
            hl = P.dbar(nc.sync, "dS")
            P.wait(nc.tensor, "dS", hl)
            P.wait(nc.tensor, "cT", P.cnt["cT"])
            for kk in range(22):
                for half in range(2):
                    tb = tps[(kk * 2 + half) % 2]
                    P.wait(nc.tensor, "cS", tp_free[(kk * 2 + half) % 2])
                    tpc = P.inc(nc.tensor.transpose(
                        tb[:, :], hql[:, half * 2816 + kk * 128: half * 2816 + (kk + 1) * 128],
                        aux["ident"][:, :]), "cT")
                    P.wait(nc.scalar, "cT", tpc)
                    tp_free[(kk * 2 + half) % 2] = P.inc(nc.scalar.activation(
                        hqT[:, kk * HT + half * 128: kk * HT + (half + 1) * 128],
                        tb[:, :], AF.Copy), "cS")
            hqT_ready = P.cnt["cS"]
            hql_free = P.cnt["cT"]
            P.wait(nc.tensor, "cS", hqT_ready)
            P.wait(nc.tensor, "dS", wB_done)
            for m in range(KC):
                it = tt * KC + m
                par = it % 2
                bi = 4 + (it % 2)
                bank = pb[bi]
                rt = res_t if par == 0 else res2
                yo = yout if par == 0 else yout2
                yb = ybf if par == 0 else ybf2
                P.wait(nc.gpsimd, "cV", res_busy[par])
                rld = P.dma(nc.gpsimd, "dG", rt[:, 0:HT],
                            xin_d[m * 128:(m + 1) * 128, tt * HT:(tt + 1) * HT])
                P.wait(nc.tensor, "cV", pclean_v[bi])
                for kk in range(22):
                    mm = P.inc(nc.tensor.matmul(bank[:, 0:HT],
                                    wdq_sb[:, kk * D + m * 128: kk * D + (m + 1) * 128],
                                    hqT[:, kk * HT:(kk + 1) * HT],
                                    start=(kk == 0), stop=(kk == 21)), "cT")
                P.wait(nc.vector, "cT", mm)
                P.wait(nc.vector, "dG", rld)
                P.wait(nc.vector, "dS", yst[par])
                cv = P.inc(nc.vector.scalar_tensor_tensor(
                    yo[:, 0:HT], bank[:, 0:HT], 0.5, rt[:, 0:HT],
                    ALU.mult, ALU.add), "cV")
                pclean_v[bi] = cv
                res_busy[par] = cv
                P.wait(nc.sync, "cV", cv)
                yst[par] = P.dma(nc.sync, "dS",
                                 yout_d[m * 128:(m + 1) * 128, tt * HT:(tt + 1) * HT],
                                 yo[:, 0:HT])
                if rbf_out is not None:
                    P.wait(nc.vector, "cV", cv)
                    cb = P.inc(nc.vector.tensor_copy(yb[:, 0:HT], yo[:, 0:HT]), "cV")
                    res_busy[par] = cb
                    P.wait(nc.sync, "cV", cb)
                    yst[par] = P.dma(nc.sync, "dS",
                                     rbf_out[m * 128:(m + 1) * 128, tt * HT:(tt + 1) * HT],
                                     yb[:, 0:HT])
    P.barrier()


def emit_l1attn_f(P, aux, xin_d, xkv_d, agh, halo_idx_d, hmask_d, Wd, bias0_d,
                  biasn_d, yout_d, rbf_out):
    nc = P.nc
    from concourse.bass import IndirectOffsetOnAxis
    with P.scope():
        xhat = P.sb("w_xhat", [128, KC * NTK], BF16)
        attnT = xhat[:, 0:KC * TOK]
        pr_bf = xhat[:, KC * TOK:KC * TOK + 2048]
        prT = xhat[:, KC * TOK + 2048:KC * TOK + 4096]
        qnT = P.sb("w_qnT", [128, KC * TOK], BF16)
        knT = P.sb("w_knT", [128, KC * NTK], BF16)
        v_sb = P.sb("w_v", [128, (NTK // 128) * D], BF16)
        w_sb = P.sb("w_w", [128, KC * D], BF16)
        bias_sb = P.sb("w_bias", [128, 2 * 8 * WIN], BF16)
        t1 = P.sb("w_t1", [128, 512], F32)
        rowr = P.sb("w_rowr", [128, 1], F32)
        rows_sb = P.sb("w_rows", [128, 1], F32)
        nw_sb = P.sb("w_nw", [128, 2 * KC], F32)
        qsc = P.sb("w_qsc", [128, 512], BF16)
        halo_sb = P.sb("w_halo", [128, 64], F32)
        halo2_sb = P.sb("w_halo2", [128, 64], F32)
        hzero_sb = P.sb("w_hzero", [128, 64], F32)
        hmask_sb = P.sb("w_hmask", [128, 64], mybir.dt.int32)
        hidx_sb = P.sb("w_hidx", [128, KC], mybir.dt.int32)
        ybf = P.sb("w_ybf", [128, 512], BF16)
        ps = [P.psum(f"w_ps{i}", [128, 512], F32) for i in range(4)]
        pav = [P.psum(f"w_pav{i}", [128, 512], F32) for i in range(2)]
        tps = [P.psum("w_tps0", [128, 128], BF16), P.psum("w_tps1", [128, 128], BF16)]
        aux["pA"] = pav[0]; aux["pB"] = pav[1]
        aux["pS"] = ps[0][0:1, :]; aux["pQ"] = ps[1][0:1, :]
        aux.pop("stats_mm", None); aux.pop("rows_read", None)
        rr_row = aux["mrow"]

        hsrc = bass.AP(xin_d.tensor if hasattr(xin_d, "tensor") else xin_d, 496,
                       [[TOK, D], [512, 4], [1, 16]])
        P.dma(nc.sync, "dS", agh["in"].ap(), hsrc)
        P.wait(nc.gpsimd, "dS", P.cnt["dS"])
        P.inc(nc.gpsimd.collective_compute(
            "AllGather", ALU.bypass, replica_groups=[list(range(NCORE))],
            ins=[agh["in"].ap().opt()], outs=[agh["out"].ap().opt()]), "cC")
        for s in range(NSEQ):
            P.dma_nb(nc.sync, "dS",
                     xkv_d.ap()[:, s * SEQT + 16: s * SEQT + 528],
                     xin_d[:, s * 512:(s + 1) * 512])
        P.wait(nc.vector, "cV", P.cnt["cV"])
        zz = P.inc(nc.vector.memset(t1[:, 0:448], 0.0), "cV")
        P.wait(nc.sync, "cV", zz)
        for kc in range(KC):
            dstp = bass.AP(xkv_d, kc * 128 * NTK + 528, [[NTK, 128], [SEQT, 4], [1, 112]])
            P.dma_nb(nc.sync, "dS", dstp, t1[:, 0:448].rearrange("p (s w) -> p s w", w=112))
        pads_done = P.dbar(nc.sync, "dS")
        hld = P.dma(nc.sync, "dS", hidx_sb[:, :], halo_idx_d)
        hld = P.dma(nc.sync, "dS", hmask_sb[:, :], hmask_d)
        P.wait(nc.vector, "dS", hld)
        P.wait(nc.vector, "cV", P.cnt["cV"])
        P.inc(nc.vector.memset(halo_sb[:], 0.0), "cV")
        P.wait(nc.vector, "cV", P.cnt["cV"])
        hz = P.inc(nc.vector.memset(hzero_sb[:], 0.0), "cV")
        P.wait(nc.gpsimd, "cV", hz)
        P.wait(nc.gpsimd, "cC", P.cnt["cC"])
        for kc in range(KC):
            g = P.inc(nc.gpsimd.indirect_dma_start(
                halo_sb[:, :], None, agh["out"].ap(),
                IndirectOffsetOnAxis(hidx_sb[:, kc:kc + 1], 0)), "dG")
            nc.gpsimd.wait_ge(P.sem("dG"), g)
            P.wait(nc.vector, "dG", g)
            P.wait(nc.vector, "cV", P.cnt["cV"])
            sel = P.inc(nc.vector.select(halo2_sb[:], hmask_sb[:], halo_sb[:],
                                         hzero_sb[:]), "cV")
            P.wait(nc.gpsimd, "cV", sel)
            dsth = bass.AP(xkv_d, kc * 128 * NTK, [[NTK, 128], [SEQT, 4], [1, 16]])
            P.dma(nc.gpsimd, "dG", dsth, halo2_sb[:].rearrange("p (s w) -> p s w", w=16))
        halo_done = P.cnt["dG"]
        P.wait(nc.sync, "dG", halo_done)

        P.dma_nb(nc.sync, "dS", bias_sb[:, 0:8 * WIN], bias0_d)
        P.dma_nb(nc.sync, "dS", bias_sb[:, 8 * WIN:16 * WIN], biasn_d)
        for j in range(KC):
            P.dma_nb(nc.sync, "dS", nw_sb[:, j:j + 1], Wd["qnw"][j * 128:(j + 1) * 128].unsqueeze(-1))
            P.dma_nb(nc.sync, "dS", nw_sb[:, KC + j:KC + j + 1], Wd["knw"][j * 128:(j + 1) * 128].unsqueeze(-1))
        const_ld = P.dbar(nc.sync, "dS")

        ln_scope = P.scope()
        ln_scope.__enter__()
        aux["xf32_t"] = P.sb("w_lnx", [128, KC * 512], F32)
        emit_layernorm(P, xkv_d.ap(), xhat, aux, ntok=NTK)
        ln_scope.__exit__(None, None, None)

        own_cols = [(s * SEQT + 16, 512) for s in range(NSEQ)]

        def proj(dst, w_dram, cols, rms_w_col, ncol_dst):
            P.wait(nc.gpsimd, "cT", P.cnt["cT"])
            for kc_ in range(KC):
                P.dma_nb(nc.gpsimd, "dG", w_sb[:, kc_ * D:(kc_ + 1) * D],
                         w_dram[kc_ * 128:(kc_ + 1) * 128, :])
            wd = P.dbar(nc.gpsimd, "dG")
            P.wait(nc.tensor, "dG", wd)
            pj_free = getattr(P, "_pj_free", [0, 0])
            for (c0, n, d0) in cols:
                for off in range(0, n, 512):
                    ns = min(512, n - off)
                    for m in range(KC):
                        bank = ps[2 + (m % 2)]
                        P.wait(nc.tensor, "cS", pj_free[m % 2])
                        for kc_ in range(KC):
                            mm = P.inc(nc.tensor.matmul(
                                bank[:, 0:ns],
                                w_sb[:, kc_ * D + m * 128:(kc_ * D + (m + 1) * 128)],
                                xhat[:, kc_ * NTK + c0 + off: kc_ * NTK + c0 + off + ns],
                                start=(kc_ == 0), stop=(kc_ == KC - 1)), "cT")
                        P.wait(nc.scalar, "cT", mm)
                        pj_free[m % 2] = P.inc(nc.scalar.activation(
                            dst[:, m * ncol_dst + d0 + off: m * ncol_dst + d0 + off + ns],
                            bank[:, 0:ns], AF.Copy), "cS")
                    P._pj_free = pj_free
                    if rms_w_col is not None:
                        evac = P.cnt["cS"]
                        for m in range(KC):
                            P.wait(nc.scalar, "cS", evac)
                            P.wait(nc.scalar, "cT", P.cnt["cT"])
                            sqd = P.inc(nc.scalar.activation(
                                qsc[:, 0:ns],
                                dst[:, m * ncol_dst + d0 + off: m * ncol_dst + d0 + off + ns],
                                AF.Square), "cS")
                            P.wait(nc.tensor, "cS", sqd)
                            P.wait(nc.tensor, "cV", P.cnt["cV"])
                            smm = P.inc(nc.tensor.matmul(ps[0][0:1, 0:ns], aux["ones_bf"][:, :],
                                                         qsc[:, 0:ns],
                                                         start=(m == 0), stop=(m == KC - 1)), "cT")
                        P.wait(nc.vector, "cT", smm)
                        P.inc(nc.vector.tensor_scalar(rr_row[:, 0:ns], ps[0][0:1, 0:ns],
                                                      1.0 / D, None, ALU.mult), "cV")
                        P.wait(nc.scalar, "cV", P.cnt["cV"])
                        P.inc(nc.scalar.activation(rr_row[:, 0:ns], rr_row[:, 0:ns],
                                                   AF.Ln, bias=np.float32(1e-6)), "cS")
                        P.wait(nc.scalar, "cS", P.cnt["cS"])
                        sdd = P.inc(nc.scalar.activation(rr_row[:, 0:ns], rr_row[:, 0:ns],
                                                         AF.Exp, scale=np.float32(-0.5)), "cS")
                        rrd = P.cnt["cV"]
                        P.wait(nc.tensor, "cV", rrd)
                        P.wait(nc.tensor, "cS", sdd)
                        bc = P.inc(nc.tensor.matmul(pav[0][:, 0:ns], aux["ones_f"][:, :],
                                                    rr_row[:, 0:ns], start=True, stop=True), "cT")
                        P.wait(nc.vector, "cT", bc)
                        for m in range(KC):
                            sl = dst[:, m * ncol_dst + d0 + off: m * ncol_dst + d0 + off + ns]
                            P.wait(nc.vector, "cV", P.cnt["cV"])
                            P.inc(nc.vector.tensor_tensor(t1[:, 0:ns], sl, pav[0][:, 0:ns],
                                                          ALU.mult), "cV")
                            P.wait(nc.vector, "cV", P.cnt["cV"])
                            P.inc(nc.vector.tensor_scalar(sl, t1[:, 0:ns],
                                                          nw_sb[:, rms_w_col + m:rms_w_col + m + 1],
                                                          None, ALU.mult), "cV")

        P.wait(nc.vector, "dS", const_ld)
        proj(knT, Wd["wk"], [(0, NTK, 0)], KC, NTK)
        proj(qnT, Wd["wq"], [(c, n, s * 512) for s, (c, n) in enumerate(own_cols)], 0, TOK)
        P.wait(nc.gpsimd, "cT", P.cnt["cT"])
        for kc_ in range(KC):
            P.dma_nb(nc.gpsimd, "dG", w_sb[:, kc_ * D:(kc_ + 1) * D],
                     Wd["wv"][kc_ * 128:(kc_ + 1) * 128, :])
        wd = P.dbar(nc.gpsimd, "dG")
        P.wait(nc.tensor, "dG", wd)
        vp_free = [0, 0]
        for ch in range(NTK // 128):
            for half in range(2):
                bank = ps[2 + half]
                P.wait(nc.tensor, "cS", vp_free[half])
                for kc_ in range(KC):
                    mm = P.inc(nc.tensor.matmul(
                        bank[:, 0:512],
                        xhat[:, kc_ * NTK + ch * 128: kc_ * NTK + (ch + 1) * 128],
                        w_sb[:, kc_ * D + half * 512: kc_ * D + half * 512 + 512],
                        start=(kc_ == 0), stop=(kc_ == KC - 1)), "cT")
                P.wait(nc.scalar, "cT", mm)
                vp_free[half] = P.inc(nc.scalar.activation(
                    v_sb[:, ch * D + half * 512: ch * D + half * 512 + 512],
                    bank[:, 0:512], AF.Copy), "cS")
        v_done = P.cnt["cS"]

        P.wait(nc.tensor, "cS", v_done)
        P.wait(nc.tensor, "cV", aux["init_cv"])
        INV = float(1.0 / np.sqrt(128.0))
        t1b = P.sb("w_t1b", [128, 512], F32)
        rsum8 = P.sb("w_rsum8", [128, 8], F32)
        rr8 = P.sb("w_rr8", [128, 8], F32)
        s_clean = [0] * 4
        t1_free = [0, 0]
        pr_free = 0
        tp_copy = [0, 0]
        prT_free = 0
        av_copy = [0, 0]
        for s in range(NSEQ):
            for qt in range(4):
                qcol = s * 512 + qt * 128
                kvcol = s * SEQT + qt * 128
                bvar = 0 if qt == 0 else 8 * WIN
                sc_mm = [0] * 4
                for h in range(8):
                    bank = ps[h // 2]
                    bo_ = (h % 2) * WIN
                    if h % 2 == 0:
                        P.wait(nc.tensor, "cV", s_clean[h // 2])
                    sc_mm[h // 2] = P.inc(nc.tensor.matmul(
                        bank[:, bo_:bo_ + WIN],
                        qnT[:, h * TOK + qcol: h * TOK + qcol + 128],
                        knT[:, h * NTK + kvcol: h * NTK + kvcol + WIN],
                        start=True, stop=True), "cT")
                for b in range(4):
                    tb1 = t1 if b % 2 == 0 else t1b
                    P.wait(nc.vector, "cT", sc_mm[b])
                    P.wait(nc.vector, "cS", t1_free[b % 2])
                    sc = P.inc(nc.vector.scalar_tensor_tensor(
                        tb1[:, 0:512], ps[b][:, 0:512], INV,
                        bias_sb[:, bvar + 2 * b * WIN: bvar + (2 * b + 2) * WIN],
                        ALU.mult, ALU.add), "cV")
                    s_clean[b] = sc
                    P.wait(nc.scalar, "cV", sc)
                    if b == 0:
                        P.wait(nc.scalar, "cT", pr_free)
                    ex = P.inc(nc.scalar.activation(pr_bf[:, 2 * b * WIN:(2 * b + 2) * WIN],
                                                    tb1[:, 0:512], AF.Exp), "cS")
                    t1_free[b % 2] = ex
                    P.wait(nc.vector, "cS", ex)
                    P.inc(nc.vector.tensor_reduce(
                        rsum8[:, 2 * b:2 * b + 2],
                        pr_bf[:, 2 * b * WIN:(2 * b + 2) * WIN].rearrange(
                            "p (h w) -> p h w", w=WIN),
                        AX.X, ALU.add), "cV")
                P.wait(nc.vector, "cV", P.cnt["cV"])
                P.inc(nc.vector.reciprocal(rr8[:, :], rsum8[:, :]), "cV")
                P.wait(nc.vector, "cV", P.cnt["cV"])
                for b in range(4):
                    P.inc(nc.vector.tensor_tensor(
                        pr_bf[:, 2 * b * WIN:(2 * b + 2) * WIN].rearrange(
                            "p (h w) -> p h w", w=WIN),
                        pr_bf[:, 2 * b * WIN:(2 * b + 2) * WIN].rearrange(
                            "p (h w) -> p h w", w=WIN),
                        rr8[:, 2 * b:2 * b + 2].unsqueeze(-1).broadcast_to([128, 2, WIN]),
                        ALU.mult), "cV")
                norm_done = P.cnt["cV"]
                P.wait(nc.tensor, "cV", norm_done)
                for b in range(16):
                    tb = tps[b % 2]
                    P.wait(nc.tensor, "cS", tp_copy[b % 2])
                    if b == 0:
                        P.wait(nc.tensor, "cT", prT_free)
                    tp = P.inc(nc.tensor.transpose(tb[:, :], pr_bf[:, b * 128:(b + 1) * 128],
                                                   aux["ident"][:, :]), "cT")
                    P.wait(nc.scalar, "cT", tp)
                    tp_copy[b % 2] = P.inc(nc.scalar.activation(
                        prT[:, b * 128:(b + 1) * 128], tb[:, :], AF.Copy), "cS")
                pr_free = P.cnt["cT"]
                prT_done = P.cnt["cS"]
                P.wait(nc.tensor, "cS", prT_done)
                for h in range(8):
                    bank = pav[h % 2]
                    c0 = 5 * s + qt
                    P.wait(nc.tensor, "cS", av_copy[h % 2])
                    nc.tensor.matmul(
                        bank[:, 0:128],
                        v_sb[:, c0 * D + h * 128:(c0 * D + (h + 1) * 128)],
                        prT[:, h * 256:h * 256 + 128],
                        start=True, stop=False)
                    av = P.inc(nc.tensor.matmul(
                        bank[:, 0:128],
                        v_sb[0:16, (c0 + 1) * D + h * 128:((c0 + 1) * D + (h + 1) * 128)],
                        prT[0:16, h * 256 + 128:h * 256 + 256],
                        start=False, stop=True), "cT")
                    P.wait(nc.scalar, "cT", av)
                    av_copy[h % 2] = P.inc(nc.scalar.activation(
                        attnT[:, h * TOK + qcol: h * TOK + qcol + 128],
                        bank[:, 0:128], AF.Copy), "cS")
                prT_free = P.cnt["cT"]
        attn_done_v = P.cnt["cV"]
        attn_done_s = P.cnt["cS"]

        P.wait(nc.gpsimd, "cT", P.cnt["cT"])
        for kc_ in range(KC):
            P.dma_nb(nc.gpsimd, "dG", w_sb[:, kc_ * D:(kc_ + 1) * D],
                     Wd["wo"][kc_ * 128:(kc_ + 1) * 128, :])
        wd = P.dbar(nc.gpsimd, "dG")
        P.wait(nc.tensor, "dG", wd)
        P.wait(nc.tensor, "cS", attn_done_s)
        res_busy = P.cnt["cV"]
        yst = 0
        for s in range(NSEQ):
            for m in range(KC):
                bank = ps[m % 4]
                P.wait(nc.tensor, "cV", P.cnt["cV"])
                for kc_ in range(KC):
                    mm = P.inc(nc.tensor.matmul(
                        bank[:, 0:512],
                        w_sb[:, kc_ * D + m * 128: kc_ * D + (m + 1) * 128],
                        attnT[:, kc_ * TOK + s * 512:(kc_ * TOK + (s + 1) * 512)],
                        start=(kc_ == 0), stop=(kc_ == KC - 1)), "cT")
                P.wait(nc.sync, "cV", res_busy)
                rld = P.dma(nc.sync, "dS", t1[:, 0:512],
                            xin_d[m * 128:(m + 1) * 128, s * 512:(s + 1) * 512])
                P.wait(nc.vector, "cT", mm)
                P.wait(nc.vector, "dS", rld)
                P.wait(nc.vector, "dS", yst)
                cv = P.inc(nc.vector.tensor_tensor(t1[:, 0:512],
                                                   bank[:, 0:512], t1[:, 0:512], ALU.add), "cV")
                res_busy = cv
                P.wait(nc.sync, "cV", cv)
                yst = P.dma(nc.sync, "dS",
                            yout_d[m * 128:(m + 1) * 128, s * 512:(s + 1) * 512], t1[:, 0:512])
                P.wait(nc.vector, "cV", cv)
                P.wait(nc.vector, "dS", P.cnt["dS"])
                cb = P.inc(nc.vector.tensor_copy(ybf[:], t1[:, 0:512]), "cV")
                res_busy = cb
                P.wait(nc.sync, "cV", cb)
                yst = P.dma(nc.sync, "dS",
                            rbf_out[m * 128:(m + 1) * 128, s * 512:(s + 1) * 512], ybf[:])
    P.barrier()


def build_fused(P, dbg=False):
    nc = P.nc
    xT = nc.dram_tensor("xT", [D, TOK], F32, kind="ExternalInput")
    l0 = {
        "wg": nc.dram_tensor("l0_wg", [D, 3 * D], F32, kind="ExternalInput").ap(),
        "ug": nc.dram_tensor("l0_ug", [D, 3 * D], F32, kind="ExternalInput").ap(),
        "wo": nc.dram_tensor("l0_wo", [D, D], F32, kind="ExternalInput").ap(),
        "uo": nc.dram_tensor("l0_uo", [D, D], F32, kind="ExternalInput").ap(),
        "bg": nc.dram_tensor("l0_bg", [3 * D], F32, kind="ExternalInput").ap(),
        "bo": nc.dram_tensor("l0_bo", [D], F32, kind="ExternalInput").ap(),
        "inc_": nc.dram_tensor("l0_inc", [D], F32, kind="ExternalInput").ap(),
    }
    W1qk_d = nc.dram_tensor("W1qk", [D, 2 * D], F32, kind="ExternalInput").ap()
    b1qk_d = nc.dram_tensor("b1qk", [2 * D], F32, kind="ExternalInput").ap()
    W1v_d = nc.dram_tensor("W1v", [D, D], F32, kind="ExternalInput").ap()
    b1v_d = nc.dram_tensor("b1v", [D], F32, kind="ExternalInput").ap()
    W2_d = nc.dram_tensor("W2", [D, D], F32, kind="ExternalInput").ap()
    b2_d = nc.dram_tensor("b2", [D], F32, kind="ExternalInput").ap()
    wgs = nc.dram_tensor("wgs", [NWQ, D], F32, kind="ExternalInput").ap()
    wus = nc.dram_tensor("wus", [NWQ, D], F32, kind="ExternalInput").ap()
    wds = nc.dram_tensor("wds", [NDQ, 2816], F32, kind="ExternalInput").ap()
    wat = {
        "wq": nc.dram_tensor("wq", [D, D], F32, kind="ExternalInput").ap(),
        "wk": nc.dram_tensor("wk", [D, D], F32, kind="ExternalInput").ap(),
        "wv": nc.dram_tensor("wv", [D, D], F32, kind="ExternalInput").ap(),
        "wo": nc.dram_tensor("wo1", [D, D], F32, kind="ExternalInput").ap(),
        "qnw": nc.dram_tensor("qnw", [D], F32, kind="ExternalInput").ap(),
        "knw": nc.dram_tensor("knw", [D], F32, kind="ExternalInput").ap(),
    }
    bias0_d = nc.dram_tensor("bias0", [128, 8 * WIN], BF16, kind="ExternalInput").ap()
    biasn_d = nc.dram_tensor("biasn", [128, 8 * WIN], BF16, kind="ExternalInput").ap()
    halo_idx_d = nc.dram_tensor("halo_idx", [128, KC], mybir.dt.int32, kind="ExternalInput").ap()
    hmask_d = nc.dram_tensor("hmask", [128, 64], mybir.dt.int32, kind="ExternalInput").ap()
    yT = nc.dram_tensor("yT", [D, TOK], F32, kind="ExternalOutput")

    rd = [nc.dram_tensor(f"rd{i}", [D, TOK], F32) for i in range(2)]
    rbf = nc.dram_tensor("rbf", [D, TOK], BF16)
    xq_d = nc.dram_tensor("xq_d", [TOK, D], BF16)
    hq_d = nc.dram_tensor("hq_d", [TOK, 2816], BF16)
    agw = {
        "g_in": nc.dram_tensor("agw_g_in", [NWQ, D], BF16),
        "g_out": nc.dram_tensor("agw_g_out", [2816, D], BF16, addr_space="Shared"),
        "u_in": nc.dram_tensor("agw_u_in", [NWQ, D], BF16),
        "u_out": nc.dram_tensor("agw_u_out", [2816, D], BF16, addr_space="Shared"),
        "d_in": nc.dram_tensor("agw_d_in", [NDQ, 2816], BF16),
        "d_out": nc.dram_tensor("agw_d_out", [D, 2816], BF16, addr_space="Shared"),
    }
    a2aqk = {"in": nc.dram_tensor("a2aqk_in", [TOK, TOK], BF16),
             "out": nc.dram_tensor("a2aqk_out", [TOK, TOK], BF16)}
    a2av = {"in": nc.dram_tensor("a2av_in", [NCORE * TOK, 128], BF16),
            "out": nc.dram_tensor("a2av_out", [NCORE * TOK, 128], BF16)}
    a2aat_in = nc.dram_tensor("a2aat_in", [NCORE * TOK, 128], BF16)
    a2aat_out = nc.dram_tensor("a2aat_out", [NCORE * TOK, 128], BF16)
    agh = {"in": nc.dram_tensor("agh_in", [D, 64], F32),
           "out": nc.dram_tensor("agh_out", [NCORE * D, 64], F32, addr_space="Shared")}
    xkv_d = nc.dram_tensor("xkv_d", [D, NTK], F32)

    P.block()
    aux = fused_aux(P)

    def snap(name, src_t):
        if not dbg:
            return
        dt_ = nc.dram_tensor(name, list(src_t.shape), src_t.dtype, kind="ExternalOutput")
        P.barrier()
        P.dma(nc.sync, "dS", dt_.ap(), src_t.ap())
        P.barrier()

    emit_fused_init(P, aux)
    emit_prep_f(P, aux, wgs, wus, wds, agw)
    emit_l0ffn_f(P, aux, xT, rd[0], l0)
    snap("s_ffn0", rd[0])
    emit_l0qkv_f(P, aux, rd[0], W1qk_d, b1qk_d, W1v_d, b1v_d, a2aqk, a2av)
    snap("s_qk", a2aqk["out"])
    snap("s_v", a2av["out"])
    emit_l0attn_f(P, aux, a2aqk["out"], a2av["out"], a2aat_in, a2aat_out)
    snap("s_at", a2aat_out)
    emit_l0proj_f(P, aux, rd[0], a2aat_out, W2_d, b2_d, rd[1])
    snap("s_proj", rd[1])
    emit_l0ffn_f(P, aux, rd[1], rd[0], l0, rbf_d=rbf)
    snap("s_ffn1", rd[0])
    emit_l1ffn_f(P, aux, rd[0], rbf, agw, rd[1], xq_d, hq_d)
    snap("s_l1f0", rd[1])
    snap("s_agwg", agw["g_out"])
    emit_l1attn_f(P, aux, rd[1], xkv_d, agh, halo_idx_d, hmask_d, wat, bias0_d,
                  biasn_d, rd[0], rbf)
    snap("s_l1at", rd[0])
    snap("s_xkv", xkv_d)
    emit_l1ffn_f(P, aux, rd[0], rbf, agw, yT, xq_d, hq_d)
    return P


def kernel(**inputs, ):
    return _kernel_impl(False, inputs)


def _kernel_impl(dbg, inputs):
    from concourse.aot_env import allow_ici_all_to_all
    x = np.asarray(inputs["L0_attn_x"], dtype=np.float32)
    W0 = {k: np.asarray(inputs[f"L0_ffn_{k}"], dtype=np.float32)
          for k in ["wf", "uf", "bf", "wi", "ui", "bi", "wc", "uc", "bc", "wo", "uo", "bo", "in_c"]}
    W1a = np.asarray(inputs["L0_attn_W1"], dtype=np.float32)
    b1a = np.asarray(inputs["L0_attn_b1"], dtype=np.float32)
    W2a = np.asarray(inputs["L0_attn_W2"], dtype=np.float32)
    b2a = np.asarray(inputs["L0_attn_b2"], dtype=np.float32)
    W1f = {k: np.asarray(inputs[f"L1_ffn_{k}"], dtype=np.float32)
           for k in ["w_gate", "w_up", "w_down"]}
    W1t = {k: np.asarray(inputs[f"L1_attn_{k}"], dtype=np.float32)
           for k in ["wq", "wk", "wv", "wo", "q_norm_w", "k_norm_w"]}

    with allow_ici_all_to_all():
        P = Prog()
        build_fused(P, dbg=dbg)
        P.close()

    w3 = W1a.reshape(1024, 16, 3, 64)
    W1qk = np.ascontiguousarray(
        np.concatenate([w3[:, :, 0, :].reshape(1024, 1024),
                        w3[:, :, 1, :].reshape(1024, 1024)], axis=1))
    W1v = np.ascontiguousarray(w3[:, :, 2, :].reshape(1024, 1024))
    b3 = b1a.reshape(16, 3, 64)
    b1qk = np.ascontiguousarray(
        np.concatenate([b3[:, 0, :].reshape(-1), b3[:, 1, :].reshape(-1)]))
    b1v = np.ascontiguousarray(b3[:, 2, :].reshape(-1))

    shards = shard_xT(x)
    wg = np.ascontiguousarray(np.concatenate([W0["wf"].T, W0["wi"].T, W0["wc"].T], axis=1))
    ug = np.ascontiguousarray(np.concatenate([W0["uf"].T, W0["ui"].T, W0["uc"].T], axis=1))
    common = {
        "l0_wg": wg, "l0_ug": ug,
        "l0_wo": np.ascontiguousarray(W0["wo"].T), "l0_uo": np.ascontiguousarray(W0["uo"].T),
        "l0_bg": np.ascontiguousarray(np.concatenate([W0["bf"], W0["bi"], W0["bc"]])),
        "l0_bo": np.ascontiguousarray(W0["bo"]), "l0_inc": np.ascontiguousarray(W0["in_c"]),
        "W1qk": W1qk, "b1qk": b1qk, "W1v": W1v, "b1v": b1v, "W2": W2a, "b2": b2a,
        "wq": np.ascontiguousarray(W1t["wq"].T), "wk": np.ascontiguousarray(W1t["wk"].T),
        "wv": np.ascontiguousarray(W1t["wv"].T), "wo1": np.ascontiguousarray(W1t["wo"].T),
        "qnw": np.ascontiguousarray(W1t["q_norm_w"]), "knw": np.ascontiguousarray(W1t["k_norm_w"]),
    }
    in_maps = []
    for c in range(NCORE):
        b0, bn = make_l1attn_bias(c)
        if c > 0:
            hidx = ((c - 1) * D + np.arange(KC)[None, :] * 128
                    + np.arange(128)[:, None]).astype(np.int32)
        else:
            hidx = np.zeros((128, KC), dtype=np.int32)
        in_maps.append({
            **common,
            "xT": shards[c],
            "wgs": np.ascontiguousarray(W1f["w_gate"][c * NWQ:(c + 1) * NWQ]),
            "wus": np.ascontiguousarray(W1f["w_up"][c * NWQ:(c + 1) * NWQ]),
            "wds": np.ascontiguousarray(W1f["w_down"][c * NDQ:(c + 1) * NDQ]),
            "bias0": b0, "biasn": bn,
            "halo_idx": np.ascontiguousarray(hidx),
            "hmask": np.full((128, 64), 0 if c == 0 else 1, dtype=np.int32),
        })
    with allow_ici_all_to_all():
        res = run_bass_kernel_spmd(P.nc, in_maps, core_ids=list(range(NCORE)))
    if dbg:
        return res.results
    return unshard_xT([r["yT"] for r in res.results])


# version 18 (final)
# speedup vs baseline: 1.0364x; 1.0214x over previous
import numpy as np
import concourse.bass as bass
import concourse.mybir as mybir
from concourse.bass_utils import run_bass_kernel_spmd

F32 = mybir.dt.float32
BF16 = mybir.dt.bfloat16
AF = mybir.ActivationFunctionType
ALU = mybir.AluOpType
AX = mybir.AxisListType

D = 1024
KC = 8
NCORE = 8
TOK = 2048
NSEQ = 4
POS = 512


class Prog:

    def __init__(self):
        self.nc = bass.Bass("TRN2", target_bir_lowering=False)
        self._ctxs = []
        self.sems = {}
        self.cnt = {}

    def sem(self, name):
        if name not in self.sems:
            self.sems[name] = self._enter(self.nc.semaphore(name))
            self.cnt[name] = 0
        return self.sems[name]

    def _enter(self, cm):
        v = cm.__enter__()
        self._ctxs.append(cm)
        return v

    def _uname(self, name):
        if not hasattr(self, "_names"):
            self._names = {}
        n = self._names.get(name, 0)
        self._names[name] = n + 1
        return name if n == 0 else f"{name}__{n}"

    def sb(self, name, shape, dt):
        return self._enter(self.nc.sbuf_tensor(self._uname(name), shape, dt))

    def psum(self, name, shape, dt=F32):
        return self._enter(self.nc.psum_tensor(self._uname(name), shape, dt))

    def inc(self, instr, name, n=None):
        if n is None:
            n = 16 if name.startswith("d") else 1
        instr.then_inc(self.sem(name), n)
        self.cnt[name] += n
        return self.cnt[name]

    def wait(self, eng, name, val):
        if val > 0:
            eng.wait_ge(self.sem(name), val)

    def dma(self, eng, name, out, in_):
        c = self.inc(eng.dma_start(out, in_), name)
        eng.wait_ge(self.sem(name), c)
        return c

    def dma_t(self, eng, name, out, in_):
        c = self.inc(eng.dma_start_transpose(out, in_), name)
        eng.wait_ge(self.sem(name), c)
        return c

    def dma_nb(self, eng, name, out, in_):
        return self.inc(eng.dma_start(out, in_), name)

    def dma_tnb(self, eng, name, out, in_):
        return self.inc(eng.dma_start_transpose(out, in_), name)

    def dbar(self, eng, name):
        eng.wait_ge(self.sem(name), self.cnt[name])
        return self.cnt[name]

    def block(self):
        for s in ("dS", "dG", "dA", "cT", "cV", "cS", "cG", "cC"):
            self.sem(s)
        return self._enter(self.nc.Block())

    def scope(self):
        P = self

        class _Scope:
            def __enter__(s):
                s.mark = len(P._ctxs)
                return s

            def __exit__(s, *a):
                for cm in reversed(P._ctxs[s.mark:]):
                    cm.__exit__(None, None, None)
                del P._ctxs[s.mark:]

        return _Scope()

    def barrier(self):
        nc = self.nc
        engs = [nc.tensor, nc.vector, nc.scalar, nc.gpsimd, nc.sync]
        snap = [(n, self.cnt[n]) for n in ("cT", "cV", "cS", "cG", "cC", "dS", "dG")]
        for e in engs:
            for n, v in snap:
                if v > 0:
                    e.wait_ge(self.sem(n), v)

    def close(self):
        for cm in reversed(self._ctxs):
            cm.__exit__(None, None, None)
        self._ctxs = []


def ln_aux(P, ntok=TOK, psums=None):
    if psums is None:
        psums = {
            "pS": P.psum("ln_pS", [1, 512], F32),
            "pQ": P.psum("ln_pQ", [1, 512], F32),
            "pA": P.psum("ln_pA", [128, 512], F32),
            "pB": P.psum("ln_pB", [128, 512], F32),
        }
    return {
        **psums,
        "ones_bf": P.sb("ln_ones_bf", [128, 1], BF16),
        "ones_f": P.sb("ln_ones_f", [1, 128], F32),
        "xf32_t": P.sb("ln_xf32_t", [128, KC * 512], F32),
        "xbf": P.sb("ln_xbf", [128, 512], BF16),
        "sq": P.sb("ln_sq", [128, 512], BF16),
        "arow": P.sb("ln_arow", [1, 512], F32),
        "brow": P.sb("ln_brow", [1, 512], F32),
        "mrow": P.sb("ln_mrow", [1, 512], F32),
        "init": False,
    }


def emit_ln_init(P, aux):
    nc = P.nc
    P.inc(nc.vector.memset(aux["ones_bf"][:], 1.0), "cV")
    P.inc(nc.vector.memset(aux["ones_f"][:], 1.0), "cV")
    aux["init"] = True


def emit_layernorm(P, xT_dram, xhat_sb, aux, ntok=TOK, eps=1e-5, keep_x=None):
    nc = P.nc
    assert aux["init"]
    ones_bf, ones_f = aux["ones_bf"], aux["ones_f"]
    xt, xbf, sq = aux["xf32_t"], aux["xbf"], aux["sq"]
    pS, pQ, pA, pB = aux["pS"], aux["pQ"], aux["pA"], aux["pB"]
    arow, brow, mrow = aux["arow"], aux["brow"], aux["mrow"]
    vrow = brow

    for tt in range(ntok // 512):
        prev_apply = P.cnt["cV"]
        for kc in range(KC):
            P.wait(nc.sync, "cV", prev_apply)
            P.dma(nc.sync, "dS", xt[:, kc * 512:(kc + 1) * 512],
                xT_dram[kc * 128:(kc + 1) * 128, tt * 512:(tt + 1) * 512])
        ld = P.cnt["dS"]
        prev_rows = aux.get("rows_read", 0)
        for kc in range(KC):
            P.wait(nc.vector, "dS", ld)
            P.wait(nc.vector, "cT", aux.get("stats_mm", 0))
            c1 = P.inc(nc.vector.tensor_copy(xbf[:], xt[:, kc * 512:(kc + 1) * 512]), "cV")
            P.wait(nc.scalar, "dS", ld)
            P.wait(nc.scalar, "cT", aux.get("stats_mm", 0))
            c2 = P.inc(nc.scalar.activation(sq[:], xt[:, kc * 512:(kc + 1) * 512], AF.Square), "cS")
            P.wait(nc.tensor, "cV", c1)
            P.wait(nc.tensor, "cS", c2)
            P.wait(nc.tensor, "cV", prev_rows)
            nc.tensor.matmul(pS[:, :], ones_bf[:, :], xbf[:, :],
                             start=(kc == 0), stop=(kc == KC - 1))
            aux["stats_mm"] = P.inc(nc.tensor.matmul(pQ[:, :], ones_bf[:, :], sq[:, :],
                                                     start=(kc == 0), stop=(kc == KC - 1)), "cT")
        P.wait(nc.vector, "cT", aux["stats_mm"])
        P.inc(nc.vector.tensor_scalar(mrow[:], pS[:, :], 1.0 / D, None, ALU.mult), "cV")
        P.inc(nc.vector.tensor_scalar(vrow[:], pQ[:, :], 1.0 / D, None, ALU.mult), "cV")
        aux["rows_read"] = P.cnt["cV"]
        P.wait(nc.vector, "cV", P.cnt["cV"])
        P.inc(nc.vector.tensor_tensor(arow[:], mrow[:], mrow[:], ALU.mult), "cV")
        P.wait(nc.vector, "cV", P.cnt["cV"])
        P.inc(nc.vector.tensor_tensor(vrow[:], arow[:], vrow[:], ALU.subtract), "cV")
        P.wait(nc.scalar, "cV", P.cnt["cV"])
        P.inc(nc.scalar.activation(vrow[:], vrow[:], AF.Ln, bias=np.float32(eps), scale=np.float32(-1.0)), "cS")
        P.wait(nc.scalar, "cS", P.cnt["cS"])
        sva = P.inc(nc.scalar.activation(arow[:], vrow[:], AF.Exp, scale=np.float32(-0.5)), "cS")
        P.wait(nc.vector, "cS", sva)
        P.inc(nc.vector.scalar_tensor_tensor(brow[:], mrow[:], -1.0, arow[:],
                                             ALU.mult, ALU.mult), "cV")
        rows_done = P.cnt["cV"]
        P.wait(nc.tensor, "cV", rows_done)
        nc.tensor.matmul(pA[:, :], ones_f[:, :], arow[:, :], start=True, stop=True)
        bcast_mm = P.inc(nc.tensor.matmul(pB[:, :], ones_f[:, :], brow[:, :],
                                          start=True, stop=True), "cT")
        P.wait(nc.vector, "cT", bcast_mm)
        for kc in range(KC):
            xsl = xt[:, kc * 512:(kc + 1) * 512]
            osl = xhat_sb[:, kc * ntok + tt * 512: kc * ntok + (tt + 1) * 512]
            P.wait(nc.vector, "cV", P.cnt["cV"])
            P.inc(nc.vector.tensor_tensor(xbf[:], xsl, pA[:, :], ALU.mult), "cV")
            P.wait(nc.vector, "cV", P.cnt["cV"])
            P.inc(nc.vector.tensor_tensor(osl, xbf[:], pB[:, :], ALU.add), "cV")
            if keep_x is not None:
                P.inc(nc.vector.tensor_copy(
                    keep_x[:, kc * ntok + tt * 512: kc * ntok + (tt + 1) * 512], xsl), "cV")
    return P.cnt["cV"]



def build_l0ffn(P):
    nc = P.nc
    xT = nc.dram_tensor("xT", [D, TOK], F32, kind="ExternalInput").ap()
    wg = nc.dram_tensor("wg", [D, 3 * D], F32, kind="ExternalInput").ap()
    ug = nc.dram_tensor("ug", [D, 3 * D], F32, kind="ExternalInput").ap()
    wo = nc.dram_tensor("wo", [D, D], F32, kind="ExternalInput").ap()
    uo = nc.dram_tensor("uo", [D, D], F32, kind="ExternalInput").ap()
    bg = nc.dram_tensor("bg", [3 * D], F32, kind="ExternalInput").ap()
    bo = nc.dram_tensor("bo", [D], F32, kind="ExternalInput").ap()
    inc_ = nc.dram_tensor("inc_", [D], F32, kind="ExternalInput").ap()
    yT = nc.dram_tensor("yT", [D, TOK], F32, kind="ExternalOutput").ap()

    wg_sb = P.sb("wg_sb", [128, KC * 3 * D], BF16)
    ug_sb = P.sb("ug_sb", [128, KC * 3 * D], BF16)
    big = P.sb("big", [128, 16384], BF16)
    gf = big[:, 0:KC * POS]
    gi = big[:, KC * POS:2 * KC * POS]
    gg = big[:, 2 * KC * POS:3 * KC * POS]
    xhat = P.sb("xhat", [128, KC * TOK], BF16)
    c_sb = P.sb("c_sb", [128, KC * POS], BF16)
    tmp1 = P.sb("tmp1", [128, POS], BF16)
    tmp2 = P.sb("tmp2", [128, POS], BF16)
    hout = P.sb("hout", [128, 512], F32)
    res_t = P.sb("res_t", [128, 512], F32)
    yout = P.sb("yout", [128, 512], F32)
    bgs = P.sb("bgs", [128, 24], F32)
    bos = P.sb("bos", [128, KC], F32)
    incs = P.sb("incs", [128, KC], F32)
    aux = ln_aux(P)
    pg = [P.psum(f"pg{i}", [128, 512], F32) for i in range(4)]

    P.block()
    emit_ln_init(P, aux)
    for kc in range(KC):
        P.dma(nc.gpsimd, "dG", wg_sb[:, kc * 3 * D:(kc + 1) * 3 * D],
                                  wg[kc * 128:(kc + 1) * 128, :])
        P.dma(nc.gpsimd, "dG", ug_sb[:, kc * 3 * D:(kc + 1) * 3 * D],
                                  ug[kc * 128:(kc + 1) * 128, :])
    for j in range(24):
        P.dma(nc.sync, "dS", bgs[:, j:j + 1], bg[j * 128:(j + 1) * 128].unsqueeze(-1))
    for j in range(KC):
        P.dma(nc.sync, "dS", bos[:, j:j + 1], bo[j * 128:(j + 1) * 128].unsqueeze(-1))
        P.dma(nc.sync, "dS", incs[:, j:j + 1], inc_[j * 128:(j + 1) * 128].unsqueeze(-1))
    w_done = P.cnt["dG"]
    const_done = P.cnt["dS"]
    nc.sync.wait_ge(P.sem("dS"), const_done)

    emit_layernorm(P, xT, xhat, aux)

    P.wait(nc.vector, "dS", const_done)
    for kc in range(KC):
        P.inc(nc.vector.tensor_copy(
            c_sb[:, kc * POS:(kc + 1) * POS],
            incs[:, kc:kc + 1].broadcast_to([128, POS])), "cV")
    c_ready = P.cnt["cV"]

    P.wait(nc.tensor, "dG", w_done)
    gate_view = {0: gf, 1: gi, 2: gg}
    act_fn = {0: AF.Sigmoid, 1: AF.Sigmoid, 2: AF.Tanh}
    pclean = [0, 0, 0, 0]
    for t in range(4):
        P.wait(nc.tensor, "cV", c_ready)
        for gate in range(3):
            for m in range(KC):
                bi = (gate * KC + m) % 4
                bank = pg[bi]
                P.wait(nc.tensor, "cS", pclean[bi])
                for kc in range(KC):
                    nc.tensor.matmul(
                        bank[:, :],
                        wg_sb[:, kc * 3 * D + gate * D + m * 128: kc * 3 * D + gate * D + (m + 1) * 128],
                        xhat[:, kc * TOK + t * POS: kc * TOK + (t + 1) * POS],
                        start=(kc == 0), stop=False)
                for kc in range(KC):
                    last = P.inc(nc.tensor.matmul(
                        bank[:, :],
                        ug_sb[:, kc * 3 * D + gate * D + m * 128: kc * 3 * D + gate * D + (m + 1) * 128],
                        c_sb[:, kc * POS:(kc + 1) * POS],
                        start=False, stop=(kc == KC - 1)), "cT")
                P.wait(nc.scalar, "cT", last)
                pclean[bi] = P.inc(nc.scalar.activation(
                    gate_view[gate][:, m * POS:(m + 1) * POS], bank[:, :],
                    act_fn[gate], bias=bgs[:, gate * KC + m: gate * KC + m + 1]), "cS")
        gates_done = P.cnt["cS"]
        all_mm = P.cnt["cT"]
        P.wait(nc.vector, "cS", gates_done)
        P.wait(nc.vector, "cT", all_mm)
        for m in range(KC):
            sl = slice(m * POS, (m + 1) * POS)
            P.wait(nc.vector, "cV", P.cnt["cV"])
            P.inc(nc.vector.tensor_tensor(tmp1[:], gi[:, sl], gg[:, sl], ALU.mult), "cV")
            P.wait(nc.vector, "cV", P.cnt["cV"])
            P.inc(nc.vector.tensor_tensor(tmp2[:], gf[:, sl], c_sb[:, sl], ALU.mult), "cV")
            P.wait(nc.vector, "cV", P.cnt["cV"])
            P.inc(nc.vector.tensor_tensor(c_sb[:, sl], tmp1[:], tmp2[:], ALU.add), "cV")
        c_ready = P.cnt["cV"]

    P.wait(nc.gpsimd, "cV", c_ready)
    wo_sb = big[:, 0:KC * D]
    uo_sb = big[:, KC * D:2 * KC * D]
    for kc in range(KC):
        P.dma(nc.gpsimd, "dG", wo_sb[:, kc * D:(kc + 1) * D],
                                  wo[kc * 128:(kc + 1) * 128, :])
        P.dma(nc.gpsimd, "dG", uo_sb[:, kc * D:(kc + 1) * D],
                                  uo[kc * 128:(kc + 1) * 128, :])
    wo_done = P.cnt["dG"]

    P.wait(nc.tensor, "dG", wo_done)
    res_ld = P.cnt["dS"]
    for t in range(4):
        for m in range(KC):
            bi = (t * KC + m) % 4
            bank = pg[bi]
            P.wait(nc.sync, "cV", P.cnt["cV"])
            P.wait(nc.sync, "dS", res_ld)
            res_ld = P.dma(nc.sync, "dS", res_t[:], xT[m * 128:(m + 1) * 128, t * POS:(t + 1) * POS])
            P.wait(nc.tensor, "cS", pclean[bi])
            for kc in range(KC):
                nc.tensor.matmul(
                    bank[:, :],
                    wo_sb[:, kc * D + m * 128: kc * D + (m + 1) * 128],
                    xhat[:, kc * TOK + t * POS: kc * TOK + (t + 1) * POS],
                    start=(kc == 0), stop=False)
            for kc in range(KC):
                last = P.inc(nc.tensor.matmul(
                    bank[:, :],
                    uo_sb[:, kc * D + m * 128: kc * D + (m + 1) * 128],
                    c_sb[:, kc * POS:(kc + 1) * POS],
                    start=False, stop=(kc == KC - 1)), "cT")
            P.wait(nc.scalar, "cT", last)
            P.wait(nc.scalar, "cV", P.cnt["cV"])
            pclean[bi] = P.inc(nc.scalar.activation(
                hout[:], bank[:, :], AF.Tanh, bias=bos[:, m:m + 1]), "cS")
            P.wait(nc.vector, "cS", pclean[bi])
            P.wait(nc.vector, "dS", res_ld)
            cv = P.inc(nc.vector.scalar_tensor_tensor(
                yout[:], hout[:], 0.5, res_t[:], ALU.mult, ALU.add), "cV")
            P.wait(nc.sync, "cV", cv)
            P.dma(nc.sync, "dS", yT[m * 128:(m + 1) * 128, t * POS:(t + 1) * POS],
                                    yout[:])
    nc.sync.wait_ge(P.sem("dS"), P.cnt["dS"])
    return P



def shard_xT(x):
    out = []
    for i in range(NCORE):
        sl = x[:, i * POS:(i + 1) * POS, :]
        out.append(np.ascontiguousarray(sl.transpose(2, 0, 1).reshape(D, TOK)))
    return out


def unshard_xT(shards):
    x = np.empty((NSEQ, NCORE * POS, D), dtype=np.float32)
    for i, sh in enumerate(shards):
        x[:, i * POS:(i + 1) * POS, :] = sh.reshape(D, NSEQ, POS).transpose(1, 2, 0)
    return x


def run_l0ffn(xT_shards, W):
    P = Prog()
    build_l0ffn(P)
    P.close()
    wg = np.ascontiguousarray(np.concatenate([W["wf"].T, W["wi"].T, W["wc"].T], axis=1))
    ug = np.ascontiguousarray(np.concatenate([W["uf"].T, W["ui"].T, W["uc"].T], axis=1))
    wo = np.ascontiguousarray(W["wo"].T)
    uo = np.ascontiguousarray(W["uo"].T)
    bgc = np.ascontiguousarray(np.concatenate([W["bf"], W["bi"], W["bc"]]))
    in_maps = [{"xT": s, "wg": wg, "ug": ug, "wo": wo, "uo": uo,
                "bg": bgc, "bo": np.ascontiguousarray(W["bo"]),
                "inc_": np.ascontiguousarray(W["in_c"])} for s in xT_shards]
    res = run_bass_kernel_spmd(P.nc, in_maps, core_ids=list(range(NCORE)))
    return [r["yT"] for r in res.results]


def kernel(**inputs):
    x = np.asarray(inputs["L0_attn_x"], dtype=np.float32)
    W0 = {k: np.asarray(inputs[f"L0_ffn_{k}"], dtype=np.float32)
          for k in ["wf", "uf", "bf", "wi", "ui", "bi", "wc", "uc", "bc", "wo", "uo", "bo", "in_c"]}
    W1a = np.asarray(inputs["L0_attn_W1"], dtype=np.float32)
    b1a = np.asarray(inputs["L0_attn_b1"], dtype=np.float32)
    W2a = np.asarray(inputs["L0_attn_W2"], dtype=np.float32)
    b2a = np.asarray(inputs["L0_attn_b2"], dtype=np.float32)
    W1f = {k: np.asarray(inputs[f"L1_ffn_{k}"], dtype=np.float32)
           for k in ["w_gate", "w_up", "w_down"]}
    W1t = {k: np.asarray(inputs[f"L1_attn_{k}"], dtype=np.float32)
           for k in ["wq", "wk", "wv", "wo", "q_norm_w", "k_norm_w"]}

    shards = shard_xT(x)
    shards = run_l0ffn(shards, W0)
    shards = run_l0attn_all(shards, W1a, b1a, W2a, b2a)
    shards = run_l0ffn(shards, W0)
    qw = run_l1prep(W1f)
    shards = run_l1ffn(shards, qw)
    x_mid = unshard_xT(shards)
    shards = run_l1attn(x_mid, W1t)
    shards = run_l1ffn(shards, qw)
    return unshard_xT(shards)



def emit_quant(P, x_ap, out_ap, scr, np_, nb, after=None):
    nc = P.nc
    a, t, u = scr["a"], scr["t"], scr["u"]
    bm, scale, rsc = scr["bm"], scr["scale"], scr["rsc"]
    F = nb * 16
    xv = x_ap.rearrange("p (b k) -> p b k", k=16)
    if after:
        for s, v in after:
            P.wait(nc.vector, s, v)
    P.wait(nc.vector, "cV", scr.get("busy", 0))
    P.wait(nc.vector, "cS", scr.get("busy_s", 0))
    P.inc(nc.vector.tensor_reduce(bm[:np_, :nb], xv, AX.X, ALU.max,
                                  apply_absolute_value=True), "cV")
    P.wait(nc.vector, "cV", P.cnt["cV"])
    P.inc(nc.vector.tensor_scalar(scale[:np_, :nb], bm[:np_, :nb],
                                  1.0 / 6.0, 1e-12, ALU.mult, ALU.max), "cV")
    P.wait(nc.vector, "cV", P.cnt["cV"])
    P.inc(nc.vector.reciprocal(rsc[:np_, :nb], scale[:np_, :nb]), "cV")
    P.wait(nc.vector, "cV", P.cnt["cV"])
    rscb = rsc[:np_, :nb].unsqueeze(-1).broadcast_to([np_, nb, 16])
    scaleb = scale[:np_, :nb].unsqueeze(-1).broadcast_to([np_, nb, 16])
    tv16 = t[:np_, :F].rearrange("p (b k) -> p b k", k=16)
    P.inc(nc.vector.tensor_tensor(tv16, xv, rscb, ALU.mult), "cV")
    P.wait(nc.scalar, "cV", P.cnt["cV"])
    ab = P.inc(nc.scalar.activation(a[:np_, :F], t[:np_, :F], AF.Abs), "cS")
    P.wait(nc.vector, "cS", ab)
    P.inc(nc.vector.tensor_scalar(t[:np_, :F], a[:np_, :F], 0.25, None, ALU.is_gt), "cV")
    for th in (0.75, 1.25, 1.75):
        P.wait(nc.vector, "cV", P.cnt["cV"])
        P.inc(nc.vector.scalar_tensor_tensor(t[:np_, :F], a[:np_, :F], th,
                                             t[:np_, :F], ALU.is_gt, ALU.add), "cV")
    P.wait(nc.vector, "cV", P.cnt["cV"])
    P.inc(nc.vector.tensor_scalar(u[:np_, :F], a[:np_, :F], 2.5, None, ALU.is_gt), "cV")
    P.wait(nc.vector, "cV", P.cnt["cV"])
    P.inc(nc.vector.scalar_tensor_tensor(u[:np_, :F], a[:np_, :F], 3.5,
                                         u[:np_, :F], ALU.is_gt, ALU.add), "cV")
    P.wait(nc.vector, "cV", P.cnt["cV"])
    P.inc(nc.vector.scalar_tensor_tensor(t[:np_, :F], t[:np_, :F], 0.5,
                                         u[:np_, :F], ALU.mult, ALU.add), "cV")
    P.wait(nc.vector, "cV", P.cnt["cV"])
    P.inc(nc.vector.tensor_scalar(a[:np_, :F], a[:np_, :F], 5.0, 2.0,
                                  ALU.is_gt, ALU.mult), "cV")
    P.wait(nc.gpsimd, "cV", P.cnt["cV"])
    P.wait(nc.gpsimd, "cG", scr.get("busy_g", 0))
    P.inc(nc.gpsimd.tensor_tensor(t[:np_, :F], t[:np_, :F], a[:np_, :F], ALU.add), "cG")
    if after:
        for s, v in after:
            P.wait(nc.scalar, s, v)
    P.wait(nc.scalar, "cV", P.cnt["cV"])
    sg = P.inc(nc.scalar.activation(u[:np_, :F], x_ap, AF.Sign), "cS")
    P.wait(nc.gpsimd, "cS", sg)
    P.wait(nc.gpsimd, "cG", P.cnt["cG"])
    gg_ = P.inc(nc.gpsimd.tensor_tensor(t[:np_, :F], t[:np_, :F], u[:np_, :F], ALU.mult), "cG")
    scr["busy_g"] = gg_
    P.wait(nc.vector, "cG", gg_)
    tv = t[:np_, :F].rearrange("p (b k) -> p b k", k=16)
    ov = out_ap.rearrange("p (b k) -> p b k", k=16)
    done = P.inc(nc.vector.tensor_tensor(ov, tv, scaleb, ALU.mult), "cV")
    scr["busy"] = done
    scr["busy_s"] = sg
    return done



def build_l1prep(P):
    nc = P.nc
    NW = 2816 // NCORE
    ND = D // NCORE
    wgs = nc.dram_tensor("wgs", [NW, D], F32, kind="ExternalInput").ap()
    wus = nc.dram_tensor("wus", [NW, D], F32, kind="ExternalInput").ap()
    wds = nc.dram_tensor("wds", [ND, 2816], F32, kind="ExternalInput").ap()
    qgs = nc.dram_tensor("qgs", [NW, D], BF16, kind="ExternalOutput").ap()
    qus = nc.dram_tensor("qus", [NW, D], BF16, kind="ExternalOutput").ap()
    qds = nc.dram_tensor("qds", [ND, 2816], BF16, kind="ExternalOutput").ap()
    xin = P.sb("xin", [128, 2816], F32)
    qout = P.sb("qout", [128, 2816], BF16)
    scr = {
        "a": P.sb("qa", [128, 2816], F32),
        "t": P.sb("qt", [128, 2816], F32),
        "u": P.sb("qu", [128, 2816], F32),
        "bm": P.sb("qbm", [128, 176], F32),
        "scale": P.sb("qscale", [128, 176], F32),
        "rsc": P.sb("qrsc", [128, 176], F32),
    }
    P.block()
    jobs = []
    for r0 in range(0, NW, 128):
        npart = min(128, NW - r0)
        jobs.append((wgs, qgs, r0, npart, D))
        jobs.append((wus, qus, r0, npart, D))
    for r0 in range(0, ND, 128):
        npart = min(128, ND - r0)
        jobs.append((wds, qds, r0, npart, 2816))
    st_done = 0
    for (src, dst, r0, npart, F) in jobs:
        P.wait(nc.sync, "cV", scr.get("busy", 0))
        P.wait(nc.sync, "cS", scr.get("busy_s", 0))
        P.wait(nc.sync, "dS", st_done)
        ld = P.dma(nc.sync, "dS", xin[:npart, :F], src[r0:r0 + npart, :])
        emit_quant(P, xin[:npart, :F], qout[:npart, :F], scr, npart, F // 16,
                   after=[("dS", ld)])
        P.wait(nc.sync, "cV", P.cnt["cV"])
        st_done = P.dma(nc.sync, "dS", dst[r0:r0 + npart, :], qout[:npart, :F])
    nc.sync.wait_ge(P.sem("dS"), P.cnt["dS"])
    return P


def run_l1prep(W1f):
    P = Prog()
    build_l1prep(P)
    P.close()
    NW = 2816 // NCORE
    ND = D // NCORE
    in_maps = []
    for c in range(NCORE):
        in_maps.append({
            "wgs": np.ascontiguousarray(W1f["w_gate"][c * NW:(c + 1) * NW]),
            "wus": np.ascontiguousarray(W1f["w_up"][c * NW:(c + 1) * NW]),
            "wds": np.ascontiguousarray(W1f["w_down"][c * ND:(c + 1) * ND]),
        })
    res = run_bass_kernel_spmd(P.nc, in_maps, core_ids=list(range(NCORE)))
    qg = np.concatenate([r["qgs"] for r in res.results], axis=0)
    qu = np.concatenate([r["qus"] for r in res.results], axis=0)
    qd = np.concatenate([r["qds"] for r in res.results], axis=0)
    return (np.ascontiguousarray(qg.T), np.ascontiguousarray(qu.T),
            np.ascontiguousarray(qd.T))



NSL = [(0, 512), (512, 512), (1024, 512), (1536, 512), (2048, 512), (2560, 256)]
HTT = 256


def build_l1ffn(P):
    nc = P.nc
    x_tok = nc.dram_tensor("x_tok", [TOK, D], F32, kind="ExternalInput").ap()
    xT = nc.dram_tensor("xT", [D, TOK], F32, kind="ExternalInput").ap()
    wgq = nc.dram_tensor("wgq", [D, 2816], BF16, kind="ExternalInput").ap()
    wuq = nc.dram_tensor("wuq", [D, 2816], BF16, kind="ExternalInput").ap()
    wdq = nc.dram_tensor("wdq", [2816, D], BF16, kind="ExternalInput").ap()
    yT = nc.dram_tensor("yT", [D, TOK], F32, kind="ExternalOutput").ap()
    xq_d = nc.dram_tensor("xq_d", [TOK, D], BF16)
    hq_d = nc.dram_tensor("hq_d", [TOK, 2816], BF16)

    wbig = P.sb("wbig", [128, 45056], BF16)
    wgq_sb = wbig[:, 0:KC * 2816]
    wuq_sb = wbig[:, KC * 2816:2 * KC * 2816]
    wdq_sb = wbig[:, 0:22 * D]
    xqT = P.sb("xqT", [128, KC * TOK], BF16)
    hqT = P.sb("hqT", [128, 22 * HTT], BF16)
    xtc = P.sb("xtc", [128, D], F32)
    sqc = P.sb("sqc", [128, D], BF16)
    xqc = P.sb("xqc", [128, D], BF16)
    h_sb = P.sb("h_sb", [128, 2816], F32)
    hq_sb = P.sb("hq_sb", [128, 2816], BF16)
    sil = P.sb("sil", [128, 512], F32)
    res_t = P.sb("res_t", [128, 512], F32)
    yout = P.sb("yout", [128, 512], F32)
    scr = {
        "a": P.sb("qa", [128, 2816], F32), "t": P.sb("qt", [128, 2816], F32),
        "u": P.sb("qu", [128, 2816], F32), "bm": P.sb("qbm", [128, 176], F32),
        "scale": P.sb("qscale", [128, 176], F32), "rsc": P.sb("qrsc", [128, 176], F32),
    }
    cs1 = P.sb("cs1", [128, 1], F32)
    cs2 = P.sb("cs2", [128, 1], F32)
    mean = P.sb("mean", [128, 1], F32)
    rstd = P.sb("rstd", [128, 1], F32)
    bcol = P.sb("bcol", [128, 1], F32)
    pb = [P.psum(f"pb{i}", [128, 512], F32) for i in range(6)]

    P.block()
    for kc in range(KC):
        P.dma(nc.gpsimd, "dG", wgq_sb[:, kc * 2816:(kc + 1) * 2816],
                                  wgq[kc * 128:(kc + 1) * 128, :])
        P.dma(nc.gpsimd, "dG", wuq_sb[:, kc * 2816:(kc + 1) * 2816],
                                  wuq[kc * 128:(kc + 1) * 128, :])
    wA_done = P.cnt["dG"]

    xq_st = 0
    for i in range(TOK // 128):
        P.wait(nc.sync, "cV", scr.get("busy", 0))
        P.wait(nc.sync, "cS", scr.get("busy_s", 0))
        ld = P.dma(nc.sync, "dS", xtc[:], x_tok[i * 128:(i + 1) * 128, :])
        P.wait(nc.vector, "dS", ld)
        P.inc(nc.vector.tensor_reduce(cs1[:], xtc[:], AX.X, ALU.add), "cV")
        P.wait(nc.scalar, "dS", ld)
        sq_done = P.inc(nc.scalar.activation(sqc[:], xtc[:], AF.Square), "cS")
        P.wait(nc.vector, "cS", sq_done)
        P.inc(nc.vector.tensor_reduce(cs2[:], sqc[:], AX.X, ALU.add), "cV")
        P.wait(nc.vector, "cV", P.cnt["cV"])
        P.inc(nc.vector.tensor_scalar(mean[:], cs1[:], 1.0 / D, None, ALU.mult), "cV")
        P.wait(nc.vector, "cV", P.cnt["cV"])
        P.inc(nc.vector.tensor_scalar(cs2[:], cs2[:], 1.0 / D, None, ALU.mult), "cV")
        P.wait(nc.vector, "cV", P.cnt["cV"])
        P.inc(nc.vector.tensor_tensor(cs1[:], mean[:], mean[:], ALU.mult), "cV")
        P.wait(nc.vector, "cV", P.cnt["cV"])
        P.inc(nc.vector.tensor_tensor(cs2[:], cs1[:], cs2[:], ALU.subtract), "cV")
        P.wait(nc.scalar, "cV", P.cnt["cV"])
        P.inc(nc.scalar.activation(cs2[:], cs2[:], AF.Ln,
                                   bias=np.float32(1e-5), scale=np.float32(-1.0)), "cS")
        P.wait(nc.scalar, "cS", P.cnt["cS"])
        sd = P.inc(nc.scalar.activation(rstd[:], cs2[:], AF.Exp, scale=np.float32(-0.5)), "cS")
        P.wait(nc.vector, "cS", sd)
        P.inc(nc.vector.scalar_tensor_tensor(bcol[:], mean[:], -1.0, rstd[:],
                                             ALU.mult, ALU.mult), "cV")
        P.wait(nc.vector, "cV", P.cnt["cV"])
        P.inc(nc.vector.tensor_scalar(xtc[:], xtc[:], rstd[:, 0:1], bcol[:, 0:1],
                                      ALU.mult, ALU.add), "cV")
        P.wait(nc.vector, "dG", xq_st)
        emit_quant(P, xtc[:, :], xqc[:, :], scr, 128, D // 16,
                   after=[("cV", P.cnt["cV"])])
        qv = P.cnt["cV"]
        P.wait(nc.gpsimd, "cV", qv)
        xq_st = P.dma(nc.gpsimd, "dG", xq_d[i * 128:(i + 1) * 128, :], xqc[:])
    P.wait(nc.sync, "dG", P.cnt["dG"])
    for kc in range(KC):
        P.dma_t(nc.sync, "dS", xqT[:, kc * TOK:(kc + 1) * TOK],
                                          xq_d[:, kc * 128:(kc + 1) * 128])
    xqT_done = P.cnt["dS"]

    P.wait(nc.tensor, "dS", xqT_done)
    P.wait(nc.tensor, "dG", wA_done)
    pclean_v = [0] * 6
    pclean_s = [0] * 6
    sil_busy = 0
    for i in range(TOK // 128):
        for si, (n0, ns) in enumerate(NSL):
            bg_, bu_ = pb[2 * (si % 3)], pb[2 * (si % 3) + 1]
            P.wait(nc.tensor, "cV", pclean_v[2 * (si % 3)])
            P.wait(nc.tensor, "cS", pclean_s[2 * (si % 3)])
            P.wait(nc.tensor, "cV", pclean_v[2 * (si % 3) + 1])
            P.wait(nc.tensor, "cS", pclean_s[2 * (si % 3) + 1])
            for kc in range(KC):
                nc.tensor.matmul(bg_[:, 0:ns],
                                 xqT[:, kc * TOK + i * 128: kc * TOK + (i + 1) * 128],
                                 wgq_sb[:, kc * 2816 + n0: kc * 2816 + n0 + ns],
                                 start=(kc == 0), stop=(kc == KC - 1))
            for kc in range(KC):
                mmu = P.inc(nc.tensor.matmul(bu_[:, 0:ns],
                                 xqT[:, kc * TOK + i * 128: kc * TOK + (i + 1) * 128],
                                 wuq_sb[:, kc * 2816 + n0: kc * 2816 + n0 + ns],
                                 start=(kc == 0), stop=(kc == KC - 1)), "cT")
            P.wait(nc.scalar, "cT", mmu)
            P.wait(nc.scalar, "cV", sil_busy)
            ssil = P.inc(nc.scalar.activation(sil[:, 0:ns], bg_[:, 0:ns], AF.Silu), "cS")
            pclean_s[2 * (si % 3)] = ssil
            pclean_s[2 * (si % 3) + 1] = ssil
            P.wait(nc.vector, "cS", ssil)
            sil_busy = P.inc(nc.vector.tensor_tensor(
                h_sb[:, n0:n0 + ns], sil[:, 0:ns], bu_[:, 0:ns], ALU.mult), "cV")
            pclean_v[2 * (si % 3)] = sil_busy
            pclean_v[2 * (si % 3) + 1] = sil_busy
        P.wait(nc.vector, "dG", P.cnt["dG"])
        emit_quant(P, h_sb[:, :], hq_sb[:, :], scr, 128, 176,
                   after=[("cV", P.cnt["cV"])])
        qv = P.cnt["cV"]
        P.wait(nc.gpsimd, "cV", qv)
        P.dma(nc.gpsimd, "dG", hq_d[i * 128:(i + 1) * 128, :], hq_sb[:])
    hq_all = P.cnt["dG"]

    P.wait(nc.gpsimd, "cT", P.cnt["cT"])
    for kk in range(22):
        P.dma(nc.gpsimd, "dG", wdq_sb[:, kk * D:(kk + 1) * D],
                                  wdq[kk * 128:(kk + 1) * 128, :])
    wB_done = P.cnt["dG"]
    P.wait(nc.sync, "dG", hq_all)
    res_busy = 0
    yst = 0
    for tt in range(TOK // HTT):
        P.wait(nc.sync, "cT", P.cnt["cT"])
        for kk in range(22):
            P.dma_t(nc.sync, "dS", hqT[:, kk * HTT:(kk + 1) * HTT],
                hq_d[tt * HTT:(tt + 1) * HTT, kk * 128:(kk + 1) * 128])
        hqT_ld = P.cnt["dS"]
        P.wait(nc.tensor, "dS", hqT_ld)
        P.wait(nc.tensor, "dG", wB_done)
        for m in range(KC):
            bi = m % 4
            bank = pb[bi]
            P.wait(nc.tensor, "cV", pclean_v[bi])
            P.wait(nc.tensor, "cS", pclean_s[bi])
            for kk in range(22):
                mm = P.inc(nc.tensor.matmul(bank[:, 0:HTT],
                                wdq_sb[:, kk * D + m * 128: kk * D + (m + 1) * 128],
                                hqT[:, kk * HTT:(kk + 1) * HTT],
                                start=(kk == 0), stop=(kk == 21)), "cT")
            P.wait(nc.sync, "cV", res_busy)
            rld = P.dma(nc.sync, "dS", res_t[:, 0:HTT], xT[m * 128:(m + 1) * 128, tt * HTT:(tt + 1) * HTT])
            P.wait(nc.vector, "cT", mm)
            P.wait(nc.vector, "dS", rld)
            P.wait(nc.vector, "dS", yst)
            cv = P.inc(nc.vector.scalar_tensor_tensor(
                yout[:, 0:HTT], bank[:, 0:HTT], 0.5, res_t[:, 0:HTT],
                ALU.mult, ALU.add), "cV")
            pclean_v[bi] = cv
            res_busy = cv
            P.wait(nc.sync, "cV", cv)
            yst = P.dma(nc.sync, "dS", yT[m * 128:(m + 1) * 128, tt * HTT:(tt + 1) * HTT], yout[:, 0:HTT])
    nc.sync.wait_ge(P.sem("dS"), P.cnt["dS"])
    return P


def run_l1ffn(xT_shards, qw):
    wgq_T, wuq_T, wdq_T = qw
    P = Prog()
    build_l1ffn(P)
    P.close()
    in_maps = []
    for s in xT_shards:
        x_tok = np.ascontiguousarray(s.T)
        in_maps.append({"x_tok": x_tok, "xT": s, "wgq": wgq_T, "wuq": wuq_T,
                        "wdq": wdq_T})
    res = run_bass_kernel_spmd(P.nc, in_maps, core_ids=list(range(NCORE)))
    return [r["yT"] for r in res.results]



SEQT = 640
NTK = NSEQ * SEQT
WIN = 256


def build_l1attn(P, dbg=False):
    nc = P.nc
    xkv = nc.dram_tensor("xkv", [D, NTK], F32, kind="ExternalInput").ap()
    wq = nc.dram_tensor("wq", [D, D], F32, kind="ExternalInput").ap()
    wk = nc.dram_tensor("wk", [D, D], F32, kind="ExternalInput").ap()
    wv = nc.dram_tensor("wv", [D, D], F32, kind="ExternalInput").ap()
    wo = nc.dram_tensor("wo", [D, D], F32, kind="ExternalInput").ap()
    qnw = nc.dram_tensor("qnw", [D], F32, kind="ExternalInput").ap()
    knw = nc.dram_tensor("knw", [D], F32, kind="ExternalInput").ap()
    bias0 = nc.dram_tensor("bias0", [128, 8 * WIN], BF16, kind="ExternalInput").ap()
    biasn = nc.dram_tensor("biasn", [128, 8 * WIN], BF16, kind="ExternalInput").ap()
    yT = nc.dram_tensor("yT", [D, TOK], F32, kind="ExternalOutput").ap()

    xhat = P.sb("xhat", [128, KC * NTK], BF16)
    attnT = xhat[:, 0:KC * TOK]
    pr_bf = xhat[:, KC * TOK:KC * TOK + 2048]
    prT = xhat[:, KC * TOK + 2048:KC * TOK + 4096]
    qnT = P.sb("qnT", [128, KC * TOK], BF16)
    knT = P.sb("knT", [128, KC * NTK], BF16)
    v_sb = P.sb("v_sb", [128, (NTK // 128) * D], BF16)
    w_sb = P.sb("w_sb", [128, KC * D], BF16)
    bias_sb = P.sb("bias_sb", [128, 2 * 8 * WIN], BF16)
    t1 = P.sb("t1", [128, 512], F32)
    rowr = P.sb("rowr", [128, 1], F32)
    rows_sb = P.sb("rows_sb", [128, 1], F32)
    ident = P.sb("ident", [128, 128], BF16)
    nw_sb = P.sb("nw_sb", [128, 2 * KC], F32)
    qsc = P.sb("qsc", [128, 512], BF16)

    ps = [P.psum(f"ps{i}", [128, 512], F32) for i in range(4)]
    pav = [P.psum(f"pav{i}", [128, 512], F32) for i in range(2)]
    tps = [P.psum("tps0", [128, 128], BF16), P.psum("tps1", [128, 128], BF16)]
    aux = ln_aux(P, psums={"pA": pav[0], "pB": pav[1],
                           "pS": ps[0][0:1, :], "pQ": ps[1][0:1, :]})
    rr_row = aux["mrow"]

    P.block()
    emit_ln_init(P, aux)
    idf = t1[:, 0:128]
    P.inc(nc.gpsimd.iota(idf, [[-1, 128]], base=0, channel_multiplier=1,
                         allow_small_or_imprecise_dtypes=True), "cG")
    P.wait(nc.vector, "cG", P.cnt["cG"])
    P.inc(nc.vector.tensor_scalar(ident[:], idf, 0.0, None, ALU.is_equal), "cV")
    id_done = P.cnt["cV"]
    P.dma(nc.sync, "dS", bias_sb[:, 0:8 * WIN], bias0)
    P.dma(nc.sync, "dS", bias_sb[:, 8 * WIN:16 * WIN], biasn)
    for j in range(KC):
        P.dma(nc.sync, "dS", nw_sb[:, j:j + 1], qnw[j * 128:(j + 1) * 128].unsqueeze(-1))
        P.dma(nc.sync, "dS", nw_sb[:, KC + j:KC + j + 1], knw[j * 128:(j + 1) * 128].unsqueeze(-1))
    const_ld = P.cnt["dS"]
    nc.sync.wait_ge(P.sem("dS"), const_ld)

    emit_layernorm(P, xkv, xhat, aux, ntok=NTK)
    if dbg:
        d_xh = nc.dram_tensor("d_xh", [128, KC * NTK], BF16, kind="ExternalOutput").ap()
        P.wait(nc.gpsimd, "cV", P.cnt["cV"])
        xh_dump = P.dma(nc.gpsimd, "dG", d_xh, xhat[:, :])
    else:
        xh_dump = 0
    P.wait(nc.tensor, "dG", xh_dump)

    own_cols = [(s * SEQT + 16, 512) for s in range(NSEQ)]

    def proj(dst, w_dram, cols, rms_w_col, ncol_dst):
        P.wait(nc.gpsimd, "cT", P.cnt["cT"])
        for kc_ in range(KC):
            P.dma(nc.gpsimd, "dG", w_sb[:, kc_ * D:(kc_ + 1) * D],
                                      w_dram[kc_ * 128:(kc_ + 1) * 128, :])
        wd = P.cnt["dG"]
        P.wait(nc.tensor, "dG", wd)
        for (c0, n, d0) in cols:
            for off in range(0, n, 512):
                ns = min(512, n - off)
                for m in range(KC):
                    bi = 2 + (m % 2)
                    bank = ps[bi]
                    P.wait(nc.tensor, "cV", P.cnt["cV"])
                    P.wait(nc.tensor, "cS", P.cnt["cS"])
                    for kc_ in range(KC):
                        mm = P.inc(nc.tensor.matmul(
                            bank[:, 0:ns],
                            w_sb[:, kc_ * D + m * 128:(kc_ * D + (m + 1) * 128)],
                            xhat[:, kc_ * NTK + c0 + off: kc_ * NTK + c0 + off + ns],
                            start=(kc_ == 0), stop=(kc_ == KC - 1)), "cT")
                    P.wait(nc.vector, "cT", mm)
                    P.inc(nc.vector.tensor_copy(
                        dst[:, m * ncol_dst + d0 + off: m * ncol_dst + d0 + off + ns],
                        bank[:, 0:ns]), "cV")
                if rms_w_col is not None:
                    evac = P.cnt["cV"]
                    for m in range(KC):
                        P.wait(nc.scalar, "cV", evac)
                        P.wait(nc.scalar, "cT", P.cnt["cT"])
                        sqd = P.inc(nc.scalar.activation(
                            qsc[:, 0:ns],
                            dst[:, m * ncol_dst + d0 + off: m * ncol_dst + d0 + off + ns],
                            AF.Square), "cS")
                        P.wait(nc.tensor, "cS", sqd)
                        P.wait(nc.tensor, "cV", P.cnt["cV"])
                        smm = P.inc(nc.tensor.matmul(ps[0][0:1, 0:ns], aux["ones_bf"][:, :],
                                                     qsc[:, 0:ns],
                                                     start=(m == 0), stop=(m == KC - 1)), "cT")
                    P.wait(nc.vector, "cT", smm)
                    P.inc(nc.vector.tensor_scalar(rr_row[:, 0:ns], ps[0][0:1, 0:ns],
                                                  1.0 / D, None, ALU.mult), "cV")
                    P.wait(nc.scalar, "cV", P.cnt["cV"])
                    P.inc(nc.scalar.activation(rr_row[:, 0:ns], rr_row[:, 0:ns],
                                               AF.Ln, bias=np.float32(1e-6)), "cS")
                    P.wait(nc.scalar, "cS", P.cnt["cS"])
                    sdd = P.inc(nc.scalar.activation(rr_row[:, 0:ns], rr_row[:, 0:ns],
                                                     AF.Exp, scale=np.float32(-0.5)), "cS")
                    rrd = P.cnt["cV"]
                    P.wait(nc.tensor, "cV", rrd)
                    P.wait(nc.tensor, "cS", sdd)
                    bc = P.inc(nc.tensor.matmul(pav[0][:, 0:ns], aux["ones_f"][:, :],
                                                rr_row[:, 0:ns], start=True, stop=True), "cT")
                    P.wait(nc.vector, "cT", bc)
                    for m in range(KC):
                        sl = dst[:, m * ncol_dst + d0 + off: m * ncol_dst + d0 + off + ns]
                        P.wait(nc.vector, "cV", P.cnt["cV"])
                        P.inc(nc.vector.tensor_tensor(t1[:, 0:ns], sl, pav[0][:, 0:ns],
                                                      ALU.mult), "cV")
                        P.wait(nc.vector, "cV", P.cnt["cV"])
                        P.inc(nc.vector.tensor_scalar(sl, t1[:, 0:ns],
                                                      nw_sb[:, rms_w_col + m:rms_w_col + m + 1],
                                                      None, ALU.mult), "cV")

    P.wait(nc.vector, "dS", const_ld)
    proj(knT, wk, [(0, NTK, 0)], KC, NTK)
    proj(qnT, wq, [(c, n, s * 512) for s, (c, n) in enumerate(own_cols)], 0, TOK)
    P.wait(nc.gpsimd, "cT", P.cnt["cT"])
    for kc_ in range(KC):
        P.dma(nc.gpsimd, "dG", w_sb[:, kc_ * D:(kc_ + 1) * D],
                                  wv[kc_ * 128:(kc_ + 1) * 128, :])
    wd = P.cnt["dG"]
    P.wait(nc.tensor, "dG", wd)
    for ch in range(NTK // 128):
        for half in range(2):
            bank = ps[2 + half]
            P.wait(nc.tensor, "cV", P.cnt["cV"])
            for kc_ in range(KC):
                mm = P.inc(nc.tensor.matmul(
                    bank[:, 0:512],
                    xhat[:, kc_ * NTK + ch * 128: kc_ * NTK + (ch + 1) * 128],
                    w_sb[:, kc_ * D + half * 512: kc_ * D + half * 512 + 512],
                    start=(kc_ == 0), stop=(kc_ == KC - 1)), "cT")
            P.wait(nc.vector, "cT", mm)
            P.inc(nc.vector.tensor_copy(
                v_sb[:, ch * D + half * 512: ch * D + half * 512 + 512],
                bank[:, 0:512]), "cV")
    v_done = P.cnt["cV"]

    P.wait(nc.tensor, "cV", max(v_done, id_done))
    INV = float(1.0 / np.sqrt(128.0))
    attn_done_v = 0
    for s in range(NSEQ):
        for qt in range(4):
            qcol = s * 512 + qt * 128
            kvcol = s * SEQT + qt * 128
            bvar = 0 if qt == 0 else 8 * WIN
            sc_mm = [0] * 4
            for h in range(8):
                bank = ps[h // 2]
                bo_ = (h % 2) * WIN
                if h % 2 == 0:
                    P.wait(nc.tensor, "cV", P.cnt["cV"])
                sc_mm[h // 2] = P.inc(nc.tensor.matmul(
                    bank[:, bo_:bo_ + WIN],
                    qnT[:, h * TOK + qcol: h * TOK + qcol + 128],
                    knT[:, h * NTK + kvcol: h * NTK + kvcol + WIN],
                    start=True, stop=True), "cT")
            for h in range(8):
                bank = ps[h // 2]
                bo_ = (h % 2) * WIN
                P.wait(nc.vector, "cT", sc_mm[h // 2])
                P.wait(nc.vector, "cS", P.cnt["cS"])
                P.inc(nc.vector.scalar_tensor_tensor(
                    t1[:, 0:WIN], bank[:, bo_:bo_ + WIN], INV,
                    bias_sb[:, bvar + h * WIN: bvar + (h + 1) * WIN],
                    ALU.mult, ALU.add), "cV")
                P.wait(nc.scalar, "cV", P.cnt["cV"])
                ex = P.inc(nc.scalar.activation(pr_bf[:, h * WIN:(h + 1) * WIN],
                                                t1[:, 0:WIN], AF.Exp), "cS")
                P.wait(nc.vector, "cS", ex)
                P.inc(nc.vector.tensor_reduce(rows_sb[:], pr_bf[:, h * WIN:(h + 1) * WIN],
                                              AX.X, ALU.add), "cV")
                P.wait(nc.vector, "cV", P.cnt["cV"])
                P.inc(nc.vector.reciprocal(rowr[:], rows_sb[:]), "cV")
                P.wait(nc.vector, "cV", P.cnt["cV"])
                P.inc(nc.vector.tensor_scalar(pr_bf[:, h * WIN:(h + 1) * WIN],
                                              pr_bf[:, h * WIN:(h + 1) * WIN],
                                              rowr[:, 0:1], None, ALU.mult), "cV")
            norm_done = P.cnt["cV"]
            P.wait(nc.tensor, "cV", norm_done)
            tp_copy = [0, 0]
            for b in range(16):
                tb = tps[b % 2]
                P.wait(nc.tensor, "cV", tp_copy[b % 2])
                tp = P.inc(nc.tensor.transpose(tb[:, :], pr_bf[:, b * 128:(b + 1) * 128],
                                               ident[:, :]), "cT")
                P.wait(nc.vector, "cT", tp)
                tp_copy[b % 2] = P.inc(nc.vector.tensor_copy(
                    prT[:, b * 128:(b + 1) * 128], tb[:, :]), "cV")
            prT_done = P.cnt["cV"]
            P.wait(nc.tensor, "cV", prT_done)
            av_copy = [0, 0]
            for h in range(8):
                bank = pav[h % 2]
                c0 = 5 * s + qt
                P.wait(nc.tensor, "cV", av_copy[h % 2])
                av1 = nc.tensor.matmul(
                    bank[:, 0:128],
                    v_sb[:, c0 * D + h * 128:(c0 * D + (h + 1) * 128)],
                    prT[:, h * 256:h * 256 + 128],
                    start=True, stop=False)
                av = P.inc(nc.tensor.matmul(
                    bank[:, 0:128],
                    v_sb[0:16, (c0 + 1) * D + h * 128:((c0 + 1) * D + (h + 1) * 128)],
                    prT[0:16, h * 256 + 128:h * 256 + 256],
                    start=False, stop=True), "cT")
                P.wait(nc.vector, "cT", av)
                attn_done_v = P.inc(nc.vector.tensor_copy(
                    attnT[:, h * TOK + qcol: h * TOK + qcol + 128], bank[:, 0:128]), "cV")
                av_copy[h % 2] = attn_done_v

    P.wait(nc.gpsimd, "cT", P.cnt["cT"])
    for kc_ in range(KC):
        P.dma(nc.gpsimd, "dG", w_sb[:, kc_ * D:(kc_ + 1) * D],
                                  wo[kc_ * 128:(kc_ + 1) * 128, :])
    wd = P.cnt["dG"]
    P.wait(nc.tensor, "dG", wd)
    res_busy = P.cnt["cV"]
    yst = 0
    for s in range(NSEQ):
        for m in range(KC):
            bank = ps[m % 4]
            P.wait(nc.tensor, "cV", P.cnt["cV"])
            for kc_ in range(KC):
                mm = P.inc(nc.tensor.matmul(
                    bank[:, 0:512],
                    w_sb[:, kc_ * D + m * 128: kc_ * D + (m + 1) * 128],
                    attnT[:, kc_ * TOK + s * 512:(kc_ * TOK + (s + 1) * 512)],
                    start=(kc_ == 0), stop=(kc_ == KC - 1)), "cT")
            P.wait(nc.sync, "cV", res_busy)
            rld = P.dma(nc.sync, "dS", t1[:, 0:512],
                xkv[m * 128:(m + 1) * 128, s * SEQT + 16: s * SEQT + 528])
            P.wait(nc.vector, "cT", mm)
            P.wait(nc.vector, "dS", rld)
            P.wait(nc.vector, "dS", yst)
            cv = P.inc(nc.vector.tensor_tensor(t1[:, 0:512],
                                               bank[:, 0:512], t1[:, 0:512], ALU.add), "cV")
            res_busy = cv
            P.wait(nc.sync, "cV", cv)
            yst = P.dma(nc.sync, "dS", yT[m * 128:(m + 1) * 128, s * 512:(s + 1) * 512], t1[:, 0:512])
    if dbg:
        d_qnT = nc.dram_tensor("d_qnT", [128, KC * TOK], BF16, kind="ExternalOutput").ap()
        d_knT = nc.dram_tensor("d_knT", [128, KC * NTK], BF16, kind="ExternalOutput").ap()
        d_v = nc.dram_tensor("d_v", [128, (NTK // 128) * D], BF16, kind="ExternalOutput").ap()
        d_at = nc.dram_tensor("d_at", [128, KC * TOK], BF16, kind="ExternalOutput").ap()
        P.wait(nc.gpsimd, "cV", P.cnt["cV"])
        P.wait(nc.gpsimd, "cT", P.cnt["cT"])
        P.dma(nc.gpsimd, "dG", d_qnT, qnT[:, :])
        P.dma(nc.gpsimd, "dG", d_knT, knT[:, :])
        P.dma(nc.gpsimd, "dG", d_v, v_sb[:, :])
        P.dma(nc.gpsimd, "dG", d_at, attnT[:, :])
        nc.gpsimd.wait_ge(P.sem("dG"), P.cnt["dG"])
    nc.sync.wait_ge(P.sem("dS"), P.cnt["dS"])
    return P


def make_l1attn_bias(core):
    import ml_dtypes
    slopes = (2.0 ** (-np.arange(1, 9))).astype(np.float32)
    qrow = np.arange(128)[:, None]
    kidx = np.arange(WIN)[None, :]
    rel = (kidx - 16) - qrow
    inwin = (rel <= 0) & (rel >= -16)
    b0 = np.where(inwin[None], slopes[:, None, None] * rel[None], -1e9).astype(np.float32)
    bn = b0.copy()
    if core == 0:
        b0 = np.where((kidx < 16)[None], np.float32(-1e9), b0)
    def pack(b):
        return np.ascontiguousarray(
            b.transpose(1, 0, 2).reshape(128, 8 * WIN)).astype(ml_dtypes.bfloat16)
    return pack(b0), pack(bn)


def make_xkv(x, core):
    out = np.zeros((D, NTK), dtype=np.float32)
    for s in range(NSEQ):
        p0 = core * POS
        own = x[s, p0:p0 + POS, :].T
        out[:, s * SEQT + 16: s * SEQT + 528] = own
        if core > 0:
            out[:, s * SEQT: s * SEQT + 16] = x[s, p0 - 16:p0, :].T
    return out


def run_l1attn(x_full, W, dbg=False):
    P = Prog()
    build_l1attn(P, dbg=dbg)
    P.close()
    in_maps = []
    for c in range(NCORE):
        b0, bn = make_l1attn_bias(c)
        in_maps.append({
            "xkv": make_xkv(x_full, c),
            "wq": np.ascontiguousarray(W["wq"].T), "wk": np.ascontiguousarray(W["wk"].T),
            "wv": np.ascontiguousarray(W["wv"].T), "wo": np.ascontiguousarray(W["wo"].T),
            "qnw": np.ascontiguousarray(W["q_norm_w"]), "knw": np.ascontiguousarray(W["k_norm_w"]),
            "bias0": b0, "biasn": bn,
        })
    res = run_bass_kernel_spmd(P.nc, in_maps, core_ids=list(range(NCORE)))
    if dbg:
        return res.results
    return [r["yT"] for r in res.results]



def build_l0qkv(P):
    nc = P.nc
    xT = nc.dram_tensor("xT", [D, TOK], F32, kind="ExternalInput").ap()
    W1 = nc.dram_tensor("W1", [D, 3 * D], F32, kind="ExternalInput").ap()
    b1 = nc.dram_tensor("b1", [3 * D], F32, kind="ExternalInput").ap()
    qkvT = nc.dram_tensor("qkvT", [3 * D, TOK], BF16, kind="ExternalOutput").ap()

    w_sb = P.sb("w_sb", [128, KC * 3 * D], BF16)
    xhat = P.sb("xhat", [128, KC * TOK], BF16)
    b1_sb = P.sb("b1_sb", [1, 3 * D], BF16)
    ones_row = P.sb("ones_row", [1, 512], BF16)
    ev = P.sb("ev", [128, 512], BF16)
    psA = P.psum("psA", [1, 512], F32)
    psQ = P.psum("psQ", [1, 512], F32)
    pA = P.psum("pA", [128, 512], F32)
    pB = P.psum("pB", [128, 512], F32)
    pg = [P.psum(f"pg{i}", [128, 512], F32) for i in range(4)]
    aux = ln_aux(P, psums={"pS": psA, "pQ": psQ, "pA": pA, "pB": pB})

    P.block()
    emit_ln_init(P, aux)
    P.inc(nc.vector.memset(ones_row[:], 1.0), "cV")
    for kc in range(KC):
        P.dma(nc.gpsimd, "dG", w_sb[:, kc * 3 * D:(kc + 1) * 3 * D],
                                  W1[kc * 128:(kc + 1) * 128, :])
    P.dma(nc.gpsimd, "dG", b1_sb[:, :], b1.unsqueeze(0))
    w_done = P.cnt["dG"]
    emit_layernorm(P, xT, xhat, aux)
    ln_done = P.cnt["cV"]
    P.wait(nc.tensor, "dG", w_done)
    P.wait(nc.tensor, "cV", ln_done)
    pclean = [0] * 4
    st = 0
    for m in range(3 * KC):
        for tt in range(TOK // 512):
            bi = (m * 4 + tt) % 4
            bank = pg[bi]
            P.wait(nc.tensor, "cV", pclean[bi])
            nc.tensor.matmul(bank[:, :], b1_sb[:, m * 128:(m + 1) * 128],
                             ones_row[:, :], start=True, stop=False)
            for kc in range(KC):
                mm = P.inc(nc.tensor.matmul(
                    bank[:, :],
                    w_sb[:, kc * 3 * D + m * 128: kc * 3 * D + (m + 1) * 128],
                    xhat[:, kc * TOK + tt * 512: kc * TOK + (tt + 1) * 512],
                    start=False, stop=(kc == KC - 1)), "cT")
            P.wait(nc.vector, "cT", mm)
            P.wait(nc.vector, "dS", st)
            cv = P.inc(nc.vector.tensor_copy(ev[:], bank[:, :]), "cV")
            pclean[bi] = cv
            P.wait(nc.sync, "cV", cv)
            st = P.dma(nc.sync, "dS", qkvT[m * 128:(m + 1) * 128, tt * 512:(tt + 1) * 512], ev[:])
    nc.sync.wait_ge(P.sem("dS"), P.cnt["dS"])
    return P


def build_l0attn(P):
    nc = P.nc
    S = 4096
    NCH = S // 128
    qT = nc.dram_tensor("qT", [128, NSEQ * S], BF16, kind="ExternalInput").ap()
    kT = nc.dram_tensor("kT", [128, NSEQ * S], BF16, kind="ExternalInput").ap()
    v_in = nc.dram_tensor("v_in", [NSEQ * S, 132], BF16, kind="ExternalInput").ap()
    attn_o = nc.dram_tensor("attn_o", [NSEQ * S, 128], BF16, kind="ExternalOutput").ap()

    q_sb = P.sb("q_sb", [128, S], BF16)
    k_sb = P.sb("k_sb", [128, S], BF16)
    v_sb = P.sb("v_sb", [128, NCH * 132], BF16)
    prT = [P.sb(f"prT{i}", [128, 512], BF16) for i in range(2)]
    maskT = P.sb("maskT", [128, 128], BF16)
    idf = P.sb("idf", [128, 128], F32)
    out_sb = P.sb("out_sb", [128, 8 * 64], BF16)
    rcol = P.sb("rcol", [128, 1], F32)
    pS_ = [P.psum(f"pS{i}", [128, 512], F32) for i in range(2)]
    pO = [P.psum(f"pO{i}", [128, 264], F32) for i in range(2)]

    P.block()
    P.inc(nc.gpsimd.iota(idf[:], [[-1, 128]], base=0, channel_multiplier=1,
                         allow_small_or_imprecise_dtypes=True), "cG")
    P.wait(nc.vector, "cG", P.cnt["cG"])
    P.inc(nc.vector.tensor_scalar(maskT[:], idf[:], 0.0, None, ALU.is_le), "cV")
    INV8 = float(1.0 / 8.0)
    ld = 0
    st = 0
    for s in range(NSEQ):
        for h in range(2):
            hb = h * 64
            if h == 0:
                P.wait(nc.sync, "cT", P.cnt["cT"])
                P.dma(nc.sync, "dS", q_sb[:, :], qT[:, s * S:(s + 1) * S])
                P.dma(nc.sync, "dS", k_sb[:, :], kT[:, s * S:(s + 1) * S])
                vv = v_in[s * S:(s + 1) * S, :].rearrange("(c p) w -> p c w", p=128)
                ld = P.dma(nc.sync, "dS", v_sb[:].rearrange("p (c w) -> p c w", w=132), vv)
            P.wait(nc.tensor, "dS", ld)
            for qs in range(8):
                po = pO[qs % 2]
                po_clean = P.cnt["cV"]
                first_kc = [True] * 4
                for kc in range(4 * qs + 4):
                    q0 = max(qs * 512, kc * 128)
                    w = qs * 512 + 512 - q0
                    bank = pS_[kc % 2]
                    prb = prT[kc % 2]
                    P.wait(nc.tensor, "cS", P.cnt["cS"])
                    smm = P.inc(nc.tensor.matmul(
                        bank[:, 0:w],
                        k_sb[hb:hb + 64, kc * 128:(kc + 1) * 128],
                        q_sb[hb:hb + 64, q0:q0 + w],
                        start=True, stop=True), "cT")
                    P.wait(nc.scalar, "cT", smm)
                    P.wait(nc.scalar, "cV", P.cnt["cV"])
                    ex = P.inc(nc.scalar.activation(prb[:, 0:w], bank[:, 0:w],
                                                    AF.Exp, scale=np.float32(INV8)), "cS")
                    mdone = ex
                    if q0 == kc * 128:
                        P.wait(nc.vector, "cS", ex)
                        mdone = P.inc(nc.vector.tensor_tensor(
                            prb[:, 0:128], prb[:, 0:128], maskT[:], ALU.mult), "cV")
                        P.wait(nc.tensor, "cV", mdone)
                    P.wait(nc.tensor, "cS", ex)
                    for j in range(4):
                        jq = qs * 512 + j * 128
                        if jq < q0:
                            continue
                        last = (kc == 4 * qs + 3)
                        av = P.inc(nc.tensor.matmul(
                            po[:, j * 66: j * 66 + 65],
                            prb[:, jq - q0: jq - q0 + 128],
                            v_sb[:, kc * 132 + h * 66: kc * 132 + h * 66 + 65],
                            start=(kc == 0 and j == 0), stop=last), "cT")
                        first_kc[j] = False
                P.wait(nc.vector, "cT", P.cnt["cT"])
                for j in range(4):
                    P.wait(nc.vector, "cV", P.cnt["cV"])
                    P.inc(nc.vector.reciprocal(rcol[:], po[:, j * 66 + 64: j * 66 + 65]), "cV")
                    P.wait(nc.vector, "cV", P.cnt["cV"])
                    P.wait(nc.vector, "dS", st)
                    P.inc(nc.vector.tensor_scalar(
                        out_sb[:, (qs % 2) * 256 + j * 64: (qs % 2) * 256 + (j + 1) * 64],
                        po[:, j * 66: j * 66 + 64], rcol[:, 0:1], None, ALU.mult), "cV")
                ev = P.cnt["cV"]
                if qs % 2 == 1:
                    P.wait(nc.sync, "cV", ev)
                    dst = attn_o[s * S + (qs - 1) * 512: s * S + (qs + 1) * 512,
                                 h * 64:(h + 1) * 64]
                    dd = dst.rearrange("(j p) c -> p j c", p=128)
                    st = P.dma(nc.sync, "dS", dd, out_sb[:].rearrange("p (j c) -> p j c", c=64))
                    nc.sync.wait_ge(P.sem("dS"), st)
    nc.sync.wait_ge(P.sem("dS"), P.cnt["dS"])
    return P


def build_l0proj(P):
    nc = P.nc
    xT = nc.dram_tensor("xT", [D, TOK], F32, kind="ExternalInput").ap()
    attnT = nc.dram_tensor("attnT", [D, TOK], BF16, kind="ExternalInput").ap()
    W2 = nc.dram_tensor("W2", [D, D], F32, kind="ExternalInput").ap()
    b2 = nc.dram_tensor("b2", [D], F32, kind="ExternalInput").ap()
    yT = nc.dram_tensor("yT", [D, TOK], F32, kind="ExternalOutput").ap()

    w_sb = P.sb("w_sb", [128, KC * D], BF16)
    a_sb = P.sb("a_sb", [128, KC * TOK], BF16)
    b2_sb = P.sb("b2_sb", [1, D], BF16)
    ones_row = P.sb("ones_row", [1, 512], BF16)
    res_t = P.sb("res_t", [128, 512], F32)
    yout = P.sb("yout", [128, 512], F32)
    pg = [P.psum(f"pg{i}", [128, 512], F32) for i in range(4)]

    P.block()
    P.inc(nc.vector.memset(ones_row[:], 1.0), "cV")
    for kc in range(KC):
        P.dma(nc.gpsimd, "dG", w_sb[:, kc * D:(kc + 1) * D],
                                  W2[kc * 128:(kc + 1) * 128, :])
        P.dma(nc.gpsimd, "dG", a_sb[:, kc * TOK:(kc + 1) * TOK],
                                  attnT[kc * 128:(kc + 1) * 128, :])
    P.dma(nc.gpsimd, "dG", b2_sb[:, :], b2.unsqueeze(0))
    w_done = P.cnt["dG"]
    P.wait(nc.tensor, "dG", w_done)
    P.wait(nc.tensor, "cV", 1)
    pclean = [0] * 4
    res_busy = 0
    yst = 0
    for m in range(KC):
        for tt in range(TOK // 512):
            bi = (m * 4 + tt) % 4
            bank = pg[bi]
            P.wait(nc.tensor, "cV", pclean[bi])
            nc.tensor.matmul(bank[:, :], b2_sb[:, m * 128:(m + 1) * 128],
                             ones_row[:, :], start=True, stop=False)
            for kc in range(KC):
                mm = P.inc(nc.tensor.matmul(
                    bank[:, :],
                    w_sb[:, kc * D + m * 128: kc * D + (m + 1) * 128],
                    a_sb[:, kc * TOK + tt * 512: kc * TOK + (tt + 1) * 512],
                    start=False, stop=(kc == KC - 1)), "cT")
            P.wait(nc.sync, "cV", res_busy)
            rld = P.dma(nc.sync, "dS", res_t[:], xT[m * 128:(m + 1) * 128, tt * 512:(tt + 1) * 512])
            P.wait(nc.vector, "cT", mm)
            P.wait(nc.vector, "dS", rld)
            P.wait(nc.vector, "dS", yst)
            cv = P.inc(nc.vector.tensor_tensor(yout[:], bank[:, :], res_t[:], ALU.add), "cV")
            pclean[bi] = cv
            res_busy = cv
            P.wait(nc.sync, "cV", cv)
            yst = P.dma(nc.sync, "dS", yT[m * 128:(m + 1) * 128, tt * 512:(tt + 1) * 512], yout[:])
    nc.sync.wait_ge(P.sem("dS"), P.cnt["dS"])
    return P


def run_l0attn_all(xT_shards, W1, b1, W2, b2):
    import ml_dtypes
    P = Prog(); build_l0qkv(P); P.close()
    in_maps = [{"xT": s, "W1": W1, "b1": b1} for s in xT_shards]
    res = run_bass_kernel_spmd(P.nc, in_maps, core_ids=list(range(NCORE)))
    qkv = [r["qkvT"] for r in res.results]
    S = 4096
    qkv_full = np.concatenate(qkv, axis=1)
    cols = np.empty((NCORE * TOK,), np.int64)
    idx = np.arange(NCORE * TOK)
    core, rem = idx // TOK, idx % TOK
    s_, p_ = rem // POS, rem % POS
    gcol = s_ * S + core * POS + p_
    qkv_g = np.empty_like(qkv_full)
    qkv_g[:, gcol] = qkv_full[:, idx]
    in_maps = []
    for c in range(NCORE):
        h0 = 2 * c
        qTh = np.ascontiguousarray(np.concatenate(
            [qkv_g[h * 192: h * 192 + 64, :] for h in (h0, h0 + 1)], axis=0))
        kTh = np.ascontiguousarray(np.concatenate(
            [qkv_g[h * 192 + 64: h * 192 + 128, :] for h in (h0, h0 + 1)], axis=0))
        v_rows = np.concatenate(
            [qkv_g[h * 192 + 128: h * 192 + 192, :] for h in (h0, h0 + 1)], axis=0).T
        v_in = np.zeros((NSEQ * S, 132), dtype=ml_dtypes.bfloat16)
        v_in[:, 0:64] = v_rows[:, 0:64]
        v_in[:, 64] = 1.0
        v_in[:, 66:130] = v_rows[:, 64:128]
        v_in[:, 130] = 1.0
        in_maps.append({"qT": qTh, "kT": kTh, "v_in": v_in})
    P = Prog(); build_l0attn(P); P.close()
    res = run_bass_kernel_spmd(P.nc, in_maps, core_ids=list(range(NCORE)))
    attn = [r["attn_o"] for r in res.results]
    attn_g = np.concatenate(attn, axis=1)
    attn_T = np.ascontiguousarray(attn_g.T)
    in_maps = []
    for c in range(NCORE):
        cols_c = (np.arange(NSEQ)[:, None] * S + c * POS + np.arange(POS)[None, :]).reshape(-1)
        aT = np.ascontiguousarray(attn_T[:, cols_c])
        in_maps.append({"xT": xT_shards[c], "attnT": aT, "W2": W2, "b2": b2})
    P = Prog(); build_l0proj(P); P.close()
    res = run_bass_kernel_spmd(P.nc, in_maps, core_ids=list(range(NCORE)))
    return [r["yT"] for r in res.results]



NWQ = 2816 // NCORE
NDQ = D // NCORE


def fused_aux(P):
    aux = {
        "ones_bf": P.sb("g_ones_bf", [128, 1], BF16),
        "ones_f": P.sb("g_ones_f", [1, 128], F32),
        "xbf": P.sb("g_ln_xbf", [128, 512], BF16),
        "sq": P.sb("g_ln_sq", [128, 512], BF16),
        "arow": P.sb("g_ln_arow", [1, 512], F32),
        "brow": P.sb("g_ln_brow", [1, 512], F32),
        "mrow": P.sb("g_ln_mrow", [1, 512], F32),
        "init": False,
    }
    aux["ones_row"] = P.sb("g_ones_row", [1, 512], BF16)
    aux["ident"] = P.sb("g_ident", [128, 128], BF16)
    return aux


def emit_fused_init(P, aux):
    nc = P.nc
    P.inc(nc.vector.memset(aux["ones_bf"][:], 1.0), "cV")
    P.inc(nc.vector.memset(aux["ones_f"][:], 1.0), "cV")
    P.inc(nc.vector.memset(aux["ones_row"][:], 1.0), "cV")
    idf = P.sb("g_idf", [128, 128], F32)
    P.inc(nc.gpsimd.iota(idf[:], [[-1, 128]], base=0, channel_multiplier=1,
                         allow_small_or_imprecise_dtypes=True), "cG")
    P.wait(nc.vector, "cG", P.cnt["cG"])
    P.inc(nc.vector.tensor_scalar(aux["ident"][:], idf[:], 0.0, None, ALU.is_equal), "cV")
    aux["init"] = True
    aux["init_cv"] = P.cnt["cV"]


def emit_prep_f(P, aux, wgs, wus, wds, agw):
    nc = P.nc
    with P.scope():
        xin = P.sb("p_xin", [128, 2816], F32)
        qout = P.sb("p_qout", [128, 2816], BF16)
        scr = {
            "a": P.sb("p_qa", [128, 2816], F32), "t": P.sb("p_qt", [128, 2816], F32),
            "u": P.sb("p_qu", [128, 2816], F32), "bm": P.sb("p_qbm", [128, 176], F32),
            "scale": P.sb("p_qsc", [128, 176], F32), "rsc": P.sb("p_qrs", [128, 176], F32),
        }
        jobs = []
        for r0 in range(0, NWQ, 128):
            npart = min(128, NWQ - r0)
            jobs.append((wgs, agw["g_in"], r0, npart, D))
            jobs.append((wus, agw["u_in"], r0, npart, D))
        jobs.append((wds, agw["d_in"], 0, NDQ, 2816))
        st_done = 0
        for (src_, dst, r0, npart, F) in jobs:
            P.wait(nc.sync, "cV", scr.get("busy", 0))
            P.wait(nc.sync, "cS", scr.get("busy_s", 0))
            ld = P.dma(nc.sync, "dS", xin[:npart, :F], src_[r0:r0 + npart, :])
            emit_quant(P, xin[:npart, :F], qout[:npart, :F], scr, npart, F // 16,
                       after=[("dS", ld)])
            P.wait(nc.sync, "cV", P.cnt["cV"])
            st_done = P.dma(nc.sync, "dS", dst[r0:r0 + npart, :], qout[:npart, :F])
        P.wait(nc.gpsimd, "dS", st_done)
        for nm in ("g", "u", "d"):
            P.inc(nc.gpsimd.collective_compute(
                "AllGather", ALU.bypass, replica_groups=[list(range(NCORE))],
                ins=[agw[nm + "_in"].ap().opt()], outs=[agw[nm + "_out"].ap().opt()]), "cC")
    P.barrier()


def emit_l0ffn_f(P, aux, xin_d, yout_d, Wd, rbf_d=None):
    nc = P.nc
    with P.scope():
        wg_sb = P.sb("f_wg", [128, KC * 3 * D], BF16)
        ug_sb = P.sb("f_ug", [128, KC * 3 * D], BF16)
        big = P.sb("f_big", [128, 16384], BF16)
        gf = big[:, 0:KC * POS]; gi = big[:, KC * POS:2 * KC * POS]; gg = big[:, 2 * KC * POS:3 * KC * POS]
        xhat = P.sb("f_xhat", [128, KC * TOK], BF16)
        c_sb = P.sb("f_c", [128, KC * POS], BF16)
        tmp1 = P.sb("f_tmp1", [128, POS], BF16)
        tmp2 = P.sb("f_tmp2", [128, POS], BF16)
        hout = P.sb("f_hout", [128, 512], F32)
        res_t = P.sb("f_res", [128, 512], F32)
        yout = P.sb("f_yout", [128, 512], F32)
        ybf = P.sb("f_ybf", [128, 512], BF16)
        bgs = P.sb("f_bgs", [128, 24], F32)
        bos = P.sb("f_bos", [128, KC], F32)
        incs = P.sb("f_incs", [128, KC], F32)
        pg = [P.psum(f"f_pg{i}", [128, 512], F32) for i in range(4)]
        aux.pop("stats_mm", None); aux.pop("rows_read", None)

        for kc in range(KC):
            P.dma_nb(nc.gpsimd, "dG", wg_sb[:, kc * 3 * D:(kc + 1) * 3 * D],
                     Wd["wg"][kc * 128:(kc + 1) * 128, :])
            P.dma_nb(nc.gpsimd, "dG", ug_sb[:, kc * 3 * D:(kc + 1) * 3 * D],
                     Wd["ug"][kc * 128:(kc + 1) * 128, :])
        w_done = P.dbar(nc.gpsimd, "dG")
        for j in range(24):
            P.dma_nb(nc.sync, "dS", bgs[:, j:j + 1], Wd["bg"][j * 128:(j + 1) * 128].unsqueeze(-1))
        for j in range(KC):
            P.dma_nb(nc.sync, "dS", bos[:, j:j + 1], Wd["bo"][j * 128:(j + 1) * 128].unsqueeze(-1))
            P.dma_nb(nc.sync, "dS", incs[:, j:j + 1], Wd["inc_"][j * 128:(j + 1) * 128].unsqueeze(-1))
        const_done = P.dbar(nc.sync, "dS")

        with P.scope():
            aux["xf32_t"] = P.sb("f_lnx", [128, KC * 512], F32)
            aux["pS"] = P.psum("f_pS", [1, 512], F32)
            aux["pQ"] = P.psum("f_pQ", [1, 512], F32)
            aux["pA"] = P.psum("f_pA", [128, 512], F32)
            aux["pB"] = P.psum("f_pB", [128, 512], F32)
            emit_layernorm(P, xin_d, xhat, aux)

        P.wait(nc.vector, "dS", const_done)
        for kc in range(KC):
            P.wait(nc.vector, "cV", P.cnt["cV"])
            P.inc(nc.vector.tensor_copy(
                c_sb[:, kc * POS:(kc + 1) * POS],
                incs[:, kc:kc + 1].broadcast_to([128, POS])), "cV")
        c_ready = P.cnt["cV"]

        P.wait(nc.tensor, "dG", w_done)
        gate_view = {0: gf, 1: gi, 2: gg}
        act_fn = {0: AF.Sigmoid, 1: AF.Sigmoid, 2: AF.Tanh}
        pclean = [0, 0, 0, 0]
        for t in range(4):
            P.wait(nc.tensor, "cV", c_ready)
            for gate in range(3):
                for m in range(KC):
                    bi = (gate * KC + m) % 4
                    bank = pg[bi]
                    P.wait(nc.tensor, "cS", pclean[bi])
                    for kc in range(KC):
                        nc.tensor.matmul(
                            bank[:, :],
                            wg_sb[:, kc * 3 * D + gate * D + m * 128: kc * 3 * D + gate * D + (m + 1) * 128],
                            xhat[:, kc * TOK + t * POS: kc * TOK + (t + 1) * POS],
                            start=(kc == 0), stop=False)
                    for kc in range(KC):
                        last = P.inc(nc.tensor.matmul(
                            bank[:, :],
                            ug_sb[:, kc * 3 * D + gate * D + m * 128: kc * 3 * D + gate * D + (m + 1) * 128],
                            c_sb[:, kc * POS:(kc + 1) * POS],
                            start=False, stop=(kc == KC - 1)), "cT")
                    P.wait(nc.scalar, "cT", last)
                    pclean[bi] = P.inc(nc.scalar.activation(
                        gate_view[gate][:, m * POS:(m + 1) * POS], bank[:, :],
                        act_fn[gate], bias=bgs[:, gate * KC + m: gate * KC + m + 1]), "cS")
            gates_done = P.cnt["cS"]
            all_mm = P.cnt["cT"]
            P.wait(nc.vector, "cS", gates_done)
            P.wait(nc.vector, "cT", all_mm)
            for m in range(KC):
                sl = slice(m * POS, (m + 1) * POS)
                P.wait(nc.vector, "cV", P.cnt["cV"])
                P.inc(nc.vector.tensor_tensor(tmp1[:], gi[:, sl], gg[:, sl], ALU.mult), "cV")
                P.wait(nc.vector, "cV", P.cnt["cV"])
                P.inc(nc.vector.tensor_tensor(tmp2[:], gf[:, sl], c_sb[:, sl], ALU.mult), "cV")
                P.wait(nc.vector, "cV", P.cnt["cV"])
                P.inc(nc.vector.tensor_tensor(c_sb[:, sl], tmp1[:], tmp2[:], ALU.add), "cV")
            c_ready = P.cnt["cV"]

        P.wait(nc.gpsimd, "cV", c_ready)
        wo_sb = big[:, 0:KC * D]
        uo_sb = big[:, KC * D:2 * KC * D]
        for kc in range(KC):
            P.dma_nb(nc.gpsimd, "dG", wo_sb[:, kc * D:(kc + 1) * D],
                     Wd["wo"][kc * 128:(kc + 1) * 128, :])
            P.dma_nb(nc.gpsimd, "dG", uo_sb[:, kc * D:(kc + 1) * D],
                     Wd["uo"][kc * 128:(kc + 1) * 128, :])
        wo_done = P.dbar(nc.gpsimd, "dG")

        P.wait(nc.tensor, "dG", wo_done)
        P.wait(nc.gpsimd, "cV", c_ready)
        res2 = P.sb("f_res2", [128, 512], F32)
        yout2 = P.sb("f_yout2", [128, 512], F32)
        ybf2 = P.sb("f_ybf2", [128, 512], BF16)
        hout2 = P.sb("f_hout2", [128, 512], F32)
        res_ld = [0, 0]
        y_free = [0, 0]
        y_st = [0, 0]
        h_free = [0, 0]
        for t in range(4):
            for m in range(KC):
                it = t * KC + m
                par = it % 2
                bi = it % 4
                bank = pg[bi]
                rt = res_t if par == 0 else res2
                yo = yout if par == 0 else yout2
                yb = ybf if par == 0 else ybf2
                ho = hout if par == 0 else hout2
                P.wait(nc.gpsimd, "cV", y_free[par])
                res_ld[par] = P.dma(nc.gpsimd, "dG",
                                    rt[:], xin_d[m * 128:(m + 1) * 128, t * POS:(t + 1) * POS])
                P.wait(nc.tensor, "cS", pclean[bi])
                for kc in range(KC):
                    nc.tensor.matmul(
                        bank[:, :],
                        wo_sb[:, kc * D + m * 128: kc * D + (m + 1) * 128],
                        xhat[:, kc * TOK + t * POS: kc * TOK + (t + 1) * POS],
                        start=(kc == 0), stop=False)
                for kc in range(KC):
                    last = P.inc(nc.tensor.matmul(
                        bank[:, :],
                        uo_sb[:, kc * D + m * 128: kc * D + (m + 1) * 128],
                        c_sb[:, kc * POS:(kc + 1) * POS],
                        start=False, stop=(kc == KC - 1)), "cT")
                P.wait(nc.scalar, "cT", last)
                P.wait(nc.scalar, "cV", h_free[par])
                pclean[bi] = P.inc(nc.scalar.activation(
                    ho[:], bank[:, :], AF.Tanh, bias=bos[:, m:m + 1]), "cS")
                P.wait(nc.vector, "cS", pclean[bi])
                P.wait(nc.vector, "dG", res_ld[par])
                P.wait(nc.vector, "dS", y_st[par])
                cv = P.inc(nc.vector.scalar_tensor_tensor(
                    yo[:], ho[:], 0.5, rt[:], ALU.mult, ALU.add), "cV")
                h_free[par] = cv
                y_free[par] = cv
                P.wait(nc.sync, "cV", cv)
                y_st[par] = P.dma(nc.sync, "dS",
                                  yout_d[m * 128:(m + 1) * 128, t * POS:(t + 1) * POS], yo[:])
                if rbf_d is not None:
                    P.wait(nc.vector, "cV", cv)
                    cb = P.inc(nc.vector.tensor_copy(yb[:], yo[:]), "cV")
                    y_free[par] = cb
                    P.wait(nc.sync, "cV", cb)
                    y_st[par] = P.dma(nc.sync, "dS",
                                      rbf_d[m * 128:(m + 1) * 128, t * POS:(t + 1) * POS], yb[:])
    P.barrier()


def emit_l0qkv_f(P, aux, xin_d, W1qk_d, b1qk_d, W1v_d, b1v_d, a2aqk, a2av):
    nc = P.nc
    with P.scope():
        wqk_sb = P.sb("qk_wqk", [128, KC * 2 * D], BF16)
        wv_sb = P.sb("qk_wv", [128, KC * D], BF16)
        xhat = P.sb("qk_xhat", [128, KC * TOK], BF16)
        bqk_sb = P.sb("qk_bqk", [1, 2 * D], BF16)
        bv_sb = P.sb("qk_bv", [1, D], BF16)
        ev = P.sb("qk_ev", [128, 512], BF16)
        aux["xf32_t"] = P.sb("qk_lnx", [128, KC * 512], F32)
        aux["pS"] = P.psum("qk_pS", [1, 512], F32)
        aux["pQ"] = P.psum("qk_pQ", [1, 512], F32)
        aux["pA"] = P.psum("qk_pA", [128, 512], F32)
        aux["pB"] = P.psum("qk_pB", [128, 512], F32)
        aux.pop("stats_mm", None); aux.pop("rows_read", None)
        pg = [P.psum(f"qk_pg{i}", [128, 512], F32) for i in range(4)]

        for kc in range(KC):
            P.dma_nb(nc.gpsimd, "dG", wqk_sb[:, kc * 2 * D:(kc + 1) * 2 * D],
                     W1qk_d[kc * 128:(kc + 1) * 128, :])
            P.dma_nb(nc.gpsimd, "dG", wv_sb[:, kc * D:(kc + 1) * D],
                     W1v_d[kc * 128:(kc + 1) * 128, :])
        P.dma_nb(nc.gpsimd, "dG", bqk_sb[:, :], b1qk_d.unsqueeze(0))
        P.dma_nb(nc.gpsimd, "dG", bv_sb[:, :], b1v_d.unsqueeze(0))
        w_done = P.dbar(nc.gpsimd, "dG")
        emit_layernorm(P, xin_d, xhat, aux)
        ln_done = P.cnt["cV"]
        P.wait(nc.tensor, "dG", w_done)
        P.wait(nc.tensor, "cV", ln_done)
        P.wait(nc.tensor, "cV", aux["init_cv"])
        pclean = [0] * 4
        st = 0
        for part in range(2):
            for j in range(8):
                for tt in range(TOK // 512):
                    bi = ((part * 8 + j) * 4 + tt) % 4
                    bank = pg[bi]
                    P.wait(nc.tensor, "cV", pclean[bi])
                    nc.tensor.matmul(
                        bank[:, :],
                        bqk_sb[:, part * D + j * 128: part * D + (j + 1) * 128],
                        aux["ones_row"][:, :], start=True, stop=False)
                    for kc in range(KC):
                        mm = P.inc(nc.tensor.matmul(
                            bank[:, :],
                            wqk_sb[:, kc * 2 * D + part * D + j * 128:
                                   kc * 2 * D + part * D + (j + 1) * 128],
                            xhat[:, kc * TOK + tt * 512: kc * TOK + (tt + 1) * 512],
                            start=False, stop=(kc == KC - 1)), "cT")
                    P.wait(nc.vector, "cT", mm)
                    P.wait(nc.vector, "dS", st)
                    cv = P.inc(nc.vector.tensor_copy(ev[:], bank[:, :]), "cV")
                    pclean[bi] = cv
                    P.wait(nc.sync, "cV", cv)
                    st = P.dma(nc.sync, "dS",
                               a2aqk["in"][256 * j + 128 * part: 256 * j + 128 * part + 128,
                                           tt * 512:(tt + 1) * 512],
                               ev[:])
        for ch in range(TOK // 128):
            for js in range(2):
                bi = (ch * 2 + js) % 4
                bank = pg[bi]
                P.wait(nc.tensor, "cV", pclean[bi])
                nc.tensor.matmul(bank[:, :], aux["ones_row"][:, 0:128],
                                 bv_sb[:, js * 512:(js + 1) * 512], start=True, stop=False)
                for kc in range(KC):
                    mm = P.inc(nc.tensor.matmul(
                        bank[:, :],
                        xhat[:, kc * TOK + ch * 128: kc * TOK + (ch + 1) * 128],
                        wv_sb[:, kc * D + js * 512: kc * D + (js + 1) * 512],
                        start=False, stop=(kc == KC - 1)), "cT")
                P.wait(nc.vector, "cT", mm)
                P.wait(nc.vector, "dS", st)
                cv = P.inc(nc.vector.tensor_copy(ev[:], bank[:, :]), "cV")
                pclean[bi] = cv
                P.wait(nc.sync, "cV", cv)
                dst = bass.AP(a2av["in"], (2048 * 4 * js + ch * 128) * 128,
                              [[128, 128], [2048 * 128, 4], [1, 128]])
                st = P.dma(nc.sync, "dS", dst,
                           ev[:].rearrange("p (pr w) -> p pr w", w=128))
        P.wait(nc.gpsimd, "dS", st)
        P.wait(nc.gpsimd, "cV", P.cnt["cV"])
        from concourse.aot_env import allow_ici_all_to_all
        with allow_ici_all_to_all():
            P.inc(nc.gpsimd.collective_compute(
                "AllToAll", ALU.bypass, replica_groups=[list(range(NCORE))],
                ins=[a2aqk["in"].ap().opt()], outs=[a2aqk["out"].ap().opt()]), "cC")
            P.inc(nc.gpsimd.collective_compute(
                "AllToAll", ALU.bypass, replica_groups=[list(range(NCORE))],
                ins=[a2av["in"].ap().opt()], outs=[a2av["out"].ap().opt()]), "cC")
        P.barrier()


def emit_l0attn_f(P, aux, a2aqk_out, a2av_out, a2aat_in, a2aat_out):
    nc = P.nc
    S = 4096
    with P.scope():
        q_b = [P.sb(f"at_q{i}", [128, S], BF16) for i in range(2)]
        k_b = [P.sb(f"at_k{i}", [128, S], BF16) for i in range(2)]
        v_b = [P.sb(f"at_v{i}", [128, 32 * 132], BF16) for i in range(2)]
        prb = [[P.sb(f"at_pr{h}{i}", [128, 512], BF16) for i in range(2)] for h in range(2)]
        maskT = P.sb("at_maskT", [128, 128], BF16)
        out_sb = [P.sb(f"at_out{h}", [128, 8 * 64], BF16) for h in range(2)]
        rcol = P.sb("at_rcol", [128, 1], F32)
        pS_ = [[P.psum(f"at_pS{h}{i}", [128, 512], F32) for i in range(2)] for h in range(2)]
        pO = [[P.psum(f"at_pO{h}{i}", [128, 264], F32) for i in range(2)] for h in range(2)]

        idf2 = P.sb("at_idf", [128, 128], F32)
        P.inc(nc.gpsimd.iota(idf2[:], [[-1, 128]], base=0, channel_multiplier=1,
                             allow_small_or_imprecise_dtypes=True), "cG")
        P.wait(nc.vector, "cG", P.cnt["cG"])
        P.inc(nc.vector.tensor_scalar(maskT[:], idf2[:], 0.0, None, ALU.is_le), "cV")
        for vb_ in v_b:
            vv = vb_[:].rearrange("p (c w) -> p c w", w=132)
            P.inc(nc.vector.memset(vv[:, :, 64:65], 1.0), "cV")
            P.inc(nc.vector.memset(vv[:, :, 130:131], 1.0), "cV")
        mask_done = P.cnt["cV"]
        P.wait(nc.sync, "cV", mask_done)

        INV8 = float(1.0 / 8.0)
        st = [0, 0]
        kv_free = [0, 0]
        ld_done = [0, 0]

        def emit_loads(s):
            par = s % 2
            q_sb, k_sb, v_sb = q_b[par], k_b[par], v_b[par]
            P.wait(nc.sync, "cT", kv_free[par])
            for cp in range(NCORE):
                P.dma_nb(nc.sync, "dS", q_sb[:, cp * 512:(cp + 1) * 512],
                         a2aqk_out[256 * cp: 256 * cp + 128, s * 512:(s + 1) * 512])
                P.dma_nb(nc.sync, "dS", k_sb[:, cp * 512:(cp + 1) * 512],
                         a2aqk_out[256 * cp + 128: 256 * cp + 256, s * 512:(s + 1) * 512])
            for ch in range(32):
                cp, rr = ch // 4, ch % 4
                dst = bass.AP(v_sb, ch * 132, [[32 * 132, 128], [66, 2], [1, 64]])
                P.dma_nb(nc.sync, "dS", dst,
                         a2av_out[2048 * cp + s * 512 + rr * 128:
                                  2048 * cp + s * 512 + (rr + 1) * 128, :])
            ld_done[par] = P.dbar(nc.sync, "dS")

        emit_loads(0)
        for s in range(NSEQ):
            par = s % 2
            q_sb, k_sb, v_sb = q_b[par], k_b[par], v_b[par]
            if s + 1 < NSEQ:
                emit_loads(s + 1)
            P.wait(nc.tensor, "dS", ld_done[par])
            s_free = [[0, 0], [0, 0]]
            p_free = [[0, 0], [0, 0]]
            pm_extra = [[0, 0], [0, 0]]
            ex_cnt = [[0, 0], [0, 0]]
            po_free = [[0, 0], [0, 0]]
            for qs in range(8):
                nkc = 4 * qs + 4
                first_av = [True, True]

                def emit_S(kc):
                    q0 = max(qs * 512, kc * 128)
                    w = qs * 512 + 512 - q0
                    for h in range(2):
                        hb = h * 64
                        bank = pS_[h][kc % 2]
                        P.wait(nc.tensor, "cS", s_free[h][kc % 2])
                        smm = P.inc(nc.tensor.matmul(
                            bank[0:128, 0:w] if False else bank[:, 0:w],
                            k_sb[hb:hb + 64, kc * 128:(kc + 1) * 128],
                            q_sb[hb:hb + 64, q0:q0 + w],
                            start=True, stop=True), "cT")
                        P.wait(nc.scalar, "cT", max(smm, p_free[h][kc % 2]))
                        ex = P.inc(nc.scalar.activation(
                            prb[h][kc % 2][:, 0:w], bank[:, 0:w],
                            AF.Exp, scale=np.float32(INV8)), "cS")
                        s_free[h][kc % 2] = ex
                        ex_cnt[h][kc % 2] = ex
                        if q0 == kc * 128:
                            P.wait(nc.vector, "cS", ex)
                            md = P.inc(nc.vector.tensor_tensor(
                                prb[h][kc % 2][:, 0:128], prb[h][kc % 2][:, 0:128],
                                maskT[:], ALU.mult), "cV")
                            pm_extra[h][kc % 2] = md

                def emit_AV(kc):
                    q0 = max(qs * 512, kc * 128)
                    w = qs * 512 + 512 - q0
                    for h in range(2):
                        po = pO[h][qs % 2]
                        P.wait(nc.tensor, "cS", ex_cnt[h][kc % 2])
                        P.wait(nc.tensor, "cV", pm_extra[h][kc % 2])
                        if first_av[h]:
                            P.wait(nc.tensor, "cV", po_free[h][qs % 2])
                        for j in range(4):
                            jq = qs * 512 + j * 128
                            if jq < q0:
                                continue
                            P.inc(nc.tensor.matmul(
                                po[:, j * 66: j * 66 + 65],
                                prb[h][kc % 2][:, jq - q0: jq - q0 + 128],
                                v_sb[:, kc * 132 + h * 66: kc * 132 + h * 66 + 65],
                                start=(first_av[h] and j == 0), stop=(kc == nkc - 1)), "cT")
                        first_av[h] = False
                        p_free[h][kc % 2] = P.cnt["cT"]

                for kc in range(nkc):
                    emit_S(kc)
                    if kc > 0:
                        emit_AV(kc - 1)
                emit_AV(nkc - 1)
                av_done = P.cnt["cT"]
                for h in range(2):
                    po = pO[h][qs % 2]
                    P.wait(nc.vector, "cT", av_done)
                    for j in range(4):
                        P.wait(nc.vector, "cV", P.cnt["cV"])
                        P.inc(nc.vector.reciprocal(rcol[:], po[:, j * 66 + 64: j * 66 + 65]), "cV")
                        P.wait(nc.vector, "cV", P.cnt["cV"])
                        P.wait(nc.vector, "dS", st[h])
                        P.inc(nc.vector.tensor_scalar(
                            out_sb[h][:, (qs % 2) * 256 + j * 64: (qs % 2) * 256 + (j + 1) * 64],
                            po[:, j * 66: j * 66 + 64], rcol[:, 0:1], None, ALU.mult), "cV")
                    po_free[h][qs % 2] = P.cnt["cV"]
                if qs % 2 == 1:
                    for h in range(2):
                        hb = h * 64
                        ev2 = po_free[h][1]
                        P.wait(nc.sync, "cV", ev2)
                        for jj in range(2):
                            base = (2048 * (qs - 1 + jj) + s * 512) * 128 + hb
                            dstp = bass.AP(a2aat_in, base,
                                           [[128, 128], [128 * 128, 4], [1, 64]])
                            st[h] = P.inc(nc.sync.dma_start(
                                dstp, out_sb[h][:, jj * 256:(jj + 1) * 256]
                                .rearrange("p (jr c) -> p jr c", c=64)), "dS")
                        nc.sync.wait_ge(P.sem("dS"), st[h])
            kv_free[par] = P.cnt["cT"]
        from concourse.aot_env import allow_ici_all_to_all
        P.wait(nc.gpsimd, "dS", P.cnt["dS"])
        with allow_ici_all_to_all():
            P.inc(nc.gpsimd.collective_compute(
                "AllToAll", ALU.bypass, replica_groups=[list(range(NCORE))],
                ins=[a2aat_in.ap().opt()], outs=[a2aat_out.ap().opt()]), "cC")
        P.barrier()


def emit_l0proj_f(P, aux, xin_d, a2aat_out, W2_d, b2_d, yout_d):
    nc = P.nc
    with P.scope():
        w_sb = P.sb("pj_w", [128, KC * D], BF16)
        a_sb = P.sb("pj_a", [128, KC * TOK], BF16)
        b2_sb = P.sb("pj_b2", [1, D], BF16)
        res_t = P.sb("pj_res", [128, 512], F32)
        yout = P.sb("pj_yout", [128, 512], F32)
        pg = [P.psum(f"pj_pg{i}", [128, 512], F32) for i in range(4)]

        for kc in range(KC):
            P.dma_nb(nc.gpsimd, "dG", w_sb[:, kc * D:(kc + 1) * D],
                     W2_d[kc * 128:(kc + 1) * 128, :])
        P.dma_nb(nc.gpsimd, "dG", b2_sb[:, :], b2_d.unsqueeze(0))
        w_done = P.dbar(nc.gpsimd, "dG")
        for kc in range(KC):
            for tt in range(TOK // 512):
                P.dma_tnb(nc.sync, "dS", a_sb[:, kc * TOK + tt * 512: kc * TOK + (tt + 1) * 512],
                          a2aat_out[kc * 2048 + tt * 512: kc * 2048 + (tt + 1) * 512, :])
        a_done = P.dbar(nc.sync, "dS")
        P.wait(nc.tensor, "dG", w_done)
        P.wait(nc.tensor, "dS", a_done)
        P.wait(nc.tensor, "cV", aux["init_cv"])
        pclean = [0] * 4
        res_busy = 0
        yst = 0
        for m in range(KC):
            for tt in range(TOK // 512):
                bi = (m * 4 + tt) % 4
                bank = pg[bi]
                P.wait(nc.tensor, "cV", pclean[bi])
                nc.tensor.matmul(bank[:, :], b2_sb[:, m * 128:(m + 1) * 128],
                                 aux["ones_row"][:, :], start=True, stop=False)
                for kc in range(KC):
                    mm = P.inc(nc.tensor.matmul(
                        bank[:, :],
                        w_sb[:, kc * D + m * 128: kc * D + (m + 1) * 128],
                        a_sb[:, kc * TOK + tt * 512: kc * TOK + (tt + 1) * 512],
                        start=False, stop=(kc == KC - 1)), "cT")
                P.wait(nc.sync, "cV", res_busy)
                rld = P.dma(nc.sync, "dS",
                            res_t[:], xin_d[m * 128:(m + 1) * 128, tt * 512:(tt + 1) * 512])
                P.wait(nc.vector, "cT", mm)
                P.wait(nc.vector, "dS", rld)
                P.wait(nc.vector, "dS", yst)
                cv = P.inc(nc.vector.tensor_tensor(yout[:], bank[:, :], res_t[:], ALU.add), "cV")
                pclean[bi] = cv
                res_busy = cv
                P.wait(nc.sync, "cV", cv)
                yst = P.dma(nc.sync, "dS",
                            yout_d[m * 128:(m + 1) * 128, tt * 512:(tt + 1) * 512], yout[:])
    P.barrier()


def emit_l1ffn_f(P, aux, xin_d, rbf_d, agw, yout_d, xq_d, hq_d, rbf_out=None):
    nc = P.nc
    HT = 256
    with P.scope():
        wbig = P.sb("l1_wbig", [128, 45056], BF16)
        wgq_sb = wbig[:, 0:KC * 2816]
        wuq_sb = wbig[:, KC * 2816:2 * KC * 2816]
        wdq_sb = wbig[:, 0:22 * D]
        xqT = P.sb("l1_xqT", [128, KC * TOK], BF16)
        pb = [P.psum(f"l1_pb{i}", [128, 512], F32) for i in range(6)]
        tps = [P.psum(f"l1_tp{i}", [128, 128], BF16) for i in range(2)]
        scopeA = P.scope()
        scopeA.__enter__()
        xtb = P.sb("l1_xtb", [128, D], BF16)
        xtc = P.sb("l1_xtc", [128, D], F32)
        sqc = P.sb("l1_sqc", [128, D], BF16)
        xqc = P.sb("l1_xqc", [128, D], BF16)
        h_sb = P.sb("l1_h", [128, 2816], F32)
        hq_sb = P.sb("l1_hq", [128, 2816], BF16)
        sil = P.sb("l1_sil", [128, 512], F32)
        scr = {
            "a": P.sb("l1_qa", [128, 2816], F32), "t": P.sb("l1_qt", [128, 2816], F32),
            "u": P.sb("l1_qu", [128, 2816], F32), "bm": P.sb("l1_qbm", [128, 176], F32),
            "scale": P.sb("l1_qsc", [128, 176], F32), "rsc": P.sb("l1_qrs", [128, 176], F32),
        }
        cs1 = P.sb("l1_cs1", [128, 1], F32)
        cs2 = P.sb("l1_cs2", [128, 1], F32)
        mean = P.sb("l1_mean", [128, 1], F32)
        rstd = P.sb("l1_rstd", [128, 1], F32)
        bcol = P.sb("l1_bcol", [128, 1], F32)

        P.wait(nc.sync, "cC", P.cnt["cC"])
        for kc in range(KC):
            P.dma_tnb(nc.sync, "dS", wgq_sb[:, kc * 2816:(kc + 1) * 2816],
                      agw["g_out"].ap()[:, kc * 128:(kc + 1) * 128])
            P.dma_tnb(nc.sync, "dS", wuq_sb[:, kc * 2816:(kc + 1) * 2816],
                      agw["u_out"].ap()[:, kc * 128:(kc + 1) * 128])
        wA_done = P.dbar(nc.sync, "dS")

        pclean_v = [0] * 6
        pclean_s = [0] * 6
        sil_busy = 0
        tp_free = [0, 0]
        xqT_ready = 0
        hq_st = 0
        P.wait(nc.tensor, "dS", wA_done)
        for i in range(TOK // 128):
            P.wait(nc.sync, "cG", scr.get("xtb_free", 0))
            ld = P.dma_t(nc.sync, "dS", xtb[:], rbf_d[:, i * 128:(i + 1) * 128])
            P.wait(nc.gpsimd, "dS", ld)
            P.wait(nc.gpsimd, "cV", scr.get("busy", 0))
            P.wait(nc.gpsimd, "cS", scr.get("busy_s", 0))
            xc = P.inc(nc.gpsimd.tensor_copy(xtc[:], xtb[:]), "cG")
            scr["xtb_free"] = xc
            P.wait(nc.vector, "cG", xc)
            P.inc(nc.vector.tensor_reduce(cs1[:], xtc[:], AX.X, ALU.add), "cV")
            P.wait(nc.scalar, "cG", xc)
            sq_done = P.inc(nc.scalar.activation(sqc[:], xtc[:], AF.Square), "cS")
            P.wait(nc.vector, "cS", sq_done)
            P.inc(nc.vector.tensor_reduce(cs2[:], sqc[:], AX.X, ALU.add), "cV")
            P.wait(nc.vector, "cV", P.cnt["cV"])
            P.inc(nc.vector.tensor_scalar(mean[:], cs1[:], 1.0 / D, None, ALU.mult), "cV")
            P.wait(nc.vector, "cV", P.cnt["cV"])
            P.inc(nc.vector.tensor_scalar(cs2[:], cs2[:], 1.0 / D, None, ALU.mult), "cV")
            P.wait(nc.vector, "cV", P.cnt["cV"])
            P.inc(nc.vector.tensor_tensor(cs1[:], mean[:], mean[:], ALU.mult), "cV")
            P.wait(nc.vector, "cV", P.cnt["cV"])
            P.inc(nc.vector.tensor_tensor(cs2[:], cs1[:], cs2[:], ALU.subtract), "cV")
            P.wait(nc.scalar, "cV", P.cnt["cV"])
            P.inc(nc.scalar.activation(cs2[:], cs2[:], AF.Ln,
                                       bias=np.float32(1e-5), scale=np.float32(-1.0)), "cS")
            P.wait(nc.scalar, "cS", P.cnt["cS"])
            sd = P.inc(nc.scalar.activation(rstd[:], cs2[:], AF.Exp, scale=np.float32(-0.5)), "cS")
            P.wait(nc.vector, "cS", sd)
            P.inc(nc.vector.scalar_tensor_tensor(bcol[:], mean[:], -1.0, rstd[:],
                                                 ALU.mult, ALU.mult), "cV")
            P.wait(nc.vector, "cV", P.cnt["cV"])
            P.inc(nc.vector.tensor_scalar(xtc[:], xtc[:], rstd[:, 0:1], bcol[:, 0:1],
                                          ALU.mult, ALU.add), "cV")
            emit_quant(P, xtc[:, :], xqc[:, :], scr, 128, D // 16,
                       after=[("cV", P.cnt["cV"])])
            qv = P.cnt["cV"]
            P.wait(nc.tensor, "cV", qv)
            for kc in range(KC):
                tb = tps[kc % 2]
                P.wait(nc.tensor, "cS", tp_free[kc % 2])
                tpc = P.inc(nc.tensor.transpose(tb[:, :], xqc[:, kc * 128:(kc + 1) * 128],
                                                aux["ident"][:, :]), "cT")
                P.wait(nc.scalar, "cT", tpc)
                tp_free[kc % 2] = P.inc(nc.scalar.activation(
                    xqT[:, kc * TOK + i * 128: kc * TOK + (i + 1) * 128],
                    tb[:, :], AF.Copy), "cS")
            xqT_ready = P.cnt["cS"]
            for si, (n0, ns) in enumerate(NSL):
                bg_, bu_ = pb[2 * (si % 3)], pb[2 * (si % 3) + 1]
                P.wait(nc.tensor, "cS", xqT_ready)
                P.wait(nc.tensor, "cV", pclean_v[2 * (si % 3)])
                P.wait(nc.tensor, "cS", pclean_s[2 * (si % 3)])
                P.wait(nc.tensor, "cV", pclean_v[2 * (si % 3) + 1])
                P.wait(nc.tensor, "cS", pclean_s[2 * (si % 3) + 1])
                for kc in range(KC):
                    nc.tensor.matmul(bg_[:, 0:ns],
                                     xqT[:, kc * TOK + i * 128: kc * TOK + (i + 1) * 128],
                                     wgq_sb[:, kc * 2816 + n0: kc * 2816 + n0 + ns],
                                     start=(kc == 0), stop=(kc == KC - 1))
                for kc in range(KC):
                    mmu = P.inc(nc.tensor.matmul(bu_[:, 0:ns],
                                     xqT[:, kc * TOK + i * 128: kc * TOK + (i + 1) * 128],
                                     wuq_sb[:, kc * 2816 + n0: kc * 2816 + n0 + ns],
                                     start=(kc == 0), stop=(kc == KC - 1)), "cT")
                P.wait(nc.scalar, "cT", mmu)
                P.wait(nc.scalar, "cV", sil_busy)
                ssil = P.inc(nc.scalar.activation(sil[:, 0:ns], bg_[:, 0:ns], AF.Silu), "cS")
                pclean_s[2 * (si % 3)] = ssil
                pclean_s[2 * (si % 3) + 1] = ssil
                P.wait(nc.vector, "cS", ssil)
                sil_busy = P.inc(nc.vector.tensor_tensor(
                    h_sb[:, n0:n0 + ns], sil[:, 0:ns], bu_[:, 0:ns], ALU.mult), "cV")
                pclean_v[2 * (si % 3)] = sil_busy
                pclean_v[2 * (si % 3) + 1] = sil_busy
            P.wait(nc.vector, "dG", hq_st)
            emit_quant(P, h_sb[:, :], hq_sb[:, :], scr, 128, 176,
                       after=[("cV", P.cnt["cV"])])
            qv = P.cnt["cV"]
            P.wait(nc.gpsimd, "cV", qv)
            hq_st = P.dma(nc.gpsimd, "dG", hq_d.ap()[i * 128:(i + 1) * 128, :], hq_sb[:])
        hq_all = P.cnt["dG"]
        scopeA.__exit__(None, None, None)
        P.barrier()
        hql_b = [P.sb(f"l1_hql{i}", [128, 2 * 2816], BF16) for i in range(2)]
        hqT = P.sb("l1_hqT", [128, 22 * HT], BF16)
        res_t = P.sb("l1_res", [128, HT], F32)
        res2 = P.sb("l1_res2", [128, HT], F32)
        yout = P.sb("l1_yout", [128, HT], F32)
        yout2 = P.sb("l1_yout2", [128, HT], F32)
        ybf = P.sb("l1_ybf", [128, HT], BF16)
        ybf2 = P.sb("l1_ybf2", [128, HT], BF16)

        P.wait(nc.sync, "cT", P.cnt["cT"])
        for kk in range(22):
            P.dma_tnb(nc.sync, "dS", wdq_sb[:, kk * D:(kk + 1) * D],
                      agw["d_out"].ap()[:, kk * 128:(kk + 1) * 128])
        wB_done = P.dbar(nc.sync, "dS")
        P.wait(nc.sync, "dG", hq_all)
        res_busy = [0, 0]
        yst = [0, 0]
        hql_free = [0, 0]
        hql_ld = [0, 0]

        def load_hql(tt):
            par = tt % 2
            P.wait(nc.sync, "cT", hql_free[par])
            for half in range(2):
                P.dma_nb(nc.sync, "dS", hql_b[par][:, half * 2816:(half + 1) * 2816],
                         hq_d.ap()[tt * HT + half * 128: tt * HT + (half + 1) * 128, :])
            hql_ld[par] = P.dbar(nc.sync, "dS")

        load_hql(0)
        for tt in range(TOK // HT):
            par = tt % 2
            hql = hql_b[par]
            if tt + 1 < TOK // HT:
                load_hql(tt + 1)
            P.wait(nc.tensor, "dS", hql_ld[par])
            P.wait(nc.tensor, "cT", P.cnt["cT"])
            for kk in range(22):
                for half in range(2):
                    tb = tps[(kk * 2 + half) % 2]
                    P.wait(nc.tensor, "cS", tp_free[(kk * 2 + half) % 2])
                    tpc = P.inc(nc.tensor.transpose(
                        tb[:, :], hql[:, half * 2816 + kk * 128: half * 2816 + (kk + 1) * 128],
                        aux["ident"][:, :]), "cT")
                    P.wait(nc.scalar, "cT", tpc)
                    tp_free[(kk * 2 + half) % 2] = P.inc(nc.scalar.activation(
                        hqT[:, kk * HT + half * 128: kk * HT + (half + 1) * 128],
                        tb[:, :], AF.Copy), "cS")
            hqT_ready = P.cnt["cS"]
            hql_free[par] = P.cnt["cT"]
            P.wait(nc.tensor, "cS", hqT_ready)
            P.wait(nc.tensor, "dS", wB_done)
            for m in range(KC):
                it = tt * KC + m
                par = it % 2
                bi = 4 + (it % 2)
                bank = pb[bi]
                rt = res_t if par == 0 else res2
                yo = yout if par == 0 else yout2
                yb = ybf if par == 0 else ybf2
                P.wait(nc.gpsimd, "cV", res_busy[par])
                rld = P.dma(nc.gpsimd, "dG", rt[:, 0:HT],
                            xin_d[m * 128:(m + 1) * 128, tt * HT:(tt + 1) * HT])
                P.wait(nc.tensor, "cV", pclean_v[bi])
                for kk in range(22):
                    mm = P.inc(nc.tensor.matmul(bank[:, 0:HT],
                                    wdq_sb[:, kk * D + m * 128: kk * D + (m + 1) * 128],
                                    hqT[:, kk * HT:(kk + 1) * HT],
                                    start=(kk == 0), stop=(kk == 21)), "cT")
                P.wait(nc.vector, "cT", mm)
                P.wait(nc.vector, "dG", rld)
                P.wait(nc.vector, "dS", yst[par])
                cv = P.inc(nc.vector.scalar_tensor_tensor(
                    yo[:, 0:HT], bank[:, 0:HT], 0.5, rt[:, 0:HT],
                    ALU.mult, ALU.add), "cV")
                pclean_v[bi] = cv
                res_busy[par] = cv
                P.wait(nc.sync, "cV", cv)
                yst[par] = P.dma(nc.sync, "dS",
                                 yout_d[m * 128:(m + 1) * 128, tt * HT:(tt + 1) * HT],
                                 yo[:, 0:HT])
                if rbf_out is not None:
                    P.wait(nc.vector, "cV", cv)
                    cb = P.inc(nc.vector.tensor_copy(yb[:, 0:HT], yo[:, 0:HT]), "cV")
                    res_busy[par] = cb
                    P.wait(nc.sync, "cV", cb)
                    yst[par] = P.dma(nc.sync, "dS",
                                     rbf_out[m * 128:(m + 1) * 128, tt * HT:(tt + 1) * HT],
                                     yb[:, 0:HT])
    P.barrier()


def emit_l1attn_f(P, aux, xin_d, xkv_d, agh, halo_idx_d, hmask_d, Wd, bias0_d,
                  biasn_d, yout_d, rbf_out):
    nc = P.nc
    from concourse.bass import IndirectOffsetOnAxis
    with P.scope():
        xhat = P.sb("w_xhat", [128, KC * NTK], BF16)
        attnT = xhat[:, 0:KC * TOK]
        pr_bf = xhat[:, KC * TOK:KC * TOK + 2048]
        prT = xhat[:, KC * TOK + 2048:KC * TOK + 4096]
        qnT = P.sb("w_qnT", [128, KC * TOK], BF16)
        knT = P.sb("w_knT", [128, KC * NTK], BF16)
        v_sb = P.sb("w_v", [128, (NTK // 128) * D], BF16)
        w_sb = P.sb("w_w", [128, KC * D], BF16)
        bias_sb = P.sb("w_bias", [128, 2 * 8 * WIN], BF16)
        t1 = P.sb("w_t1", [128, 512], F32)
        rowr = P.sb("w_rowr", [128, 1], F32)
        rows_sb = P.sb("w_rows", [128, 1], F32)
        nw_sb = P.sb("w_nw", [128, 2 * KC], F32)
        qsc = P.sb("w_qsc", [128, 512], BF16)
        halo_sb = P.sb("w_halo", [128, 64], F32)
        halo2_sb = P.sb("w_halo2", [128, 64], F32)
        hzero_sb = P.sb("w_hzero", [128, 64], F32)
        hmask_sb = P.sb("w_hmask", [128, 64], mybir.dt.int32)
        hidx_sb = P.sb("w_hidx", [128, KC], mybir.dt.int32)
        ybf = P.sb("w_ybf", [128, 512], BF16)
        ps = [P.psum(f"w_ps{i}", [128, 512], F32) for i in range(4)]
        pav = [P.psum(f"w_pav{i}", [128, 512], F32) for i in range(2)]
        tps = [P.psum("w_tps0", [128, 128], BF16), P.psum("w_tps1", [128, 128], BF16)]
        aux["pA"] = pav[0]; aux["pB"] = pav[1]
        aux["pS"] = ps[0][0:1, :]; aux["pQ"] = ps[1][0:1, :]
        aux.pop("stats_mm", None); aux.pop("rows_read", None)
        rr_row = aux["mrow"]

        hsrc = bass.AP(xin_d.tensor if hasattr(xin_d, "tensor") else xin_d, 496,
                       [[TOK, D], [512, 4], [1, 16]])
        P.dma(nc.sync, "dS", agh["in"].ap(), hsrc)
        P.wait(nc.gpsimd, "dS", P.cnt["dS"])
        P.inc(nc.gpsimd.collective_compute(
            "AllGather", ALU.bypass, replica_groups=[list(range(NCORE))],
            ins=[agh["in"].ap().opt()], outs=[agh["out"].ap().opt()]), "cC")
        for s in range(NSEQ):
            P.dma_nb(nc.sync, "dS",
                     xkv_d.ap()[:, s * SEQT + 16: s * SEQT + 528],
                     xin_d[:, s * 512:(s + 1) * 512])
        P.wait(nc.vector, "cV", P.cnt["cV"])
        zz = P.inc(nc.vector.memset(t1[:, 0:448], 0.0), "cV")
        P.wait(nc.sync, "cV", zz)
        for kc in range(KC):
            dstp = bass.AP(xkv_d, kc * 128 * NTK + 528, [[NTK, 128], [SEQT, 4], [1, 112]])
            P.dma_nb(nc.sync, "dS", dstp, t1[:, 0:448].rearrange("p (s w) -> p s w", w=112))
        pads_done = P.dbar(nc.sync, "dS")
        hld = P.dma(nc.sync, "dS", hidx_sb[:, :], halo_idx_d)
        hld = P.dma(nc.sync, "dS", hmask_sb[:, :], hmask_d)
        P.wait(nc.vector, "dS", hld)
        P.wait(nc.vector, "cV", P.cnt["cV"])
        P.inc(nc.vector.memset(halo_sb[:], 0.0), "cV")
        P.wait(nc.vector, "cV", P.cnt["cV"])
        hz = P.inc(nc.vector.memset(hzero_sb[:], 0.0), "cV")
        P.wait(nc.gpsimd, "cV", hz)
        P.wait(nc.gpsimd, "cC", P.cnt["cC"])
        for kc in range(KC):
            g = P.inc(nc.gpsimd.indirect_dma_start(
                halo_sb[:, :], None, agh["out"].ap(),
                IndirectOffsetOnAxis(hidx_sb[:, kc:kc + 1], 0)), "dG")
            nc.gpsimd.wait_ge(P.sem("dG"), g)
            P.wait(nc.vector, "dG", g)
            P.wait(nc.vector, "cV", P.cnt["cV"])
            sel = P.inc(nc.vector.select(halo2_sb[:], hmask_sb[:], halo_sb[:],
                                         hzero_sb[:]), "cV")
            P.wait(nc.gpsimd, "cV", sel)
            dsth = bass.AP(xkv_d, kc * 128 * NTK, [[NTK, 128], [SEQT, 4], [1, 16]])
            P.dma(nc.gpsimd, "dG", dsth, halo2_sb[:].rearrange("p (s w) -> p s w", w=16))
        halo_done = P.cnt["dG"]
        P.wait(nc.sync, "dG", halo_done)

        P.dma_nb(nc.sync, "dS", bias_sb[:, 0:8 * WIN], bias0_d)
        P.dma_nb(nc.sync, "dS", bias_sb[:, 8 * WIN:16 * WIN], biasn_d)
        for j in range(KC):
            P.dma_nb(nc.sync, "dS", nw_sb[:, j:j + 1], Wd["qnw"][j * 128:(j + 1) * 128].unsqueeze(-1))
            P.dma_nb(nc.sync, "dS", nw_sb[:, KC + j:KC + j + 1], Wd["knw"][j * 128:(j + 1) * 128].unsqueeze(-1))
        const_ld = P.dbar(nc.sync, "dS")

        ln_scope = P.scope()
        ln_scope.__enter__()
        aux["xf32_t"] = P.sb("w_lnx", [128, KC * 512], F32)
        emit_layernorm(P, xkv_d.ap(), xhat, aux, ntok=NTK)
        ln_scope.__exit__(None, None, None)

        own_cols = [(s * SEQT + 16, 512) for s in range(NSEQ)]

        def proj(dst, w_dram, cols, rms_w_col, ncol_dst):
            P.wait(nc.gpsimd, "cT", P.cnt["cT"])
            for kc_ in range(KC):
                P.dma_nb(nc.gpsimd, "dG", w_sb[:, kc_ * D:(kc_ + 1) * D],
                         w_dram[kc_ * 128:(kc_ + 1) * 128, :])
            wd = P.dbar(nc.gpsimd, "dG")
            P.wait(nc.tensor, "dG", wd)
            pj_free = getattr(P, "_pj_free", [0, 0])
            for (c0, n, d0) in cols:
                for off in range(0, n, 512):
                    ns = min(512, n - off)
                    for m in range(KC):
                        bank = ps[2 + (m % 2)]
                        P.wait(nc.tensor, "cS", pj_free[m % 2])
                        for kc_ in range(KC):
                            mm = P.inc(nc.tensor.matmul(
                                bank[:, 0:ns],
                                w_sb[:, kc_ * D + m * 128:(kc_ * D + (m + 1) * 128)],
                                xhat[:, kc_ * NTK + c0 + off: kc_ * NTK + c0 + off + ns],
                                start=(kc_ == 0), stop=(kc_ == KC - 1)), "cT")
                        P.wait(nc.scalar, "cT", mm)
                        pj_free[m % 2] = P.inc(nc.scalar.activation(
                            dst[:, m * ncol_dst + d0 + off: m * ncol_dst + d0 + off + ns],
                            bank[:, 0:ns], AF.Copy), "cS")
                    P._pj_free = pj_free
                    if rms_w_col is not None:
                        evac = P.cnt["cS"]
                        for m in range(KC):
                            P.wait(nc.scalar, "cS", evac)
                            P.wait(nc.scalar, "cT", P.cnt["cT"])
                            sqd = P.inc(nc.scalar.activation(
                                qsc[:, 0:ns],
                                dst[:, m * ncol_dst + d0 + off: m * ncol_dst + d0 + off + ns],
                                AF.Square), "cS")
                            P.wait(nc.tensor, "cS", sqd)
                            P.wait(nc.tensor, "cV", P.cnt["cV"])
                            smm = P.inc(nc.tensor.matmul(ps[0][0:1, 0:ns], aux["ones_bf"][:, :],
                                                         qsc[:, 0:ns],
                                                         start=(m == 0), stop=(m == KC - 1)), "cT")
                        P.wait(nc.vector, "cT", smm)
                        P.inc(nc.vector.tensor_scalar(rr_row[:, 0:ns], ps[0][0:1, 0:ns],
                                                      1.0 / D, None, ALU.mult), "cV")
                        P.wait(nc.scalar, "cV", P.cnt["cV"])
                        P.inc(nc.scalar.activation(rr_row[:, 0:ns], rr_row[:, 0:ns],
                                                   AF.Ln, bias=np.float32(1e-6)), "cS")
                        P.wait(nc.scalar, "cS", P.cnt["cS"])
                        sdd = P.inc(nc.scalar.activation(rr_row[:, 0:ns], rr_row[:, 0:ns],
                                                         AF.Exp, scale=np.float32(-0.5)), "cS")
                        rrd = P.cnt["cV"]
                        P.wait(nc.tensor, "cV", rrd)
                        P.wait(nc.tensor, "cS", sdd)
                        bc = P.inc(nc.tensor.matmul(pav[0][:, 0:ns], aux["ones_f"][:, :],
                                                    rr_row[:, 0:ns], start=True, stop=True), "cT")
                        P.wait(nc.vector, "cT", bc)
                        for m in range(KC):
                            sl = dst[:, m * ncol_dst + d0 + off: m * ncol_dst + d0 + off + ns]
                            P.wait(nc.vector, "cV", P.cnt["cV"])
                            P.inc(nc.vector.tensor_tensor(t1[:, 0:ns], sl, pav[0][:, 0:ns],
                                                          ALU.mult), "cV")
                            P.wait(nc.vector, "cV", P.cnt["cV"])
                            P.inc(nc.vector.tensor_scalar(sl, t1[:, 0:ns],
                                                          nw_sb[:, rms_w_col + m:rms_w_col + m + 1],
                                                          None, ALU.mult), "cV")

        P.wait(nc.vector, "dS", const_ld)
        proj(knT, Wd["wk"], [(0, NTK, 0)], KC, NTK)
        proj(qnT, Wd["wq"], [(c, n, s * 512) for s, (c, n) in enumerate(own_cols)], 0, TOK)
        P.wait(nc.gpsimd, "cT", P.cnt["cT"])
        for kc_ in range(KC):
            P.dma_nb(nc.gpsimd, "dG", w_sb[:, kc_ * D:(kc_ + 1) * D],
                     Wd["wv"][kc_ * 128:(kc_ + 1) * 128, :])
        wd = P.dbar(nc.gpsimd, "dG")
        P.wait(nc.tensor, "dG", wd)
        vp_free = [0, 0]
        for ch in range(NTK // 128):
            for half in range(2):
                bank = ps[2 + half]
                P.wait(nc.tensor, "cS", vp_free[half])
                for kc_ in range(KC):
                    mm = P.inc(nc.tensor.matmul(
                        bank[:, 0:512],
                        xhat[:, kc_ * NTK + ch * 128: kc_ * NTK + (ch + 1) * 128],
                        w_sb[:, kc_ * D + half * 512: kc_ * D + half * 512 + 512],
                        start=(kc_ == 0), stop=(kc_ == KC - 1)), "cT")
                P.wait(nc.scalar, "cT", mm)
                vp_free[half] = P.inc(nc.scalar.activation(
                    v_sb[:, ch * D + half * 512: ch * D + half * 512 + 512],
                    bank[:, 0:512], AF.Copy), "cS")
        v_done = P.cnt["cS"]

        P.wait(nc.tensor, "cS", v_done)
        P.wait(nc.tensor, "cV", aux["init_cv"])
        INV = float(1.0 / np.sqrt(128.0))
        t1b = P.sb("w_t1b", [128, 512], F32)
        rsum8 = P.sb("w_rsum8", [128, 8], F32)
        rr8 = P.sb("w_rr8", [128, 8], F32)
        s_clean = [0] * 4
        t1_free = [0, 0]
        pr_free = 0
        tp_copy = [0, 0]
        prT_free = 0
        av_copy = [0, 0]
        for s in range(NSEQ):
            for qt in range(4):
                qcol = s * 512 + qt * 128
                kvcol = s * SEQT + qt * 128
                bvar = 0 if qt == 0 else 8 * WIN
                sc_mm = [0] * 4
                for h in range(8):
                    bank = ps[h // 2]
                    bo_ = (h % 2) * WIN
                    if h % 2 == 0:
                        P.wait(nc.tensor, "cV", s_clean[h // 2])
                    sc_mm[h // 2] = P.inc(nc.tensor.matmul(
                        bank[:, bo_:bo_ + WIN],
                        qnT[:, h * TOK + qcol: h * TOK + qcol + 128],
                        knT[:, h * NTK + kvcol: h * NTK + kvcol + WIN],
                        start=True, stop=True), "cT")
                for b in range(4):
                    tb1 = t1 if b % 2 == 0 else t1b
                    P.wait(nc.vector, "cT", sc_mm[b])
                    P.wait(nc.vector, "cS", t1_free[b % 2])
                    sc = P.inc(nc.vector.scalar_tensor_tensor(
                        tb1[:, 0:512], ps[b][:, 0:512], INV,
                        bias_sb[:, bvar + 2 * b * WIN: bvar + (2 * b + 2) * WIN],
                        ALU.mult, ALU.add), "cV")
                    s_clean[b] = sc
                    P.wait(nc.scalar, "cV", sc)
                    if b == 0:
                        P.wait(nc.scalar, "cT", pr_free)
                    ex = P.inc(nc.scalar.activation(pr_bf[:, 2 * b * WIN:(2 * b + 2) * WIN],
                                                    tb1[:, 0:512], AF.Exp), "cS")
                    t1_free[b % 2] = ex
                    P.wait(nc.vector, "cS", ex)
                    P.inc(nc.vector.tensor_reduce(
                        rsum8[:, 2 * b:2 * b + 2],
                        pr_bf[:, 2 * b * WIN:(2 * b + 2) * WIN].rearrange(
                            "p (h w) -> p h w", w=WIN),
                        AX.X, ALU.add), "cV")
                P.wait(nc.vector, "cV", P.cnt["cV"])
                P.inc(nc.vector.reciprocal(rr8[:, :], rsum8[:, :]), "cV")
                P.wait(nc.vector, "cV", P.cnt["cV"])
                for b in range(4):
                    P.inc(nc.vector.tensor_tensor(
                        pr_bf[:, 2 * b * WIN:(2 * b + 2) * WIN].rearrange(
                            "p (h w) -> p h w", w=WIN),
                        pr_bf[:, 2 * b * WIN:(2 * b + 2) * WIN].rearrange(
                            "p (h w) -> p h w", w=WIN),
                        rr8[:, 2 * b:2 * b + 2].unsqueeze(-1).broadcast_to([128, 2, WIN]),
                        ALU.mult), "cV")
                norm_done = P.cnt["cV"]
                P.wait(nc.tensor, "cV", norm_done)
                for b in range(16):
                    tb = tps[b % 2]
                    P.wait(nc.tensor, "cS", tp_copy[b % 2])
                    if b == 0:
                        P.wait(nc.tensor, "cT", prT_free)
                    tp = P.inc(nc.tensor.transpose(tb[:, :], pr_bf[:, b * 128:(b + 1) * 128],
                                                   aux["ident"][:, :]), "cT")
                    P.wait(nc.scalar, "cT", tp)
                    tp_copy[b % 2] = P.inc(nc.scalar.activation(
                        prT[:, b * 128:(b + 1) * 128], tb[:, :], AF.Copy), "cS")
                pr_free = P.cnt["cT"]
                prT_done = P.cnt["cS"]
                P.wait(nc.tensor, "cS", prT_done)
                for h in range(8):
                    bank = pav[h % 2]
                    c0 = 5 * s + qt
                    P.wait(nc.tensor, "cS", av_copy[h % 2])
                    nc.tensor.matmul(
                        bank[:, 0:128],
                        v_sb[:, c0 * D + h * 128:(c0 * D + (h + 1) * 128)],
                        prT[:, h * 256:h * 256 + 128],
                        start=True, stop=False)
                    av = P.inc(nc.tensor.matmul(
                        bank[:, 0:128],
                        v_sb[0:16, (c0 + 1) * D + h * 128:((c0 + 1) * D + (h + 1) * 128)],
                        prT[0:16, h * 256 + 128:h * 256 + 256],
                        start=False, stop=True), "cT")
                    P.wait(nc.scalar, "cT", av)
                    av_copy[h % 2] = P.inc(nc.scalar.activation(
                        attnT[:, h * TOK + qcol: h * TOK + qcol + 128],
                        bank[:, 0:128], AF.Copy), "cS")
                prT_free = P.cnt["cT"]
        attn_done_v = P.cnt["cV"]
        attn_done_s = P.cnt["cS"]

        P.wait(nc.gpsimd, "cT", P.cnt["cT"])
        for kc_ in range(KC):
            P.dma_nb(nc.gpsimd, "dG", w_sb[:, kc_ * D:(kc_ + 1) * D],
                     Wd["wo"][kc_ * 128:(kc_ + 1) * 128, :])
        wd = P.dbar(nc.gpsimd, "dG")
        P.wait(nc.tensor, "dG", wd)
        P.wait(nc.tensor, "cS", attn_done_s)
        res_busy = P.cnt["cV"]
        yst = 0
        for s in range(NSEQ):
            for m in range(KC):
                bank = ps[m % 4]
                P.wait(nc.tensor, "cV", P.cnt["cV"])
                for kc_ in range(KC):
                    mm = P.inc(nc.tensor.matmul(
                        bank[:, 0:512],
                        w_sb[:, kc_ * D + m * 128: kc_ * D + (m + 1) * 128],
                        attnT[:, kc_ * TOK + s * 512:(kc_ * TOK + (s + 1) * 512)],
                        start=(kc_ == 0), stop=(kc_ == KC - 1)), "cT")
                P.wait(nc.sync, "cV", res_busy)
                rld = P.dma(nc.sync, "dS", t1[:, 0:512],
                            xin_d[m * 128:(m + 1) * 128, s * 512:(s + 1) * 512])
                P.wait(nc.vector, "cT", mm)
                P.wait(nc.vector, "dS", rld)
                P.wait(nc.vector, "dS", yst)
                cv = P.inc(nc.vector.tensor_tensor(t1[:, 0:512],
                                                   bank[:, 0:512], t1[:, 0:512], ALU.add), "cV")
                res_busy = cv
                P.wait(nc.sync, "cV", cv)
                yst = P.dma(nc.sync, "dS",
                            yout_d[m * 128:(m + 1) * 128, s * 512:(s + 1) * 512], t1[:, 0:512])
                P.wait(nc.vector, "cV", cv)
                P.wait(nc.vector, "dS", P.cnt["dS"])
                cb = P.inc(nc.vector.tensor_copy(ybf[:], t1[:, 0:512]), "cV")
                res_busy = cb
                P.wait(nc.sync, "cV", cb)
                yst = P.dma(nc.sync, "dS",
                            rbf_out[m * 128:(m + 1) * 128, s * 512:(s + 1) * 512], ybf[:])
    P.barrier()


def build_fused(P, dbg=False):
    nc = P.nc
    xT = nc.dram_tensor("xT", [D, TOK], F32, kind="ExternalInput")
    l0 = {
        "wg": nc.dram_tensor("l0_wg", [D, 3 * D], F32, kind="ExternalInput").ap(),
        "ug": nc.dram_tensor("l0_ug", [D, 3 * D], F32, kind="ExternalInput").ap(),
        "wo": nc.dram_tensor("l0_wo", [D, D], F32, kind="ExternalInput").ap(),
        "uo": nc.dram_tensor("l0_uo", [D, D], F32, kind="ExternalInput").ap(),
        "bg": nc.dram_tensor("l0_bg", [3 * D], F32, kind="ExternalInput").ap(),
        "bo": nc.dram_tensor("l0_bo", [D], F32, kind="ExternalInput").ap(),
        "inc_": nc.dram_tensor("l0_inc", [D], F32, kind="ExternalInput").ap(),
    }
    W1qk_d = nc.dram_tensor("W1qk", [D, 2 * D], F32, kind="ExternalInput").ap()
    b1qk_d = nc.dram_tensor("b1qk", [2 * D], F32, kind="ExternalInput").ap()
    W1v_d = nc.dram_tensor("W1v", [D, D], F32, kind="ExternalInput").ap()
    b1v_d = nc.dram_tensor("b1v", [D], F32, kind="ExternalInput").ap()
    W2_d = nc.dram_tensor("W2", [D, D], F32, kind="ExternalInput").ap()
    b2_d = nc.dram_tensor("b2", [D], F32, kind="ExternalInput").ap()
    wgs = nc.dram_tensor("wgs", [NWQ, D], F32, kind="ExternalInput").ap()
    wus = nc.dram_tensor("wus", [NWQ, D], F32, kind="ExternalInput").ap()
    wds = nc.dram_tensor("wds", [NDQ, 2816], F32, kind="ExternalInput").ap()
    wat = {
        "wq": nc.dram_tensor("wq", [D, D], F32, kind="ExternalInput").ap(),
        "wk": nc.dram_tensor("wk", [D, D], F32, kind="ExternalInput").ap(),
        "wv": nc.dram_tensor("wv", [D, D], F32, kind="ExternalInput").ap(),
        "wo": nc.dram_tensor("wo1", [D, D], F32, kind="ExternalInput").ap(),
        "qnw": nc.dram_tensor("qnw", [D], F32, kind="ExternalInput").ap(),
        "knw": nc.dram_tensor("knw", [D], F32, kind="ExternalInput").ap(),
    }
    bias0_d = nc.dram_tensor("bias0", [128, 8 * WIN], BF16, kind="ExternalInput").ap()
    biasn_d = nc.dram_tensor("biasn", [128, 8 * WIN], BF16, kind="ExternalInput").ap()
    halo_idx_d = nc.dram_tensor("halo_idx", [128, KC], mybir.dt.int32, kind="ExternalInput").ap()
    hmask_d = nc.dram_tensor("hmask", [128, 64], mybir.dt.int32, kind="ExternalInput").ap()
    yT = nc.dram_tensor("yT", [D, TOK], F32, kind="ExternalOutput")

    rd = [nc.dram_tensor(f"rd{i}", [D, TOK], F32) for i in range(2)]
    rbf = nc.dram_tensor("rbf", [D, TOK], BF16)
    xq_d = nc.dram_tensor("xq_d", [TOK, D], BF16)
    hq_d = nc.dram_tensor("hq_d", [TOK, 2816], BF16)
    agw = {
        "g_in": nc.dram_tensor("agw_g_in", [NWQ, D], BF16),
        "g_out": nc.dram_tensor("agw_g_out", [2816, D], BF16, addr_space="Shared"),
        "u_in": nc.dram_tensor("agw_u_in", [NWQ, D], BF16),
        "u_out": nc.dram_tensor("agw_u_out", [2816, D], BF16, addr_space="Shared"),
        "d_in": nc.dram_tensor("agw_d_in", [NDQ, 2816], BF16),
        "d_out": nc.dram_tensor("agw_d_out", [D, 2816], BF16, addr_space="Shared"),
    }
    a2aqk = {"in": nc.dram_tensor("a2aqk_in", [TOK, TOK], BF16),
             "out": nc.dram_tensor("a2aqk_out", [TOK, TOK], BF16)}
    a2av = {"in": nc.dram_tensor("a2av_in", [NCORE * TOK, 128], BF16),
            "out": nc.dram_tensor("a2av_out", [NCORE * TOK, 128], BF16)}
    a2aat_in = nc.dram_tensor("a2aat_in", [NCORE * TOK, 128], BF16)
    a2aat_out = nc.dram_tensor("a2aat_out", [NCORE * TOK, 128], BF16)
    agh = {"in": nc.dram_tensor("agh_in", [D, 64], F32),
           "out": nc.dram_tensor("agh_out", [NCORE * D, 64], F32, addr_space="Shared")}
    xkv_d = nc.dram_tensor("xkv_d", [D, NTK], F32)

    P.block()
    aux = fused_aux(P)

    def snap(name, src_t):
        if not dbg:
            return
        dt_ = nc.dram_tensor(name, list(src_t.shape), src_t.dtype, kind="ExternalOutput")
        P.barrier()
        P.dma(nc.sync, "dS", dt_.ap(), src_t.ap())
        P.barrier()

    emit_fused_init(P, aux)
    emit_prep_f(P, aux, wgs, wus, wds, agw)
    emit_l0ffn_f(P, aux, xT, rd[0], l0)
    snap("s_ffn0", rd[0])
    emit_l0qkv_f(P, aux, rd[0], W1qk_d, b1qk_d, W1v_d, b1v_d, a2aqk, a2av)
    snap("s_qk", a2aqk["out"])
    snap("s_v", a2av["out"])
    emit_l0attn_f(P, aux, a2aqk["out"], a2av["out"], a2aat_in, a2aat_out)
    snap("s_at", a2aat_out)
    emit_l0proj_f(P, aux, rd[0], a2aat_out, W2_d, b2_d, rd[1])
    snap("s_proj", rd[1])
    emit_l0ffn_f(P, aux, rd[1], rd[0], l0, rbf_d=rbf)
    snap("s_ffn1", rd[0])
    emit_l1ffn_f(P, aux, rd[0], rbf, agw, rd[1], xq_d, hq_d)
    snap("s_l1f0", rd[1])
    snap("s_agwg", agw["g_out"])
    emit_l1attn_f(P, aux, rd[1], xkv_d, agh, halo_idx_d, hmask_d, wat, bias0_d,
                  biasn_d, rd[0], rbf)
    snap("s_l1at", rd[0])
    snap("s_xkv", xkv_d)
    emit_l1ffn_f(P, aux, rd[0], rbf, agw, yT, xq_d, hq_d)
    return P


def kernel(**inputs, ):
    return _kernel_impl(False, inputs)


def _kernel_impl(dbg, inputs):
    from concourse.aot_env import allow_ici_all_to_all
    x = np.asarray(inputs["L0_attn_x"], dtype=np.float32)
    W0 = {k: np.asarray(inputs[f"L0_ffn_{k}"], dtype=np.float32)
          for k in ["wf", "uf", "bf", "wi", "ui", "bi", "wc", "uc", "bc", "wo", "uo", "bo", "in_c"]}
    W1a = np.asarray(inputs["L0_attn_W1"], dtype=np.float32)
    b1a = np.asarray(inputs["L0_attn_b1"], dtype=np.float32)
    W2a = np.asarray(inputs["L0_attn_W2"], dtype=np.float32)
    b2a = np.asarray(inputs["L0_attn_b2"], dtype=np.float32)
    W1f = {k: np.asarray(inputs[f"L1_ffn_{k}"], dtype=np.float32)
           for k in ["w_gate", "w_up", "w_down"]}
    W1t = {k: np.asarray(inputs[f"L1_attn_{k}"], dtype=np.float32)
           for k in ["wq", "wk", "wv", "wo", "q_norm_w", "k_norm_w"]}

    with allow_ici_all_to_all():
        P = Prog()
        build_fused(P, dbg=dbg)
        P.close()

    w3 = W1a.reshape(1024, 16, 3, 64)
    W1qk = np.ascontiguousarray(
        np.concatenate([w3[:, :, 0, :].reshape(1024, 1024),
                        w3[:, :, 1, :].reshape(1024, 1024)], axis=1))
    W1v = np.ascontiguousarray(w3[:, :, 2, :].reshape(1024, 1024))
    b3 = b1a.reshape(16, 3, 64)
    b1qk = np.ascontiguousarray(
        np.concatenate([b3[:, 0, :].reshape(-1), b3[:, 1, :].reshape(-1)]))
    b1v = np.ascontiguousarray(b3[:, 2, :].reshape(-1))

    shards = shard_xT(x)
    wg = np.ascontiguousarray(np.concatenate([W0["wf"].T, W0["wi"].T, W0["wc"].T], axis=1))
    ug = np.ascontiguousarray(np.concatenate([W0["uf"].T, W0["ui"].T, W0["uc"].T], axis=1))
    common = {
        "l0_wg": wg, "l0_ug": ug,
        "l0_wo": np.ascontiguousarray(W0["wo"].T), "l0_uo": np.ascontiguousarray(W0["uo"].T),
        "l0_bg": np.ascontiguousarray(np.concatenate([W0["bf"], W0["bi"], W0["bc"]])),
        "l0_bo": np.ascontiguousarray(W0["bo"]), "l0_inc": np.ascontiguousarray(W0["in_c"]),
        "W1qk": W1qk, "b1qk": b1qk, "W1v": W1v, "b1v": b1v, "W2": W2a, "b2": b2a,
        "wq": np.ascontiguousarray(W1t["wq"].T), "wk": np.ascontiguousarray(W1t["wk"].T),
        "wv": np.ascontiguousarray(W1t["wv"].T), "wo1": np.ascontiguousarray(W1t["wo"].T),
        "qnw": np.ascontiguousarray(W1t["q_norm_w"]), "knw": np.ascontiguousarray(W1t["k_norm_w"]),
    }
    in_maps = []
    for c in range(NCORE):
        b0, bn = make_l1attn_bias(c)
        if c > 0:
            hidx = ((c - 1) * D + np.arange(KC)[None, :] * 128
                    + np.arange(128)[:, None]).astype(np.int32)
        else:
            hidx = np.zeros((128, KC), dtype=np.int32)
        in_maps.append({
            **common,
            "xT": shards[c],
            "wgs": np.ascontiguousarray(W1f["w_gate"][c * NWQ:(c + 1) * NWQ]),
            "wus": np.ascontiguousarray(W1f["w_up"][c * NWQ:(c + 1) * NWQ]),
            "wds": np.ascontiguousarray(W1f["w_down"][c * NDQ:(c + 1) * NDQ]),
            "bias0": b0, "biasn": bn,
            "halo_idx": np.ascontiguousarray(hidx),
            "hmask": np.full((128, 64), 0 if c == 0 else 1, dtype=np.int32),
        })
    with allow_ici_all_to_all():
        res = run_bass_kernel_spmd(P.nc, in_maps, core_ids=list(range(NCORE)))
    if dbg:
        return res.results
    return unshard_xT([r["yT"] for r in res.results])
